# version 07
# baseline (speedup 1.0000x reference)
import numpy as np
import concourse.bass as bass
import concourse.mybir as mybir

F32 = mybir.dt.float32
BF16 = mybir.dt.bfloat16
AF = mybir.ActivationFunctionType
ALU = mybir.AluOpType
AX = mybir.AxisListType
U32 = mybir.dt.uint32
from concourse.bass_isa_rust import ReduceOp as _ReduceOp
from concourse.library_config import attn as _LIB_ATTN
_RED_MAX = _ReduceOp.max if hasattr(_ReduceOp, 'max') else None
I32 = mybir.dt.int32

T = 4096
E = 1024
NF2 = 33
EW = 256
NH = E // EW
PERM = np.array([2 * p + h for h in (0, 1) for p in range(32)])
DBG_NOACC = False


def make_consts():
    c = {}
    s2 = np.arange(64)[:, None]
    f2 = np.arange(64)[None, :]
    M1 = np.exp(-2j * np.pi * s2 * f2 / 64.0)
    z = np.zeros((64, 64))
    c["c1re_bd"] = np.block([[M1.real, z], [z, M1.real]]).astype(np.float32)
    c["c1im_bd"] = np.block([[M1.imag, z], [z, M1.imag]]).astype(np.float32)

    s1 = np.arange(64)
    f1 = np.arange(64)
    L2 = np.zeros((NF2, 128, 128), np.float32)
    for ff2 in range(NF2):
        ang = -2.0 * np.pi * (s1[:, None] * (ff2 + 64.0 * f1[None, :])) / T
        cr, ci = np.cos(ang), np.sin(ang)
        L2[ff2, 0:64, 0:64] = cr[PERM, :]
        L2[ff2, 0:64, 64:128] = ci[PERM, :]
        L2[ff2, 64:128, 0:64] = -ci[PERM, :]
        L2[ff2, 64:128, 64:128] = cr[PERM, :]
    c["stage2_L"] = L2

    inner = np.exp(+2j * np.pi * np.outer(f1, s1) / 64.0)
    WA = np.zeros((128, 128), np.float32)
    WA[0:64, 0:64] = inner.real
    WA[0:64, 64:128] = inner.imag
    WA[64:128, 0:64] = -inner.imag
    WA[64:128, 64:128] = inner.real
    c["macA_rhs"] = WA

    d = np.where((np.arange(NF2) >= 1) & (np.arange(NF2) <= 31), 2.0, 1.0)
    tw = np.exp(+2j * np.pi * np.outer(s1, np.arange(NF2)) / T).T
    twf = d[:, None] * tw / (T * E)
    c["mac_tw_re"] = np.concatenate([twf.real, twf.real], axis=1).astype(np.float32)
    c["mac_tw_im"] = np.concatenate([twf.imag, twf.imag], axis=1).astype(np.float32)

    ang = +2.0 * np.pi * np.outer(np.arange(NF2), s1) / 64.0
    c["macB_cos"] = np.cos(ang).astype(np.float32)
    c["macB_msin"] = (-np.sin(ang)).astype(np.float32)

    import ml_dtypes
    bf = lambda a: np.asarray(a, dtype=ml_dtypes.bfloat16)
    c["c1re_bf"] = bf(c["c1re_bd"])
    c["c1im_bf"] = bf(c["c1im_bd"])
    c["st2v_re"] = bf(c["stage2_L"][:, :, 0:64].reshape(NF2 * 128, 64))
    c["st2v_im"] = bf(c["stage2_L"][:, :, 64:128].reshape(NF2 * 128, 64))
    LA = np.zeros((NF2, 128, 128), np.float32)
    f1 = np.arange(64)
    for ff2 in range(NF2):
        dd = 2.0 if 1 <= ff2 <= 31 else 1.0
        inner = np.exp(+2j * np.pi * np.outer(f1, s1.ravel() if s1.ndim > 1 else s1) / 64.0)
        tw = np.exp(+2j * np.pi * np.arange(64) * ff2 / T)[None, :]
        Mfull = dd * (1.0 / T) * inner * tw
        LA[ff2, 0:64, 0:64] = Mfull.real
        LA[ff2, 0:64, 64:128] = Mfull.imag
        LA[ff2, 64:128, 0:64] = -Mfull.imag
        LA[ff2, 64:128, 64:128] = Mfull.real
    c["invA_re"] = bf(LA[:, 0:64, :].reshape(NF2 * 64, 128))
    c["invA_im"] = bf(LA[:, 64:128, :].reshape(NF2 * 64, 128))
    angB = +2.0 * np.pi * np.outer(np.arange(NF2), np.arange(64)) / 64.0
    LB = np.zeros((66, 64), np.float32)
    LB[0:NF2, :] = np.cos(angB)
    LB[NF2 : 2 * NF2, :] = -np.sin(angB)
    c["invB"] = bf(LB)
    fvals = (np.arange(64)[:, None] * 64 + np.arange(NF2)[None, :]).reshape(-1)
    c["f_rep"] = np.broadcast_to(fvals.astype(np.float32), (128, 64 * NF2)).copy()
    return c


def host_prep_attn(inputs, core):
    x = np.asarray(inputs["L0_attn_hidden_states"])[core]
    d = {}
    xt = np.ascontiguousarray(x.T).reshape(E, 64, 64)
    xp = np.empty((32, E, 128), np.float32)
    for p in range(32):
        xp[p, :, 0:64] = xt[:, :, 2 * p]
        xp[p, :, 64:128] = xt[:, :, 2 * p + 1]
    d["xpair"] = xp
    d["Wq"] = np.ascontiguousarray(np.asarray(inputs["L0_attn_Wq"], np.float32))
    d["Wk"] = np.ascontiguousarray(np.asarray(inputs["L0_attn_Wk"], np.float32))
    d["bq"] = np.ascontiguousarray(np.asarray(inputs["L0_attn_bq"], np.float32))[None, :]
    d["bk"] = np.ascontiguousarray(np.asarray(inputs["L0_attn_bk"], np.float32))[None, :]
    d["Wv"] = np.ascontiguousarray(np.asarray(inputs["L0_attn_Wv"], np.float32))
    d["bv"] = np.ascontiguousarray(np.asarray(inputs["L0_attn_bv"], np.float32))[None, :]
    d["Wo"] = np.ascontiguousarray(np.asarray(inputs["L0_attn_Wo"], np.float32))
    d["bo"] = np.ascontiguousarray(np.asarray(inputs["L0_attn_bo"], np.float32))[None, :]
    d["x_tm"] = np.ascontiguousarray(x.astype(np.float32))
    d["WfcT"] = np.ascontiguousarray(np.asarray(inputs["L0_ffn_W_fc"], np.float32).T)
    d["WprojT"] = np.ascontiguousarray(np.asarray(inputs["L0_ffn_W_proj"], np.float32).T)
    d["Afc"] = np.ascontiguousarray(np.asarray(inputs["L0_ffn_A_fc"], np.float32))
    d["BfcT"] = np.ascontiguousarray(np.asarray(inputs["L0_ffn_B_fc"], np.float32).T)
    d["Aproj"] = np.ascontiguousarray(np.asarray(inputs["L0_ffn_A_proj"], np.float32))
    d["BprojT"] = np.ascontiguousarray(np.asarray(inputs["L0_ffn_B_proj"], np.float32).T)
    d["bfc"] = np.ascontiguousarray(np.asarray(inputs["L0_ffn_b_fc"], np.float32))[None, :]
    d["bproj"] = np.ascontiguousarray(np.asarray(inputs["L0_ffn_b_proj"], np.float32))[None, :]
    return d


class B:

    def __init__(self):
        self.nc = bass.Bass("TRN2", target_bir_lowering=False)
        self.cmds = {k: [] for k in ["sync", "tensor", "vector", "scalar", "gpsimd"]}
        self.semv = {}
        self.sems = {}
        self.const_arrays = {}

    def sem(self, name):
        if name not in self.sems:
            self.sems[name] = self.nc.semaphore(name).__enter__()
            self.semv[name] = 0
        return self.sems[name]

    def bump(self, name, n):
        self.sem(name)
        self.semv[name] += n
        return self.semv[name]

    def on(self, eng, fn):
        self.cmds[eng].append(fn)

    def dram_in(self, name, shape, dtype=F32):
        return self.nc.dram_tensor(name, list(shape), dtype, kind="ExternalInput").ap()

    def dram_out(self, name, shape, dtype=F32):
        return self.nc.dram_tensor(name, list(shape), dtype, kind="ExternalOutput").ap()

    def sb(self, name, shape, dtype=F32):
        return self.nc.alloc_sbuf_tensor(name, list(shape), dtype)

    def ps(self, name, shape, dtype=F32):
        return self.nc.alloc_psum_tensor(name, list(shape), dtype)

    def const_in(self, name, arr):
        arr = np.ascontiguousarray(arr)
        import ml_dtypes
        dt_ = F32 if arr.dtype == np.float32 else (BF16 if arr.dtype == ml_dtypes.bfloat16 else None)
        assert dt_ is not None, arr.dtype
        ap = self.dram_in(name, arr.shape, dt_)
        self.const_arrays[name] = arr
        return ap

    class _AutoSer:
        _SKIP = {"wait_ge", "dma_start", "dma_start_transpose", "then_inc",
                 "sem_inc", "nop", "drain"}

        class _Proxy:
            def __init__(self, ser, ins):
                self._ser = ser
                self._ins = ins

            def then_inc(self, sem, n):
                self._ins.then_inc(sem, n)
                c = self._ser._ucount.get(id(sem), 0) + n
                self._ser._ucount[id(sem)] = c
                self._ser._token = (sem, c)
                self._ser._last = None
                return self._ins

            def __getattr__(self, name):
                return getattr(self._ins, name)

        def __init__(self, eng, sem):
            self._eng = eng
            self._sem = sem
            self._count = 0
            self._token = None
            self._last = None
            self._ucount = {}

        def _flush(self):
            if self._last is not None:
                self._last.then_inc(self._sem, 1)
                self._count += 1
                self._token = (self._sem, self._count)
                self._last = None

        def __getattr__(self, name):
            attr = getattr(self._eng, name)
            if name.startswith("_") or name in self._SKIP or not callable(attr):
                return attr

            def call(*a, **k):
                self._flush()
                if self._token is not None:
                    self._eng.wait_ge(self._token[0], self._token[1])
                    self._token = None
                ins = attr(*a, **k)
                if hasattr(ins, "then_inc"):
                    self._last = ins
                    return B._AutoSer._Proxy(self, ins)
                return ins

            return call

    def finalize(self):
        nc = self.nc
        with nc.Block() as block:
            @block.sync
            def _(eng):
                for fn in self.cmds["sync"]:
                    fn(eng)

            @block.tensor
            def _(eng):
                for fn in self.cmds["tensor"]:
                    fn(eng)

            @block.vector
            def _(eng):
                w = B._AutoSer(eng, self.sem("autoV"))
                for fn in self.cmds["vector"]:
                    fn(w)

            @block.scalar
            def _(eng):
                w = B._AutoSer(eng, self.sem("autoS"))
                for fn in self.cmds["scalar"]:
                    fn(w)

            @block.gpsimd
            def _(eng):
                w = B._AutoSer(eng, self.sem("autoG"))
                for fn in self.cmds["gpsimd"]:
                    fn(w)
        return nc


def build_p123(dump=(), dump_slice=0, upto="mac", p5_stop=None, use_cc=False):
    b = B()
    C = make_consts()
    dump = set(dump)

    xpair = b.dram_in("xpair", [32, E, 128])
    Wq_d = b.dram_in("Wq", [E, E])
    Wk_d = b.dram_in("Wk", [E, E])
    bq_d = b.dram_in("bq", [1, E])
    bk_d = b.dram_in("bk", [1, E])
    Wv_d = b.dram_in("Wv", [E, E])
    bv_d = b.dram_in("bv", [1, E])
    c1re_d = b.const_in("c1re_bd", C["c1re_bd"])
    c1im_d = b.const_in("c1im_bd", C["c1im_bd"])
    st2_d = b.const_in("stage2_L", C["stage2_L"].reshape(NF2 * 128, 128))
    macA_d = b.const_in("macA_rhs", C["macA_rhs"])
    mtwre_d = b.const_in("mac_tw_re", C["mac_tw_re"])
    mtwim_d = b.const_in("mac_tw_im", C["mac_tw_im"])
    macBc_d = b.const_in("macB_cos", C["macB_cos"])
    macBs_d = b.const_in("macB_msin", C["macB_msin"])
    c1rebf_d = b.const_in("c1re_bf", C["c1re_bf"])
    c1imbf_d = b.const_in("c1im_bf", C["c1im_bf"])
    st2vre_d = b.const_in("st2v_re", C["st2v_re"])
    st2vim_d = b.const_in("st2v_im", C["st2v_im"])
    invAre_d = b.const_in("invA_re", C["invA_re"])
    invAim_d = b.const_in("invA_im", C["invA_im"])
    invB_d = b.const_in("invB", C["invB"])
    mac_out = b.dram_out("mac", [64, 64])
    cvals = np.broadcast_to(np.array([-np.pi, 2.0 * np.pi / 4096.0, 0.0, 1.0,
                                      -1.0, 16512.5, -16384.0, 255.0,
                                      16384.0, -16256.0], np.float32), (128, 10)).copy()
    ident_d = b.const_in("ident128", np.eye(128, dtype=np.float32))
    ident_sb = b.sb("ident_sb", [128, 128])
    import ml_dtypes as _mld
    identbf_d = b.const_in("ident64bf", np.asarray(np.tile(np.eye(64), (2, 1)), dtype=_mld.bfloat16))
    identbf_sb = b.sb("identbf_sb", [128, 64], BF16)
    cvals_d = b.const_in("cvals", cvals)
    csb = b.sb("csb", [128, 10])
    if upto in ("topk",):
        g_out = b.dram_out("g", [2, 64, 8 * NF2])

    dbg = {}
    if "rbq" in dump:
        dbg["rbq"] = b.dram_out("d_rbq", [128, NF2 * EW])
    if "qf" in dump:
        dbg["qf"] = b.dram_out("d_qf", [2, 128, EW])
    if "ppack" in dump:
        dbg["ppack"] = b.dram_out("d_ppack", [128, NF2])
    if "qbuf" in dump:
        dbg["qbuf"] = b.dram_out("d_qbuf", [128, EW])
    if "s1buf" in dump:
        dbg["s1buf"] = b.dram_out("d_s1buf", [2, 128, EW])
    if "pparts" in dump:
        dbg["pparts"] = b.dram_out("d_pparts", [64, NF2 * 4])

    xp_sb = [b.sb(f"xp{j}", [128, E]) for j in range(2)]
    wq_sb = b.sb("wq_sb", [128, 8 * EW])
    wk_sb = b.sb("wk_sb", [128, 8 * EW])
    bias_sb = b.sb("bias_sb", [1, 2 * E])
    ones1 = b.sb("ones1", [1, 128])
    qbuf = [b.sb(f"qbuf{j}", [128, EW]) for j in range(2)]
    kbuf = [b.sb(f"kbuf{j}", [128, EW]) for j in range(2)]
    s1buf = [[b.sb(f"s1buf{j}_{c}", [128, EW]) for c in range(4)] for j in range(2)]
    c1re_sb = b.sb("c1re_sb", [128, 128])
    c1im_sb = b.sb("c1im_sb", [128, 128])
    b._pool_base = 229344 - b.nc.sbuf_bytes_remaining
    pool = b.sb("pool", [128, 28800])

    def bfat(name, shape, off_f32):
        return b.nc.alloc_sbuf_tensor_at(name, list(shape), BF16, offset=b._pool_base + off_f32 * 4)

    rbq = pool[:, 0:8448]
    rbk = pool[:, 8448:16896]
    st2_sb = pool[:, 16896:21120]
    pparts = b.sb("pparts", [64, NF2 * 4])
    ppack = b.sb("ppack", [128, NF2])
    macA_sb = b.sb("macA_sb", [128, 128])
    mtw_sb = b.sb("mtw_sb", [NF2, 256])
    u_sb = b.sb("u_sb", [NF2, 128])
    up_sb = b.sb("up_sb", [NF2, 128])
    macB_sb = b.sb("macB_sb", [NF2, 128])
    mac_sb = b.sb("mac_sb", [64, 64])

    qps_f = b.ps("qps", [128, 512])
    kps_f = b.ps("kps", [128, 512])
    s1ps_f = [b.ps(f"s1ps{c}", [128, 512]) for c in range(4)]
    f2ps_f = [b.ps(f"f2ps{c}", [128, 512]) for c in range(2)]
    qps = qps_f[:, 0:EW]
    kps = kps_f[:, 0:EW]
    s1ps = [t[:, 0:EW] for t in s1ps_f]
    f2ps = [t[:, 0:EW] for t in f2ps_f]

    def p0(eng):
        for dst, src in [
            (c1re_sb[:], c1re_d), (c1im_sb[:], c1im_d),
            (st2_sb[:], st2_d.rearrange("(f p) m -> p f m", p=128)),
            (bias_sb[0:1, 0:E], bq_d), (bias_sb[0:1, E : 2 * E], bk_d),
            (macA_sb[:], macA_d), (mtw_sb[:, 0:128], mtwre_d),
            (mtw_sb[:, 128:256], mtwim_d), (macB_sb[:, 0:64], macBc_d),
            (macB_sb[:, 64:128], macBs_d),
            (csb[:], cvals_d),
            (ident_sb[:], ident_d),
            (identbf_sb[:], identbf_d),
        ]:
            eng.dma_start(out=dst, in_=src).then_inc(b.sem("boot"), 16)
    b.bump("boot", 16 * 13)
    b.on("sync", p0)
    for _i, _v in enumerate([-np.pi, 2.0 * np.pi / 4096.0, 0.0, 1.0, -1.0, 16512.5, -16384.0, 255.0, 16384.0, -16256.0]):
        b.nc.const_aps.register(F32, _v, csb[:, _i : _i + 1])
    boot_t = b.semv["boot"]

    def p0v(eng):
        eng.memset(ones1[:], 1.0).then_inc(b.sem("ones"), 1)
    b.bump("ones", 1)
    b.on("vector", p0v)

    for H in range(NH):
        wload_t = b.bump("wload", 32)

        def p1w(eng, H=H, wt=b.semv["proj"] if "proj" in b.semv else 0):
            if H >= 1:
                eng.wait_ge(b.sem("proj"), wt)
            src_q = Wq_d.rearrange("(c p) o -> p c o", p=128)[:, :, H * EW : (H + 1) * EW]
            src_k = Wk_d.rearrange("(c p) o -> p c o", p=128)[:, :, H * EW : (H + 1) * EW]
            eng.dma_start(out=wq_sb[:], in_=src_q).then_inc(b.sem("wload"), 16)
            eng.dma_start(out=wk_sb[:], in_=src_k).then_inc(b.sem("wload"), 16)
        b.on("sync", p1w)

        for p in range(32):
            j = p % 2
            need_proj = b.semv.get("proj", 0)
            lsem = f"load{j}"
            load_t = b.bump(lsem, 16)

            def l(eng, p=p, j=j, H=H, need=need_proj, lsem=lsem):
                if p >= 2 or H >= 1:
                    eng.wait_ge(b.sem("proj"), need)
                src = xpair[p].rearrange("(c i) m -> i c m", i=128)
                eng.dma_start(out=xp_sb[j][:], in_=src).then_inc(b.sem(lsem), 16)
            b.on("sync", l)

            qevac_need = b.semv.get("qevac", 0)
            proj_t = b.bump("proj", 1)

            def mm(eng, p=p, j=j, H=H, lt=load_t, wt=wload_t, qn=qevac_need, lsem=lsem):
                eng.wait_ge(b.sem(lsem), lt)
                if p == 0:
                    eng.wait_ge(b.sem("boot"), boot_t)
                    eng.wait_ge(b.sem("ones"), 1)
                eng.wait_ge(b.sem("wload"), wt)
                if (p >= 2 or H >= 1) and qn >= 1:
                    eng.wait_ge(b.sem("qevac"), qn - 1)
                pq = [qps, f2ps[0]][p % 2]
                pk = [kps, f2ps[1]][p % 2]
                eng.matmul(pq[:], ones1[:], bias_sb[0:1, H * EW : (H + 1) * EW], start=True, stop=False)
                eng.matmul(pk[:], ones1[:], bias_sb[0:1, E + H * EW : E + (H + 1) * EW], start=True, stop=False)
                i = None
                for c in range(8):
                    lhsT = xp_sb[j][:, c * 128 : (c + 1) * 128]
                    eng.matmul(pq[:], lhsT, wq_sb[:, c * EW : (c + 1) * EW], start=False, stop=(c == 7))
                    i = eng.matmul(pk[:], lhsT, wk_sb[:, c * EW : (c + 1) * EW], start=False, stop=(c == 7))
                i.then_inc(b.sem("proj"), 1)
            b.on("tensor", mm)

            s1_need = b.semv.get("s1mm", 0)
            qevac_t = b.bump("qevac", 1)

            def ev(eng, p=p, j=j, H=H, pt=proj_t, sn=s1_need):
                eng.wait_ge(b.sem("proj"), pt)
                if p >= 2 or H >= 1:
                    eng.wait_ge(b.sem("s1mm"), sn)
                pq = [qps, f2ps[0]][p % 2]
                pk = [kps, f2ps[1]][p % 2]
                eng.tensor_copy(qbuf[j][:], pq[:])
                eng.tensor_copy(kbuf[j][:], pk[:]).then_inc(b.sem("qevac"), 1)
            b.on("vector", ev)

            s1evac_need = b.semv.get("s1evac", 0)
            s1_t = b.bump("s1mm", 1)

            def s1(eng, p=p, j=j, H=H, qt=qevac_t, en=s1evac_need):
                eng.wait_ge(b.sem("qevac"), qt)
                if p >= 1 or H >= 1:
                    eng.wait_ge(b.sem("s1evac"), en)
                eng.matmul(s1ps[0][:], c1re_sb[:], qbuf[j][:], start=True, stop=True)
                eng.matmul(s1ps[1][:], c1im_sb[:], qbuf[j][:], start=True, stop=True)
                eng.matmul(s1ps[2][:], c1re_sb[:], kbuf[j][:], start=True, stop=True)
                eng.matmul(s1ps[3][:], c1im_sb[:], kbuf[j][:], start=True, stop=True).then_inc(b.sem("s1mm"), 1)
            b.on("tensor", s1)

            corner_need = b.semv.get("corner", 0)
            s1evac_t = b.bump("s1evac", 1)

            def ev1(eng, p=p, j=j, H=H, st=s1_t, cn=corner_need):
                eng.wait_ge(b.sem("s1mm"), st)
                if p >= 2 or H >= 1:
                    eng.wait_ge(b.sem("corner"), cn)
                for c in range(4):
                    i = eng.tensor_copy(s1buf[j][c][:], s1ps[c][:])
                i.then_inc(b.sem("s1evac"), 1)
            b.on("vector", ev1)

            if "qbuf" in dump and H == dump_slice and p == 5:
                def dqb(eng, j=j, t=b.semv["qevac"]):
                    eng.wait_ge(b.sem("qevac"), t)
                    eng.dma_start(out=dbg["qbuf"], in_=qbuf[j][:]).then_inc(b.sem("dbg"), 16)
                b.bump("dbg", 16)
                b.on("gpsimd", dqb)

            if "s1buf" in dump and H == dump_slice and p == 5:
                def dsb(eng, j=j, t=b.semv["s1evac"]):
                    eng.wait_ge(b.sem("s1evac"), t)
                    eng.dma_start(out=dbg["s1buf"][0], in_=s1buf[j][0][:]).then_inc(b.sem("dbg"), 16)
                    eng.dma_start(out=dbg["s1buf"][1], in_=s1buf[j][1][:]).then_inc(b.sem("dbg"), 16)
                b.bump("dbg", 32)
                b.on("gpsimd", dsb)

            corner_t = b.bump("corner", 16 * 8)

            def corner(eng, p=p, j=j, H=H, et=s1evac_t):
                eng.wait_ge(b.sem("s1evac"), et)
                for (c, rb, base) in ((0, rbq, 0), (1, rbq, 64), (2, rbk, 0), (3, rbk, 64)):
                    for h in (0, 1):
                        src = s1buf[j][c][h * 64 : h * 64 + NF2, :]
                        r = base + h * 32 + p
                        eng.dma_start(out=rb[r : r + 1, :], in_=src).then_inc(b.sem("corner"), 16)
            b.on("sync", corner)

        corner_all = b.semv["corner"]
        if "rbq" in dump and H == dump_slice:
            def drb(eng, ca=corner_all):
                eng.wait_ge(b.sem("corner"), ca)
                eng.dma_start(out=dbg["rbq"], in_=rbq[:]).then_inc(b.sem("dbg"), 16)
            b.bump("dbg", 16)
            b.on("gpsimd", drb)

        for f2 in range(NF2):
            pprod_need = b.semv.get("pprod", 0)
            st2_t = b.bump("st2mm", 1)

            def st2(eng, f2=f2, H=H, ca=corner_all, pn=pprod_need):
                eng.wait_ge(b.sem("corner"), ca)
                if not (H == 0 and f2 == 0):
                    eng.wait_ge(b.sem("pprod"), pn)
                L = st2_sb[:, f2 * 128 : (f2 + 1) * 128]
                eng.matmul(f2ps[0][:], L, rbq[:, f2 * EW : (f2 + 1) * EW], start=True, stop=True)
                eng.matmul(f2ps[1][:], L, rbk[:, f2 * EW : (f2 + 1) * EW], start=True, stop=True).then_inc(b.sem("st2mm"), 1)
            b.on("tensor", st2)

            if "qf" in dump and H == 0 and f2 == 0:
                def dqf(eng, st=st2_t):
                    eng.wait_ge(b.sem("st2mm"), st)
                    eng.tensor_copy(u_sb[0:NF2, :] if False else qbuf[0][:], f2ps[0][:])
                    eng.tensor_copy(kbuf[0][:], f2ps[1][:]).then_inc(b.sem("dbgv"), 1)
                b.bump("dbgv", 1)
                b.on("vector", dqf)

                def dqf2(eng, t=b.semv["dbgv"]):
                    eng.wait_ge(b.sem("dbgv"), t)
                    eng.dma_start(out=dbg["qf"][0], in_=qbuf[0][:]).then_inc(b.sem("dbg"), 16)
                    eng.dma_start(out=dbg["qf"][1], in_=kbuf[0][:]).then_inc(b.sem("dbg"), 16)
                b.bump("dbg", 32)
                b.on("gpsimd", dqf2)

            pprod_t = b.bump("pprod", 1)

            def pp(eng, f2=f2, H=H, st=st2_t):
                eng.wait_ge(b.sem("st2mm"), st)
                eng.tensor_copy(qf_sb[:], f2ps[0][:])
                qre = qf_sb[0:64, :]
                qim = qf_sb[64:128, :]
                kre = f2ps[1][0:64, :]
                kim = f2ps[1][64:128, :]
                col = lambda k: pparts[0:64, f2 * 4 + k : f2 * 4 + k + 1]
                pairs = ((qre, kre), (qim, kim), (qim, kre), (qre, kim))
                if H == 0 or DBG_NOACC:
                    for k, (a, bb) in enumerate(pairs):
                        eng.tensor_tensor(ttr_scr[:], a, bb, ALU.mult)
                        i = eng.reduce_sum(col(k), ttr_scr[:], axis=AX.X)
                    i.then_inc(b.sem("pprod"), 1)
                else:
                    for k, (a, bb) in enumerate(pairs):
                        eng.tensor_tensor(ttr_scr[:], a, bb, ALU.mult)
                        eng.reduce_sum(pscr[0:64, k : k + 1], ttr_scr[:], axis=AX.X)
                    for k in range(4):
                        i = eng.tensor_add(col(k), col(k), pscr[0:64, k : k + 1])
                    i.then_inc(b.sem("pprod"), 1)
            b.on("vector", pp)

        if "pparts" in dump and H == dump_slice:
            def dppp(eng, t=b.semv["pprod"]):
                eng.wait_ge(b.sem("pprod"), t)
                eng.dma_start(out=dbg["pparts"], in_=pparts[:]).then_inc(b.sem("dbg"), 16)
            b.bump("dbg", 16)
            b.on("gpsimd", dppp)

    pscr = b.sb("pscr", [64, 4])
    ttr_scr = b.sb("ttr_scr", [64, EW])
    qf_sb = b.sb("qf_sb", [128, EW])

    pprod_all = b.semv["pprod"]

    def pcomb(eng, pa=pprod_all):
        eng.wait_ge(b.sem("pprod"), pa)
        v = pparts[0:64, :].rearrange("p (f k) -> p f k", k=4)
        eng.tensor_add(ppack[0:64, :], v[:, :, 0], v[:, :, 1])
        eng.tensor_sub(ppack[64:128, :], v[:, :, 2], v[:, :, 3]).then_inc(b.sem("pcomb"), 1)
    b.bump("pcomb", 1)
    b.on("vector", pcomb)
    if "ppack" in dump:
        def dpp(eng, t=b.semv["pcomb"]):
            eng.wait_ge(b.sem("pcomb"), t)
            eng.dma_start(out=dbg["ppack"], in_=ppack[:]).then_inc(b.sem("dbg"), 16)
        b.bump("dbg", 16)
        b.on("gpsimd", dpp)

    uaps = f2ps[0][0:NF2, 0:128]
    macps = f2ps[1][0:64, 0:64]

    def macA(eng, t=b.semv["pcomb"]):
        eng.wait_ge(b.sem("pcomb"), t)
        eng.matmul(uaps, ppack[:], macA_sb[:], start=True, stop=True).then_inc(b.sem("macA"), 1)
    b.bump("macA", 1)
    b.on("tensor", macA)

    def mactw(eng, t=b.semv["macA"]):
        eng.wait_ge(b.sem("macA"), t)
        eng.tensor_copy(u_sb[:], uaps)
        ure, uim = u_sb[:, 0:64], u_sb[:, 64:128]
        twre_r, twre_i = mtw_sb[:, 0:64], mtw_sb[:, 128:192]
        eng.tensor_tensor(up_sb[:, 0:64], ure, twre_r, ALU.mult)
        eng.tensor_tensor(pscr2[:, 0:64], uim, twre_i, ALU.mult)
        eng.tensor_sub(up_sb[:, 0:64], up_sb[:, 0:64], pscr2[:, 0:64])
        eng.tensor_tensor(up_sb[:, 64:128], ure, twre_i, ALU.mult)
        eng.tensor_tensor(pscr2[:, 64:128], uim, twre_r, ALU.mult)
        eng.tensor_add(up_sb[:, 64:128], up_sb[:, 64:128], pscr2[:, 64:128]).then_inc(b.sem("mactw"), 1)
    pscr2 = b.sb("pscr2", [NF2, 128])
    b.bump("mactw", 1)
    b.on("vector", mactw)

    def macB(eng, t=b.semv["mactw"]):
        eng.wait_ge(b.sem("mactw"), t)
        eng.matmul(macps, macB_sb[:, 0:64], up_sb[:, 0:64], start=True, stop=False)
        eng.matmul(macps, macB_sb[:, 64:128], up_sb[:, 64:128], start=False, stop=True).then_inc(b.sem("macB"), 1)
    b.bump("macB", 1)
    b.on("tensor", macB)

    def macev(eng, t=b.semv["macB"]):
        eng.wait_ge(b.sem("macB"), t)
        eng.tensor_copy(mac_sb[:], macps).then_inc(b.sem("macev"), 1)
    b.bump("macev", 1)
    b.on("vector", macev)

    b.bump("outd", 16)
    dbg_total = b.semv.get("dbg", 0)

    def macout2(eng, t=b.semv["macev"], dt=dbg_total):
        eng.wait_ge(b.sem("macev"), t)
        eng.dma_start(out=mac_out, in_=mac_sb[:]).then_inc(b.sem("outd"), 16)
        eng.wait_ge(b.sem("outd"), 16)
        if dt:
            eng.wait_ge(b.sem("dbg"), dt)
    b.on("gpsimd", macout2)

    if upto == "mac":
        return b

    mac_all = pool[0:8, 0:4096]
    workA = pool[0:8, 4096:8192]
    workB = pool[0:8, 8192:12288]
    wm = pool[0:8, 12288:16384]
    wmp = pool[:, 25344:25600]
    rbw = pool[:, 25600:25864]
    m8 = b.sb("m8", [8, 8])
    mscr = pool[0:8, 21120:25216]
    s1w_ev = b.sb("s1w_ev", [128, 16])
    gt_re = b.sb("gt_re", [64, 8 * NF2])
    gt_im = b.sb("gt_im", [64, 8 * NF2])

    if use_cc:
        cc_in = b.nc.dram_tensor("cc_in", [1, 4096], F32)
        cc_out = b.nc.dram_tensor("cc_out", [8, 4096], F32)
        nrep_t = b.bump("macrep", 16 * 2)
        b.bump("ccd", 1)

        def macrep(eng, t=b.semv["macev"]):
            eng.wait_ge(b.sem("macev"), t)
            eng.dma_start(out=cc_in.ap().rearrange("q (a c) -> (q a) c", a=64), in_=mac_sb[:]).then_inc(b.sem("macrep"), 16)
            eng.wait_ge(b.sem("macrep"), 16)
            eng.collective_compute(
                "AllGather", ALU.bypass,
                replica_groups=[list(range(8))],
                ins=[cc_in.ap().opt()], outs=[cc_out.ap().opt()],
            ).then_inc(b.sem("ccd"), 1)
            eng.wait_ge(b.sem("ccd"), 1)
            eng.dma_start(out=mac_all, in_=cc_out.ap()).then_inc(b.sem("macrep"), 16)
        b.on("gpsimd", macrep)
    else:
        nrep_t = b.bump("macrep", 16 * 8)

        def macrep(eng, t=b.semv["macev"]):
            eng.wait_ge(b.sem("macev"), t)
            for bb in range(8):
                eng.dma_start(out=mac_all[bb : bb + 1, :], in_=mac_sb[:]).then_inc(b.sem("macrep"), 16)
        b.on("gpsimd", macrep)

    _tks0 = [b.semv.get("tks", 0)]
    b.bump("tks", 9)
    tk_t = b.bump("topk", 1)

    def topk(eng, t=nrep_t):
        eng.wait_ge(b.sem("macrep"), t)
        wtmp = wm
        bufs = [mac_all, workA, workB, wtmp]
        for r in range(3):
            src, dst = bufs[r], bufs[r + 1]
            eng.max(out=m8[:], in_=src).then_inc(b.sem("tks"), 1)
            eng.wait_ge(b.sem("tks"), b.bump("tks", 0) if False else 3 * r + 1 + _tks0[0])
            eng.tensor_single_scalar(mscr, src, m8[:, 7:8], ALU.is_ge)
            eng.scalar_tensor_tensor(dst, mscr, -1e30, src, ALU.mult, ALU.add).then_inc(b.sem("tks"), 2)
            eng.wait_ge(b.sem("tks"), 3 * r + 3 + _tks0[0])
        eng.tensor_single_scalar(mscr, bufs[3], -1e29, ALU.is_le)
        eng.tensor_mul(wm, mscr, mac_all).then_inc(b.sem("topk"), 1)
    b.on("vector", topk)

    for p in range(32):
        wtr_t = b.bump("wtr", 1)

        def wtr(eng, p=p, t=tk_t, ev=b.semv.get("wtrev", 0)):
            eng.wait_ge(b.sem("topk"), t)
            if p > 0:
                eng.wait_ge(b.sem("wtrev"), ev)
            wmv = wm.rearrange("q (s2 s1) -> q s1 s2", s1=64)
            eng.matmul(s1ps[3][0:64, 0:8], wmv[:, 2 * p, :], ident_sb[0:8, 0:8], start=True, stop=True)
            eng.matmul(s1ps[3][64:128, 0:8], wmv[:, 2 * p + 1, :], ident_sb[0:8, 0:8], start=True, stop=True).then_inc(b.sem("wtr"), 1)
        b.on("tensor", wtr)

        wtrev_t = b.bump("wtrev", 1)

        def wtrev(eng, p=p, t=wtr_t):
            eng.wait_ge(b.sem("wtr"), t)
            eng.tensor_copy(wmp[:, p * 8 : (p + 1) * 8], s1ps[3][:, 0:8]).then_inc(b.sem("wtrev"), 1)
        b.on("vector", wtrev)
    wml_t = b.semv["wtrev"]

    for p in range(32):
        ws1_t = b.bump("ws1", 1)

        def ws1(eng, p=p, t=wml_t, ev=b.semv.get("ws1ev", 0)):
            eng.wait_ge(b.sem("wtrev"), t)
            if p > 0:
                eng.wait_ge(b.sem("ws1ev"), ev)
            eng.matmul(s1ps[0][:, 0:8], c1re_sb[:], wmp[:, p * 8 : (p + 1) * 8], start=True, stop=True)
            eng.matmul(s1ps[1][:, 0:8], c1im_sb[:], wmp[:, p * 8 : (p + 1) * 8], start=True, stop=True).then_inc(b.sem("ws1"), 1)
        b.on("tensor", ws1)

        ws1ev_t = b.bump("ws1ev", 1)

        def ws1ev(eng, p=p, t=ws1_t, cn=b.semv.get("wcorner", 0)):
            eng.wait_ge(b.sem("ws1"), t)
            if p > 0:
                eng.wait_ge(b.sem("wcorner"), cn)
            eng.tensor_copy(s1w_ev[:, 0:8], s1ps[0][:, 0:8])
            eng.tensor_copy(s1w_ev[:, 8:16], s1ps[1][:, 0:8]).then_inc(b.sem("ws1ev"), 1)
        b.on("vector", ws1ev)

        wc_t = b.bump("wcorner", 16 * 4)

        def wcorner(eng, p=p, t=ws1ev_t):
            eng.wait_ge(b.sem("ws1ev"), t)
            for (c0, base) in ((0, 0), (8, 64)):
                for h in (0, 1):
                    src = s1w_ev[h * 64 : h * 64 + NF2, c0 : c0 + 8]
                    r = base + h * 32 + p
                    eng.dma_start(out=rbw[r : r + 1, :].rearrange("q (f m) -> q f m", f=NF2), in_=src).then_inc(b.sem("wcorner"), 16)
        b.on("gpsimd", wcorner)

    wall = b.semv["wcorner"]
    for f2 in range(NF2):
        wg_t = b.bump("wg", 1)

        def wg(eng, f2=f2, t=wall, ev=b.semv.get("wgev", 0)):
            eng.wait_ge(b.sem("wcorner"), t)
            if f2 > 0:
                eng.wait_ge(b.sem("wgev"), ev)
            eng.matmul(s1ps[2][:, 0:8], st2_sb[:, f2 * 128 : (f2 + 1) * 128], rbw[:, f2 * 8 : (f2 + 1) * 8], start=True, stop=True).then_inc(b.sem("wg"), 1)
        b.on("tensor", wg)

        wgev_t = b.bump("wgev", 1)

        def wgev(eng, f2=f2, t=wg_t):
            eng.wait_ge(b.sem("wg"), t)
            dst_re = gt_re[:, f2::NF2]
            dst_im = gt_im[:, f2::NF2]
            eng.tensor_copy(dst_re, s1ps[2][0:64, 0:8])
            eng.tensor_scalar_mul(dst_im, s1ps[2][64:128, 0:8], -1.0).then_inc(b.sem("wgev"), 1)
        b.on("vector", wgev)

    if upto == "topk":
        dbg_total2 = b.semv.get("dbg", 0)
        wm_out = b.dram_out("wm", [8, 4096])
        wk_out = b.dram_out("wk_dbg", [8, 4096])

        def tkout(eng, t=b.semv["wgev"], dt=dbg_total2):
            eng.wait_ge(b.sem("wgev"), t)
            eng.dma_start(out=wk_out, in_=workA).then_inc(b.sem("outd2"), 16)
            eng.dma_start(out=wm_out, in_=wm).then_inc(b.sem("outd2"), 16)
            eng.dma_start(out=g_out[0], in_=gt_re[:]).then_inc(b.sem("outd2"), 16)
            eng.dma_start(out=g_out[1], in_=gt_im[:]).then_inc(b.sem("outd2"), 16)
            eng.wait_ge(b.sem("outd2"), b.semv["outd2"])
            if dt:
                eng.wait_ge(b.sem("dbg"), dt)
        b.bump("outd2", 64)
        b.on("gpsimd", tkout)
        return b

    aggfm_d = b.nc.dram_tensor("aggfm", [1024, 4096], BF16, kind="Internal").ap()
    xpair_bf_d = b.nc.dram_tensor("xpair_bf", [32, E, 128], BF16, kind="Internal").ap()
    rbv = bfat("rbv", [128, 8448], 0)
    vfre = bfat("vfre", [64, 8448], 4224)
    vfim = bfat("vfim", [64, 8448], 8448)
    agre = bfat("agre", [64, 8448], 12672)
    agim = bfat("agim", [64, 8448], 16896)
    gsc1 = bfat("gsc1", [64, 2112], 21120)
    gsc2 = bfat("gsc2", [64, 2112], 22176)
    rbB = bfat("rbB", [66, 16384], 4224)
    aggt = bfat("aggt", [64, 16384], 12672)
    invAre_sb = b.sb("invAre_sb", [64, 4224], BF16)
    invAim_sb = b.sb("invAim_sb", [64, 4224], BF16)
    st2vre_sb = bfat("st2vre_sb", [128, 2112], 23232)
    st2vim_sb = bfat("st2vim_sb", [128, 2112], 24288)
    gt_re_bf = bfat("gt_re_bf", [64, 264], 25864)
    gt_im_bf = bfat("gt_im_bf", [64, 264], 26000)

    xpv = [b.sb(f"xpv{j}", [128, E], BF16) for j in range(2)]
    vbuf = [b.sb(f"vbuf{j}", [128, EW], BF16) for j in range(2)]
    vs1ev = [b.sb(f"vs1ev{j}", [128, 2 * EW], BF16) for j in range(2)]
    wv_sb = b.sb("wv_sb", [128, 8 * EW], BF16)
    ones_bf = b.sb("ones_bf", [1, 512], BF16)
    bv_sb = b.sb("bv_sb", [1, E])
    bv_bf = b.sb("bv_bf", [1, E], BF16)
    c1rebf_sb = b.sb("c1rebf_sb", [128, 128], BF16)
    c1imbf_sb = b.sb("c1imbf_sb", [128, 128], BF16)
    invB_sb = b.sb("invB_sb", [66, 64], BF16)
    uaev = b.sb("uaev", [128, EW], BF16)
    uaT = b.sb("uaT", [128, EW], BF16)

    def xcast(eng):
        eng.dma_start(out=xpair_bf_d, in_=xpair).then_inc(b.sem("xcast"), 16)
    b.bump("xcast", 16)
    b.on("gpsimd", xcast)

    vb_t = b.bump("vboot", 16 * 5)

    def vboot(eng, t=b.semv["topk"]):
        eng.wait_ge(b.sem("topk"), t)
        eng.dma_start(out=st2vre_sb[:], in_=st2vre_d.rearrange("(f p) m -> p f m", p=128)).then_inc(b.sem("vboot"), 16)
        eng.dma_start(out=st2vim_sb[:], in_=st2vim_d.rearrange("(f p) m -> p f m", p=128)).then_inc(b.sem("vboot"), 16)
        eng.dma_start(out=c1rebf_sb[:], in_=c1rebf_d).then_inc(b.sem("vboot"), 16)
        eng.dma_start(out=c1imbf_sb[:], in_=c1imbf_d).then_inc(b.sem("vboot"), 16)
        eng.dma_start(out=bv_sb[:], in_=bv_d).then_inc(b.sem("vboot"), 16)
        eng.dma_start(out=invAre_sb[:], in_=invAre_d.rearrange("(f p) m -> p f m", p=64)).then_inc(b.sem("iaload"), 16)
        eng.dma_start(out=invAim_sb[:], in_=invAim_d.rearrange("(f p) m -> p f m", p=64)).then_inc(b.sem("iaload"), 16)
    b.bump("iaload", 32)
    b.on("sync", vboot)

    def vboot2(eng, t=vb_t):
        eng.wait_ge(b.sem("vboot"), t)
        eng.memset(ones_bf[:], 1.0)
        eng.tensor_copy(bv_bf[:], bv_sb[:]).then_inc(b.sem("vboot2"), 1)
    vb2_t = b.bump("vboot2", 1)
    b.on("vector", vboot2)

    def gtbf(eng, t=b.semv["wgev"], t2=vb2_t):
        eng.wait_ge(b.sem("wgev"), t)
        eng.wait_ge(b.sem("vboot2"), t2 - 1 if False else t2)
        eng.tensor_copy(gt_re_bf[:], gt_re[:])
        eng.tensor_copy(gt_im_bf[:], gt_im[:]).then_inc(b.sem("gtbf"), 1)
    gtbf_t = b.bump("gtbf", 1)
    b.on("vector", gtbf)

    for H in range(NH):
        wvl_t = b.bump("wvl", 16)

        def wvl(eng, H=H, t=vb_t, pv=b.semv.get("vproj", 0)):
            eng.wait_ge(b.sem("vboot"), t)
            if H > 0:
                eng.wait_ge(b.sem("vproj"), pv)
            src = Wv_d.rearrange("(c p) o -> p c o", p=128)[:, :, H * EW : (H + 1) * EW]
            eng.dma_start(out=wv_sb[:], in_=src).then_inc(b.sem("wvl"), 16)
        b.on("gpsimd", wvl)

        for p in range(32):
            j = p % 2
            vlsem = f"vload{j}"
            vload_t = b.bump(vlsem, 16)

            def vload(eng, p=p, j=j, H=H, pv=b.semv.get("vproj", 0), vlsem=vlsem):
                if p == 0 and H == 0:
                    eng.wait_ge(b.sem("xcast"), 16)
                if p >= 2 or H >= 1:
                    eng.wait_ge(b.sem("vproj"), pv)
                src = xpair_bf_d[p].rearrange("(c i) m -> i c m", i=128)
                eng.dma_start(out=xpv[j][:], in_=src).then_inc(b.sem(vlsem), 16)
            b.on("sync", vload)

            vproj_t = b.bump("vproj", 1)

            def vproj(eng, p=p, j=j, H=H, lt=vload_t, wt=wvl_t, ev=b.semv.get("vevac", 0), vlsem=vlsem):
                eng.wait_ge(b.sem(vlsem), lt)
                eng.wait_ge(b.sem("wvl"), wt)
                if p == 0 and H == 0:
                    eng.wait_ge(b.sem("vboot2"), 1)
                if (p >= 2 or H >= 1) and ev >= 1:
                    eng.wait_ge(b.sem("vevac"), ev - 1)
                vp = [qps, kps][p % 2]
                eng.matmul(vp[:], ones_bf[0:1, 0:128], bv_bf[0:1, H * EW : (H + 1) * EW], start=True, stop=False)
                i = None
                for c in range(8):
                    i = eng.matmul(vp[:], xpv[j][:, c * 128 : (c + 1) * 128], wv_sb[:, c * EW : (c + 1) * EW], start=False, stop=(c == 7))
                i.then_inc(b.sem("vproj"), 1)
            b.on("tensor", vproj)

            vevac_t = b.bump("vevac", 1)

            def vevac(eng, p=p, j=j, t=vproj_t, sn=b.semv.get("vs1", 0)):
                eng.wait_ge(b.sem("vproj"), t)
                if p >= 2 or H >= 1:
                    eng.wait_ge(b.sem("vs1"), sn)
                eng.tensor_copy(vbuf[j][:], [qps, kps][p % 2][:]).then_inc(b.sem("vevac"), 1)
            b.on("vector", vevac)

            vs1_t = b.bump("vs1", 1)

            def vs1(eng, p=p, j=j, t=vevac_t, en=b.semv.get("vs1ev", 0)):
                eng.wait_ge(b.sem("vevac"), t)
                if (p >= 2 or H >= 1) and en >= 1:
                    eng.wait_ge(b.sem("vs1ev"), en - 1)
                pr = s1ps[0 + 2 * (p % 2)]
                pi = s1ps[1 + 2 * (p % 2)]
                eng.matmul(pr[:], c1rebf_sb[:], vbuf[j][:], start=True, stop=True)
                eng.matmul(pi[:], c1imbf_sb[:], vbuf[j][:], start=True, stop=True).then_inc(b.sem("vs1"), 1)
            b.on("tensor", vs1)

            vs1ev_t = b.bump("vs1ev", 1)

            def vs1e(eng, p=p, j=j, t=vs1_t, cn=b.semv.get("vcorner", 0)):
                eng.wait_ge(b.sem("vs1"), t)
                if p >= 2 or H >= 1:
                    eng.wait_ge(b.sem("vcorner"), cn)
                pr = s1ps[0 + 2 * (p % 2)]
                pi = s1ps[1 + 2 * (p % 2)]
                eng.tensor_copy(vs1ev[j][:, 0:EW], pr[:])
                eng.tensor_copy(vs1ev[j][:, EW : 2 * EW], pi[:]).then_inc(b.sem("vs1ev"), 1)
            b.on("vector", vs1e)

            vc_t = b.bump("vcorner", 16 * 4)

            def vcorner(eng, p=p, j=j, t=vs1ev_t):
                eng.wait_ge(b.sem("vs1ev"), t)
                for (c0, base) in ((0, 0), (EW, 64)):
                    for h in (0, 1):
                        src = vs1ev[j][h * 64 : h * 64 + NF2, c0 : c0 + EW]
                        r = base + h * 32 + p
                        eng.dma_start(out=rbv[r : r + 1, :], in_=src).then_inc(b.sem("vcorner"), 16)
            b.on("sync", vcorner)

        if p5_stop == "pairs":
            continue
        vc_all = b.semv["vcorner"]
        for f2 in range(NF2):
            vst2_t = b.bump("vst2", 1)

            def vst2(eng, f2=f2, t=vc_all, ev=b.semv.get("vfev", 0)):
                eng.wait_ge(b.sem("vcorner"), t)
                if not (f2 == 0):
                    eng.wait_ge(b.sem("vfev"), ev)
                eng.matmul(f2ps[0][0:64, :], st2vre_sb[:, f2 * 64 : (f2 + 1) * 64], rbv[:, f2 * EW : (f2 + 1) * EW], start=True, stop=True)
                eng.matmul(f2ps[1][0:64, :], st2vim_sb[:, f2 * 64 : (f2 + 1) * 64], rbv[:, f2 * EW : (f2 + 1) * EW], start=True, stop=True).then_inc(b.sem("vst2"), 1)
            b.on("tensor", vst2)

            vfev_t = b.bump("vfev", 1)

            def vfev(eng, f2=f2, t=vst2_t):
                eng.wait_ge(b.sem("vst2"), t)
                eng.tensor_copy(vfre[:, f2 * EW : (f2 + 1) * EW], f2ps[0][0:64, :])
                eng.tensor_copy(vfim[:, f2 * EW : (f2 + 1) * EW], f2ps[1][0:64, :]).then_inc(b.sem("vfev"), 1)
            b.on("vector", vfev)

        if p5_stop == "vst2":
            continue
        vf_all = b.semv["vfev"]
        gm_t = b.bump("gmul", 1)

        def gmul(eng, H=H, t=vf_all, gt=gtbf_t, ac=b.semv.get("aggc", 0)):
            eng.wait_ge(b.sem("vfev"), t)
            eng.wait_ge(b.sem("gtbf"), gt)
            if ac:
                eng.wait_ge(b.sem("aggc"), ac)
            for jj in range(4):
                m = (H * 4 + jj) % 8
                gr = gt_re_bf[:, m * NF2 : (m + 1) * NF2].unsqueeze(2).broadcast_to([64, NF2, 64])
                gi = gt_im_bf[:, m * NF2 : (m + 1) * NF2].unsqueeze(2).broadcast_to([64, NF2, 64])
                sl = lambda tns: tns[:, jj * 64 : 8448 : EW].rearrange("p (f e) -> p f e", f=NF2) if False else tns.rearrange("p (f e) -> p f e", e=EW)[:, :, jj * 64 : (jj + 1) * 64]
                vre = sl(vfre)
                vim = sl(vfim)
                are = sl(agre)
                aim = sl(agim)
                g1 = gsc1[:].rearrange("p (f e) -> p f e", e=64)
                g2 = gsc2[:].rearrange("p (f e) -> p f e", e=64)
                eng.tensor_tensor(g1, gr, vre, ALU.mult)
                eng.tensor_tensor(g2, gi, vim, ALU.mult)
                eng.tensor_sub(are, g1, g2)
                eng.tensor_tensor(g1, gr, vim, ALU.mult)
                eng.tensor_tensor(g2, gi, vre, ALU.mult)
                eng.tensor_add(aim, g1, g2)
            eng.engine_nop().then_inc(b.sem("gmul"), 1)
        b.on("vector", gmul)

        if p5_stop == "gmul":
            continue

        for f2 in range(NF2):
            ia_mm_t = b.bump("iamm", 1)

            def iamm(eng, f2=f2, H=H, t=gm_t, ev=b.semv.get("iaev", 0), tv=b.semv.get("iatrev", 0)):
                eng.wait_ge(b.sem("gmul"), t)
                eng.wait_ge(b.sem("iaload"), 32)
                if not (H == 0 and f2 == 0):
                    eng.wait_ge(b.sem("iaev"), ev)
                eng.matmul(kps[:], invAre_sb[:, f2 * 128 : (f2 + 1) * 128], agre[:, f2 * EW : (f2 + 1) * EW], start=True, stop=False)
                eng.matmul(kps[:], invAim_sb[:, f2 * 128 : (f2 + 1) * 128], agim[:, f2 * EW : (f2 + 1) * EW], start=False, stop=True).then_inc(b.sem("iamm"), 1)
            b.on("tensor", iamm)

            iaev_t = b.bump("iaev", 1)

            def iaev(eng, f2=f2, t=ia_mm_t, tn=b.semv.get("iatr", 0)):
                eng.wait_ge(b.sem("iamm"), t)
                if tn:
                    eng.wait_ge(b.sem("iatr"), tn)
                eng.tensor_copy(uaev[:], kps[:]).then_inc(b.sem("iaev"), 1)
            b.on("vector", iaev)

            if p5_stop == "iamm1":
                continue
            iatr_t = b.bump("iatr", 1)

            def iatr(eng, f2=f2, t=iaev_t, en=b.semv.get("iatrev", 0)):
                eng.wait_ge(b.sem("iaev"), t)
                if en:
                    eng.wait_ge(b.sem("iatrev"), en)
                tb = [s1ps[0], s1ps[1], f2ps[0], f2ps[1]]
                i = None
                for c in range(2):
                    for h in range(2):
                        i = eng.matmul(tb[c * 2 + h][:, 0:64],
                                       uaev[c * 64 : (c + 1) * 64, h * 128 : (h + 1) * 128],
                                       identbf_sb[c * 64 : (c + 1) * 64, :], start=True, stop=True)
                i.then_inc(b.sem("iatr"), 1)
            b.on("tensor", iatr)

            iatrev_t = b.bump("iatrev", 1)

            def iatrev(eng, f2=f2, t=iatr_t, cn=b.semv.get("iac", 0)):
                eng.wait_ge(b.sem("iatr"), t)
                if cn:
                    eng.wait_ge(b.sem("iac"), cn)
                tb = [s1ps[0], s1ps[1], f2ps[0], f2ps[1]]
                for q in range(4):
                    i = eng.tensor_copy(uaT[:, q * 64 : (q + 1) * 64], tb[q][:, 0:64])
                i.then_inc(b.sem("iatrev"), 1)
            b.on("vector", iatrev)

            if p5_stop == "iamm2":
                continue
            iac_t = b.bump("iac", 16 * 4)

            def iac(eng, f2=f2, t=iatrev_t):
                eng.wait_ge(b.sem("iatrev"), t)
                for c in range(2):
                    for h in range(2):
                        src = uaT[:, (c * 2 + h) * 64 : (c * 2 + h) * 64 + 64]
                        dst = rbB[c * NF2 + f2 : c * NF2 + f2 + 1, h * 8192 : (h + 1) * 8192]
                        eng.dma_start(out=dst, in_=src).then_inc(b.sem("iac"), 16)
            b.on("sync", iac)

        if p5_stop in ("iamm", "iamm1", "iamm2"):
            continue
        iac_all = b.semv["iac"]
        if H == 0:
            def ibload(eng, t=vb_t):
                eng.wait_ge(b.sem("vboot"), t)
                eng.dma_start(out=invB_sb[:], in_=invB_d).then_inc(b.sem("ibload"), 16)
            b.bump("ibload", 16)
            b.on("sync", ibload)

        for ch in range(64):
            sb_t = b.bump("stB", 1)

            def stB(eng, ch=ch, t=iac_all, ev=b.semv.get("stBev", 0)):
                eng.wait_ge(b.sem("iac"), t)
                eng.wait_ge(b.sem("ibload"), 16)
                if not (ch == 0):
                    eng.wait_ge(b.sem("stBev"), ev)
                eng.matmul(s1ps[2][0:64, :], invB_sb[:], rbB[:, ch * EW : (ch + 1) * EW], start=True, stop=True).then_inc(b.sem("stB"), 1)
            b.on("tensor", stB)

            sbe_t = b.bump("stBev", 1)

            def stBev(eng, ch=ch, t=sb_t, cn=b.semv.get("aggc", 0)):
                eng.wait_ge(b.sem("stB"), t)
                if ch == 0 and cn:
                    eng.wait_ge(b.sem("aggc"), cn)
                eng.tensor_copy(aggt[:, ch * EW : (ch + 1) * EW], s1ps[2][0:64, :]).then_inc(b.sem("stBev"), 1)
            b.on("vector", stBev)

        agc_t = b.bump("aggc", 16 * 64)
        sbev_all = b.semv["stBev"]

        def aggc(eng, H=H, t=sbev_all):
            eng.wait_ge(b.sem("stBev"), t)
            for s2 in range(64):
                src = aggt[s2 : s2 + 1, :].rearrange("q (e s1) -> q e s1", s1=64)
                dst = aggfm_d[H * EW : (H + 1) * EW, 64 * s2 : 64 * s2 + 64]
                eng.dma_start(out=dst, in_=src).then_inc(b.sem("aggc"), 16)
        b.on("sync", aggc)

    Wo_d = b.dram_in("Wo", [E, E])
    bo_d = b.dram_in("bo", [1, E])
    xtm_d = b.dram_in("x_tm", [T, E])
    WfcT_d = b.dram_in("WfcT", [E, 4096])
    WprojT_d = b.dram_in("WprojT", [4096, E])
    Afc_d = b.dram_in("Afc", [16, E])
    BfcT_d = b.dram_in("BfcT", [16, 4096])
    Aproj_d = b.dram_in("Aproj", [16, 4096])
    BprojT_d = b.dram_in("BprojT", [16, E])
    bfc_d = b.dram_in("bfc", [1, 4096])
    bproj_d = b.dram_in("bproj", [1, E])
    out_d = b.dram_out("out", [T, E])
    y1tm_d = b.nc.dram_tensor("y1tm", [T, E], BF16, kind="Internal").ap()
    y1fm_d = b.nc.dram_tensor("y1fm", [E, T], BF16, kind="Internal").ap()
    wefffc_d = b.nc.dram_tensor("wefffc", [E, 4096], BF16, kind="Internal").ap()
    weffpr_d = b.nc.dram_tensor("weffpr", [4096, E], BF16, kind="Internal").ap()

    wo_all = bfat("wo_all", [128, 8192], 5760)
    agg_all = [bfat(f"agg_all{j}", [128, 1024], 9856 + j * 512) for j in range(2)]
    x_tile = pool[:, 1536:2560]
    y1t = pool[:, 2560:3584]
    sqt = pool[:, 3584:4608]
    stat = pool[:, 4608:4640]
    y1bf = bfat("y1bf", [128, 1024], 4640)
    trsp = bfat("trsp", [128, 128], 5152)
    bo_bf = bfat("bo_bf", [1, 1024], 5224)
    afc_sb = pool[:, 5760:6784]
    aproj_sb = pool[:, 6784:10880]
    afc_bf = bfat("afc_bf", [16, 1024], 5248)
    aproj_bf = bfat("aproj_bf", [16, 4096], 26400)
    b2fc = bfat("b2fc", [16, 4096], 10880)
    b2pr = bfat("b2pr", [16, 1024], 12928)
    bfc_bf = bfat("bfc_bf", [1, 4096], 13440)
    bpr_bf = bfat("bpr_bf", [1, 1024], 15488)
    wchunk = pool[:, 16000:20096]
    wchunkI = pool[:, 16000:20096].bitcast(I32)[:, 0:4096]
    scrA = pool[:, 20096:24192]
    scrAI = pool[:, 20096:24192].bitcast(I32)[:, 0:4096]
    weffo = bfat("weffo", [128, 4096], 24192)
    qstat = b.sb("qstat", [128, 8])
    y1f_all = bfat("y1f_all", [128, 4096], 10880)
    gtile = bfat("gtile", [128, 16384], 16000)
    wfc_col = [bfat(f"wfc_col{j}", [128, 1024], 24192 + j * 512) for j in range(2)]
    wpr_buf = [bfat(f"wpr_buf{j}", [128, 1024], j * 512) for j in range(2)]
    y1res = bfat("y1res", [128, 1024], 4640)
    out_tile = pool[:, 1536:2560]
    y2t = pool[:, 2560:3584]
    sq2 = pool[:, 3584:4608]
    bo_sb_f = b.sb("bo_sb_f", [1, E])

    aggc_all = b.semv["aggc"]

    def p6boot(eng, t=aggc_all):
        eng.wait_ge(b.sem("aggc"), t)
        eng.dma_start(out=bo_sb_f[:], in_=bo_d).then_inc(b.sem("p6b"), 16)
    b.bump("p6b", 16)
    b.on("sync", p6boot)

    def p6wo(eng, t=aggc_all):
        eng.wait_ge(b.sem("aggc"), t)
        eng.dma_start(out=wo_all[:], in_=Wo_d.rearrange("(c p) o -> p c o", p=128)).then_inc(b.sem("p6w"), 16)
    b.bump("p6w", 16)
    b.on("gpsimd", p6wo)

    def p6boot2(eng):
        eng.wait_ge(b.sem("p6b"), 16)
        eng.tensor_copy(bo_bf[:], bo_sb_f[:]).then_inc(b.sem("p6b2"), 1)
    b.bump("p6b2", 1)
    b.on("vector", p6boot2)

    for tt in range(32):
        xl_t = b.bump("p6x", 16)

        def p6x(eng, tt=tt, pv=b.semv.get("ln1", 0)):
            if tt == 0:
                eng.wait_ge(b.sem("p6b"), 16)
            else:
                eng.wait_ge(b.sem("ln1"), pv)
            eng.dma_start(out=x_tile, in_=xtm_d[tt * 128 : (tt + 1) * 128, :]).then_inc(b.sem("p6x"), 16)
        b.on("sync", p6x)

        j = tt % 2
        al_t = b.bump(f"p6a{j}", 16)

        def p6a(eng, tt=tt, j=j, pv=b.semv.get("p6mm", 0)):
            if tt >= 2:
                eng.wait_ge(b.sem("p6mm"), pv - 8)
            src = aggfm_d.rearrange("(c p) t -> p c t", p=128)[:, :, tt * 128 : (tt + 1) * 128]
            eng.dma_start(out=agg_all[j][:], in_=src).then_inc(b.sem(f"p6a{j}"), 16)
        b.on("sync", p6a)

        for ec in range(8):
            mm_t = b.bump("p6mm", 1)

            def p6mm(eng, tt=tt, ec=ec, j=j, at=al_t, wt=16, ln=b.semv.get("ln1add", 0)):
                if ec == 0:
                    eng.wait_ge(b.sem(f"p6a{j}"), at)
                    eng.wait_ge(b.sem("p6w"), wt)
                    if tt > 0:
                        eng.wait_ge(b.sem("ln1add"), ln)
                    eng.wait_ge(b.sem("p6b2"), 1)
                    eng.matmul(qps_f[:], ones_bf[0:1, 0:128], bo_bf[0:1, 0:512], start=True, stop=False)
                    eng.matmul(kps_f[:], ones_bf[0:1, 0:128], bo_bf[0:1, 512:1024], start=True, stop=False)
                eng.matmul(qps_f[:], agg_all[j][:, ec * 128 : (ec + 1) * 128], wo_all[:, ec * 1024 : ec * 1024 + 512], start=False, stop=(ec == 7))
                eng.matmul(kps_f[:], agg_all[j][:, ec * 128 : (ec + 1) * 128], wo_all[:, ec * 1024 + 512 : (ec + 1) * 1024], start=False, stop=(ec == 7)).then_inc(b.sem("p6mm"), 1)
            b.on("tensor", p6mm)

        mma = b.semv["p6mm"]
        la_t = b.bump("ln1add", 1)

        def ln1a(eng, tt=tt, t=mma, xt=xl_t):
            eng.wait_ge(b.sem("p6mm"), t)
            eng.wait_ge(b.sem("p6x"), xt)
            eng.tensor_add(y1t[:, 0:512], x_tile[:, 0:512], qps_f[:])
            eng.tensor_add(y1t[:, 512:1024], x_tile[:, 512:1024], kps_f[:]).then_inc(b.sem("ln1add"), 1)
        b.on("vector", ln1a)

        ls_t = b.bump("ln1s", 1)

        def ln1s(eng, tt=tt, t=la_t):
            eng.wait_ge(b.sem("ln1add"), t)
            eng.tensor_tensor(sqt, y1t, y1t, ALU.mult)
            eng.reduce_sum(stat[:, 0:1], y1t, axis=AX.X)
            eng.reduce_sum(stat[:, 1:2], sqt, axis=AX.X)
            eng.tensor_scalar_mul(stat[:, 2:3], stat[:, 0:1], 1.0 / 1024.0)
            eng.tensor_scalar_mul(stat[:, 3:4], stat[:, 1:2], 1.0 / 1024.0)
            eng.tensor_tensor(stat[:, 4:5], stat[:, 2:3], stat[:, 2:3], ALU.mult)
            eng.tensor_sub(stat[:, 5:6], stat[:, 3:4], stat[:, 4:5])
            eng.tensor_scalar_add(stat[:, 6:7], stat[:, 5:6], 1e-5).then_inc(b.sem("ln1s"), 1)
        b.on("vector", ln1s)

        lq_t = b.bump("ln1q", 1)

        def ln1q(eng, tt=tt, t=ls_t):
            eng.wait_ge(b.sem("ln1s"), t)
            eng.activation(out=stat[:, 7:8], in_=stat[:, 6:7], func=AF.Sqrt, bias=0.0, scale=1.0).then_inc(b.sem("ln1q"), 1)
        b.on("scalar", ln1q)

        lr_t = b.bump("ln1r", 1)

        def ln1r(eng, tt=tt, t=lq_t):
            eng.wait_ge(b.sem("ln1q"), t)
            eng.reciprocal(stat[:, 0:1], stat[:, 7:8])
            eng.tensor_tensor(stat[:, 1:2], stat[:, 2:3], stat[:, 0:1], ALU.mult)
            eng.tensor_scalar_mul(stat[:, 1:2], stat[:, 1:2], -1.0).then_inc(b.sem("ln1r"), 1)
        b.on("vector", ln1r)

        ln_t = b.bump("ln1", 1)

        def ln1n(eng, tt=tt, t=lr_t, pv=b.semv.get("y1st", 0), pt=b.semv.get("y1tr", 0)):
            eng.wait_ge(b.sem("ln1r"), t)
            if pv:
                eng.wait_ge(b.sem("y1st"), pv)
            if pt:
                eng.wait_ge(b.sem("y1tr"), pt)
            eng.activation(out=y1bf[:], in_=y1t, func=AF.Identity, bias=stat[:, 1:2], scale=stat[:, 0:1]).then_inc(b.sem("ln1"), 1)
        b.on("scalar", ln1n)

        st_t = b.bump("y1st", 16)
        tr_t = b.bump("y1tr", 16 * 16)

        def y1st(eng, tt=tt, t=ln_t):
            eng.wait_ge(b.sem("ln1"), t)
            eng.dma_start(out=y1tm_d[tt * 128 : (tt + 1) * 128, :], in_=y1bf[:]).then_inc(b.sem("y1st"), 16)
        b.on("sync", y1st)

        def y1tr(eng, tt=tt, t=ln_t, base=tr_t - 16 * 16):
            eng.wait_ge(b.sem("ln1"), t)
            for ec in range(8):
                if ec > 0:
                    eng.wait_ge(b.sem("y1tr"), base + 16 * (2 * ec - 1))
                eng.dma_start_transpose(out=trsp[:], in_=y1bf[:, ec * 128 : (ec + 1) * 128]).then_inc(b.sem("y1tr"), 16)
                eng.wait_ge(b.sem("y1tr"), base + 16 * (2 * ec + 1))
                eng.dma_start(out=y1fm_d[ec * 128 : (ec + 1) * 128, tt * 128 : (tt + 1) * 128], in_=trsp[:]).then_inc(b.sem("y1tr"), 16)
        b.on("scalar", y1tr)

    y1_all = b.semv["y1st"]
    y1tr_all = b.semv["y1tr"]

    def q_boot(eng, t=y1_all):
        eng.wait_ge(b.sem("y1st"), t)
        eng.dma_start(out=afc_sb[0:16, :], in_=Afc_d).then_inc(b.sem("qb"), 16)
        eng.dma_start(out=aproj_sb[0:16, :], in_=Aproj_d).then_inc(b.sem("qb"), 16)
        eng.dma_start(out=wchunk[0:16, :], in_=BfcT_d).then_inc(b.sem("qb"), 16)
        eng.dma_start(out=scrA[0:16, 0:1024], in_=BprojT_d).then_inc(b.sem("qb"), 16)
        eng.dma_start(out=wchunk[32:33, :], in_=bfc_d).then_inc(b.sem("qb"), 16)
        eng.dma_start(out=scrA[0:1, 1024:2048], in_=bproj_d).then_inc(b.sem("qb"), 16)
    b.bump("qb", 96)
    b.on("sync", q_boot)

    def q_boot2(eng):
        eng.wait_ge(b.sem("qb"), 96)
        eng.tensor_scalar_mul(b2fc[0:16, :], wchunk[0:16, :], 2.0)
        eng.tensor_scalar_mul(b2pr[0:16, :], scrA[0:16, 0:1024], 2.0)
        eng.tensor_copy(afc_bf[0:16, :], afc_sb[0:16, :])
        eng.tensor_copy(aproj_bf[0:16, :], aproj_sb[0:16, :])
        eng.tensor_copy(bfc_bf[:], wchunk[32:33, :])
        eng.tensor_copy(bpr_bf[:], scrA[0:1, 1024:2048]).then_inc(b.sem("qb2"), 1)
    b.bump("qb2", 1)
    b.on("vector", q_boot2)

    for (name, src_d, nch, width) in (("fc", WfcT_d, 8, 4096), ("pr", WprojT_d, 32, 1024)):
        for c in range(nch):
            ql_t = b.bump("ql", 16)

            def ql(eng, c=c, src_d=src_d, width=width, pv=b.semv.get("qm", 0)):
                eng.wait_ge(b.sem("qb2"), 1)
                if pv:
                    eng.wait_ge(b.sem("qm"), pv)
                eng.dma_start(out=wchunk[:, 0:width], in_=src_d[c * 128 : (c + 1) * 128, :]).then_inc(b.sem("ql"), 16)
            b.on("sync", ql)

            qm_t = b.bump("qm", 1)

            def qm(eng, name=name, c=c, width=width, t=ql_t):
                eng.wait_ge(b.sem("ql"), t)
                col = 0 if name == "fc" else 1
                eng.tensor_reduce(stat[:, 2:3], wchunk[:, 0:width], axis=AX.X, op=ALU.max, apply_absolute_value=True)
                if c == 0:
                    eng.tensor_copy(qstat[:, col : col + 1], stat[:, 2:3]).then_inc(b.sem("qm"), 1)
                else:
                    eng.tensor_max(qstat[:, col : col + 1], qstat[:, col : col + 1], stat[:, 2:3]).then_inc(b.sem("qm"), 1)
            b.on("vector", qm)

    qm_all = b.semv["qm"]

    def qredmm(eng, t=qm_all):
        eng.wait_ge(b.sem("qm"), t)
        eng.matmul(f2ps_f[0][0:2, 0:128], qstat[:, 0:2], ident_sb[:], start=True, stop=True).then_inc(b.sem("qred"), 1)
    b.bump("qred", 1)
    b.on("tensor", qredmm)

    def qredv(eng):
        eng.wait_ge(b.sem("qred"), 1)
        eng.tensor_copy(sqt[0:2, 0:128], f2ps_f[0][0:2, 0:128])
        eng.reduce_max(stat[0:2, 0:1], sqt[0:2, 0:128], axis=AX.X).then_inc(b.sem("qred"), 1)
    b.bump("qred", 1)
    b.on("vector", qredv)

    def qredmm2(eng):
        eng.wait_ge(b.sem("qred"), 2)
        eng.matmul(f2ps_f[0][0:1, 0:2], stat[0:2, 0:1], ident_sb[0:2, 0:2], start=True, stop=True).then_inc(b.sem("qred"), 1)
    b.bump("qred", 1)
    b.on("tensor", qredmm2)

    def qredv2(eng):
        eng.wait_ge(b.sem("qred"), 3)
        eng.tensor_copy(stat[0:1, 2:4], f2ps_f[0][0:1, 0:2]).then_inc(b.sem("qred"), 1)
    b.bump("qred", 1)
    b.on("vector", qredv2)

    def qredmm3(eng):
        eng.wait_ge(b.sem("qred"), 4)
        eng.matmul(f2ps_f[0][:, 0:2], ones1[:], stat[0:1, 2:4], start=True, stop=True).then_inc(b.sem("qred"), 1)
    b.bump("qred", 1)
    b.on("tensor", qredmm3)

    def qredv3(eng):
        eng.wait_ge(b.sem("qred"), 5)
        eng.tensor_copy(qstat[:, 2:4], f2ps_f[0][:, 0:2]).then_inc(b.sem("qred"), 1)
    b.bump("qred", 1)
    b.on("vector", qredv3)

    def qrs(eng):
        eng.wait_ge(b.sem("qred"), 6)
        eng.reciprocal(qstat[:, 4:5], qstat[:, 2:3])
        eng.tensor_scalar_mul(qstat[:, 4:5], qstat[:, 4:5], 127.0)
        eng.reciprocal(qstat[:, 5:6], qstat[:, 3:4])
        eng.tensor_scalar_mul(qstat[:, 5:6], qstat[:, 5:6], 127.0)
        eng.tensor_scalar_mul(qstat[:, 6:7], qstat[:, 2:3], 1.0 / 127.0)
        eng.tensor_scalar_mul(qstat[:, 7:8], qstat[:, 3:4], 1.0 / 127.0)
        eng.tensor_scalar_mul(qstat[:, 0:1], qstat[:, 6:7], -1.0)
        eng.tensor_scalar_mul(qstat[:, 1:2], qstat[:, 7:8], -1.0).then_inc(b.sem("qrs"), 1)
    b.bump("qrs", 1)
    b.on("vector", qrs)

    for (name, src_d, dst_d, nch, width, A_sb, B_sb, rcol, ncol, amax_col) in (
        ("fc", WfcT_d, wefffc_d, 8, 4096, afc_bf, b2fc, 4, 0, 2),
        ("pr", WprojT_d, weffpr_d, 32, 1024, aproj_bf, b2pr, 5, 1, 3),
    ):
        for c in range(nch):
            ql2_t = b.bump("ql2", 16)

            def ql2(eng, c=c, src_d=src_d, width=width, pv=b.semv.get("qst", 0)):
                eng.wait_ge(b.sem("qrs"), 1)
                if pv:
                    eng.wait_ge(b.sem("qst"), pv)
                eng.dma_start(out=wchunk[:, 0:width], in_=src_d[c * 128 : (c + 1) * 128, :]).then_inc(b.sem("ql2"), 16)
            b.on("sync", ql2)

            nos = width // 1024
            for osi in range(nos):
                lo_t = b.bump("qlo", 1)

                def qlo(eng, name=name, c=c, osi=osi, A_sb=A_sb, B_sb=B_sb, t=ql2_t, pv=b.semv.get("qev", 0)):
                    eng.wait_ge(b.sem("ql2"), t)
                    if pv:
                        eng.wait_ge(b.sem("qev"), pv)
                    eng.matmul(qps_f[:], A_sb[0:16, c * 128 : (c + 1) * 128], B_sb[0:16, osi * 1024 : osi * 1024 + 512], start=True, stop=True)
                    eng.matmul(kps_f[:], A_sb[0:16, c * 128 : (c + 1) * 128], B_sb[0:16, osi * 1024 + 512 : (osi + 1) * 1024], start=True, stop=True).then_inc(b.sem("qlo"), 1)
                b.on("tensor", qlo)

                qa_t = b.bump("qact", 1)

                def qact(eng, name=name, osi=osi, rcol=rcol, amax_col=amax_col, ncol=ncol, t=lo_t, pq=b.semv.get("qint", 0)):
                    sl = slice(osi * 1024, (osi + 1) * 1024)
                    eng.wait_ge(b.sem("qlo"), t)
                    if pq:
                        eng.wait_ge(b.sem("qint"), pq)
                    eng.activation(out=scrA[:, sl], in_=wchunk[:, sl], func=AF.Identity, bias=16384.0, scale=qstat[:, rcol : rcol + 1]).then_inc(b.sem("qact"), 1)
                b.on("scalar", qact)

                qi_t = b.bump("qint", 1)

                def qint(eng, osi=osi, t=qa_t):
                    sl = slice(osi * 1024, (osi + 1) * 1024)
                    eng.wait_ge(b.sem("qact"), t)
                    eng.tensor_copy(wchunkI[:, sl], scrA[:, sl])
                    eng.tensor_copy(wchunk[:, sl], wchunkI[:, sl])
                    eng.tensor_sub(scrA[:, sl], scrA[:, sl], wchunk[:, sl])
                    eng.tensor_single_scalar(scrA[:, sl], scrA[:, sl], 0.5, ALU.is_ge)
                    eng.tensor_add(scrA[:, sl], scrA[:, sl], wchunk[:, sl]).then_inc(b.sem("qint"), 1)
                b.on("vector", qint)

                qc_t = b.bump("qchain", 1)

                def qchain(eng, osi=osi, ncol=ncol, amax_col=amax_col, t=qi_t):
                    sl = slice(osi * 1024, (osi + 1) * 1024)
                    eng.wait_ge(b.sem("qint"), t)
                    eng.activation(out=wchunk[:, sl], in_=scrA[:, sl], func=AF.Relu, bias=-16256.0, scale=1.0)
                    eng.activation(out=scrA[:, sl], in_=wchunk[:, sl], func=AF.Relu, bias=255.0, scale=-1.0)
                    eng.activation(out=wchunk[:, sl], in_=scrA[:, sl], func=AF.Identity, bias=qstat[:, amax_col : amax_col + 1], scale=qstat[:, ncol : ncol + 1]).then_inc(b.sem("qchain"), 1)
                b.on("scalar", qchain)

                qe_t = b.bump("qev", 1)

                def qev(eng, osi=osi, t=qc_t):
                    eng.wait_ge(b.sem("qchain"), t)
                    eng.tensor_add(weffo[:, osi * 1024 : osi * 1024 + 512], wchunk[:, osi * 1024 : osi * 1024 + 512], qps_f[:])
                    eng.tensor_add(weffo[:, osi * 1024 + 512 : (osi + 1) * 1024], wchunk[:, osi * 1024 + 512 : (osi + 1) * 1024], kps_f[:]).then_inc(b.sem("qev"), 1)
                b.on("vector", qev)

            qs_t = b.bump("qst", 16)
            qe_all = b.semv["qev"]

            def qst(eng, c=c, dst_d=dst_d, width=width, t=qe_all):
                eng.wait_ge(b.sem("qev"), t)
                eng.dma_start(out=dst_d[c * 128 : (c + 1) * 128, :], in_=weffo[:, 0:width]).then_inc(b.sem("qst"), 16)
            b.on("gpsimd", qst)

    y2ps = [(s1ps_f[0], s1ps_f[1]), (s1ps_f[2], s1ps_f[3]), (f2ps_f[0], f2ps_f[1]), (qps_f, kps_f)]
    for t5 in range(8):
        yl_t = b.bump("fy1", 16 * 8)

        def fy1(eng, t5=t5, pv=b.semv.get("fgel", 0), qt=b.semv.get("qst", 0), ytr=y1tr_all):
            eng.wait_ge(b.sem("qst"), qt)
            eng.wait_ge(b.sem("y1tr"), ytr)
            if t5 > 0:
                eng.wait_ge(b.sem("fgel"), pv)
            for ic in range(8):
                eng.dma_start(out=y1f_all[:, ic * 512 : (ic + 1) * 512], in_=y1fm_d[ic * 128 : (ic + 1) * 128, t5 * 512 : (t5 + 1) * 512]).then_inc(b.sem("fy1"), 16)
        b.on("sync", fy1)

        for oc in range(32):
            j = oc % 2
            wb_t = b.bump(f"fwb{j}", 16)

            def fwb(eng, t5=t5, oc=oc, j=j, pv=b.semv.get("fmm", 0), qt=b.semv.get("qst", 0)):
                if t5 * 32 + oc < 2:
                    eng.wait_ge(b.sem("qst"), qt)
                else:
                    eng.wait_ge(b.sem("fmm"), pv - 8)
                src = wefffc_d.rearrange("(c p) o -> p c o", p=128)[:, :, oc * 128 : (oc + 1) * 128]
                eng.dma_start(out=wfc_col[j][:], in_=src).then_inc(b.sem(f"fwb{j}"), 16)
            b.on("sync", fwb)

            for ic in range(8):
                fm_t = b.bump("fmm", 1)

                def fmm(eng, t5=t5, oc=oc, ic=ic, j=j, wt=wb_t, yt=yl_t, gl=b.semv.get("fgel", 0), la=b.semv.get("ln2add", 0)):
                    if ic == 0:
                        eng.wait_ge(b.sem(f"fwb{j}"), wt)
                        eng.wait_ge(b.sem("fy1"), yt)
                        if oc == 0 and t5 > 0 and la:
                            eng.wait_ge(b.sem("ln2add"), la)
                        if t5 + oc > 0:
                            eng.wait_ge(b.sem("fgel"), gl)
                        eng.matmul(qps_f[:], bfc_bf[0:1, oc * 128 : (oc + 1) * 128], ones_bf[0:1, 0:512], start=True, stop=False)
                    eng.matmul(qps_f[:], wfc_col[j][:, ic * 128 : (ic + 1) * 128], y1f_all[:, ic * 512 : (ic + 1) * 512], start=False, stop=(ic == 7)).then_inc(b.sem("fmm"), 1)
                b.on("tensor", fmm)

            fg_t = b.bump("fgel", 1)

            def fgel(eng, t5=t5, oc=oc, t=b.semv["fmm"]):
                eng.wait_ge(b.sem("fmm"), t)
                eng.activation(out=gtile[:, oc * 512 : (oc + 1) * 512], in_=qps_f[:], func=AF.Gelu, bias=0.0, scale=1.0).then_inc(b.sem("fgel"), 1)
            b.on("scalar", fgel)

        fg_all = b.semv["fgel"]
        for oc in range(32):
            j = oc % 2
            wp_t = b.bump(f"fwp{j}", 16)

            def fwp(eng, t5=t5, oc=oc, j=j, pv=b.semv.get("f2mm", 0)):
                if t5 + oc > 0:
                    eng.wait_ge(b.sem("f2mm"), max(0, pv - 1))
                eng.dma_start(out=wpr_buf[j][:], in_=weffpr_d[oc * 128 : (oc + 1) * 128, :]).then_inc(b.sem(f"fwp{j}"), 16)
            b.on("scalar", fwp)

            f2_t = b.bump("f2mm", 1)

            def f2mm(eng, t5=t5, oc=oc, j=j, wt=wp_t, ga=fg_all, ln=b.semv.get("ln2add", 0)):
                eng.wait_ge(b.sem(f"fwp{j}"), wt)
                eng.wait_ge(b.sem("fgel"), ga)
                if oc == 0 and t5 > 0:
                    eng.wait_ge(b.sem("ln2add"), ln)
                for ts4 in range(4):
                    pa, pb = y2ps[ts4]
                    if oc == 0:
                        eng.matmul(pa[:], ones_bf[0:1, 0:128], bpr_bf[0:1, 0:512], start=True, stop=False)
                        eng.matmul(pb[:], ones_bf[0:1, 0:128], bpr_bf[0:1, 512:1024], start=True, stop=False)
                    g_blk = gtile[:, oc * 512 + ts4 * 128 : oc * 512 + (ts4 + 1) * 128]
                    eng.matmul(pa[:], g_blk, wpr_buf[j][:, 0:512], start=False, stop=(oc == 31))
                    i = eng.matmul(pb[:], g_blk, wpr_buf[j][:, 512:1024], start=False, stop=(oc == 31))
                i.then_inc(b.sem("f2mm"), 1)
            b.on("tensor", f2mm)

        f2_all = b.semv["f2mm"]
        for ts4 in range(4):
            rl_t = b.bump("fyr", 16)

            def fyr(eng, t5=t5, ts4=ts4, pv=b.semv.get("ln2add", 0)):
                if t5 + ts4 > 0:
                    eng.wait_ge(b.sem("ln2add"), pv)
                r0 = t5 * 512 + ts4 * 128
                eng.dma_start(out=y1res[:], in_=y1tm_d[r0 : r0 + 128, :]).then_inc(b.sem("fyr"), 16)
            b.on("sync", fyr)

            l2a_t = b.bump("ln2add", 1)

            def ln2a(eng, t5=t5, ts4=ts4, t=f2_all, rt=rl_t):
                eng.wait_ge(b.sem("f2mm"), t)
                eng.wait_ge(b.sem("fyr"), rt)
                pa, pb = y2ps[ts4]
                eng.tensor_add(y2t[:, 0:512], y1res[:, 0:512], pa[:])
                eng.tensor_add(y2t[:, 512:1024], y1res[:, 512:1024], pb[:]).then_inc(b.sem("ln2add"), 1)
            b.on("vector", ln2a)

            l2s_t = b.bump("ln2s", 1)

            def ln2s(eng, t=l2a_t):
                eng.wait_ge(b.sem("ln2add"), t)
                eng.tensor_tensor(sq2, y2t, y2t, ALU.mult)
                eng.reduce_sum(stat[:, 0:1], y2t, axis=AX.X)
                eng.reduce_sum(stat[:, 1:2], sq2, axis=AX.X)
                eng.tensor_scalar_mul(stat[:, 2:3], stat[:, 0:1], 1.0 / 1024.0)
                eng.tensor_scalar_mul(stat[:, 3:4], stat[:, 1:2], 1.0 / 1024.0)
                eng.tensor_tensor(stat[:, 4:5], stat[:, 2:3], stat[:, 2:3], ALU.mult)
                eng.tensor_sub(stat[:, 5:6], stat[:, 3:4], stat[:, 4:5])
                eng.tensor_scalar_add(stat[:, 6:7], stat[:, 5:6], 1e-5).then_inc(b.sem("ln2s"), 1)
            b.on("vector", ln2s)

            l2q_t = b.bump("ln2q", 1)

            def ln2q(eng, t=l2s_t):
                eng.wait_ge(b.sem("ln2s"), t)
                eng.activation(out=stat[:, 7:8], in_=stat[:, 6:7], func=AF.Sqrt, bias=0.0, scale=1.0).then_inc(b.sem("ln2q"), 1)
            b.on("scalar", ln2q)

            l2r_t = b.bump("ln2r", 1)

            def ln2r(eng, t=l2q_t):
                eng.wait_ge(b.sem("ln2q"), t)
                eng.reciprocal(stat[:, 0:1], stat[:, 7:8])
                eng.tensor_tensor(stat[:, 1:2], stat[:, 2:3], stat[:, 0:1], ALU.mult)
                eng.tensor_scalar_mul(stat[:, 1:2], stat[:, 1:2], -1.0).then_inc(b.sem("ln2r"), 1)
            b.on("vector", ln2r)

            l2n_t = b.bump("ln2", 1)

            def ln2n(eng, t=l2r_t, ot=b.semv.get("outst", 0)):
                eng.wait_ge(b.sem("ln2r"), t)
                if ot:
                    eng.wait_ge(b.sem("outst"), ot)
                eng.activation(out=out_tile, in_=y2t, func=AF.Identity, bias=stat[:, 1:2], scale=stat[:, 0:1]).then_inc(b.sem("ln2"), 1)
            b.on("scalar", ln2n)

            os_t = b.bump("outst", 16)

            def outst(eng, t5=t5, ts4=ts4, t=l2n_t):
                eng.wait_ge(b.sem("ln2"), t)
                r0 = t5 * 512 + ts4 * 128
                eng.dma_start(out=out_d[r0 : r0 + 128, :], in_=out_tile).then_inc(b.sem("outst"), 16)
            b.on("sync", outst)

    out_all = b.semv["outst"]

    def fin(eng, t=out_all):
        eng.wait_ge(b.sem("outst"), t)
    b.on("gpsimd", fin)


    if upto == "agg" and p5_stop is not None:
        return b
    if upto == "agg":
        agg_out = b.dram_out("agg_dbg", [1024, 4096], BF16)
        bounce = pool[:, 0:4096].bitcast(BF16)[:, 0:4096]

        def aggout(eng, t=b.semv["aggc"]):
            eng.wait_ge(b.sem("aggc"), t)
            n = 0
            for c in range(8):
                eng.dma_start(out=bounce, in_=aggfm_d[c * 128 : (c + 1) * 128, :]).then_inc(b.sem("outd3"), 16)
                n += 16
                eng.wait_ge(b.sem("outd3"), n)
                eng.dma_start(out=agg_out[c * 128 : (c + 1) * 128, :], in_=bounce).then_inc(b.sem("outd3"), 16)
                n += 16
                eng.wait_ge(b.sem("outd3"), n)
        b.bump("outd3", 16 * 16)
        b.on("gpsimd", aggout)
        return b

    return b




_BUILD_CACHE = {}


def _get_build():
    if "b" not in _BUILD_CACHE:
        b = build_p123(upto="full", use_cc=True)
        nc = b.finalize()
        _BUILD_CACHE["b"] = b
        _BUILD_CACHE["nc"] = nc
    return _BUILD_CACHE["b"], _BUILD_CACHE["nc"]


def kernel(**inputs):
    from concourse.bass_utils import run_bass_kernel_spmd

    b, nc = _get_build()
    in_maps = []
    for core in range(8):
        m = host_prep_attn(inputs, core)
        m.update(b.const_arrays)
        in_maps.append(m)
    res = run_bass_kernel_spmd(nc, in_maps, core_ids=list(range(8)))
    out = np.stack([np.asarray(res.results[c]["out"]) for c in range(8)], axis=0)
    return out.astype(np.float32)


# version 08
# speedup vs baseline: 1.1600x; 1.1600x over previous
import numpy as np
import concourse.bass as bass
import concourse.mybir as mybir

F32 = mybir.dt.float32
BF16 = mybir.dt.bfloat16
AF = mybir.ActivationFunctionType
ALU = mybir.AluOpType
AX = mybir.AxisListType
U32 = mybir.dt.uint32
from concourse.bass_isa_rust import ReduceOp as _ReduceOp
from concourse.library_config import attn as _LIB_ATTN
_RED_MAX = _ReduceOp.max if hasattr(_ReduceOp, 'max') else None
I32 = mybir.dt.int32

T = 4096
E = 1024
NF2 = 33
EW = 256
NH = E // EW
PERM = np.array([2 * p + h for h in (0, 1) for p in range(32)])
DBG_NOACC = False


def make_consts():
    c = {}
    s2 = np.arange(64)[:, None]
    f2 = np.arange(64)[None, :]
    M1 = np.exp(-2j * np.pi * s2 * f2 / 64.0)
    z = np.zeros((64, 64))
    c["c1re_bd"] = np.block([[M1.real, z], [z, M1.real]]).astype(np.float32)
    c["c1im_bd"] = np.block([[M1.imag, z], [z, M1.imag]]).astype(np.float32)

    s1 = np.arange(64)
    f1 = np.arange(64)
    L2 = np.zeros((NF2, 128, 128), np.float32)
    for ff2 in range(NF2):
        ang = -2.0 * np.pi * (s1[:, None] * (ff2 + 64.0 * f1[None, :])) / T
        cr, ci = np.cos(ang), np.sin(ang)
        L2[ff2, 0:64, 0:64] = cr[PERM, :]
        L2[ff2, 0:64, 64:128] = ci[PERM, :]
        L2[ff2, 64:128, 0:64] = -ci[PERM, :]
        L2[ff2, 64:128, 64:128] = cr[PERM, :]
    c["stage2_L"] = L2

    inner = np.exp(+2j * np.pi * np.outer(f1, s1) / 64.0)
    WA = np.zeros((128, 128), np.float32)
    WA[0:64, 0:64] = inner.real
    WA[0:64, 64:128] = inner.imag
    WA[64:128, 0:64] = -inner.imag
    WA[64:128, 64:128] = inner.real
    c["macA_rhs"] = WA

    d = np.where((np.arange(NF2) >= 1) & (np.arange(NF2) <= 31), 2.0, 1.0)
    tw = np.exp(+2j * np.pi * np.outer(s1, np.arange(NF2)) / T).T
    twf = d[:, None] * tw / (T * E)
    c["mac_tw_re"] = np.concatenate([twf.real, twf.real], axis=1).astype(np.float32)
    c["mac_tw_im"] = np.concatenate([twf.imag, twf.imag], axis=1).astype(np.float32)

    ang = +2.0 * np.pi * np.outer(np.arange(NF2), s1) / 64.0
    c["macB_cos"] = np.cos(ang).astype(np.float32)
    c["macB_msin"] = (-np.sin(ang)).astype(np.float32)

    import ml_dtypes
    bf = lambda a: np.asarray(a, dtype=ml_dtypes.bfloat16)
    c["c1re_bf"] = bf(c["c1re_bd"])
    c["c1im_bf"] = bf(c["c1im_bd"])
    c["st2v_re"] = bf(c["stage2_L"][:, :, 0:64].reshape(NF2 * 128, 64))
    c["st2v_im"] = bf(c["stage2_L"][:, :, 64:128].reshape(NF2 * 128, 64))
    LA = np.zeros((NF2, 128, 128), np.float32)
    f1 = np.arange(64)
    for ff2 in range(NF2):
        dd = 2.0 if 1 <= ff2 <= 31 else 1.0
        inner = np.exp(+2j * np.pi * np.outer(f1, s1.ravel() if s1.ndim > 1 else s1) / 64.0)
        tw = np.exp(+2j * np.pi * np.arange(64) * ff2 / T)[None, :]
        Mfull = dd * (1.0 / T) * inner * tw
        LA[ff2, 0:64, 0:64] = Mfull.real
        LA[ff2, 0:64, 64:128] = Mfull.imag
        LA[ff2, 64:128, 0:64] = -Mfull.imag
        LA[ff2, 64:128, 64:128] = Mfull.real
    c["invA_re"] = bf(LA[:, 0:64, :].reshape(NF2 * 64, 128))
    c["invA_im"] = bf(LA[:, 64:128, :].reshape(NF2 * 64, 128))
    angB = +2.0 * np.pi * np.outer(np.arange(NF2), np.arange(64)) / 64.0
    LB = np.zeros((66, 64), np.float32)
    LB[0:NF2, :] = np.cos(angB)
    LB[NF2 : 2 * NF2, :] = -np.sin(angB)
    c["invB"] = bf(LB)
    fvals = (np.arange(64)[:, None] * 64 + np.arange(NF2)[None, :]).reshape(-1)
    c["f_rep"] = np.broadcast_to(fvals.astype(np.float32), (128, 64 * NF2)).copy()
    return c


def host_prep_attn(inputs, core):
    x = np.asarray(inputs["L0_attn_hidden_states"])[core]
    d = {}
    xt = np.ascontiguousarray(x.T).reshape(E, 64, 64)
    xp = np.empty((32, E, 128), np.float32)
    for p in range(32):
        xp[p, :, 0:64] = xt[:, :, 2 * p]
        xp[p, :, 64:128] = xt[:, :, 2 * p + 1]
    d["xpair"] = xp
    d["Wq"] = np.ascontiguousarray(np.asarray(inputs["L0_attn_Wq"], np.float32))
    d["Wk"] = np.ascontiguousarray(np.asarray(inputs["L0_attn_Wk"], np.float32))
    d["bq"] = np.ascontiguousarray(np.asarray(inputs["L0_attn_bq"], np.float32))[None, :]
    d["bk"] = np.ascontiguousarray(np.asarray(inputs["L0_attn_bk"], np.float32))[None, :]
    d["Wv"] = np.ascontiguousarray(np.asarray(inputs["L0_attn_Wv"], np.float32))
    d["bv"] = np.ascontiguousarray(np.asarray(inputs["L0_attn_bv"], np.float32))[None, :]
    d["Wo"] = np.ascontiguousarray(np.asarray(inputs["L0_attn_Wo"], np.float32))
    d["bo"] = np.ascontiguousarray(np.asarray(inputs["L0_attn_bo"], np.float32))[None, :]
    d["x_tm"] = np.ascontiguousarray(x.astype(np.float32))
    d["WfcT"] = np.ascontiguousarray(np.asarray(inputs["L0_ffn_W_fc"], np.float32).T)
    d["WprojT"] = np.ascontiguousarray(np.asarray(inputs["L0_ffn_W_proj"], np.float32).T)
    d["Afc"] = np.ascontiguousarray(np.asarray(inputs["L0_ffn_A_fc"], np.float32))
    d["BfcT"] = np.ascontiguousarray(np.asarray(inputs["L0_ffn_B_fc"], np.float32).T)
    d["Aproj"] = np.ascontiguousarray(np.asarray(inputs["L0_ffn_A_proj"], np.float32))
    d["BprojT"] = np.ascontiguousarray(np.asarray(inputs["L0_ffn_B_proj"], np.float32).T)
    d["bfc"] = np.ascontiguousarray(np.asarray(inputs["L0_ffn_b_fc"], np.float32))[None, :]
    d["bproj"] = np.ascontiguousarray(np.asarray(inputs["L0_ffn_b_proj"], np.float32))[None, :]
    return d


class B:

    def __init__(self):
        self.nc = bass.Bass("TRN2", target_bir_lowering=False)
        self.cmds = {k: [] for k in ["sync", "tensor", "vector", "scalar", "gpsimd"]}
        self.semv = {}
        self.sems = {}
        self.const_arrays = {}

    def sem(self, name):
        if name not in self.sems:
            self.sems[name] = self.nc.semaphore(name).__enter__()
            self.semv[name] = 0
        return self.sems[name]

    def bump(self, name, n):
        self.sem(name)
        self.semv[name] += n
        return self.semv[name]

    def on(self, eng, fn):
        self.cmds[eng].append(fn)

    def dram_in(self, name, shape, dtype=F32):
        return self.nc.dram_tensor(name, list(shape), dtype, kind="ExternalInput").ap()

    def dram_out(self, name, shape, dtype=F32):
        return self.nc.dram_tensor(name, list(shape), dtype, kind="ExternalOutput").ap()

    def sb(self, name, shape, dtype=F32):
        return self.nc.alloc_sbuf_tensor(name, list(shape), dtype)

    def ps(self, name, shape, dtype=F32):
        return self.nc.alloc_psum_tensor(name, list(shape), dtype)

    def const_in(self, name, arr):
        arr = np.ascontiguousarray(arr)
        import ml_dtypes
        dt_ = F32 if arr.dtype == np.float32 else (BF16 if arr.dtype == ml_dtypes.bfloat16 else None)
        assert dt_ is not None, arr.dtype
        ap = self.dram_in(name, arr.shape, dt_)
        self.const_arrays[name] = arr
        return ap

    class _AutoSer:
        _SKIP = {"wait_ge", "dma_start", "dma_start_transpose", "then_inc",
                 "sem_inc", "nop", "drain"}

        class _Proxy:
            def __init__(self, ser, ins):
                self._ser = ser
                self._ins = ins

            def then_inc(self, sem, n):
                self._ins.then_inc(sem, n)
                c = self._ser._ucount.get(id(sem), 0) + n
                self._ser._ucount[id(sem)] = c
                self._ser._token = (sem, c)
                self._ser._last = None
                return self._ins

            def __getattr__(self, name):
                return getattr(self._ins, name)

        def __init__(self, eng, sem):
            self._eng = eng
            self._sem = sem
            self._count = 0
            self._token = None
            self._last = None
            self._ucount = {}

        def _flush(self):
            if self._last is not None:
                self._last.then_inc(self._sem, 1)
                self._count += 1
                self._token = (self._sem, self._count)
                self._last = None

        def __getattr__(self, name):
            attr = getattr(self._eng, name)
            if name.startswith("_") or name in self._SKIP or not callable(attr):
                return attr

            def call(*a, **k):
                self._flush()
                if self._token is not None:
                    self._eng.wait_ge(self._token[0], self._token[1])
                    self._token = None
                ins = attr(*a, **k)
                if hasattr(ins, "then_inc"):
                    self._last = ins
                    return B._AutoSer._Proxy(self, ins)
                return ins

            return call

    def finalize(self):
        nc = self.nc
        with nc.Block() as block:
            @block.sync
            def _(eng):
                for fn in self.cmds["sync"]:
                    fn(eng)

            @block.tensor
            def _(eng):
                for fn in self.cmds["tensor"]:
                    fn(eng)

            @block.vector
            def _(eng):
                w = B._AutoSer(eng, self.sem("autoV"))
                for fn in self.cmds["vector"]:
                    fn(w)

            @block.scalar
            def _(eng):
                w = B._AutoSer(eng, self.sem("autoS"))
                for fn in self.cmds["scalar"]:
                    fn(w)

            @block.gpsimd
            def _(eng):
                w = B._AutoSer(eng, self.sem("autoG"))
                for fn in self.cmds["gpsimd"]:
                    fn(w)
        return nc


def build_p123(dump=(), dump_slice=0, upto="mac", p5_stop=None, use_cc=False):
    b = B()
    C = make_consts()
    dump = set(dump)

    xpair = b.dram_in("xpair", [32, E, 128])
    Wq_d = b.dram_in("Wq", [E, E])
    Wk_d = b.dram_in("Wk", [E, E])
    bq_d = b.dram_in("bq", [1, E])
    bk_d = b.dram_in("bk", [1, E])
    Wv_d = b.dram_in("Wv", [E, E])
    bv_d = b.dram_in("bv", [1, E])
    c1re_d = b.const_in("c1re_bd", C["c1re_bd"])
    c1im_d = b.const_in("c1im_bd", C["c1im_bd"])
    st2_d = b.const_in("stage2_L", C["stage2_L"].reshape(NF2 * 128, 128))
    macA_d = b.const_in("macA_rhs", C["macA_rhs"])
    mtwre_d = b.const_in("mac_tw_re", C["mac_tw_re"])
    mtwim_d = b.const_in("mac_tw_im", C["mac_tw_im"])
    macBc_d = b.const_in("macB_cos", C["macB_cos"])
    macBs_d = b.const_in("macB_msin", C["macB_msin"])
    c1rebf_d = b.const_in("c1re_bf", C["c1re_bf"])
    c1imbf_d = b.const_in("c1im_bf", C["c1im_bf"])
    st2vre_d = b.const_in("st2v_re", C["st2v_re"])
    st2vim_d = b.const_in("st2v_im", C["st2v_im"])
    invAre_d = b.const_in("invA_re", C["invA_re"])
    invAim_d = b.const_in("invA_im", C["invA_im"])
    invB_d = b.const_in("invB", C["invB"])
    mac_out = b.dram_out("mac", [64, 64])
    cvals = np.broadcast_to(np.array([-np.pi, 2.0 * np.pi / 4096.0, 0.0, 1.0,
                                      -1.0, 16512.5, -16384.0, 255.0,
                                      16384.0, -16256.0], np.float32), (128, 10)).copy()
    ident_d = b.const_in("ident128", np.eye(128, dtype=np.float32))
    ident_sb = b.sb("ident_sb", [128, 128])
    import ml_dtypes as _mld
    identbf_d = b.const_in("ident64bf", np.asarray(np.tile(np.eye(64), (2, 1)), dtype=_mld.bfloat16))
    identbf_sb = b.sb("identbf_sb", [128, 64], BF16)
    cvals_d = b.const_in("cvals", cvals)
    csb = b.sb("csb", [128, 10])
    if upto in ("topk",):
        g_out = b.dram_out("g", [2, 64, 8 * NF2])

    dbg = {}
    if "rbq" in dump:
        dbg["rbq"] = b.dram_out("d_rbq", [128, NF2 * EW])
    if "qf" in dump:
        dbg["qf"] = b.dram_out("d_qf", [2, 128, EW])
    if "ppack" in dump:
        dbg["ppack"] = b.dram_out("d_ppack", [128, NF2])
    if "qbuf" in dump:
        dbg["qbuf"] = b.dram_out("d_qbuf", [128, EW])
    if "s1buf" in dump:
        dbg["s1buf"] = b.dram_out("d_s1buf", [2, 128, EW])
    if "pparts" in dump:
        dbg["pparts"] = b.dram_out("d_pparts", [64, NF2 * 4])

    xp_sb = [b.sb(f"xp{j}", [128, E]) for j in range(2)]
    wq_sb = b.sb("wq_sb", [128, 8 * EW])
    wk_sb = b.sb("wk_sb", [128, 8 * EW])
    bias_sb = b.sb("bias_sb", [1, 2 * E])
    ones1 = b.sb("ones1", [1, 128])
    qbuf = [b.sb(f"qbuf{j}", [128, EW]) for j in range(2)]
    kbuf = [b.sb(f"kbuf{j}", [128, EW]) for j in range(2)]
    s1buf = [[b.sb(f"s1buf{j}_{c}", [128, EW]) for c in range(4)] for j in range(2)]
    c1re_sb = b.sb("c1re_sb", [128, 128])
    c1im_sb = b.sb("c1im_sb", [128, 128])
    b._pool_base = 229344 - b.nc.sbuf_bytes_remaining
    pool = b.sb("pool", [128, 28800])

    def bfat(name, shape, off_f32):
        return b.nc.alloc_sbuf_tensor_at(name, list(shape), BF16, offset=b._pool_base + off_f32 * 4)

    rbq = pool[:, 0:8448]
    rbk = pool[:, 8448:16896]
    st2_sb = pool[:, 16896:21120]
    pparts = b.sb("pparts", [64, NF2 * 4])
    ppack = b.sb("ppack", [128, NF2])
    macA_sb = b.sb("macA_sb", [128, 128])
    mtw_sb = b.sb("mtw_sb", [NF2, 256])
    u_sb = b.sb("u_sb", [NF2, 128])
    up_sb = b.sb("up_sb", [NF2, 128])
    macB_sb = b.sb("macB_sb", [NF2, 128])
    mac_sb = b.sb("mac_sb", [64, 64])

    qps_f = b.ps("qps", [128, 512])
    kps_f = b.ps("kps", [128, 512])
    s1ps_f = [b.ps(f"s1ps{c}", [128, 512]) for c in range(4)]
    f2ps_f = [b.ps(f"f2ps{c}", [128, 512]) for c in range(2)]
    qps = qps_f[:, 0:EW]
    kps = kps_f[:, 0:EW]
    s1ps = [t[:, 0:EW] for t in s1ps_f]
    f2ps = [t[:, 0:EW] for t in f2ps_f]

    def p0(eng):
        for dst, src in [
            (c1re_sb[:], c1re_d), (c1im_sb[:], c1im_d),
            (st2_sb[:], st2_d.rearrange("(f p) m -> p f m", p=128)),
            (bias_sb[0:1, 0:E], bq_d), (bias_sb[0:1, E : 2 * E], bk_d),
            (macA_sb[:], macA_d), (mtw_sb[:, 0:128], mtwre_d),
            (mtw_sb[:, 128:256], mtwim_d), (macB_sb[:, 0:64], macBc_d),
            (macB_sb[:, 64:128], macBs_d),
            (csb[:], cvals_d),
            (ident_sb[:], ident_d),
            (identbf_sb[:], identbf_d),
        ]:
            eng.dma_start(out=dst, in_=src).then_inc(b.sem("boot"), 16)
    b.bump("boot", 16 * 13)
    b.on("sync", p0)
    xpair_bf_d = b.nc.dram_tensor("xpair_bf", [32, E, 128], BF16, kind="Internal").ap()
    for _i, _v in enumerate([-np.pi, 2.0 * np.pi / 4096.0, 0.0, 1.0, -1.0, 16512.5, -16384.0, 255.0, 16384.0, -16256.0]):
        b.nc.const_aps.register(F32, _v, csb[:, _i : _i + 1])
    boot_t = b.semv["boot"]

    def p0v(eng):
        eng.memset(ones1[:], 1.0).then_inc(b.sem("ones"), 1)
    b.bump("ones", 1)
    b.on("vector", p0v)

    for H in range(NH):
        wload_t = b.bump("wload", 32)

        def p1w(eng, H=H, wt=b.semv["proj"] if "proj" in b.semv else 0):
            if H >= 1:
                eng.wait_ge(b.sem("proj"), wt)
            src_q = Wq_d.rearrange("(c p) o -> p c o", p=128)[:, :, H * EW : (H + 1) * EW]
            src_k = Wk_d.rearrange("(c p) o -> p c o", p=128)[:, :, H * EW : (H + 1) * EW]
            eng.dma_start(out=wq_sb[:], in_=src_q).then_inc(b.sem("wload"), 16)
            eng.dma_start(out=wk_sb[:], in_=src_k).then_inc(b.sem("wload"), 16)
        b.on("sync", p1w)

        for p in range(32):
            j = p % 2
            need_proj = b.semv.get("proj", 0)
            lsem = f"load{j}"
            load_t = b.bump(lsem, 16)

            def l(eng, p=p, j=j, H=H, need=need_proj, lsem=lsem):
                if p >= 2 or H >= 1:
                    eng.wait_ge(b.sem("proj"), need)
                src = xpair[p].rearrange("(c i) m -> i c m", i=128)
                eng.dma_start(out=xp_sb[j][:], in_=src).then_inc(b.sem(lsem), 16)
            b.on("sync", l)

            qevac_need = b.semv.get("qevac", 0)
            proj_t = b.bump("proj", 1)

            def mm(eng, p=p, j=j, H=H, lt=load_t, wt=wload_t, qn=qevac_need, lsem=lsem):
                eng.wait_ge(b.sem(lsem), lt)
                if p == 0:
                    eng.wait_ge(b.sem("boot"), boot_t)
                    eng.wait_ge(b.sem("ones"), 1)
                eng.wait_ge(b.sem("wload"), wt)
                if p >= 1 or H >= 1:
                    eng.wait_ge(b.sem("qevac"), qn)
                eng.matmul(qps[:], ones1[:], bias_sb[0:1, H * EW : (H + 1) * EW], start=True, stop=False)
                eng.matmul(kps[:], ones1[:], bias_sb[0:1, E + H * EW : E + (H + 1) * EW], start=True, stop=False)
                i = None
                for c in range(8):
                    lhsT = xp_sb[j][:, c * 128 : (c + 1) * 128]
                    eng.matmul(qps[:], lhsT, wq_sb[:, c * EW : (c + 1) * EW], start=False, stop=(c == 7))
                    i = eng.matmul(kps[:], lhsT, wk_sb[:, c * EW : (c + 1) * EW], start=False, stop=(c == 7))
                i.then_inc(b.sem("proj"), 1)
            b.on("tensor", mm)

            s1_need = b.semv.get("s1mm", 0)
            qevac_t = b.bump("qevac", 1)

            def ev(eng, p=p, j=j, H=H, pt=proj_t, sn=s1_need):
                eng.wait_ge(b.sem("proj"), pt)
                if p >= 2 or H >= 1:
                    eng.wait_ge(b.sem("s1mm"), sn)
                eng.tensor_copy(qbuf[j][:], qps[:])
                eng.tensor_copy(kbuf[j][:], kps[:]).then_inc(b.sem("qevac"), 1)
            b.on("vector", ev)

            s1evac_need = b.semv.get("s1evac", 0)
            s1_t = b.bump("s1mm", 1)

            def s1(eng, p=p, j=j, H=H, qt=qevac_t, en=s1evac_need):
                eng.wait_ge(b.sem("qevac"), qt)
                if p >= 1 or H >= 1:
                    eng.wait_ge(b.sem("s1evac"), en)
                eng.matmul(s1ps[0][:], c1re_sb[:], qbuf[j][:], start=True, stop=True)
                eng.matmul(s1ps[1][:], c1im_sb[:], qbuf[j][:], start=True, stop=True)
                eng.matmul(s1ps[2][:], c1re_sb[:], kbuf[j][:], start=True, stop=True)
                eng.matmul(s1ps[3][:], c1im_sb[:], kbuf[j][:], start=True, stop=True).then_inc(b.sem("s1mm"), 1)
            b.on("tensor", s1)

            corner_need = b.semv.get("corner", 0)
            s1evac_t = b.bump("s1evac", 1)

            def ev1(eng, p=p, j=j, H=H, st=s1_t, cn=corner_need):
                eng.wait_ge(b.sem("s1mm"), st)
                if p >= 2 or H >= 1:
                    eng.wait_ge(b.sem("corner"), cn)
                for c in range(4):
                    i = eng.tensor_copy(s1buf[j][c][:], s1ps[c][:])
                i.then_inc(b.sem("s1evac"), 1)
            b.on("vector", ev1)

            if "qbuf" in dump and H == dump_slice and p == 5:
                def dqb(eng, j=j, t=b.semv["qevac"]):
                    eng.wait_ge(b.sem("qevac"), t)
                    eng.dma_start(out=dbg["qbuf"], in_=qbuf[j][:]).then_inc(b.sem("dbg"), 16)
                b.bump("dbg", 16)
                b.on("gpsimd", dqb)

            if "s1buf" in dump and H == dump_slice and p == 5:
                def dsb(eng, j=j, t=b.semv["s1evac"]):
                    eng.wait_ge(b.sem("s1evac"), t)
                    eng.dma_start(out=dbg["s1buf"][0], in_=s1buf[j][0][:]).then_inc(b.sem("dbg"), 16)
                    eng.dma_start(out=dbg["s1buf"][1], in_=s1buf[j][1][:]).then_inc(b.sem("dbg"), 16)
                b.bump("dbg", 32)
                b.on("gpsimd", dsb)

            corner_t = b.bump("corner", 16 * 8)

            def corner(eng, p=p, j=j, H=H, et=s1evac_t):
                eng.wait_ge(b.sem("s1evac"), et)
                for (c, rb, base) in ((0, rbq, 0), (1, rbq, 64), (2, rbk, 0), (3, rbk, 64)):
                    for h in (0, 1):
                        src = s1buf[j][c][h * 64 : h * 64 + NF2, :]
                        r = base + h * 32 + p
                        eng.dma_start(out=rb[r : r + 1, :], in_=src).then_inc(b.sem("corner"), 16)
            b.on("gpsimd", corner)

            if H < 2 and p % 2 == 1:
                q = H * 16 + p // 2
                b.bump("xcast", 16)

                def xck(eng, q=q):
                    eng.dma_start(out=xpair_bf_d[q], in_=xpair[q]).then_inc(b.sem("xcast"), 16)
                b.on("gpsimd", xck)

        corner_all = b.semv["corner"]
        if "rbq" in dump and H == dump_slice:
            def drb(eng, ca=corner_all):
                eng.wait_ge(b.sem("corner"), ca)
                eng.dma_start(out=dbg["rbq"], in_=rbq[:]).then_inc(b.sem("dbg"), 16)
            b.bump("dbg", 16)
            b.on("gpsimd", drb)

        for f2 in range(NF2):
            pprod_need = b.semv.get("pprod", 0)
            st2_t = b.bump("st2mm", 1)

            def st2(eng, f2=f2, H=H, ca=corner_all, pn=pprod_need):
                eng.wait_ge(b.sem("corner"), ca)
                if not (H == 0 and f2 == 0):
                    eng.wait_ge(b.sem("pprod"), pn)
                L = st2_sb[:, f2 * 128 : (f2 + 1) * 128]
                eng.matmul(f2ps[0][:], L, rbq[:, f2 * EW : (f2 + 1) * EW], start=True, stop=True)
                eng.matmul(f2ps[1][:], L, rbk[:, f2 * EW : (f2 + 1) * EW], start=True, stop=True).then_inc(b.sem("st2mm"), 1)
            b.on("tensor", st2)

            if "qf" in dump and H == 0 and f2 == 0:
                def dqf(eng, st=st2_t):
                    eng.wait_ge(b.sem("st2mm"), st)
                    eng.tensor_copy(u_sb[0:NF2, :] if False else qbuf[0][:], f2ps[0][:])
                    eng.tensor_copy(kbuf[0][:], f2ps[1][:]).then_inc(b.sem("dbgv"), 1)
                b.bump("dbgv", 1)
                b.on("vector", dqf)

                def dqf2(eng, t=b.semv["dbgv"]):
                    eng.wait_ge(b.sem("dbgv"), t)
                    eng.dma_start(out=dbg["qf"][0], in_=qbuf[0][:]).then_inc(b.sem("dbg"), 16)
                    eng.dma_start(out=dbg["qf"][1], in_=kbuf[0][:]).then_inc(b.sem("dbg"), 16)
                b.bump("dbg", 32)
                b.on("gpsimd", dqf2)

            pprod_t = b.bump("pprod", 1)

            def pp(eng, f2=f2, H=H, st=st2_t):
                eng.wait_ge(b.sem("st2mm"), st)
                eng.tensor_copy(qf_sb[:], f2ps[0][:])
                qre = qf_sb[0:64, :]
                qim = qf_sb[64:128, :]
                kre = f2ps[1][0:64, :]
                kim = f2ps[1][64:128, :]
                col = lambda k: pparts[0:64, f2 * 4 + k : f2 * 4 + k + 1]
                pairs = ((qre, kre), (qim, kim), (qim, kre), (qre, kim))
                if H == 0 or DBG_NOACC:
                    for k, (a, bb) in enumerate(pairs):
                        eng.tensor_tensor(ttr_scr[:], a, bb, ALU.mult)
                        i = eng.reduce_sum(col(k), ttr_scr[:], axis=AX.X)
                    i.then_inc(b.sem("pprod"), 1)
                else:
                    for k, (a, bb) in enumerate(pairs):
                        eng.tensor_tensor(ttr_scr[:], a, bb, ALU.mult)
                        eng.reduce_sum(pscr[0:64, k : k + 1], ttr_scr[:], axis=AX.X)
                    for k in range(4):
                        i = eng.tensor_add(col(k), col(k), pscr[0:64, k : k + 1])
                    i.then_inc(b.sem("pprod"), 1)
            b.on("vector", pp)

        if "pparts" in dump and H == dump_slice:
            def dppp(eng, t=b.semv["pprod"]):
                eng.wait_ge(b.sem("pprod"), t)
                eng.dma_start(out=dbg["pparts"], in_=pparts[:]).then_inc(b.sem("dbg"), 16)
            b.bump("dbg", 16)
            b.on("gpsimd", dppp)

    pscr = b.sb("pscr", [64, 4])
    ttr_scr = b.sb("ttr_scr", [64, EW])
    qf_sb = b.sb("qf_sb", [128, EW])

    pprod_all = b.semv["pprod"]

    def pcomb(eng, pa=pprod_all):
        eng.wait_ge(b.sem("pprod"), pa)
        v = pparts[0:64, :].rearrange("p (f k) -> p f k", k=4)
        eng.tensor_add(ppack[0:64, :], v[:, :, 0], v[:, :, 1])
        eng.tensor_sub(ppack[64:128, :], v[:, :, 2], v[:, :, 3]).then_inc(b.sem("pcomb"), 1)
    b.bump("pcomb", 1)
    b.on("vector", pcomb)
    if "ppack" in dump:
        def dpp(eng, t=b.semv["pcomb"]):
            eng.wait_ge(b.sem("pcomb"), t)
            eng.dma_start(out=dbg["ppack"], in_=ppack[:]).then_inc(b.sem("dbg"), 16)
        b.bump("dbg", 16)
        b.on("gpsimd", dpp)

    uaps = f2ps[0][0:NF2, 0:128]
    macps = f2ps[1][0:64, 0:64]

    def macA(eng, t=b.semv["pcomb"]):
        eng.wait_ge(b.sem("pcomb"), t)
        eng.matmul(uaps, ppack[:], macA_sb[:], start=True, stop=True).then_inc(b.sem("macA"), 1)
    b.bump("macA", 1)
    b.on("tensor", macA)

    def mactw(eng, t=b.semv["macA"]):
        eng.wait_ge(b.sem("macA"), t)
        eng.tensor_copy(u_sb[:], uaps)
        ure, uim = u_sb[:, 0:64], u_sb[:, 64:128]
        twre_r, twre_i = mtw_sb[:, 0:64], mtw_sb[:, 128:192]
        eng.tensor_tensor(up_sb[:, 0:64], ure, twre_r, ALU.mult)
        eng.tensor_tensor(pscr2[:, 0:64], uim, twre_i, ALU.mult)
        eng.tensor_sub(up_sb[:, 0:64], up_sb[:, 0:64], pscr2[:, 0:64])
        eng.tensor_tensor(up_sb[:, 64:128], ure, twre_i, ALU.mult)
        eng.tensor_tensor(pscr2[:, 64:128], uim, twre_r, ALU.mult)
        eng.tensor_add(up_sb[:, 64:128], up_sb[:, 64:128], pscr2[:, 64:128]).then_inc(b.sem("mactw"), 1)
    pscr2 = b.sb("pscr2", [NF2, 128])
    b.bump("mactw", 1)
    b.on("vector", mactw)

    def macB(eng, t=b.semv["mactw"]):
        eng.wait_ge(b.sem("mactw"), t)
        eng.matmul(macps, macB_sb[:, 0:64], up_sb[:, 0:64], start=True, stop=False)
        eng.matmul(macps, macB_sb[:, 64:128], up_sb[:, 64:128], start=False, stop=True).then_inc(b.sem("macB"), 1)
    b.bump("macB", 1)
    b.on("tensor", macB)

    def macev(eng, t=b.semv["macB"]):
        eng.wait_ge(b.sem("macB"), t)
        eng.tensor_copy(mac_sb[:], macps).then_inc(b.sem("macev"), 1)
    b.bump("macev", 1)
    b.on("vector", macev)

    b.bump("outd", 16)
    dbg_total = b.semv.get("dbg", 0)

    def macout2(eng, t=b.semv["macev"], dt=dbg_total):
        eng.wait_ge(b.sem("macev"), t)
        eng.dma_start(out=mac_out, in_=mac_sb[:]).then_inc(b.sem("outd"), 16)
        eng.wait_ge(b.sem("outd"), 16)
        if dt:
            eng.wait_ge(b.sem("dbg"), dt)
    b.on("gpsimd", macout2)

    if upto == "mac":
        return b

    mac_all = pool[0:8, 0:4096]
    workA = pool[0:8, 4096:8192]
    workB = pool[0:8, 8192:12288]
    wm = pool[0:8, 12288:16384]
    wmp = pool[:, 25344:25600]
    rbw = pool[:, 25600:25864]
    m8 = b.sb("m8", [8, 8])
    mscr = pool[0:8, 21120:25216]
    s1w_ev = b.sb("s1w_ev", [128, 16])
    gt_re = b.sb("gt_re", [64, 8 * NF2])
    gt_im = b.sb("gt_im", [64, 8 * NF2])

    if use_cc:
        cc_in = b.nc.dram_tensor("cc_in", [1, 4096], F32)
        cc_out = b.nc.dram_tensor("cc_out", [8, 4096], F32)
        nrep_t = b.bump("macrep", 16 * 2)
        b.bump("ccd", 1)

        def macrep(eng, t=b.semv["macev"]):
            eng.wait_ge(b.sem("macev"), t)
            eng.dma_start(out=cc_in.ap().rearrange("q (a c) -> (q a) c", a=64), in_=mac_sb[:]).then_inc(b.sem("macrep"), 16)
            eng.wait_ge(b.sem("macrep"), 16)
            eng.collective_compute(
                "AllGather", ALU.bypass,
                replica_groups=[list(range(8))],
                ins=[cc_in.ap().opt()], outs=[cc_out.ap().opt()],
            ).then_inc(b.sem("ccd"), 1)
            eng.wait_ge(b.sem("ccd"), 1)
            eng.dma_start(out=mac_all, in_=cc_out.ap()).then_inc(b.sem("macrep"), 16)
        b.on("gpsimd", macrep)
    else:
        nrep_t = b.bump("macrep", 16 * 8)

        def macrep(eng, t=b.semv["macev"]):
            eng.wait_ge(b.sem("macev"), t)
            for bb in range(8):
                eng.dma_start(out=mac_all[bb : bb + 1, :], in_=mac_sb[:]).then_inc(b.sem("macrep"), 16)
        b.on("gpsimd", macrep)

    _tks0 = [b.semv.get("tks", 0)]
    b.bump("tks", 9)
    tk_t = b.bump("topk", 1)

    def topk(eng, t=nrep_t):
        eng.wait_ge(b.sem("macrep"), t)
        wtmp = wm
        bufs = [mac_all, workA, workB, wtmp]
        for r in range(3):
            src, dst = bufs[r], bufs[r + 1]
            eng.max(out=m8[:], in_=src).then_inc(b.sem("tks"), 1)
            eng.wait_ge(b.sem("tks"), b.bump("tks", 0) if False else 3 * r + 1 + _tks0[0])
            eng.tensor_single_scalar(mscr, src, m8[:, 7:8], ALU.is_ge)
            eng.scalar_tensor_tensor(dst, mscr, -1e30, src, ALU.mult, ALU.add).then_inc(b.sem("tks"), 2)
            eng.wait_ge(b.sem("tks"), 3 * r + 3 + _tks0[0])
        eng.tensor_single_scalar(mscr, bufs[3], -1e29, ALU.is_le)
        eng.tensor_mul(wm, mscr, mac_all).then_inc(b.sem("topk"), 1)
    b.on("vector", topk)

    for p in range(32):
        wtr_t = b.bump("wtr", 1)

        def wtr(eng, p=p, t=tk_t, ev=b.semv.get("wtrev", 0)):
            eng.wait_ge(b.sem("topk"), t)
            if p > 0:
                eng.wait_ge(b.sem("wtrev"), ev)
            wmv = wm.rearrange("q (s2 s1) -> q s1 s2", s1=64)
            eng.matmul(s1ps[3][0:64, 0:8], wmv[:, 2 * p, :], ident_sb[0:8, 0:8], start=True, stop=True)
            eng.matmul(s1ps[3][64:128, 0:8], wmv[:, 2 * p + 1, :], ident_sb[0:8, 0:8], start=True, stop=True).then_inc(b.sem("wtr"), 1)
        b.on("tensor", wtr)

        wtrev_t = b.bump("wtrev", 1)

        def wtrev(eng, p=p, t=wtr_t):
            eng.wait_ge(b.sem("wtr"), t)
            eng.tensor_copy(wmp[:, p * 8 : (p + 1) * 8], s1ps[3][:, 0:8]).then_inc(b.sem("wtrev"), 1)
        b.on("vector", wtrev)
    wml_t = b.semv["wtrev"]

    for p in range(32):
        ws1_t = b.bump("ws1", 1)

        def ws1(eng, p=p, t=wml_t, ev=b.semv.get("ws1ev", 0)):
            eng.wait_ge(b.sem("wtrev"), t)
            if p > 0:
                eng.wait_ge(b.sem("ws1ev"), ev)
            eng.matmul(s1ps[0][:, 0:8], c1re_sb[:], wmp[:, p * 8 : (p + 1) * 8], start=True, stop=True)
            eng.matmul(s1ps[1][:, 0:8], c1im_sb[:], wmp[:, p * 8 : (p + 1) * 8], start=True, stop=True).then_inc(b.sem("ws1"), 1)
        b.on("tensor", ws1)

        ws1ev_t = b.bump("ws1ev", 1)

        def ws1ev(eng, p=p, t=ws1_t, cn=b.semv.get("wcorner", 0)):
            eng.wait_ge(b.sem("ws1"), t)
            if p > 0:
                eng.wait_ge(b.sem("wcorner"), cn)
            eng.tensor_copy(s1w_ev[:, 0:8], s1ps[0][:, 0:8])
            eng.tensor_copy(s1w_ev[:, 8:16], s1ps[1][:, 0:8]).then_inc(b.sem("ws1ev"), 1)
        b.on("vector", ws1ev)

        wc_t = b.bump("wcorner", 16 * 4)

        def wcorner(eng, p=p, t=ws1ev_t):
            eng.wait_ge(b.sem("ws1ev"), t)
            for (c0, base) in ((0, 0), (8, 64)):
                for h in (0, 1):
                    src = s1w_ev[h * 64 : h * 64 + NF2, c0 : c0 + 8]
                    r = base + h * 32 + p
                    eng.dma_start(out=rbw[r : r + 1, :].rearrange("q (f m) -> q f m", f=NF2), in_=src).then_inc(b.sem("wcorner"), 16)
        b.on("gpsimd", wcorner)

    wall = b.semv["wcorner"]
    for f2 in range(NF2):
        wg_t = b.bump("wg", 1)

        def wg(eng, f2=f2, t=wall, ev=b.semv.get("wgev", 0)):
            eng.wait_ge(b.sem("wcorner"), t)
            if f2 > 0:
                eng.wait_ge(b.sem("wgev"), ev)
            eng.matmul(s1ps[2][:, 0:8], st2_sb[:, f2 * 128 : (f2 + 1) * 128], rbw[:, f2 * 8 : (f2 + 1) * 8], start=True, stop=True).then_inc(b.sem("wg"), 1)
        b.on("tensor", wg)

        wgev_t = b.bump("wgev", 1)

        def wgev(eng, f2=f2, t=wg_t):
            eng.wait_ge(b.sem("wg"), t)
            dst_re = gt_re[:, f2::NF2]
            dst_im = gt_im[:, f2::NF2]
            eng.tensor_copy(dst_re, s1ps[2][0:64, 0:8])
            eng.tensor_scalar_mul(dst_im, s1ps[2][64:128, 0:8], -1.0).then_inc(b.sem("wgev"), 1)
        b.on("vector", wgev)

    if upto == "topk":
        dbg_total2 = b.semv.get("dbg", 0)
        wm_out = b.dram_out("wm", [8, 4096])
        wk_out = b.dram_out("wk_dbg", [8, 4096])

        def tkout(eng, t=b.semv["wgev"], dt=dbg_total2):
            eng.wait_ge(b.sem("wgev"), t)
            eng.dma_start(out=wk_out, in_=workA).then_inc(b.sem("outd2"), 16)
            eng.dma_start(out=wm_out, in_=wm).then_inc(b.sem("outd2"), 16)
            eng.dma_start(out=g_out[0], in_=gt_re[:]).then_inc(b.sem("outd2"), 16)
            eng.dma_start(out=g_out[1], in_=gt_im[:]).then_inc(b.sem("outd2"), 16)
            eng.wait_ge(b.sem("outd2"), b.semv["outd2"])
            if dt:
                eng.wait_ge(b.sem("dbg"), dt)
        b.bump("outd2", 64)
        b.on("gpsimd", tkout)
        return b

    aggfm_d = b.nc.dram_tensor("aggfm", [1024, 4096], BF16, kind="Internal").ap()
    rbv = bfat("rbv", [128, 8448], 0)
    vfre = bfat("vfre", [64, 8448], 4224)
    vfim = bfat("vfim", [64, 8448], 8448)
    agre = bfat("agre", [64, 8448], 12672)
    agim = bfat("agim", [64, 8448], 16896)
    gsc1 = bfat("gsc1", [64, 2112], 21120)
    gsc2 = bfat("gsc2", [64, 2112], 22176)
    rbB = bfat("rbB", [66, 16384], 4224)
    aggt = bfat("aggt", [64, 16384], 12672)
    invAre_sb = b.sb("invAre_sb", [64, 4224], BF16)
    invAim_sb = b.sb("invAim_sb", [64, 4224], BF16)
    st2vre_sb = bfat("st2vre_sb", [128, 2112], 23232)
    st2vim_sb = bfat("st2vim_sb", [128, 2112], 24288)
    gt_re_bf = bfat("gt_re_bf", [64, 264], 25864)
    gt_im_bf = bfat("gt_im_bf", [64, 264], 26000)

    xpv = [b.sb(f"xpv{j}", [128, E], BF16) for j in range(2)]
    vbuf = [b.sb(f"vbuf{j}", [128, EW], BF16) for j in range(2)]
    vs1ev = [b.sb(f"vs1ev{j}", [128, 2 * EW], BF16) for j in range(2)]
    wv_sb = b.sb("wv_sb", [128, 8 * EW], BF16)
    ones_bf = b.sb("ones_bf", [1, 512], BF16)
    bv_sb = b.sb("bv_sb", [1, E])
    bv_bf = b.sb("bv_bf", [1, E], BF16)
    c1rebf_sb = b.sb("c1rebf_sb", [128, 128], BF16)
    c1imbf_sb = b.sb("c1imbf_sb", [128, 128], BF16)
    invB_sb = b.sb("invB_sb", [66, 64], BF16)
    uaev = b.sb("uaev", [128, EW], BF16)
    uaT = b.sb("uaT", [128, EW], BF16)

    vb_t = b.bump("vboot", 16 * 5)

    def vboot(eng, t=b.semv["topk"]):
        eng.wait_ge(b.sem("topk"), t)
        eng.dma_start(out=st2vre_sb[:], in_=st2vre_d.rearrange("(f p) m -> p f m", p=128)).then_inc(b.sem("vboot"), 16)
        eng.dma_start(out=st2vim_sb[:], in_=st2vim_d.rearrange("(f p) m -> p f m", p=128)).then_inc(b.sem("vboot"), 16)
        eng.dma_start(out=c1rebf_sb[:], in_=c1rebf_d).then_inc(b.sem("vboot"), 16)
        eng.dma_start(out=c1imbf_sb[:], in_=c1imbf_d).then_inc(b.sem("vboot"), 16)
        eng.dma_start(out=bv_sb[:], in_=bv_d).then_inc(b.sem("vboot"), 16)
        eng.dma_start(out=invAre_sb[:], in_=invAre_d.rearrange("(f p) m -> p f m", p=64)).then_inc(b.sem("iaload"), 16)
        eng.dma_start(out=invAim_sb[:], in_=invAim_d.rearrange("(f p) m -> p f m", p=64)).then_inc(b.sem("iaload"), 16)
    b.bump("iaload", 32)
    b.on("sync", vboot)

    def vboot2(eng, t=vb_t):
        eng.wait_ge(b.sem("vboot"), t)
        eng.memset(ones_bf[:], 1.0)
        eng.tensor_copy(bv_bf[:], bv_sb[:]).then_inc(b.sem("vboot2"), 1)
    vb2_t = b.bump("vboot2", 1)
    b.on("vector", vboot2)

    def gtbf(eng, t=b.semv["wgev"], t2=vb2_t):
        eng.wait_ge(b.sem("wgev"), t)
        eng.wait_ge(b.sem("vboot2"), t2 - 1 if False else t2)
        eng.tensor_copy(gt_re_bf[:], gt_re[:])
        eng.tensor_copy(gt_im_bf[:], gt_im[:]).then_inc(b.sem("gtbf"), 1)
    gtbf_t = b.bump("gtbf", 1)
    b.on("vector", gtbf)

    for H in range(NH):
        wvl_t = b.bump("wvl", 16)

        def wvl(eng, H=H, t=vb_t, pv=b.semv.get("vproj", 0)):
            eng.wait_ge(b.sem("vboot"), t)
            if H > 0:
                eng.wait_ge(b.sem("vproj"), pv)
            src = Wv_d.rearrange("(c p) o -> p c o", p=128)[:, :, H * EW : (H + 1) * EW]
            eng.dma_start(out=wv_sb[:], in_=src).then_inc(b.sem("wvl"), 16)
        b.on("gpsimd", wvl)

        for p in range(32):
            j = p % 2
            vlsem = f"vload{j}"
            vload_t = b.bump(vlsem, 16)

            def vload(eng, p=p, j=j, H=H, pv=b.semv.get("vproj", 0), vlsem=vlsem):
                if H == 0:
                    eng.wait_ge(b.sem("xcast"), 16 * (p + 1))
                if p >= 2 or H >= 1:
                    eng.wait_ge(b.sem("vproj"), pv)
                src = xpair_bf_d[p].rearrange("(c i) m -> i c m", i=128)
                eng.dma_start(out=xpv[j][:], in_=src).then_inc(b.sem(vlsem), 16)
            b.on("sync", vload)

            vproj_t = b.bump("vproj", 1)

            def vproj(eng, p=p, j=j, H=H, lt=vload_t, wt=wvl_t, ev=b.semv.get("vevac", 0), vlsem=vlsem):
                eng.wait_ge(b.sem(vlsem), lt)
                eng.wait_ge(b.sem("wvl"), wt)
                if p == 0 and H == 0:
                    eng.wait_ge(b.sem("vboot2"), 1)
                if (p >= 2 or H >= 1) and ev >= 1:
                    eng.wait_ge(b.sem("vevac"), ev - 1)
                vp = [qps, kps][p % 2]
                eng.matmul(vp[:], ones_bf[0:1, 0:128], bv_bf[0:1, H * EW : (H + 1) * EW], start=True, stop=False)
                i = None
                for c in range(8):
                    i = eng.matmul(vp[:], xpv[j][:, c * 128 : (c + 1) * 128], wv_sb[:, c * EW : (c + 1) * EW], start=False, stop=(c == 7))
                i.then_inc(b.sem("vproj"), 1)
            b.on("tensor", vproj)

            vevac_t = b.bump("vevac", 1)

            def vevac(eng, p=p, j=j, t=vproj_t, sn=b.semv.get("vs1", 0)):
                eng.wait_ge(b.sem("vproj"), t)
                if p >= 2 or H >= 1:
                    eng.wait_ge(b.sem("vs1"), sn)
                eng.tensor_copy(vbuf[j][:], [qps, kps][p % 2][:]).then_inc(b.sem("vevac"), 1)
            b.on("vector", vevac)

            vs1_t = b.bump("vs1", 1)

            def vs1(eng, p=p, j=j, t=vevac_t, en=b.semv.get("vs1ev", 0)):
                eng.wait_ge(b.sem("vevac"), t)
                if (p >= 2 or H >= 1) and en >= 1:
                    eng.wait_ge(b.sem("vs1ev"), en - 1)
                pr = s1ps[0 + 2 * (p % 2)]
                pi = s1ps[1 + 2 * (p % 2)]
                eng.matmul(pr[:], c1rebf_sb[:], vbuf[j][:], start=True, stop=True)
                eng.matmul(pi[:], c1imbf_sb[:], vbuf[j][:], start=True, stop=True).then_inc(b.sem("vs1"), 1)
            b.on("tensor", vs1)

            vs1ev_t = b.bump("vs1ev", 1)

            def vs1e(eng, p=p, j=j, t=vs1_t, cn=b.semv.get("vcorner", 0)):
                eng.wait_ge(b.sem("vs1"), t)
                if p >= 2 or H >= 1:
                    eng.wait_ge(b.sem("vcorner"), cn)
                pr = s1ps[0 + 2 * (p % 2)]
                pi = s1ps[1 + 2 * (p % 2)]
                eng.tensor_copy(vs1ev[j][:, 0:EW], pr[:])
                eng.tensor_copy(vs1ev[j][:, EW : 2 * EW], pi[:]).then_inc(b.sem("vs1ev"), 1)
            b.on("vector", vs1e)

            vc_t = b.bump("vcorner", 16 * 4)

            def vcorner(eng, p=p, j=j, t=vs1ev_t):
                eng.wait_ge(b.sem("vs1ev"), t)
                for (c0, base) in ((0, 0), (EW, 64)):
                    for h in (0, 1):
                        src = vs1ev[j][h * 64 : h * 64 + NF2, c0 : c0 + EW]
                        r = base + h * 32 + p
                        eng.dma_start(out=rbv[r : r + 1, :], in_=src).then_inc(b.sem("vcorner"), 16)
            b.on("sync", vcorner)

        if p5_stop == "pairs":
            continue
        vc_all = b.semv["vcorner"]
        for f2 in range(NF2):
            vst2_t = b.bump("vst2", 1)

            def vst2(eng, f2=f2, t=vc_all, ev=b.semv.get("vfev", 0)):
                eng.wait_ge(b.sem("vcorner"), t)
                if not (f2 == 0):
                    eng.wait_ge(b.sem("vfev"), ev)
                eng.matmul(f2ps[0][0:64, :], st2vre_sb[:, f2 * 64 : (f2 + 1) * 64], rbv[:, f2 * EW : (f2 + 1) * EW], start=True, stop=True)
                eng.matmul(f2ps[1][0:64, :], st2vim_sb[:, f2 * 64 : (f2 + 1) * 64], rbv[:, f2 * EW : (f2 + 1) * EW], start=True, stop=True).then_inc(b.sem("vst2"), 1)
            b.on("tensor", vst2)

            vfev_t = b.bump("vfev", 1)

            def vfev(eng, f2=f2, t=vst2_t):
                eng.wait_ge(b.sem("vst2"), t)
                eng.tensor_copy(vfre[:, f2 * EW : (f2 + 1) * EW], f2ps[0][0:64, :])
                eng.tensor_copy(vfim[:, f2 * EW : (f2 + 1) * EW], f2ps[1][0:64, :]).then_inc(b.sem("vfev"), 1)
            b.on("vector", vfev)

        if p5_stop == "vst2":
            continue
        vf_all = b.semv["vfev"]
        gm_t = b.bump("gmul", 1)

        def gmul(eng, H=H, t=vf_all, gt=gtbf_t, ac=b.semv.get("aggc", 0)):
            eng.wait_ge(b.sem("vfev"), t)
            eng.wait_ge(b.sem("gtbf"), gt)
            if ac:
                eng.wait_ge(b.sem("aggc"), ac)
            for jj in range(4):
                m = (H * 4 + jj) % 8
                gr = gt_re_bf[:, m * NF2 : (m + 1) * NF2].unsqueeze(2).broadcast_to([64, NF2, 64])
                gi = gt_im_bf[:, m * NF2 : (m + 1) * NF2].unsqueeze(2).broadcast_to([64, NF2, 64])
                sl = lambda tns: tns[:, jj * 64 : 8448 : EW].rearrange("p (f e) -> p f e", f=NF2) if False else tns.rearrange("p (f e) -> p f e", e=EW)[:, :, jj * 64 : (jj + 1) * 64]
                vre = sl(vfre)
                vim = sl(vfim)
                are = sl(agre)
                aim = sl(agim)
                g1 = gsc1[:].rearrange("p (f e) -> p f e", e=64)
                g2 = gsc2[:].rearrange("p (f e) -> p f e", e=64)
                eng.tensor_tensor(g1, gr, vre, ALU.mult)
                eng.tensor_tensor(g2, gi, vim, ALU.mult)
                eng.tensor_sub(are, g1, g2)
                eng.tensor_tensor(g1, gr, vim, ALU.mult)
                eng.tensor_tensor(g2, gi, vre, ALU.mult)
                eng.tensor_add(aim, g1, g2)
            eng.engine_nop().then_inc(b.sem("gmul"), 1)
        b.on("vector", gmul)

        if p5_stop == "gmul":
            continue

        for f2 in range(NF2):
            ia_mm_t = b.bump("iamm", 1)

            def iamm(eng, f2=f2, H=H, t=gm_t, ev=b.semv.get("iaev", 0), tv=b.semv.get("iatrev", 0)):
                eng.wait_ge(b.sem("gmul"), t)
                eng.wait_ge(b.sem("iaload"), 32)
                if not (H == 0 and f2 == 0):
                    eng.wait_ge(b.sem("iaev"), ev)
                eng.matmul(kps[:], invAre_sb[:, f2 * 128 : (f2 + 1) * 128], agre[:, f2 * EW : (f2 + 1) * EW], start=True, stop=False)
                eng.matmul(kps[:], invAim_sb[:, f2 * 128 : (f2 + 1) * 128], agim[:, f2 * EW : (f2 + 1) * EW], start=False, stop=True).then_inc(b.sem("iamm"), 1)
            b.on("tensor", iamm)

            iaev_t = b.bump("iaev", 1)

            def iaev(eng, f2=f2, t=ia_mm_t, tn=b.semv.get("iatr", 0)):
                eng.wait_ge(b.sem("iamm"), t)
                if tn:
                    eng.wait_ge(b.sem("iatr"), tn)
                eng.tensor_copy(uaev[:], kps[:]).then_inc(b.sem("iaev"), 1)
            b.on("vector", iaev)

            if p5_stop == "iamm1":
                continue
            iatr_t = b.bump("iatr", 1)

            def iatr(eng, f2=f2, t=iaev_t, en=b.semv.get("iatrev", 0)):
                eng.wait_ge(b.sem("iaev"), t)
                if en:
                    eng.wait_ge(b.sem("iatrev"), en)
                tb = [s1ps[0], s1ps[1], f2ps[0], f2ps[1]]
                i = None
                for c in range(2):
                    for h in range(2):
                        i = eng.matmul(tb[c * 2 + h][:, 0:64],
                                       uaev[c * 64 : (c + 1) * 64, h * 128 : (h + 1) * 128],
                                       identbf_sb[c * 64 : (c + 1) * 64, :], start=True, stop=True)
                i.then_inc(b.sem("iatr"), 1)
            b.on("tensor", iatr)

            iatrev_t = b.bump("iatrev", 1)

            def iatrev(eng, f2=f2, t=iatr_t, cn=b.semv.get("iac", 0)):
                eng.wait_ge(b.sem("iatr"), t)
                if cn:
                    eng.wait_ge(b.sem("iac"), cn)
                tb = [s1ps[0], s1ps[1], f2ps[0], f2ps[1]]
                for q in range(4):
                    i = eng.tensor_copy(uaT[:, q * 64 : (q + 1) * 64], tb[q][:, 0:64])
                i.then_inc(b.sem("iatrev"), 1)
            b.on("vector", iatrev)

            if p5_stop == "iamm2":
                continue
            iac_t = b.bump("iac", 16 * 4)

            def iac(eng, f2=f2, t=iatrev_t):
                eng.wait_ge(b.sem("iatrev"), t)
                for c in range(2):
                    for h in range(2):
                        src = uaT[:, (c * 2 + h) * 64 : (c * 2 + h) * 64 + 64]
                        dst = rbB[c * NF2 + f2 : c * NF2 + f2 + 1, h * 8192 : (h + 1) * 8192]
                        eng.dma_start(out=dst, in_=src).then_inc(b.sem("iac"), 16)
            b.on("sync", iac)

        if p5_stop in ("iamm", "iamm1", "iamm2"):
            continue
        iac_all = b.semv["iac"]
        if H == 0:
            def ibload(eng, t=vb_t):
                eng.wait_ge(b.sem("vboot"), t)
                eng.dma_start(out=invB_sb[:], in_=invB_d).then_inc(b.sem("ibload"), 16)
            b.bump("ibload", 16)
            b.on("sync", ibload)

        for ch in range(64):
            sb_t = b.bump("stB", 1)

            def stB(eng, ch=ch, t=iac_all, ev=b.semv.get("stBev", 0)):
                eng.wait_ge(b.sem("iac"), t)
                eng.wait_ge(b.sem("ibload"), 16)
                if not (ch == 0):
                    eng.wait_ge(b.sem("stBev"), ev)
                eng.matmul(s1ps[2][0:64, :], invB_sb[:], rbB[:, ch * EW : (ch + 1) * EW], start=True, stop=True).then_inc(b.sem("stB"), 1)
            b.on("tensor", stB)

            sbe_t = b.bump("stBev", 1)

            def stBev(eng, ch=ch, t=sb_t, cn=b.semv.get("aggc", 0)):
                eng.wait_ge(b.sem("stB"), t)
                if ch == 0 and cn:
                    eng.wait_ge(b.sem("aggc"), cn)
                eng.tensor_copy(aggt[:, ch * EW : (ch + 1) * EW], s1ps[2][0:64, :]).then_inc(b.sem("stBev"), 1)
            b.on("vector", stBev)

        agc_t = b.bump("aggc", 16 * 64)
        sbev_all = b.semv["stBev"]

        def aggc(eng, H=H, t=sbev_all):
            eng.wait_ge(b.sem("stBev"), t)
            for s2 in range(64):
                src = aggt[s2 : s2 + 1, :].rearrange("q (e s1) -> q e s1", s1=64)
                dst = aggfm_d[H * EW : (H + 1) * EW, 64 * s2 : 64 * s2 + 64]
                eng.dma_start(out=dst, in_=src).then_inc(b.sem("aggc"), 16)
        b.on("sync", aggc)

    Wo_d = b.dram_in("Wo", [E, E])
    bo_d = b.dram_in("bo", [1, E])
    xtm_d = b.dram_in("x_tm", [T, E])
    WfcT_d = b.dram_in("WfcT", [E, 4096])
    WprojT_d = b.dram_in("WprojT", [4096, E])
    Afc_d = b.dram_in("Afc", [16, E])
    BfcT_d = b.dram_in("BfcT", [16, 4096])
    Aproj_d = b.dram_in("Aproj", [16, 4096])
    BprojT_d = b.dram_in("BprojT", [16, E])
    bfc_d = b.dram_in("bfc", [1, 4096])
    bproj_d = b.dram_in("bproj", [1, E])
    out_d = b.dram_out("out", [T, E])
    y1tm_d = b.nc.dram_tensor("y1tm", [T, E], BF16, kind="Internal").ap()
    y1fm_d = b.nc.dram_tensor("y1fm", [E, T], BF16, kind="Internal").ap()
    wefffc_d = b.nc.dram_tensor("wefffc", [E, 4096], BF16, kind="Internal").ap()
    weffpr_d = b.nc.dram_tensor("weffpr", [4096, E], BF16, kind="Internal").ap()

    wo_all = bfat("wo_all", [128, 8192], 5760)
    agg_all = [bfat(f"agg_all{j}", [128, 1024], 9856 + j * 512) for j in range(2)]
    x_tile = pool[:, 1536:2560]
    y1t = pool[:, 2560:3584]
    sqt = pool[:, 3584:4608]
    stat = pool[:, 4608:4640]
    y1bf = bfat("y1bf", [128, 1024], 4640)
    trsp = bfat("trsp", [128, 128], 5152)
    bo_bf = bfat("bo_bf", [1, 1024], 5224)
    afc_sb = pool[:, 5760:6784]
    aproj_sb = pool[:, 6784:10880]
    afc_bf = bfat("afc_bf", [16, 1024], 5248)
    aproj_bf = bfat("aproj_bf", [16, 4096], 26400)
    b2fc = bfat("b2fc", [16, 4096], 10880)
    b2pr = bfat("b2pr", [16, 1024], 12928)
    bfc_bf = bfat("bfc_bf", [1, 4096], 13440)
    bpr_bf = bfat("bpr_bf", [1, 1024], 15488)
    wchunk = pool[:, 16000:20096]
    wchunkI = pool[:, 16000:20096].bitcast(I32)[:, 0:4096]
    scrA = pool[:, 20096:24192]
    scrAI = pool[:, 20096:24192].bitcast(I32)[:, 0:4096]
    weffo = bfat("weffo", [128, 4096], 24192)
    qstat = b.sb("qstat", [128, 8])
    y1f_all = bfat("y1f_all", [128, 4096], 10880)
    gtile = bfat("gtile", [128, 16384], 16000)
    wfc_col = [bfat(f"wfc_col{j}", [128, 1024], 24192 + j * 512) for j in range(2)]
    wpr_buf = [bfat(f"wpr_buf{j}", [128, 1024], j * 512) for j in range(2)]
    y1res = bfat("y1res", [128, 1024], 4640)
    out_tile = pool[:, 1536:2560]
    y2t = pool[:, 2560:3584]
    sq2 = pool[:, 3584:4608]
    bo_sb_f = b.sb("bo_sb_f", [1, E])

    aggc_all = b.semv["aggc"]

    def p6boot(eng, t=aggc_all):
        eng.wait_ge(b.sem("aggc"), t)
        eng.dma_start(out=bo_sb_f[:], in_=bo_d).then_inc(b.sem("p6b"), 16)
    b.bump("p6b", 16)
    b.on("sync", p6boot)

    def p6wo(eng, t=aggc_all):
        eng.wait_ge(b.sem("aggc"), t)
        eng.dma_start(out=wo_all[:], in_=Wo_d.rearrange("(c p) o -> p c o", p=128)).then_inc(b.sem("p6w"), 16)
    b.bump("p6w", 16)
    b.on("gpsimd", p6wo)

    def p6boot2(eng):
        eng.wait_ge(b.sem("p6b"), 16)
        eng.tensor_copy(bo_bf[:], bo_sb_f[:]).then_inc(b.sem("p6b2"), 1)
    b.bump("p6b2", 1)
    b.on("vector", p6boot2)

    for tt in range(32):
        xl_t = b.bump("p6x", 16)

        def p6x(eng, tt=tt, pv=b.semv.get("ln1", 0)):
            if tt == 0:
                eng.wait_ge(b.sem("p6b"), 16)
            else:
                eng.wait_ge(b.sem("ln1"), pv)
            eng.dma_start(out=x_tile, in_=xtm_d[tt * 128 : (tt + 1) * 128, :]).then_inc(b.sem("p6x"), 16)
        b.on("sync", p6x)

        j = tt % 2
        al_t = b.bump(f"p6a{j}", 16)

        def p6a(eng, tt=tt, j=j, pv=b.semv.get("p6mm", 0)):
            if tt >= 2:
                eng.wait_ge(b.sem("p6mm"), pv - 8)
            src = aggfm_d.rearrange("(c p) t -> p c t", p=128)[:, :, tt * 128 : (tt + 1) * 128]
            eng.dma_start(out=agg_all[j][:], in_=src).then_inc(b.sem(f"p6a{j}"), 16)
        b.on("sync", p6a)

        for ec in range(8):
            mm_t = b.bump("p6mm", 1)

            def p6mm(eng, tt=tt, ec=ec, j=j, at=al_t, wt=16, ln=b.semv.get("ln1add", 0)):
                if ec == 0:
                    eng.wait_ge(b.sem(f"p6a{j}"), at)
                    eng.wait_ge(b.sem("p6w"), wt)
                    if tt > 0:
                        eng.wait_ge(b.sem("ln1add"), ln)
                    eng.wait_ge(b.sem("p6b2"), 1)
                    eng.matmul(qps_f[:], ones_bf[0:1, 0:128], bo_bf[0:1, 0:512], start=True, stop=False)
                    eng.matmul(kps_f[:], ones_bf[0:1, 0:128], bo_bf[0:1, 512:1024], start=True, stop=False)
                eng.matmul(qps_f[:], agg_all[j][:, ec * 128 : (ec + 1) * 128], wo_all[:, ec * 1024 : ec * 1024 + 512], start=False, stop=(ec == 7))
                eng.matmul(kps_f[:], agg_all[j][:, ec * 128 : (ec + 1) * 128], wo_all[:, ec * 1024 + 512 : (ec + 1) * 1024], start=False, stop=(ec == 7)).then_inc(b.sem("p6mm"), 1)
            b.on("tensor", p6mm)

        mma = b.semv["p6mm"]
        la_t = b.bump("ln1add", 1)

        def ln1a(eng, tt=tt, t=mma, xt=xl_t):
            eng.wait_ge(b.sem("p6mm"), t)
            eng.wait_ge(b.sem("p6x"), xt)
            eng.tensor_add(y1t[:, 0:512], x_tile[:, 0:512], qps_f[:])
            eng.tensor_add(y1t[:, 512:1024], x_tile[:, 512:1024], kps_f[:]).then_inc(b.sem("ln1add"), 1)
        b.on("vector", ln1a)

        ls_t = b.bump("ln1s", 1)

        def ln1s(eng, tt=tt, t=la_t):
            eng.wait_ge(b.sem("ln1add"), t)
            eng.tensor_tensor(sqt, y1t, y1t, ALU.mult)
            eng.reduce_sum(stat[:, 0:1], y1t, axis=AX.X)
            eng.reduce_sum(stat[:, 1:2], sqt, axis=AX.X)
            eng.tensor_scalar_mul(stat[:, 2:3], stat[:, 0:1], 1.0 / 1024.0)
            eng.tensor_scalar_mul(stat[:, 3:4], stat[:, 1:2], 1.0 / 1024.0)
            eng.tensor_tensor(stat[:, 4:5], stat[:, 2:3], stat[:, 2:3], ALU.mult)
            eng.tensor_sub(stat[:, 5:6], stat[:, 3:4], stat[:, 4:5])
            eng.tensor_scalar_add(stat[:, 6:7], stat[:, 5:6], 1e-5).then_inc(b.sem("ln1s"), 1)
        b.on("vector", ln1s)

        lq_t = b.bump("ln1q", 1)

        def ln1q(eng, tt=tt, t=ls_t):
            eng.wait_ge(b.sem("ln1s"), t)
            eng.activation(out=stat[:, 7:8], in_=stat[:, 6:7], func=AF.Sqrt, bias=0.0, scale=1.0).then_inc(b.sem("ln1q"), 1)
        b.on("scalar", ln1q)

        lr_t = b.bump("ln1r", 1)

        def ln1r(eng, tt=tt, t=lq_t):
            eng.wait_ge(b.sem("ln1q"), t)
            eng.reciprocal(stat[:, 0:1], stat[:, 7:8])
            eng.tensor_tensor(stat[:, 1:2], stat[:, 2:3], stat[:, 0:1], ALU.mult)
            eng.tensor_scalar_mul(stat[:, 1:2], stat[:, 1:2], -1.0).then_inc(b.sem("ln1r"), 1)
        b.on("vector", ln1r)

        ln_t = b.bump("ln1", 1)

        def ln1n(eng, tt=tt, t=lr_t, pv=b.semv.get("y1st", 0), pt=b.semv.get("y1tr", 0)):
            eng.wait_ge(b.sem("ln1r"), t)
            if pv:
                eng.wait_ge(b.sem("y1st"), pv)
            if pt:
                eng.wait_ge(b.sem("y1tr"), pt)
            eng.activation(out=y1bf[:], in_=y1t, func=AF.Identity, bias=stat[:, 1:2], scale=stat[:, 0:1]).then_inc(b.sem("ln1"), 1)
        b.on("scalar", ln1n)

        st_t = b.bump("y1st", 16)
        tr_t = b.bump("y1tr", 16 * 16)

        def y1st(eng, tt=tt, t=ln_t):
            eng.wait_ge(b.sem("ln1"), t)
            eng.dma_start(out=y1tm_d[tt * 128 : (tt + 1) * 128, :], in_=y1bf[:]).then_inc(b.sem("y1st"), 16)
        b.on("sync", y1st)

        def y1tr(eng, tt=tt, t=ln_t, base=tr_t - 16 * 16):
            eng.wait_ge(b.sem("ln1"), t)
            for ec in range(8):
                if ec > 0:
                    eng.wait_ge(b.sem("y1tr"), base + 16 * (2 * ec - 1))
                eng.dma_start_transpose(out=trsp[:], in_=y1bf[:, ec * 128 : (ec + 1) * 128]).then_inc(b.sem("y1tr"), 16)
                eng.wait_ge(b.sem("y1tr"), base + 16 * (2 * ec + 1))
                eng.dma_start(out=y1fm_d[ec * 128 : (ec + 1) * 128, tt * 128 : (tt + 1) * 128], in_=trsp[:]).then_inc(b.sem("y1tr"), 16)
        b.on("scalar", y1tr)

    y1_all = b.semv["y1st"]
    y1tr_all = b.semv["y1tr"]

    def q_boot(eng, t=y1_all):
        eng.wait_ge(b.sem("y1st"), t)
        eng.dma_start(out=afc_sb[0:16, :], in_=Afc_d).then_inc(b.sem("qb"), 16)
        eng.dma_start(out=aproj_sb[0:16, :], in_=Aproj_d).then_inc(b.sem("qb"), 16)
        eng.dma_start(out=wchunk[0:16, :], in_=BfcT_d).then_inc(b.sem("qb"), 16)
        eng.dma_start(out=scrA[0:16, 0:1024], in_=BprojT_d).then_inc(b.sem("qb"), 16)
        eng.dma_start(out=wchunk[32:33, :], in_=bfc_d).then_inc(b.sem("qb"), 16)
        eng.dma_start(out=scrA[0:1, 1024:2048], in_=bproj_d).then_inc(b.sem("qb"), 16)
    b.bump("qb", 96)
    b.on("sync", q_boot)

    def q_boot2(eng):
        eng.wait_ge(b.sem("qb"), 96)
        eng.tensor_scalar_mul(b2fc[0:16, :], wchunk[0:16, :], 2.0)
        eng.tensor_scalar_mul(b2pr[0:16, :], scrA[0:16, 0:1024], 2.0)
        eng.tensor_copy(afc_bf[0:16, :], afc_sb[0:16, :])
        eng.tensor_copy(aproj_bf[0:16, :], aproj_sb[0:16, :])
        eng.tensor_copy(bfc_bf[:], wchunk[32:33, :])
        eng.tensor_copy(bpr_bf[:], scrA[0:1, 1024:2048]).then_inc(b.sem("qb2"), 1)
    b.bump("qb2", 1)
    b.on("vector", q_boot2)

    for (name, src_d, nch, width) in (("fc", WfcT_d, 8, 4096), ("pr", WprojT_d, 32, 1024)):
        for c in range(nch):
            ql_t = b.bump("ql", 16)

            def ql(eng, c=c, src_d=src_d, width=width, pv=b.semv.get("qm", 0)):
                eng.wait_ge(b.sem("qb2"), 1)
                if pv:
                    eng.wait_ge(b.sem("qm"), pv)
                eng.dma_start(out=wchunk[:, 0:width], in_=src_d[c * 128 : (c + 1) * 128, :]).then_inc(b.sem("ql"), 16)
            b.on("sync", ql)

            qm_t = b.bump("qm", 1)

            def qm(eng, name=name, c=c, width=width, t=ql_t):
                eng.wait_ge(b.sem("ql"), t)
                col = 0 if name == "fc" else 1
                eng.tensor_reduce(stat[:, 2:3], wchunk[:, 0:width], axis=AX.X, op=ALU.max, apply_absolute_value=True)
                if c == 0:
                    eng.tensor_copy(qstat[:, col : col + 1], stat[:, 2:3]).then_inc(b.sem("qm"), 1)
                else:
                    eng.tensor_max(qstat[:, col : col + 1], qstat[:, col : col + 1], stat[:, 2:3]).then_inc(b.sem("qm"), 1)
            b.on("vector", qm)

    qm_all = b.semv["qm"]

    def qredmm(eng, t=qm_all):
        eng.wait_ge(b.sem("qm"), t)
        eng.matmul(f2ps_f[0][0:2, 0:128], qstat[:, 0:2], ident_sb[:], start=True, stop=True).then_inc(b.sem("qred"), 1)
    b.bump("qred", 1)
    b.on("tensor", qredmm)

    def qredv(eng):
        eng.wait_ge(b.sem("qred"), 1)
        eng.tensor_copy(sqt[0:2, 0:128], f2ps_f[0][0:2, 0:128])
        eng.reduce_max(stat[0:2, 0:1], sqt[0:2, 0:128], axis=AX.X).then_inc(b.sem("qred"), 1)
    b.bump("qred", 1)
    b.on("vector", qredv)

    def qredmm2(eng):
        eng.wait_ge(b.sem("qred"), 2)
        eng.matmul(f2ps_f[0][0:1, 0:2], stat[0:2, 0:1], ident_sb[0:2, 0:2], start=True, stop=True).then_inc(b.sem("qred"), 1)
    b.bump("qred", 1)
    b.on("tensor", qredmm2)

    def qredv2(eng):
        eng.wait_ge(b.sem("qred"), 3)
        eng.tensor_copy(stat[0:1, 2:4], f2ps_f[0][0:1, 0:2]).then_inc(b.sem("qred"), 1)
    b.bump("qred", 1)
    b.on("vector", qredv2)

    def qredmm3(eng):
        eng.wait_ge(b.sem("qred"), 4)
        eng.matmul(f2ps_f[0][:, 0:2], ones1[:], stat[0:1, 2:4], start=True, stop=True).then_inc(b.sem("qred"), 1)
    b.bump("qred", 1)
    b.on("tensor", qredmm3)

    def qredv3(eng):
        eng.wait_ge(b.sem("qred"), 5)
        eng.tensor_copy(qstat[:, 2:4], f2ps_f[0][:, 0:2]).then_inc(b.sem("qred"), 1)
    b.bump("qred", 1)
    b.on("vector", qredv3)

    def qrs(eng):
        eng.wait_ge(b.sem("qred"), 6)
        eng.reciprocal(qstat[:, 4:5], qstat[:, 2:3])
        eng.tensor_scalar_mul(qstat[:, 4:5], qstat[:, 4:5], 127.0)
        eng.reciprocal(qstat[:, 5:6], qstat[:, 3:4])
        eng.tensor_scalar_mul(qstat[:, 5:6], qstat[:, 5:6], 127.0)
        eng.tensor_scalar_mul(qstat[:, 6:7], qstat[:, 2:3], 1.0 / 127.0)
        eng.tensor_scalar_mul(qstat[:, 7:8], qstat[:, 3:4], 1.0 / 127.0)
        eng.tensor_scalar_mul(qstat[:, 0:1], qstat[:, 6:7], -1.0)
        eng.tensor_scalar_mul(qstat[:, 1:2], qstat[:, 7:8], -1.0).then_inc(b.sem("qrs"), 1)
    b.bump("qrs", 1)
    b.on("vector", qrs)

    for (name, src_d, dst_d, nch, width, A_sb, B_sb, rcol, ncol, amax_col) in (
        ("fc", WfcT_d, wefffc_d, 8, 4096, afc_bf, b2fc, 4, 0, 2),
        ("pr", WprojT_d, weffpr_d, 32, 1024, aproj_bf, b2pr, 5, 1, 3),
    ):
        for c in range(nch):
            ql2_t = b.bump("ql2", 16)

            def ql2(eng, c=c, src_d=src_d, width=width, pv=b.semv.get("qst", 0)):
                eng.wait_ge(b.sem("qrs"), 1)
                if pv:
                    eng.wait_ge(b.sem("qst"), pv)
                eng.dma_start(out=wchunk[:, 0:width], in_=src_d[c * 128 : (c + 1) * 128, :]).then_inc(b.sem("ql2"), 16)
            b.on("sync", ql2)

            nos = width // 1024
            for osi in range(nos):
                lo_t = b.bump("qlo", 1)

                def qlo(eng, name=name, c=c, osi=osi, A_sb=A_sb, B_sb=B_sb, t=ql2_t, pv=b.semv.get("qev", 0)):
                    eng.wait_ge(b.sem("ql2"), t)
                    if pv:
                        eng.wait_ge(b.sem("qev"), pv)
                    eng.matmul(qps_f[:], A_sb[0:16, c * 128 : (c + 1) * 128], B_sb[0:16, osi * 1024 : osi * 1024 + 512], start=True, stop=True)
                    eng.matmul(kps_f[:], A_sb[0:16, c * 128 : (c + 1) * 128], B_sb[0:16, osi * 1024 + 512 : (osi + 1) * 1024], start=True, stop=True).then_inc(b.sem("qlo"), 1)
                b.on("tensor", qlo)

                qa_t = b.bump("qact", 1)

                def qact(eng, name=name, osi=osi, rcol=rcol, amax_col=amax_col, ncol=ncol, t=lo_t, pq=b.semv.get("qint", 0)):
                    sl = slice(osi * 1024, (osi + 1) * 1024)
                    eng.wait_ge(b.sem("qlo"), t)
                    if pq:
                        eng.wait_ge(b.sem("qint"), pq)
                    eng.activation(out=scrA[:, sl], in_=wchunk[:, sl], func=AF.Identity, bias=16384.0, scale=qstat[:, rcol : rcol + 1]).then_inc(b.sem("qact"), 1)
                b.on("scalar", qact)

                qi_t = b.bump("qint", 1)

                def qint(eng, osi=osi, t=qa_t):
                    sl = slice(osi * 1024, (osi + 1) * 1024)
                    eng.wait_ge(b.sem("qact"), t)
                    eng.tensor_copy(wchunkI[:, sl], scrA[:, sl])
                    eng.tensor_copy(wchunk[:, sl], wchunkI[:, sl])
                    eng.tensor_sub(scrA[:, sl], scrA[:, sl], wchunk[:, sl])
                    eng.tensor_single_scalar(scrA[:, sl], scrA[:, sl], 0.5, ALU.is_ge)
                    eng.tensor_add(scrA[:, sl], scrA[:, sl], wchunk[:, sl]).then_inc(b.sem("qint"), 1)
                b.on("vector", qint)

                qc_t = b.bump("qchain", 1)

                def qchain(eng, osi=osi, ncol=ncol, amax_col=amax_col, t=qi_t):
                    sl = slice(osi * 1024, (osi + 1) * 1024)
                    eng.wait_ge(b.sem("qint"), t)
                    eng.activation(out=wchunk[:, sl], in_=scrA[:, sl], func=AF.Relu, bias=-16256.0, scale=1.0)
                    eng.activation(out=scrA[:, sl], in_=wchunk[:, sl], func=AF.Relu, bias=255.0, scale=-1.0)
                    eng.activation(out=wchunk[:, sl], in_=scrA[:, sl], func=AF.Identity, bias=qstat[:, amax_col : amax_col + 1], scale=qstat[:, ncol : ncol + 1]).then_inc(b.sem("qchain"), 1)
                b.on("scalar", qchain)

                qe_t = b.bump("qev", 1)

                def qev(eng, osi=osi, t=qc_t):
                    eng.wait_ge(b.sem("qchain"), t)
                    eng.tensor_add(weffo[:, osi * 1024 : osi * 1024 + 512], wchunk[:, osi * 1024 : osi * 1024 + 512], qps_f[:])
                    eng.tensor_add(weffo[:, osi * 1024 + 512 : (osi + 1) * 1024], wchunk[:, osi * 1024 + 512 : (osi + 1) * 1024], kps_f[:]).then_inc(b.sem("qev"), 1)
                b.on("vector", qev)

            qs_t = b.bump("qst", 16)
            qe_all = b.semv["qev"]

            def qst(eng, c=c, dst_d=dst_d, width=width, t=qe_all):
                eng.wait_ge(b.sem("qev"), t)
                eng.dma_start(out=dst_d[c * 128 : (c + 1) * 128, :], in_=weffo[:, 0:width]).then_inc(b.sem("qst"), 16)
            b.on("gpsimd", qst)

    y2ps = [(s1ps_f[0], s1ps_f[1]), (s1ps_f[2], s1ps_f[3]), (f2ps_f[0], f2ps_f[1]), (qps_f, kps_f)]
    for t5 in range(8):
        yl_t = b.bump("fy1", 16 * 8)

        def fy1(eng, t5=t5, pv=b.semv.get("fgel", 0), qt=b.semv.get("qst", 0), ytr=y1tr_all):
            eng.wait_ge(b.sem("qst"), qt)
            eng.wait_ge(b.sem("y1tr"), ytr)
            if t5 > 0:
                eng.wait_ge(b.sem("fgel"), pv)
            for ic in range(8):
                eng.dma_start(out=y1f_all[:, ic * 512 : (ic + 1) * 512], in_=y1fm_d[ic * 128 : (ic + 1) * 128, t5 * 512 : (t5 + 1) * 512]).then_inc(b.sem("fy1"), 16)
        b.on("sync", fy1)

        for oc in range(32):
            j = oc % 2
            wb_t = b.bump(f"fwb{j}", 16)

            def fwb(eng, t5=t5, oc=oc, j=j, pv=b.semv.get("fmm", 0), qt=b.semv.get("qst", 0)):
                if t5 * 32 + oc < 2:
                    eng.wait_ge(b.sem("qst"), qt)
                else:
                    eng.wait_ge(b.sem("fmm"), pv - 8)
                src = wefffc_d.rearrange("(c p) o -> p c o", p=128)[:, :, oc * 128 : (oc + 1) * 128]
                eng.dma_start(out=wfc_col[j][:], in_=src).then_inc(b.sem(f"fwb{j}"), 16)
            b.on("sync", fwb)

            for ic in range(8):
                fm_t = b.bump("fmm", 1)

                def fmm(eng, t5=t5, oc=oc, ic=ic, j=j, wt=wb_t, yt=yl_t, gl=b.semv.get("fgel", 0), la=b.semv.get("ln2add", 0)):
                    if ic == 0:
                        eng.wait_ge(b.sem(f"fwb{j}"), wt)
                        eng.wait_ge(b.sem("fy1"), yt)
                        if oc == 0 and t5 > 0 and la:
                            eng.wait_ge(b.sem("ln2add"), la)
                        if t5 + oc > 0:
                            eng.wait_ge(b.sem("fgel"), gl)
                        eng.matmul(qps_f[:], bfc_bf[0:1, oc * 128 : (oc + 1) * 128], ones_bf[0:1, 0:512], start=True, stop=False)
                    eng.matmul(qps_f[:], wfc_col[j][:, ic * 128 : (ic + 1) * 128], y1f_all[:, ic * 512 : (ic + 1) * 512], start=False, stop=(ic == 7)).then_inc(b.sem("fmm"), 1)
                b.on("tensor", fmm)

            fg_t = b.bump("fgel", 1)

            def fgel(eng, t5=t5, oc=oc, t=b.semv["fmm"]):
                eng.wait_ge(b.sem("fmm"), t)
                eng.activation(out=gtile[:, oc * 512 : (oc + 1) * 512], in_=qps_f[:], func=AF.Gelu, bias=0.0, scale=1.0).then_inc(b.sem("fgel"), 1)
            b.on("scalar", fgel)

        fg_all = b.semv["fgel"]
        for oc in range(32):
            j = oc % 2
            wp_t = b.bump(f"fwp{j}", 16)

            def fwp(eng, t5=t5, oc=oc, j=j, pv=b.semv.get("f2mm", 0)):
                if t5 + oc > 0:
                    eng.wait_ge(b.sem("f2mm"), max(0, pv - 1))
                eng.dma_start(out=wpr_buf[j][:], in_=weffpr_d[oc * 128 : (oc + 1) * 128, :]).then_inc(b.sem(f"fwp{j}"), 16)
            b.on("scalar", fwp)

            f2_t = b.bump("f2mm", 1)

            def f2mm(eng, t5=t5, oc=oc, j=j, wt=wp_t, ga=fg_all, ln=b.semv.get("ln2add", 0)):
                eng.wait_ge(b.sem(f"fwp{j}"), wt)
                eng.wait_ge(b.sem("fgel"), ga)
                if oc == 0 and t5 > 0:
                    eng.wait_ge(b.sem("ln2add"), ln)
                for ts4 in range(4):
                    pa, pb = y2ps[ts4]
                    if oc == 0:
                        eng.matmul(pa[:], ones_bf[0:1, 0:128], bpr_bf[0:1, 0:512], start=True, stop=False)
                        eng.matmul(pb[:], ones_bf[0:1, 0:128], bpr_bf[0:1, 512:1024], start=True, stop=False)
                    g_blk = gtile[:, oc * 512 + ts4 * 128 : oc * 512 + (ts4 + 1) * 128]
                    eng.matmul(pa[:], g_blk, wpr_buf[j][:, 0:512], start=False, stop=(oc == 31))
                    i = eng.matmul(pb[:], g_blk, wpr_buf[j][:, 512:1024], start=False, stop=(oc == 31))
                i.then_inc(b.sem("f2mm"), 1)
            b.on("tensor", f2mm)

        f2_all = b.semv["f2mm"]
        for ts4 in range(4):
            rl_t = b.bump("fyr", 16)

            def fyr(eng, t5=t5, ts4=ts4, pv=b.semv.get("ln2add", 0)):
                if t5 + ts4 > 0:
                    eng.wait_ge(b.sem("ln2add"), pv)
                r0 = t5 * 512 + ts4 * 128
                eng.dma_start(out=y1res[:], in_=y1tm_d[r0 : r0 + 128, :]).then_inc(b.sem("fyr"), 16)
            b.on("sync", fyr)

            l2a_t = b.bump("ln2add", 1)

            def ln2a(eng, t5=t5, ts4=ts4, t=f2_all, rt=rl_t):
                eng.wait_ge(b.sem("f2mm"), t)
                eng.wait_ge(b.sem("fyr"), rt)
                pa, pb = y2ps[ts4]
                eng.tensor_add(y2t[:, 0:512], y1res[:, 0:512], pa[:])
                eng.tensor_add(y2t[:, 512:1024], y1res[:, 512:1024], pb[:]).then_inc(b.sem("ln2add"), 1)
            b.on("vector", ln2a)

            l2s_t = b.bump("ln2s", 1)

            def ln2s(eng, t=l2a_t):
                eng.wait_ge(b.sem("ln2add"), t)
                eng.tensor_tensor(sq2, y2t, y2t, ALU.mult)
                eng.reduce_sum(stat[:, 0:1], y2t, axis=AX.X)
                eng.reduce_sum(stat[:, 1:2], sq2, axis=AX.X)
                eng.tensor_scalar_mul(stat[:, 2:3], stat[:, 0:1], 1.0 / 1024.0)
                eng.tensor_scalar_mul(stat[:, 3:4], stat[:, 1:2], 1.0 / 1024.0)
                eng.tensor_tensor(stat[:, 4:5], stat[:, 2:3], stat[:, 2:3], ALU.mult)
                eng.tensor_sub(stat[:, 5:6], stat[:, 3:4], stat[:, 4:5])
                eng.tensor_scalar_add(stat[:, 6:7], stat[:, 5:6], 1e-5).then_inc(b.sem("ln2s"), 1)
            b.on("vector", ln2s)

            l2q_t = b.bump("ln2q", 1)

            def ln2q(eng, t=l2s_t):
                eng.wait_ge(b.sem("ln2s"), t)
                eng.activation(out=stat[:, 7:8], in_=stat[:, 6:7], func=AF.Sqrt, bias=0.0, scale=1.0).then_inc(b.sem("ln2q"), 1)
            b.on("scalar", ln2q)

            l2r_t = b.bump("ln2r", 1)

            def ln2r(eng, t=l2q_t):
                eng.wait_ge(b.sem("ln2q"), t)
                eng.reciprocal(stat[:, 0:1], stat[:, 7:8])
                eng.tensor_tensor(stat[:, 1:2], stat[:, 2:3], stat[:, 0:1], ALU.mult)
                eng.tensor_scalar_mul(stat[:, 1:2], stat[:, 1:2], -1.0).then_inc(b.sem("ln2r"), 1)
            b.on("vector", ln2r)

            l2n_t = b.bump("ln2", 1)

            def ln2n(eng, t=l2r_t, ot=b.semv.get("outst", 0)):
                eng.wait_ge(b.sem("ln2r"), t)
                if ot:
                    eng.wait_ge(b.sem("outst"), ot)
                eng.activation(out=out_tile, in_=y2t, func=AF.Identity, bias=stat[:, 1:2], scale=stat[:, 0:1]).then_inc(b.sem("ln2"), 1)
            b.on("scalar", ln2n)

            os_t = b.bump("outst", 16)

            def outst(eng, t5=t5, ts4=ts4, t=l2n_t):
                eng.wait_ge(b.sem("ln2"), t)
                r0 = t5 * 512 + ts4 * 128
                eng.dma_start(out=out_d[r0 : r0 + 128, :], in_=out_tile).then_inc(b.sem("outst"), 16)
            b.on("sync", outst)

    out_all = b.semv["outst"]

    def fin(eng, t=out_all):
        eng.wait_ge(b.sem("outst"), t)
    b.on("gpsimd", fin)


    if upto == "agg" and p5_stop is not None:
        return b
    if upto == "agg":
        agg_out = b.dram_out("agg_dbg", [1024, 4096], BF16)
        bounce = pool[:, 0:4096].bitcast(BF16)[:, 0:4096]

        def aggout(eng, t=b.semv["aggc"]):
            eng.wait_ge(b.sem("aggc"), t)
            n = 0
            for c in range(8):
                eng.dma_start(out=bounce, in_=aggfm_d[c * 128 : (c + 1) * 128, :]).then_inc(b.sem("outd3"), 16)
                n += 16
                eng.wait_ge(b.sem("outd3"), n)
                eng.dma_start(out=agg_out[c * 128 : (c + 1) * 128, :], in_=bounce).then_inc(b.sem("outd3"), 16)
                n += 16
                eng.wait_ge(b.sem("outd3"), n)
        b.bump("outd3", 16 * 16)
        b.on("gpsimd", aggout)
        return b

    return b




_BUILD_CACHE = {}


def _get_build():
    if "b" not in _BUILD_CACHE:
        b = build_p123(upto="full", use_cc=True)
        nc = b.finalize()
        _BUILD_CACHE["b"] = b
        _BUILD_CACHE["nc"] = nc
    return _BUILD_CACHE["b"], _BUILD_CACHE["nc"]


def kernel(**inputs):
    from concourse.bass_utils import run_bass_kernel_spmd

    b, nc = _get_build()
    in_maps = []
    for core in range(8):
        m = host_prep_attn(inputs, core)
        m.update(b.const_arrays)
        in_maps.append(m)
    res = run_bass_kernel_spmd(nc, in_maps, core_ids=list(range(8)))
    out = np.stack([np.asarray(res.results[c]["out"]) for c in range(8)], axis=0)
    return out.astype(np.float32)


# version 09
# speedup vs baseline: 1.2451x; 1.0733x over previous
import numpy as np
import concourse.bass as bass
import concourse.mybir as mybir

F32 = mybir.dt.float32
BF16 = mybir.dt.bfloat16
AF = mybir.ActivationFunctionType
ALU = mybir.AluOpType
AX = mybir.AxisListType
U32 = mybir.dt.uint32
from concourse.bass_isa_rust import ReduceOp as _ReduceOp
from concourse.library_config import attn as _LIB_ATTN
_RED_MAX = _ReduceOp.max if hasattr(_ReduceOp, 'max') else None
I32 = mybir.dt.int32

T = 4096
E = 1024
NF2 = 33
EW = 256
NH = E // EW
PERM = np.array([2 * p + h for h in (0, 1) for p in range(32)])
DBG_NOACC = False


def make_consts():
    c = {}
    s2 = np.arange(64)[:, None]
    f2 = np.arange(64)[None, :]
    M1 = np.exp(-2j * np.pi * s2 * f2 / 64.0)
    z = np.zeros((64, 64))
    c["c1re_bd"] = np.block([[M1.real, z], [z, M1.real]]).astype(np.float32)
    c["c1im_bd"] = np.block([[M1.imag, z], [z, M1.imag]]).astype(np.float32)

    s1 = np.arange(64)
    f1 = np.arange(64)
    L2 = np.zeros((NF2, 128, 128), np.float32)
    for ff2 in range(NF2):
        ang = -2.0 * np.pi * (s1[:, None] * (ff2 + 64.0 * f1[None, :])) / T
        cr, ci = np.cos(ang), np.sin(ang)
        L2[ff2, 0:64, 0:64] = cr[PERM, :]
        L2[ff2, 0:64, 64:128] = ci[PERM, :]
        L2[ff2, 64:128, 0:64] = -ci[PERM, :]
        L2[ff2, 64:128, 64:128] = cr[PERM, :]
    c["stage2_L"] = L2

    inner = np.exp(+2j * np.pi * np.outer(f1, s1) / 64.0)
    WA = np.zeros((128, 128), np.float32)
    WA[0:64, 0:64] = inner.real
    WA[0:64, 64:128] = inner.imag
    WA[64:128, 0:64] = -inner.imag
    WA[64:128, 64:128] = inner.real
    c["macA_rhs"] = WA

    d = np.where((np.arange(NF2) >= 1) & (np.arange(NF2) <= 31), 2.0, 1.0)
    tw = np.exp(+2j * np.pi * np.outer(s1, np.arange(NF2)) / T).T
    twf = d[:, None] * tw / (T * E)
    c["mac_tw_re"] = np.concatenate([twf.real, twf.real], axis=1).astype(np.float32)
    c["mac_tw_im"] = np.concatenate([twf.imag, twf.imag], axis=1).astype(np.float32)

    ang = +2.0 * np.pi * np.outer(np.arange(NF2), s1) / 64.0
    c["macB_cos"] = np.cos(ang).astype(np.float32)
    c["macB_msin"] = (-np.sin(ang)).astype(np.float32)

    import ml_dtypes
    bf = lambda a: np.asarray(a, dtype=ml_dtypes.bfloat16)
    c["c1re_bf"] = bf(c["c1re_bd"])
    c["c1im_bf"] = bf(c["c1im_bd"])
    c["st2v_re"] = bf(c["stage2_L"][:, :, 0:64].reshape(NF2 * 128, 64))
    c["st2v_im"] = bf(c["stage2_L"][:, :, 64:128].reshape(NF2 * 128, 64))
    LA = np.zeros((NF2, 128, 128), np.float32)
    f1 = np.arange(64)
    for ff2 in range(NF2):
        dd = 2.0 if 1 <= ff2 <= 31 else 1.0
        inner = np.exp(+2j * np.pi * np.outer(f1, s1.ravel() if s1.ndim > 1 else s1) / 64.0)
        tw = np.exp(+2j * np.pi * np.arange(64) * ff2 / T)[None, :]
        Mfull = dd * (1.0 / T) * inner * tw
        LA[ff2, 0:64, 0:64] = Mfull.real
        LA[ff2, 0:64, 64:128] = Mfull.imag
        LA[ff2, 64:128, 0:64] = -Mfull.imag
        LA[ff2, 64:128, 64:128] = Mfull.real
    c["invA_re"] = bf(LA[:, 0:64, :].reshape(NF2 * 64, 128))
    c["invA_im"] = bf(LA[:, 64:128, :].reshape(NF2 * 64, 128))
    angB = +2.0 * np.pi * np.outer(np.arange(NF2), np.arange(64)) / 64.0
    LB = np.zeros((66, 64), np.float32)
    LB[0:NF2, :] = np.cos(angB)
    LB[NF2 : 2 * NF2, :] = -np.sin(angB)
    c["invB"] = bf(LB)
    fvals = (np.arange(64)[:, None] * 64 + np.arange(NF2)[None, :]).reshape(-1)
    c["f_rep"] = np.broadcast_to(fvals.astype(np.float32), (128, 64 * NF2)).copy()
    return c


def host_prep_attn(inputs, core):
    x = np.asarray(inputs["L0_attn_hidden_states"])[core]
    d = {}
    xt = np.ascontiguousarray(x.T).reshape(E, 64, 64)
    xp = np.empty((32, E, 128), np.float32)
    for p in range(32):
        xp[p, :, 0:64] = xt[:, :, 2 * p]
        xp[p, :, 64:128] = xt[:, :, 2 * p + 1]
    d["xpair"] = xp
    d["Wq"] = np.ascontiguousarray(np.asarray(inputs["L0_attn_Wq"], np.float32))
    d["Wk"] = np.ascontiguousarray(np.asarray(inputs["L0_attn_Wk"], np.float32))
    d["bq"] = np.ascontiguousarray(np.asarray(inputs["L0_attn_bq"], np.float32))[None, :]
    d["bk"] = np.ascontiguousarray(np.asarray(inputs["L0_attn_bk"], np.float32))[None, :]
    d["Wv"] = np.ascontiguousarray(np.asarray(inputs["L0_attn_Wv"], np.float32))
    d["bv"] = np.ascontiguousarray(np.asarray(inputs["L0_attn_bv"], np.float32))[None, :]
    d["Wo"] = np.ascontiguousarray(np.asarray(inputs["L0_attn_Wo"], np.float32))
    d["bo"] = np.ascontiguousarray(np.asarray(inputs["L0_attn_bo"], np.float32))[None, :]
    d["x_tm"] = np.ascontiguousarray(x.astype(np.float32))
    d["WfcT"] = np.ascontiguousarray(np.asarray(inputs["L0_ffn_W_fc"], np.float32).T)
    d["WprojT"] = np.ascontiguousarray(np.asarray(inputs["L0_ffn_W_proj"], np.float32).T)
    d["Afc"] = np.ascontiguousarray(np.asarray(inputs["L0_ffn_A_fc"], np.float32))
    d["BfcT"] = np.ascontiguousarray(np.asarray(inputs["L0_ffn_B_fc"], np.float32).T)
    d["Aproj"] = np.ascontiguousarray(np.asarray(inputs["L0_ffn_A_proj"], np.float32))
    d["BprojT"] = np.ascontiguousarray(np.asarray(inputs["L0_ffn_B_proj"], np.float32).T)
    d["bfc"] = np.ascontiguousarray(np.asarray(inputs["L0_ffn_b_fc"], np.float32))[None, :]
    d["bproj"] = np.ascontiguousarray(np.asarray(inputs["L0_ffn_b_proj"], np.float32))[None, :]
    return d


class B:

    def __init__(self):
        self.nc = bass.Bass("TRN2", target_bir_lowering=False)
        self.cmds = {k: [] for k in ["sync", "tensor", "vector", "scalar", "gpsimd"]}
        self.semv = {}
        self.sems = {}
        self.const_arrays = {}

    def sem(self, name):
        if name not in self.sems:
            self.sems[name] = self.nc.semaphore(name).__enter__()
            self.semv[name] = 0
        return self.sems[name]

    def bump(self, name, n):
        self.sem(name)
        self.semv[name] += n
        return self.semv[name]

    def on(self, eng, fn):
        self.cmds[eng].append(fn)

    def dram_in(self, name, shape, dtype=F32):
        return self.nc.dram_tensor(name, list(shape), dtype, kind="ExternalInput").ap()

    def dram_out(self, name, shape, dtype=F32):
        return self.nc.dram_tensor(name, list(shape), dtype, kind="ExternalOutput").ap()

    def sb(self, name, shape, dtype=F32):
        return self.nc.alloc_sbuf_tensor(name, list(shape), dtype)

    def ps(self, name, shape, dtype=F32):
        return self.nc.alloc_psum_tensor(name, list(shape), dtype)

    def const_in(self, name, arr):
        arr = np.ascontiguousarray(arr)
        import ml_dtypes
        dt_ = F32 if arr.dtype == np.float32 else (BF16 if arr.dtype == ml_dtypes.bfloat16 else None)
        assert dt_ is not None, arr.dtype
        ap = self.dram_in(name, arr.shape, dt_)
        self.const_arrays[name] = arr
        return ap

    class _AutoSer:
        _SKIP = {"wait_ge", "dma_start", "dma_start_transpose", "then_inc",
                 "sem_inc", "nop", "drain"}

        class _Proxy:
            def __init__(self, ser, ins):
                self._ser = ser
                self._ins = ins

            def then_inc(self, sem, n):
                self._ins.then_inc(sem, n)
                c = self._ser._ucount.get(id(sem), 0) + n
                self._ser._ucount[id(sem)] = c
                self._ser._token = (sem, c)
                self._ser._last = None
                return self._ins

            def __getattr__(self, name):
                return getattr(self._ins, name)

        def __init__(self, eng, sem):
            self._eng = eng
            self._sem = sem
            self._count = 0
            self._token = None
            self._last = None
            self._ucount = {}

        def _flush(self):
            if self._last is not None:
                self._last.then_inc(self._sem, 1)
                self._count += 1
                self._token = (self._sem, self._count)
                self._last = None

        def __getattr__(self, name):
            attr = getattr(self._eng, name)
            if name.startswith("_") or name in self._SKIP or not callable(attr):
                return attr

            def call(*a, **k):
                self._flush()
                if self._token is not None:
                    self._eng.wait_ge(self._token[0], self._token[1])
                    self._token = None
                ins = attr(*a, **k)
                if hasattr(ins, "then_inc"):
                    self._last = ins
                    return B._AutoSer._Proxy(self, ins)
                return ins

            return call

    def finalize(self):
        nc = self.nc
        with nc.Block() as block:
            @block.sync
            def _(eng):
                for fn in self.cmds["sync"]:
                    fn(eng)

            @block.tensor
            def _(eng):
                for fn in self.cmds["tensor"]:
                    fn(eng)

            @block.vector
            def _(eng):
                w = B._AutoSer(eng, self.sem("autoV"))
                for fn in self.cmds["vector"]:
                    fn(w)

            @block.scalar
            def _(eng):
                w = B._AutoSer(eng, self.sem("autoS"))
                for fn in self.cmds["scalar"]:
                    fn(w)

            @block.gpsimd
            def _(eng):
                w = B._AutoSer(eng, self.sem("autoG"))
                for fn in self.cmds["gpsimd"]:
                    fn(w)
        return nc


def build_p123(dump=(), dump_slice=0, upto="mac", p5_stop=None, use_cc=False):
    b = B()
    C = make_consts()
    dump = set(dump)

    xpair = b.dram_in("xpair", [32, E, 128])
    Wq_d = b.dram_in("Wq", [E, E])
    Wk_d = b.dram_in("Wk", [E, E])
    bq_d = b.dram_in("bq", [1, E])
    bk_d = b.dram_in("bk", [1, E])
    Wv_d = b.dram_in("Wv", [E, E])
    bv_d = b.dram_in("bv", [1, E])
    c1re_d = b.const_in("c1re_bd", C["c1re_bd"])
    c1im_d = b.const_in("c1im_bd", C["c1im_bd"])
    st2_d = b.const_in("stage2_L", C["stage2_L"].reshape(NF2 * 128, 128))
    macA_d = b.const_in("macA_rhs", C["macA_rhs"])
    mtwre_d = b.const_in("mac_tw_re", C["mac_tw_re"])
    mtwim_d = b.const_in("mac_tw_im", C["mac_tw_im"])
    macBc_d = b.const_in("macB_cos", C["macB_cos"])
    macBs_d = b.const_in("macB_msin", C["macB_msin"])
    c1rebf_d = b.const_in("c1re_bf", C["c1re_bf"])
    c1imbf_d = b.const_in("c1im_bf", C["c1im_bf"])
    st2vre_d = b.const_in("st2v_re", C["st2v_re"])
    st2vim_d = b.const_in("st2v_im", C["st2v_im"])
    invAre_d = b.const_in("invA_re", C["invA_re"])
    invAim_d = b.const_in("invA_im", C["invA_im"])
    invB_d = b.const_in("invB", C["invB"])
    mac_out = b.dram_out("mac", [64, 64])
    cvals = np.broadcast_to(np.array([-np.pi, 2.0 * np.pi / 4096.0, 0.0, 1.0,
                                      -1.0, 16512.5, -16384.0, 255.0,
                                      16384.0, -16256.0], np.float32), (128, 10)).copy()
    ident_d = b.const_in("ident128", np.eye(128, dtype=np.float32))
    ident_sb = b.sb("ident_sb", [128, 128])
    import ml_dtypes as _mld
    identbf_d = b.const_in("ident64bf", np.asarray(np.tile(np.eye(64), (2, 1)), dtype=_mld.bfloat16))
    identbf_sb = b.sb("identbf_sb", [128, 64], BF16)
    cvals_d = b.const_in("cvals", cvals)
    csb = b.sb("csb", [128, 10])
    if upto in ("topk",):
        g_out = b.dram_out("g", [2, 64, 8 * NF2])

    dbg = {}
    if "rbq" in dump:
        dbg["rbq"] = b.dram_out("d_rbq", [128, NF2 * EW])
    if "qf" in dump:
        dbg["qf"] = b.dram_out("d_qf", [2, 128, EW])
    if "ppack" in dump:
        dbg["ppack"] = b.dram_out("d_ppack", [128, NF2])
    if "qbuf" in dump:
        dbg["qbuf"] = b.dram_out("d_qbuf", [128, EW])
    if "s1buf" in dump:
        dbg["s1buf"] = b.dram_out("d_s1buf", [2, 128, EW])
    if "pparts" in dump:
        dbg["pparts"] = b.dram_out("d_pparts", [64, NF2 * 4])

    xp_sb = [b.sb(f"xp{j}", [128, E]) for j in range(2)]
    wq_sb = b.sb("wq_sb", [128, 8 * EW])
    wk_sb = b.sb("wk_sb", [128, 8 * EW])
    bias_sb = b.sb("bias_sb", [1, 2 * E])
    ones1 = b.sb("ones1", [1, 128])
    qbuf = [b.sb(f"qbuf{j}", [128, EW]) for j in range(2)]
    kbuf = [b.sb(f"kbuf{j}", [128, EW]) for j in range(2)]
    s1buf = [[b.sb(f"s1buf{j}_{c}", [128, EW]) for c in range(4)] for j in range(2)]
    c1re_sb = b.sb("c1re_sb", [128, 128])
    c1im_sb = b.sb("c1im_sb", [128, 128])
    b._pool_base = 229344 - b.nc.sbuf_bytes_remaining
    pool = b.sb("pool", [128, 28800])

    def bfat(name, shape, off_f32):
        return b.nc.alloc_sbuf_tensor_at(name, list(shape), BF16, offset=b._pool_base + off_f32 * 4)

    rbq = pool[:, 0:8448]
    rbk = pool[:, 8448:16896]
    st2_sb = pool[:, 16896:21120]
    pparts = b.sb("pparts", [64, NF2 * 4])
    ppack = b.sb("ppack", [128, NF2])
    macA_sb = b.sb("macA_sb", [128, 128])
    mtw_sb = b.sb("mtw_sb", [NF2, 256])
    u_sb = b.sb("u_sb", [NF2, 128])
    up_sb = b.sb("up_sb", [NF2, 128])
    macB_sb = b.sb("macB_sb", [NF2, 128])
    mac_sb = b.sb("mac_sb", [64, 64])

    qps_f = b.ps("qps", [128, 512])
    kps_f = b.ps("kps", [128, 512])
    s1ps_f = [b.ps(f"s1ps{c}", [128, 512]) for c in range(4)]
    f2ps_f = [b.ps(f"f2ps{c}", [128, 512]) for c in range(2)]
    qps = qps_f[:, 0:EW]
    kps = kps_f[:, 0:EW]
    s1ps = [t[:, 0:EW] for t in s1ps_f]
    f2ps = [t[:, 0:EW] for t in f2ps_f]

    def p0(eng):
        for dst, src in [
            (c1re_sb[:], c1re_d), (c1im_sb[:], c1im_d),
            (st2_sb[:], st2_d.rearrange("(f p) m -> p f m", p=128)),
            (bias_sb[0:1, 0:E], bq_d), (bias_sb[0:1, E : 2 * E], bk_d),
            (macA_sb[:], macA_d), (mtw_sb[:, 0:128], mtwre_d),
            (mtw_sb[:, 128:256], mtwim_d), (macB_sb[:, 0:64], macBc_d),
            (macB_sb[:, 64:128], macBs_d),
            (csb[:], cvals_d),
            (ident_sb[:], ident_d),
            (identbf_sb[:], identbf_d),
        ]:
            eng.dma_start(out=dst, in_=src).then_inc(b.sem("boot"), 16)
    b.bump("boot", 16 * 13)
    b.on("sync", p0)
    xpair_bf_d = b.nc.dram_tensor("xpair_bf", [32, E, 128], BF16, kind="Internal").ap()
    for _i, _v in enumerate([-np.pi, 2.0 * np.pi / 4096.0, 0.0, 1.0, -1.0, 16512.5, -16384.0, 255.0, 16384.0, -16256.0]):
        b.nc.const_aps.register(F32, _v, csb[:, _i : _i + 1])
    boot_t = b.semv["boot"]

    def p0v(eng):
        eng.memset(ones1[:], 1.0).then_inc(b.sem("ones"), 1)
    b.bump("ones", 1)
    b.on("vector", p0v)

    for H in range(NH):
        wload_t = b.bump("wload", 32)

        def p1w(eng, H=H, wt=b.semv["proj"] if "proj" in b.semv else 0):
            if H >= 1:
                eng.wait_ge(b.sem("proj"), wt)
            src_q = Wq_d.rearrange("(c p) o -> p c o", p=128)[:, :, H * EW : (H + 1) * EW]
            src_k = Wk_d.rearrange("(c p) o -> p c o", p=128)[:, :, H * EW : (H + 1) * EW]
            eng.dma_start(out=wq_sb[:], in_=src_q).then_inc(b.sem("wload"), 16)
            eng.dma_start(out=wk_sb[:], in_=src_k).then_inc(b.sem("wload"), 16)
        b.on("sync", p1w)

        for p in range(32):
            j = p % 2
            need_proj = b.semv.get("proj", 0)
            lsem = f"load{j}"
            load_t = b.bump(lsem, 16)

            def l(eng, p=p, j=j, H=H, need=need_proj, lsem=lsem):
                if p >= 2 or H >= 1:
                    eng.wait_ge(b.sem("proj"), need)
                src = xpair[p].rearrange("(c i) m -> i c m", i=128)
                eng.dma_start(out=xp_sb[j][:], in_=src).then_inc(b.sem(lsem), 16)
            b.on("sync", l)

            qevac_need = b.semv.get("qevac", 0)
            proj_t = b.bump("proj", 1)

            def mm(eng, p=p, j=j, H=H, lt=load_t, wt=wload_t, qn=qevac_need, lsem=lsem):
                eng.wait_ge(b.sem(lsem), lt)
                if p == 0:
                    eng.wait_ge(b.sem("boot"), boot_t)
                    eng.wait_ge(b.sem("ones"), 1)
                eng.wait_ge(b.sem("wload"), wt)
                if p >= 1 or H >= 1:
                    eng.wait_ge(b.sem("qevac"), qn)
                eng.matmul(qps[:], ones1[:], bias_sb[0:1, H * EW : (H + 1) * EW], start=True, stop=False)
                eng.matmul(kps[:], ones1[:], bias_sb[0:1, E + H * EW : E + (H + 1) * EW], start=True, stop=False)
                i = None
                for c in range(8):
                    lhsT = xp_sb[j][:, c * 128 : (c + 1) * 128]
                    eng.matmul(qps[:], lhsT, wq_sb[:, c * EW : (c + 1) * EW], start=False, stop=(c == 7))
                    i = eng.matmul(kps[:], lhsT, wk_sb[:, c * EW : (c + 1) * EW], start=False, stop=(c == 7))
                i.then_inc(b.sem("proj"), 1)
            b.on("tensor", mm)

            s1_need = b.semv.get("s1mm", 0)
            qevac_t = b.bump("qevac", 1)

            def ev(eng, p=p, j=j, H=H, pt=proj_t, sn=s1_need):
                eng.wait_ge(b.sem("proj"), pt)
                if p >= 2 or H >= 1:
                    eng.wait_ge(b.sem("s1mm"), sn)
                eng.tensor_copy(qbuf[j][:], qps[:])
                eng.tensor_copy(kbuf[j][:], kps[:]).then_inc(b.sem("qevac"), 1)
            b.on("vector", ev)

            s1evac_need = b.semv.get("s1evac", 0)
            s1_t = b.bump("s1mm", 1)

            def s1(eng, p=p, j=j, H=H, qt=qevac_t, en=s1evac_need):
                eng.wait_ge(b.sem("qevac"), qt)
                if p >= 1 or H >= 1:
                    eng.wait_ge(b.sem("s1evac"), en)
                eng.matmul(s1ps[0][:], c1re_sb[:], qbuf[j][:], start=True, stop=True)
                eng.matmul(s1ps[1][:], c1im_sb[:], qbuf[j][:], start=True, stop=True)
                eng.matmul(s1ps[2][:], c1re_sb[:], kbuf[j][:], start=True, stop=True)
                eng.matmul(s1ps[3][:], c1im_sb[:], kbuf[j][:], start=True, stop=True).then_inc(b.sem("s1mm"), 1)
            b.on("tensor", s1)

            corner_need = b.semv.get("corner", 0)
            s1evac_t = b.bump("s1evac", 1)

            def ev1(eng, p=p, j=j, H=H, st=s1_t, cn=corner_need):
                eng.wait_ge(b.sem("s1mm"), st)
                if p >= 2 or H >= 1:
                    eng.wait_ge(b.sem("corner"), cn)
                for c in range(4):
                    i = eng.tensor_copy(s1buf[j][c][:], s1ps[c][:])
                i.then_inc(b.sem("s1evac"), 1)
            b.on("vector", ev1)

            if "qbuf" in dump and H == dump_slice and p == 5:
                def dqb(eng, j=j, t=b.semv["qevac"]):
                    eng.wait_ge(b.sem("qevac"), t)
                    eng.dma_start(out=dbg["qbuf"], in_=qbuf[j][:]).then_inc(b.sem("dbg"), 16)
                b.bump("dbg", 16)
                b.on("gpsimd", dqb)

            if "s1buf" in dump and H == dump_slice and p == 5:
                def dsb(eng, j=j, t=b.semv["s1evac"]):
                    eng.wait_ge(b.sem("s1evac"), t)
                    eng.dma_start(out=dbg["s1buf"][0], in_=s1buf[j][0][:]).then_inc(b.sem("dbg"), 16)
                    eng.dma_start(out=dbg["s1buf"][1], in_=s1buf[j][1][:]).then_inc(b.sem("dbg"), 16)
                b.bump("dbg", 32)
                b.on("gpsimd", dsb)

            corner_t = b.bump("corner", 16 * 8)

            def corner(eng, p=p, j=j, H=H, et=s1evac_t):
                eng.wait_ge(b.sem("s1evac"), et)
                for (c, rb, base) in ((0, rbq, 0), (1, rbq, 64), (2, rbk, 0), (3, rbk, 64)):
                    for h in (0, 1):
                        src = s1buf[j][c][h * 64 : h * 64 + NF2, :]
                        r = base + h * 32 + p
                        eng.dma_start(out=rb[r : r + 1, :], in_=src).then_inc(b.sem("corner"), 16)
            b.on("gpsimd", corner)

            if H < 2 and p % 2 == 1:
                q = H * 16 + p // 2
                b.bump("xcast", 16)

                def xck(eng, q=q):
                    eng.dma_start(out=xpair_bf_d[q], in_=xpair[q]).then_inc(b.sem("xcast"), 16)
                b.on("gpsimd", xck)

        corner_all = b.semv["corner"]
        if "rbq" in dump and H == dump_slice:
            def drb(eng, ca=corner_all):
                eng.wait_ge(b.sem("corner"), ca)
                eng.dma_start(out=dbg["rbq"], in_=rbq[:]).then_inc(b.sem("dbg"), 16)
            b.bump("dbg", 16)
            b.on("gpsimd", drb)

        for f2 in range(NF2):
            pprod_need = b.semv.get("pprod", 0)
            st2_t = b.bump("st2mm", 1)

            def st2(eng, f2=f2, H=H, ca=corner_all, pn=pprod_need):
                eng.wait_ge(b.sem("corner"), ca)
                if not (H == 0 and f2 == 0):
                    eng.wait_ge(b.sem("pprod"), pn)
                L = st2_sb[:, f2 * 128 : (f2 + 1) * 128]
                eng.matmul(f2ps[0][:], L, rbq[:, f2 * EW : (f2 + 1) * EW], start=True, stop=True)
                eng.matmul(f2ps[1][:], L, rbk[:, f2 * EW : (f2 + 1) * EW], start=True, stop=True).then_inc(b.sem("st2mm"), 1)
            b.on("tensor", st2)

            if "qf" in dump and H == 0 and f2 == 0:
                def dqf(eng, st=st2_t):
                    eng.wait_ge(b.sem("st2mm"), st)
                    eng.tensor_copy(u_sb[0:NF2, :] if False else qbuf[0][:], f2ps[0][:])
                    eng.tensor_copy(kbuf[0][:], f2ps[1][:]).then_inc(b.sem("dbgv"), 1)
                b.bump("dbgv", 1)
                b.on("vector", dqf)

                def dqf2(eng, t=b.semv["dbgv"]):
                    eng.wait_ge(b.sem("dbgv"), t)
                    eng.dma_start(out=dbg["qf"][0], in_=qbuf[0][:]).then_inc(b.sem("dbg"), 16)
                    eng.dma_start(out=dbg["qf"][1], in_=kbuf[0][:]).then_inc(b.sem("dbg"), 16)
                b.bump("dbg", 32)
                b.on("gpsimd", dqf2)

            pprod_t = b.bump("pprod", 1)

            def pp(eng, f2=f2, H=H, st=st2_t):
                eng.wait_ge(b.sem("st2mm"), st)
                eng.tensor_copy(qf_sb[:], f2ps[0][:])
                qre = qf_sb[0:64, :]
                qim = qf_sb[64:128, :]
                kre = f2ps[1][0:64, :]
                kim = f2ps[1][64:128, :]
                col = lambda k: pparts[0:64, f2 * 4 + k : f2 * 4 + k + 1]
                pairs = ((qre, kre), (qim, kim), (qim, kre), (qre, kim))
                if H == 0 or DBG_NOACC:
                    for k, (a, bb) in enumerate(pairs):
                        eng.tensor_tensor(ttr_scr[:], a, bb, ALU.mult)
                        i = eng.reduce_sum(col(k), ttr_scr[:], axis=AX.X)
                    i.then_inc(b.sem("pprod"), 1)
                else:
                    for k, (a, bb) in enumerate(pairs):
                        eng.tensor_tensor(ttr_scr[:], a, bb, ALU.mult)
                        eng.reduce_sum(pscr[0:64, k : k + 1], ttr_scr[:], axis=AX.X)
                    for k in range(4):
                        i = eng.tensor_add(col(k), col(k), pscr[0:64, k : k + 1])
                    i.then_inc(b.sem("pprod"), 1)
            b.on("vector", pp)

        if "pparts" in dump and H == dump_slice:
            def dppp(eng, t=b.semv["pprod"]):
                eng.wait_ge(b.sem("pprod"), t)
                eng.dma_start(out=dbg["pparts"], in_=pparts[:]).then_inc(b.sem("dbg"), 16)
            b.bump("dbg", 16)
            b.on("gpsimd", dppp)

    pscr = b.sb("pscr", [64, 4])
    ttr_scr = b.sb("ttr_scr", [64, EW])
    qf_sb = b.sb("qf_sb", [128, EW])

    pprod_all = b.semv["pprod"]

    def pcomb(eng, pa=pprod_all):
        eng.wait_ge(b.sem("pprod"), pa)
        v = pparts[0:64, :].rearrange("p (f k) -> p f k", k=4)
        eng.tensor_add(ppack[0:64, :], v[:, :, 0], v[:, :, 1])
        eng.tensor_sub(ppack[64:128, :], v[:, :, 2], v[:, :, 3]).then_inc(b.sem("pcomb"), 1)
    b.bump("pcomb", 1)
    b.on("vector", pcomb)
    if "ppack" in dump:
        def dpp(eng, t=b.semv["pcomb"]):
            eng.wait_ge(b.sem("pcomb"), t)
            eng.dma_start(out=dbg["ppack"], in_=ppack[:]).then_inc(b.sem("dbg"), 16)
        b.bump("dbg", 16)
        b.on("gpsimd", dpp)

    uaps = f2ps[0][0:NF2, 0:128]
    macps = f2ps[1][0:64, 0:64]

    def macA(eng, t=b.semv["pcomb"]):
        eng.wait_ge(b.sem("pcomb"), t)
        eng.matmul(uaps, ppack[:], macA_sb[:], start=True, stop=True).then_inc(b.sem("macA"), 1)
    b.bump("macA", 1)
    b.on("tensor", macA)

    def mactw(eng, t=b.semv["macA"]):
        eng.wait_ge(b.sem("macA"), t)
        eng.tensor_copy(u_sb[:], uaps)
        ure, uim = u_sb[:, 0:64], u_sb[:, 64:128]
        twre_r, twre_i = mtw_sb[:, 0:64], mtw_sb[:, 128:192]
        eng.tensor_tensor(up_sb[:, 0:64], ure, twre_r, ALU.mult)
        eng.tensor_tensor(pscr2[:, 0:64], uim, twre_i, ALU.mult)
        eng.tensor_sub(up_sb[:, 0:64], up_sb[:, 0:64], pscr2[:, 0:64])
        eng.tensor_tensor(up_sb[:, 64:128], ure, twre_i, ALU.mult)
        eng.tensor_tensor(pscr2[:, 64:128], uim, twre_r, ALU.mult)
        eng.tensor_add(up_sb[:, 64:128], up_sb[:, 64:128], pscr2[:, 64:128]).then_inc(b.sem("mactw"), 1)
    pscr2 = b.sb("pscr2", [NF2, 128])
    b.bump("mactw", 1)
    b.on("vector", mactw)

    def macB(eng, t=b.semv["mactw"]):
        eng.wait_ge(b.sem("mactw"), t)
        eng.matmul(macps, macB_sb[:, 0:64], up_sb[:, 0:64], start=True, stop=False)
        eng.matmul(macps, macB_sb[:, 64:128], up_sb[:, 64:128], start=False, stop=True).then_inc(b.sem("macB"), 1)
    b.bump("macB", 1)
    b.on("tensor", macB)

    def macev(eng, t=b.semv["macB"]):
        eng.wait_ge(b.sem("macB"), t)
        eng.tensor_copy(mac_sb[:], macps).then_inc(b.sem("macev"), 1)
    b.bump("macev", 1)
    b.on("vector", macev)

    b.bump("outd", 16)
    dbg_total = b.semv.get("dbg", 0)

    def macout2(eng, t=b.semv["macev"], dt=dbg_total):
        eng.wait_ge(b.sem("macev"), t)
        eng.dma_start(out=mac_out, in_=mac_sb[:]).then_inc(b.sem("outd"), 16)
        eng.wait_ge(b.sem("outd"), 16)
        if dt:
            eng.wait_ge(b.sem("dbg"), dt)
    b.on("gpsimd", macout2)

    if upto == "mac":
        return b

    mac_all = pool[0:8, 0:4096]
    workA = pool[0:8, 4096:8192]
    workB = pool[0:8, 8192:12288]
    wm = pool[0:8, 12288:16384]
    wmp = pool[:, 25344:25600]
    rbw = pool[:, 25600:25864]
    m8 = b.sb("m8", [8, 8])
    mscr = pool[0:8, 21120:25216]
    s1w_ev = b.sb("s1w_ev", [128, 16])
    gt_re = b.sb("gt_re", [64, 8 * NF2])
    gt_im = b.sb("gt_im", [64, 8 * NF2])

    if use_cc:
        cc_in = b.nc.dram_tensor("cc_in", [1, 4096], F32)
        cc_out = b.nc.dram_tensor("cc_out", [8, 4096], F32)
        nrep_t = b.bump("macrep", 16 * 2)
        b.bump("ccd", 1)

        def macrep(eng, t=b.semv["macev"]):
            eng.wait_ge(b.sem("macev"), t)
            eng.dma_start(out=cc_in.ap().rearrange("q (a c) -> (q a) c", a=64), in_=mac_sb[:]).then_inc(b.sem("macrep"), 16)
            eng.wait_ge(b.sem("macrep"), 16)
            eng.collective_compute(
                "AllGather", ALU.bypass,
                replica_groups=[list(range(8))],
                ins=[cc_in.ap().opt()], outs=[cc_out.ap().opt()],
            ).then_inc(b.sem("ccd"), 1)
            eng.wait_ge(b.sem("ccd"), 1)
            eng.dma_start(out=mac_all, in_=cc_out.ap()).then_inc(b.sem("macrep"), 16)
        b.on("gpsimd", macrep)
    else:
        nrep_t = b.bump("macrep", 16 * 8)

        def macrep(eng, t=b.semv["macev"]):
            eng.wait_ge(b.sem("macev"), t)
            for bb in range(8):
                eng.dma_start(out=mac_all[bb : bb + 1, :], in_=mac_sb[:]).then_inc(b.sem("macrep"), 16)
        b.on("gpsimd", macrep)

    _tks0 = [b.semv.get("tks", 0)]
    b.bump("tks", 9)
    tk_t = b.bump("topk", 1)

    def topk(eng, t=nrep_t):
        eng.wait_ge(b.sem("macrep"), t)
        wtmp = wm
        bufs = [mac_all, workA, workB, wtmp]
        for r in range(3):
            src, dst = bufs[r], bufs[r + 1]
            eng.max(out=m8[:], in_=src).then_inc(b.sem("tks"), 1)
            eng.wait_ge(b.sem("tks"), b.bump("tks", 0) if False else 3 * r + 1 + _tks0[0])
            eng.tensor_single_scalar(mscr, src, m8[:, 7:8], ALU.is_ge)
            eng.scalar_tensor_tensor(dst, mscr, -1e30, src, ALU.mult, ALU.add).then_inc(b.sem("tks"), 2)
            eng.wait_ge(b.sem("tks"), 3 * r + 3 + _tks0[0])
        eng.tensor_single_scalar(mscr, bufs[3], -1e29, ALU.is_le)
        eng.tensor_mul(wm, mscr, mac_all).then_inc(b.sem("topk"), 1)
    b.on("vector", topk)

    for p in range(32):
        wtr_t = b.bump("wtr", 1)

        def wtr(eng, p=p, t=tk_t, ev=b.semv.get("wtrev", 0)):
            eng.wait_ge(b.sem("topk"), t)
            if p > 0:
                eng.wait_ge(b.sem("wtrev"), ev)
            wmv = wm.rearrange("q (s2 s1) -> q s1 s2", s1=64)
            eng.matmul(s1ps[3][0:64, 0:8], wmv[:, 2 * p, :], ident_sb[0:8, 0:8], start=True, stop=True)
            eng.matmul(s1ps[3][64:128, 0:8], wmv[:, 2 * p + 1, :], ident_sb[0:8, 0:8], start=True, stop=True).then_inc(b.sem("wtr"), 1)
        b.on("tensor", wtr)

        wtrev_t = b.bump("wtrev", 1)

        def wtrev(eng, p=p, t=wtr_t):
            eng.wait_ge(b.sem("wtr"), t)
            eng.tensor_copy(wmp[:, p * 8 : (p + 1) * 8], s1ps[3][:, 0:8]).then_inc(b.sem("wtrev"), 1)
        b.on("vector", wtrev)
    wml_t = b.semv["wtrev"]

    for p in range(32):
        ws1_t = b.bump("ws1", 1)

        def ws1(eng, p=p, t=wml_t, ev=b.semv.get("ws1ev", 0)):
            eng.wait_ge(b.sem("wtrev"), t)
            if p > 0:
                eng.wait_ge(b.sem("ws1ev"), ev)
            eng.matmul(s1ps[0][:, 0:8], c1re_sb[:], wmp[:, p * 8 : (p + 1) * 8], start=True, stop=True)
            eng.matmul(s1ps[1][:, 0:8], c1im_sb[:], wmp[:, p * 8 : (p + 1) * 8], start=True, stop=True).then_inc(b.sem("ws1"), 1)
        b.on("tensor", ws1)

        ws1ev_t = b.bump("ws1ev", 1)

        def ws1ev(eng, p=p, t=ws1_t, cn=b.semv.get("wcorner", 0)):
            eng.wait_ge(b.sem("ws1"), t)
            if p > 0:
                eng.wait_ge(b.sem("wcorner"), cn)
            eng.tensor_copy(s1w_ev[:, 0:8], s1ps[0][:, 0:8])
            eng.tensor_copy(s1w_ev[:, 8:16], s1ps[1][:, 0:8]).then_inc(b.sem("ws1ev"), 1)
        b.on("vector", ws1ev)

        wc_t = b.bump("wcorner", 16 * 4)

        def wcorner(eng, p=p, t=ws1ev_t):
            eng.wait_ge(b.sem("ws1ev"), t)
            for (c0, base) in ((0, 0), (8, 64)):
                for h in (0, 1):
                    src = s1w_ev[h * 64 : h * 64 + NF2, c0 : c0 + 8]
                    r = base + h * 32 + p
                    eng.dma_start(out=rbw[r : r + 1, :].rearrange("q (f m) -> q f m", f=NF2), in_=src).then_inc(b.sem("wcorner"), 16)
        b.on("gpsimd", wcorner)

    wall = b.semv["wcorner"]
    for f2 in range(NF2):
        wg_t = b.bump("wg", 1)

        def wg(eng, f2=f2, t=wall, ev=b.semv.get("wgev", 0)):
            eng.wait_ge(b.sem("wcorner"), t)
            if f2 > 0:
                eng.wait_ge(b.sem("wgev"), ev)
            eng.matmul(s1ps[2][:, 0:8], st2_sb[:, f2 * 128 : (f2 + 1) * 128], rbw[:, f2 * 8 : (f2 + 1) * 8], start=True, stop=True).then_inc(b.sem("wg"), 1)
        b.on("tensor", wg)

        wgev_t = b.bump("wgev", 1)

        def wgev(eng, f2=f2, t=wg_t):
            eng.wait_ge(b.sem("wg"), t)
            dst_re = gt_re[:, f2::NF2]
            dst_im = gt_im[:, f2::NF2]
            eng.tensor_copy(dst_re, s1ps[2][0:64, 0:8])
            eng.tensor_scalar_mul(dst_im, s1ps[2][64:128, 0:8], -1.0).then_inc(b.sem("wgev"), 1)
        b.on("vector", wgev)

    if upto == "topk":
        dbg_total2 = b.semv.get("dbg", 0)
        wm_out = b.dram_out("wm", [8, 4096])
        wk_out = b.dram_out("wk_dbg", [8, 4096])

        def tkout(eng, t=b.semv["wgev"], dt=dbg_total2):
            eng.wait_ge(b.sem("wgev"), t)
            eng.dma_start(out=wk_out, in_=workA).then_inc(b.sem("outd2"), 16)
            eng.dma_start(out=wm_out, in_=wm).then_inc(b.sem("outd2"), 16)
            eng.dma_start(out=g_out[0], in_=gt_re[:]).then_inc(b.sem("outd2"), 16)
            eng.dma_start(out=g_out[1], in_=gt_im[:]).then_inc(b.sem("outd2"), 16)
            eng.wait_ge(b.sem("outd2"), b.semv["outd2"])
            if dt:
                eng.wait_ge(b.sem("dbg"), dt)
        b.bump("outd2", 64)
        b.on("gpsimd", tkout)
        return b

    aggfm_d = b.nc.dram_tensor("aggfm", [1024, 4096], BF16, kind="Internal").ap()
    rbv = bfat("rbv", [128, 8448], 0)
    vfre = bfat("vfre", [64, 8448], 4224)
    vfim = bfat("vfim", [64, 8448], 8448)
    agre = bfat("agre", [64, 8448], 12672)
    agim = bfat("agim", [64, 8448], 16896)
    gsc1 = bfat("gsc1", [64, 2112], 21120)
    gsc2 = bfat("gsc2", [64, 2112], 22176)
    rbB = bfat("rbB", [66, 16384], 4224)
    aggt = bfat("aggt", [64, 16384], 12672)
    invAre_sb = b.sb("invAre_sb", [64, 4224], BF16)
    invAim_sb = b.sb("invAim_sb", [64, 4224], BF16)
    st2vre_sb = bfat("st2vre_sb", [128, 2112], 23232)
    st2vim_sb = bfat("st2vim_sb", [128, 2112], 24288)
    gt_re_bf = bfat("gt_re_bf", [64, 264], 25864)
    gt_im_bf = bfat("gt_im_bf", [64, 264], 26000)

    xpv = [b.sb(f"xpv{j}", [128, E], BF16) for j in range(2)]
    vbuf = [b.sb(f"vbuf{j}", [128, EW], BF16) for j in range(2)]
    vs1ev = [b.sb(f"vs1ev{j}", [128, 2 * EW], BF16) for j in range(2)]
    wv_sb = b.sb("wv_sb", [128, 8 * EW], BF16)
    ones_bf = b.sb("ones_bf", [1, 512], BF16)
    bv_sb = b.sb("bv_sb", [1, E])
    bv_bf = b.sb("bv_bf", [1, E], BF16)
    c1rebf_sb = b.sb("c1rebf_sb", [128, 128], BF16)
    c1imbf_sb = b.sb("c1imbf_sb", [128, 128], BF16)
    invB_sb = b.sb("invB_sb", [66, 64], BF16)
    uaev = b.sb("uaev", [128, EW], BF16)
    uaT = b.sb("uaT", [128, EW], BF16)

    vb_t = b.bump("vboot", 16 * 5)

    def vboot(eng, t=b.semv["topk"]):
        eng.wait_ge(b.sem("topk"), t)
        eng.dma_start(out=st2vre_sb[:], in_=st2vre_d.rearrange("(f p) m -> p f m", p=128)).then_inc(b.sem("vboot"), 16)
        eng.dma_start(out=st2vim_sb[:], in_=st2vim_d.rearrange("(f p) m -> p f m", p=128)).then_inc(b.sem("vboot"), 16)
        eng.dma_start(out=c1rebf_sb[:], in_=c1rebf_d).then_inc(b.sem("vboot"), 16)
        eng.dma_start(out=c1imbf_sb[:], in_=c1imbf_d).then_inc(b.sem("vboot"), 16)
        eng.dma_start(out=bv_sb[:], in_=bv_d).then_inc(b.sem("vboot"), 16)
        eng.dma_start(out=invAre_sb[:], in_=invAre_d.rearrange("(f p) m -> p f m", p=64)).then_inc(b.sem("iaload"), 16)
        eng.dma_start(out=invAim_sb[:], in_=invAim_d.rearrange("(f p) m -> p f m", p=64)).then_inc(b.sem("iaload"), 16)
    b.bump("iaload", 32)
    b.on("sync", vboot)

    def vboot2(eng, t=vb_t):
        eng.wait_ge(b.sem("vboot"), t)
        eng.memset(ones_bf[:], 1.0)
        eng.tensor_copy(bv_bf[:], bv_sb[:]).then_inc(b.sem("vboot2"), 1)
    vb2_t = b.bump("vboot2", 1)
    b.on("vector", vboot2)

    def gtbf(eng, t=b.semv["wgev"], t2=vb2_t):
        eng.wait_ge(b.sem("wgev"), t)
        eng.wait_ge(b.sem("vboot2"), t2 - 1 if False else t2)
        eng.tensor_copy(gt_re_bf[:], gt_re[:])
        eng.tensor_copy(gt_im_bf[:], gt_im[:]).then_inc(b.sem("gtbf"), 1)
    gtbf_t = b.bump("gtbf", 1)
    b.on("vector", gtbf)

    for H in range(NH):
        wvl_t = b.bump("wvl", 16)

        def wvl(eng, H=H, t=vb_t, pv=b.semv.get("vproj", 0)):
            eng.wait_ge(b.sem("vboot"), t)
            if H > 0:
                eng.wait_ge(b.sem("vproj"), pv)
            src = Wv_d.rearrange("(c p) o -> p c o", p=128)[:, :, H * EW : (H + 1) * EW]
            eng.dma_start(out=wv_sb[:], in_=src).then_inc(b.sem("wvl"), 16)
        b.on("gpsimd", wvl)

        for p in range(32):
            j = p % 2
            vlsem = f"vload{j}"
            vload_t = b.bump(vlsem, 16)

            def vload(eng, p=p, j=j, H=H, pv=b.semv.get("vproj", 0), vlsem=vlsem):
                if H == 0:
                    eng.wait_ge(b.sem("xcast"), 16 * (p + 1))
                if p >= 2 or H >= 1:
                    eng.wait_ge(b.sem("vproj"), pv)
                src = xpair_bf_d[p].rearrange("(c i) m -> i c m", i=128)
                eng.dma_start(out=xpv[j][:], in_=src).then_inc(b.sem(vlsem), 16)
            b.on("sync" if p % 2 == 0 else "scalar", vload)

            vproj_t = b.bump("vproj", 1)

            def vproj(eng, p=p, j=j, H=H, lt=vload_t, wt=wvl_t, ev=b.semv.get("vevac", 0), vlsem=vlsem):
                eng.wait_ge(b.sem(vlsem), lt)
                eng.wait_ge(b.sem("wvl"), wt)
                if p == 0 and H == 0:
                    eng.wait_ge(b.sem("vboot2"), 1)
                if (p >= 2 or H >= 1) and ev >= 1:
                    eng.wait_ge(b.sem("vevac"), ev - 1)
                vp = [qps, kps][p % 2]
                eng.matmul(vp[:], ones_bf[0:1, 0:128], bv_bf[0:1, H * EW : (H + 1) * EW], start=True, stop=False)
                i = None
                for c in range(8):
                    i = eng.matmul(vp[:], xpv[j][:, c * 128 : (c + 1) * 128], wv_sb[:, c * EW : (c + 1) * EW], start=False, stop=(c == 7))
                i.then_inc(b.sem("vproj"), 1)
            b.on("tensor", vproj)

            vevac_t = b.bump("vevac", 1)

            def vevac(eng, p=p, j=j, t=vproj_t, sn=b.semv.get("vs1", 0)):
                eng.wait_ge(b.sem("vproj"), t)
                if p >= 2 or H >= 1:
                    eng.wait_ge(b.sem("vs1"), sn)
                eng.tensor_copy(vbuf[j][:], [qps, kps][p % 2][:]).then_inc(b.sem("vevac"), 1)
            b.on("vector", vevac)

            vs1_t = b.bump("vs1", 1)

            def vs1(eng, p=p, j=j, t=vevac_t, en=b.semv.get("vs1ev", 0)):
                eng.wait_ge(b.sem("vevac"), t)
                if (p >= 2 or H >= 1) and en >= 1:
                    eng.wait_ge(b.sem("vs1ev"), en - 1)
                pr = s1ps[0 + 2 * (p % 2)]
                pi = s1ps[1 + 2 * (p % 2)]
                eng.matmul(pr[:], c1rebf_sb[:], vbuf[j][:], start=True, stop=True)
                eng.matmul(pi[:], c1imbf_sb[:], vbuf[j][:], start=True, stop=True).then_inc(b.sem("vs1"), 1)
            b.on("tensor", vs1)

            vs1ev_t = b.bump("vs1ev", 1)

            def vs1e(eng, p=p, j=j, t=vs1_t, cn=b.semv.get("vcorner", 0)):
                eng.wait_ge(b.sem("vs1"), t)
                if p >= 2 or H >= 1:
                    eng.wait_ge(b.sem("vcorner"), cn)
                pr = s1ps[0 + 2 * (p % 2)]
                pi = s1ps[1 + 2 * (p % 2)]
                eng.tensor_copy(vs1ev[j][:, 0:EW], pr[:])
                eng.tensor_copy(vs1ev[j][:, EW : 2 * EW], pi[:]).then_inc(b.sem("vs1ev"), 1)
            b.on("vector", vs1e)

            vc_t = b.bump("vcorner", 16 * 4)

            def vcorner(eng, p=p, j=j, t=vs1ev_t):
                eng.wait_ge(b.sem("vs1ev"), t)
                for (c0, base) in ((0, 0), (EW, 64)):
                    for h in (0, 1):
                        src = vs1ev[j][h * 64 : h * 64 + NF2, c0 : c0 + EW]
                        r = base + h * 32 + p
                        eng.dma_start(out=rbv[r : r + 1, :], in_=src).then_inc(b.sem("vcorner"), 16)
            b.on("sync" if p % 2 == 0 else "scalar", vcorner)

        if p5_stop == "pairs":
            continue
        vc_all = b.semv["vcorner"]
        for f2 in range(NF2):
            vst2_t = b.bump("vst2", 1)

            def vst2(eng, f2=f2, t=vc_all, ev=b.semv.get("vfev", 0)):
                eng.wait_ge(b.sem("vcorner"), t)
                if not (f2 == 0):
                    eng.wait_ge(b.sem("vfev"), ev)
                eng.matmul(f2ps[0][0:64, :], st2vre_sb[:, f2 * 64 : (f2 + 1) * 64], rbv[:, f2 * EW : (f2 + 1) * EW], start=True, stop=True)
                eng.matmul(f2ps[1][0:64, :], st2vim_sb[:, f2 * 64 : (f2 + 1) * 64], rbv[:, f2 * EW : (f2 + 1) * EW], start=True, stop=True).then_inc(b.sem("vst2"), 1)
            b.on("tensor", vst2)

            vfev_t = b.bump("vfev", 1)

            def vfev(eng, f2=f2, t=vst2_t):
                eng.wait_ge(b.sem("vst2"), t)
                eng.tensor_copy(vfre[:, f2 * EW : (f2 + 1) * EW], f2ps[0][0:64, :])
                eng.tensor_copy(vfim[:, f2 * EW : (f2 + 1) * EW], f2ps[1][0:64, :]).then_inc(b.sem("vfev"), 1)
            b.on("vector", vfev)

        if p5_stop == "vst2":
            continue
        vf_all = b.semv["vfev"]
        gm_t = b.bump("gmul", 1)

        def gmul(eng, H=H, t=vf_all, gt=gtbf_t, ac=b.semv.get("aggc", 0)):
            eng.wait_ge(b.sem("vfev"), t)
            eng.wait_ge(b.sem("gtbf"), gt)
            if ac:
                eng.wait_ge(b.sem("aggc"), ac)
            for jj in range(4):
                m = (H * 4 + jj) % 8
                gr = gt_re_bf[:, m * NF2 : (m + 1) * NF2].unsqueeze(2).broadcast_to([64, NF2, 64])
                gi = gt_im_bf[:, m * NF2 : (m + 1) * NF2].unsqueeze(2).broadcast_to([64, NF2, 64])
                sl = lambda tns: tns[:, jj * 64 : 8448 : EW].rearrange("p (f e) -> p f e", f=NF2) if False else tns.rearrange("p (f e) -> p f e", e=EW)[:, :, jj * 64 : (jj + 1) * 64]
                vre = sl(vfre)
                vim = sl(vfim)
                are = sl(agre)
                aim = sl(agim)
                g1 = gsc1[:].rearrange("p (f e) -> p f e", e=64)
                g2 = gsc2[:].rearrange("p (f e) -> p f e", e=64)
                eng.tensor_tensor(g1, gr, vre, ALU.mult)
                eng.tensor_tensor(g2, gi, vim, ALU.mult)
                eng.tensor_sub(are, g1, g2)
                eng.tensor_tensor(g1, gr, vim, ALU.mult)
                eng.tensor_tensor(g2, gi, vre, ALU.mult)
                eng.tensor_add(aim, g1, g2)
            eng.engine_nop().then_inc(b.sem("gmul"), 1)
        b.on("vector", gmul)

        if p5_stop == "gmul":
            continue

        for f2 in range(NF2):
            ia_mm_t = b.bump("iamm", 1)

            def iamm(eng, f2=f2, H=H, t=gm_t, ev=b.semv.get("iaev", 0), tv=b.semv.get("iatrev", 0)):
                eng.wait_ge(b.sem("gmul"), t)
                eng.wait_ge(b.sem("iaload"), 32)
                if not (H == 0 and f2 == 0):
                    eng.wait_ge(b.sem("iaev"), ev)
                eng.matmul(kps[:], invAre_sb[:, f2 * 128 : (f2 + 1) * 128], agre[:, f2 * EW : (f2 + 1) * EW], start=True, stop=False)
                eng.matmul(kps[:], invAim_sb[:, f2 * 128 : (f2 + 1) * 128], agim[:, f2 * EW : (f2 + 1) * EW], start=False, stop=True).then_inc(b.sem("iamm"), 1)
            b.on("tensor", iamm)

            iaev_t = b.bump("iaev", 1)

            def iaev(eng, f2=f2, t=ia_mm_t, tn=b.semv.get("iatr", 0)):
                eng.wait_ge(b.sem("iamm"), t)
                if tn:
                    eng.wait_ge(b.sem("iatr"), tn)
                eng.tensor_copy(uaev[:], kps[:]).then_inc(b.sem("iaev"), 1)
            b.on("vector", iaev)

            if p5_stop == "iamm1":
                continue
            iatr_t = b.bump("iatr", 1)

            def iatr(eng, f2=f2, t=iaev_t, en=b.semv.get("iatrev", 0)):
                eng.wait_ge(b.sem("iaev"), t)
                if en:
                    eng.wait_ge(b.sem("iatrev"), en)
                tb = [s1ps[0], s1ps[1], f2ps[0], f2ps[1]]
                i = None
                for c in range(2):
                    for h in range(2):
                        i = eng.matmul(tb[c * 2 + h][:, 0:64],
                                       uaev[c * 64 : (c + 1) * 64, h * 128 : (h + 1) * 128],
                                       identbf_sb[c * 64 : (c + 1) * 64, :], start=True, stop=True)
                i.then_inc(b.sem("iatr"), 1)
            b.on("tensor", iatr)

            iatrev_t = b.bump("iatrev", 1)

            def iatrev(eng, f2=f2, t=iatr_t, cn=b.semv.get("iac", 0)):
                eng.wait_ge(b.sem("iatr"), t)
                if cn:
                    eng.wait_ge(b.sem("iac"), cn)
                tb = [s1ps[0], s1ps[1], f2ps[0], f2ps[1]]
                for q in range(4):
                    i = eng.tensor_copy(uaT[:, q * 64 : (q + 1) * 64], tb[q][:, 0:64])
                i.then_inc(b.sem("iatrev"), 1)
            b.on("vector", iatrev)

            if p5_stop == "iamm2":
                continue
            iac_t = b.bump("iac", 16 * 4)

            def iac(eng, f2=f2, t=iatrev_t):
                eng.wait_ge(b.sem("iatrev"), t)
                for c in range(2):
                    for h in range(2):
                        src = uaT[:, (c * 2 + h) * 64 : (c * 2 + h) * 64 + 64]
                        dst = rbB[c * NF2 + f2 : c * NF2 + f2 + 1, h * 8192 : (h + 1) * 8192]
                        eng.dma_start(out=dst, in_=src).then_inc(b.sem("iac"), 16)
            b.on("sync" if f2 % 2 == 0 else "scalar", iac)

        if p5_stop in ("iamm", "iamm1", "iamm2"):
            continue
        iac_all = b.semv["iac"]
        if H == 0:
            def ibload(eng, t=vb_t):
                eng.wait_ge(b.sem("vboot"), t)
                eng.dma_start(out=invB_sb[:], in_=invB_d).then_inc(b.sem("ibload"), 16)
            b.bump("ibload", 16)
            b.on("sync", ibload)

        for ch in range(64):
            sb_t = b.bump("stB", 1)

            def stB(eng, ch=ch, t=iac_all, ev=b.semv.get("stBev", 0)):
                eng.wait_ge(b.sem("iac"), t)
                eng.wait_ge(b.sem("ibload"), 16)
                if not (ch == 0):
                    eng.wait_ge(b.sem("stBev"), ev)
                eng.matmul(s1ps[2][0:64, :], invB_sb[:], rbB[:, ch * EW : (ch + 1) * EW], start=True, stop=True).then_inc(b.sem("stB"), 1)
            b.on("tensor", stB)

            sbe_t = b.bump("stBev", 1)

            def stBev(eng, ch=ch, t=sb_t, cn=b.semv.get("aggc", 0)):
                eng.wait_ge(b.sem("stB"), t)
                if ch == 0 and cn:
                    eng.wait_ge(b.sem("aggc"), cn)
                eng.tensor_copy(aggt[:, ch * EW : (ch + 1) * EW], s1ps[2][0:64, :]).then_inc(b.sem("stBev"), 1)
            b.on("vector", stBev)

        agc_t = b.bump("aggc", 16 * 64)
        sbev_all = b.semv["stBev"]

        def aggc(eng, H=H, t=sbev_all):
            eng.wait_ge(b.sem("stBev"), t)
            for s2 in range(64):
                src = aggt[s2 : s2 + 1, :].rearrange("q (e s1) -> q e s1", s1=64)
                dst = aggfm_d[H * EW : (H + 1) * EW, 64 * s2 : 64 * s2 + 64]
                eng.dma_start(out=dst, in_=src).then_inc(b.sem("aggc"), 16)
        b.on("scalar", aggc)

    Wo_d = b.dram_in("Wo", [E, E])
    bo_d = b.dram_in("bo", [1, E])
    xtm_d = b.dram_in("x_tm", [T, E])
    WfcT_d = b.dram_in("WfcT", [E, 4096])
    WprojT_d = b.dram_in("WprojT", [4096, E])
    Afc_d = b.dram_in("Afc", [16, E])
    BfcT_d = b.dram_in("BfcT", [16, 4096])
    Aproj_d = b.dram_in("Aproj", [16, 4096])
    BprojT_d = b.dram_in("BprojT", [16, E])
    bfc_d = b.dram_in("bfc", [1, 4096])
    bproj_d = b.dram_in("bproj", [1, E])
    out_d = b.dram_out("out", [T, E])
    y1tm_d = b.nc.dram_tensor("y1tm", [T, E], BF16, kind="Internal").ap()
    y1fm_d = b.nc.dram_tensor("y1fm", [E, T], BF16, kind="Internal").ap()
    wefffc_d = b.nc.dram_tensor("wefffc", [E, 4096], BF16, kind="Internal").ap()
    weffpr_d = b.nc.dram_tensor("weffpr", [4096, E], BF16, kind="Internal").ap()

    wo_all = bfat("wo_all", [128, 8192], 5760)
    agg_all = [bfat(f"agg_all{j}", [128, 1024], 9856 + j * 512) for j in range(2)]
    x_tile = pool[:, 1536:2560]
    y1t = pool[:, 2560:3584]
    sqt = pool[:, 3584:4608]
    stat = pool[:, 4608:4640]
    y1bf = bfat("y1bf", [128, 1024], 4640)
    trsp = bfat("trsp", [128, 128], 5152)
    bo_bf = bfat("bo_bf", [1, 1024], 5224)
    afc_sb = pool[:, 5760:6784]
    aproj_sb = pool[:, 6784:10880]
    afc_bf = bfat("afc_bf", [16, 1024], 5248)
    aproj_bf = bfat("aproj_bf", [16, 4096], 26400)
    b2fc = bfat("b2fc", [16, 4096], 10880)
    b2pr = bfat("b2pr", [16, 1024], 12928)
    bfc_bf = bfat("bfc_bf", [1, 4096], 13440)
    bpr_bf = bfat("bpr_bf", [1, 1024], 15488)
    wchunk = pool[:, 16000:20096]
    wchunkI = pool[:, 16000:20096].bitcast(I32)[:, 0:4096]
    scrA = pool[:, 20096:24192]
    scrAI = pool[:, 20096:24192].bitcast(I32)[:, 0:4096]
    weffo = bfat("weffo", [128, 4096], 24192)
    qstat = b.sb("qstat", [128, 8])
    y1f_all = bfat("y1f_all", [128, 4096], 10880)
    gtile = bfat("gtile", [128, 16384], 16000)
    wfc_col = [bfat(f"wfc_col{j}", [128, 1024], 24192 + j * 512) for j in range(2)]
    wpr_buf = [bfat(f"wpr_buf{j}", [128, 1024], j * 512) for j in range(2)]
    y1res = bfat("y1res", [128, 1024], 4640)
    out_tile = pool[:, 1536:2560]
    y2t = pool[:, 2560:3584]
    sq2 = pool[:, 3584:4608]
    bo_sb_f = b.sb("bo_sb_f", [1, E])

    aggc_all = b.semv["aggc"]

    def p6boot(eng, t=aggc_all):
        eng.wait_ge(b.sem("aggc"), t)
        eng.dma_start(out=bo_sb_f[:], in_=bo_d).then_inc(b.sem("p6b"), 16)
    b.bump("p6b", 16)
    b.on("sync", p6boot)

    def p6wo(eng, t=aggc_all):
        eng.wait_ge(b.sem("aggc"), t)
        eng.dma_start(out=wo_all[:], in_=Wo_d.rearrange("(c p) o -> p c o", p=128)).then_inc(b.sem("p6w"), 16)
    b.bump("p6w", 16)
    b.on("gpsimd", p6wo)

    def p6boot2(eng):
        eng.wait_ge(b.sem("p6b"), 16)
        eng.tensor_copy(bo_bf[:], bo_sb_f[:]).then_inc(b.sem("p6b2"), 1)
    b.bump("p6b2", 1)
    b.on("vector", p6boot2)

    for tt in range(32):
        xl_t = b.bump("p6x", 16)

        def p6x(eng, tt=tt, pv=b.semv.get("ln1", 0)):
            if tt == 0:
                eng.wait_ge(b.sem("p6b"), 16)
            else:
                eng.wait_ge(b.sem("ln1"), pv)
            eng.dma_start(out=x_tile, in_=xtm_d[tt * 128 : (tt + 1) * 128, :]).then_inc(b.sem("p6x"), 16)
        b.on("sync", p6x)

        j = tt % 2
        al_t = b.bump(f"p6a{j}", 16)

        def p6a(eng, tt=tt, j=j, pv=b.semv.get("p6mm", 0)):
            if tt >= 2:
                eng.wait_ge(b.sem("p6mm"), pv - 8)
            src = aggfm_d.rearrange("(c p) t -> p c t", p=128)[:, :, tt * 128 : (tt + 1) * 128]
            eng.dma_start(out=agg_all[j][:], in_=src).then_inc(b.sem(f"p6a{j}"), 16)
        b.on("sync", p6a)

        for ec in range(8):
            mm_t = b.bump("p6mm", 1)

            def p6mm(eng, tt=tt, ec=ec, j=j, at=al_t, wt=16, ln=b.semv.get("ln1add", 0)):
                if ec == 0:
                    eng.wait_ge(b.sem(f"p6a{j}"), at)
                    eng.wait_ge(b.sem("p6w"), wt)
                    if tt > 0:
                        eng.wait_ge(b.sem("ln1add"), ln)
                    eng.wait_ge(b.sem("p6b2"), 1)
                    eng.matmul(qps_f[:], ones_bf[0:1, 0:128], bo_bf[0:1, 0:512], start=True, stop=False)
                    eng.matmul(kps_f[:], ones_bf[0:1, 0:128], bo_bf[0:1, 512:1024], start=True, stop=False)
                eng.matmul(qps_f[:], agg_all[j][:, ec * 128 : (ec + 1) * 128], wo_all[:, ec * 1024 : ec * 1024 + 512], start=False, stop=(ec == 7))
                eng.matmul(kps_f[:], agg_all[j][:, ec * 128 : (ec + 1) * 128], wo_all[:, ec * 1024 + 512 : (ec + 1) * 1024], start=False, stop=(ec == 7)).then_inc(b.sem("p6mm"), 1)
            b.on("tensor", p6mm)

        mma = b.semv["p6mm"]
        la_t = b.bump("ln1add", 1)

        def ln1a(eng, tt=tt, t=mma, xt=xl_t):
            eng.wait_ge(b.sem("p6mm"), t)
            eng.wait_ge(b.sem("p6x"), xt)
            eng.tensor_add(y1t[:, 0:512], x_tile[:, 0:512], qps_f[:])
            eng.tensor_add(y1t[:, 512:1024], x_tile[:, 512:1024], kps_f[:]).then_inc(b.sem("ln1add"), 1)
        b.on("vector", ln1a)

        ls_t = b.bump("ln1s", 1)

        def ln1s(eng, tt=tt, t=la_t):
            eng.wait_ge(b.sem("ln1add"), t)
            eng.tensor_tensor(sqt, y1t, y1t, ALU.mult)
            eng.reduce_sum(stat[:, 0:1], y1t, axis=AX.X)
            eng.reduce_sum(stat[:, 1:2], sqt, axis=AX.X)
            eng.tensor_scalar_mul(stat[:, 2:3], stat[:, 0:1], 1.0 / 1024.0)
            eng.tensor_scalar_mul(stat[:, 3:4], stat[:, 1:2], 1.0 / 1024.0)
            eng.tensor_tensor(stat[:, 4:5], stat[:, 2:3], stat[:, 2:3], ALU.mult)
            eng.tensor_sub(stat[:, 5:6], stat[:, 3:4], stat[:, 4:5])
            eng.tensor_scalar_add(stat[:, 6:7], stat[:, 5:6], 1e-5).then_inc(b.sem("ln1s"), 1)
        b.on("vector", ln1s)

        lq_t = b.bump("ln1q", 1)

        def ln1q(eng, tt=tt, t=ls_t):
            eng.wait_ge(b.sem("ln1s"), t)
            eng.activation(out=stat[:, 7:8], in_=stat[:, 6:7], func=AF.Sqrt, bias=0.0, scale=1.0).then_inc(b.sem("ln1q"), 1)
        b.on("scalar", ln1q)

        lr_t = b.bump("ln1r", 1)

        def ln1r(eng, tt=tt, t=lq_t):
            eng.wait_ge(b.sem("ln1q"), t)
            eng.reciprocal(stat[:, 0:1], stat[:, 7:8])
            eng.tensor_tensor(stat[:, 1:2], stat[:, 2:3], stat[:, 0:1], ALU.mult)
            eng.tensor_scalar_mul(stat[:, 1:2], stat[:, 1:2], -1.0).then_inc(b.sem("ln1r"), 1)
        b.on("vector", ln1r)

        ln_t = b.bump("ln1", 1)

        def ln1n(eng, tt=tt, t=lr_t, pv=b.semv.get("y1st", 0), pt=b.semv.get("y1tr", 0)):
            eng.wait_ge(b.sem("ln1r"), t)
            if pv:
                eng.wait_ge(b.sem("y1st"), pv)
            if pt:
                eng.wait_ge(b.sem("y1tr"), pt)
            eng.activation(out=y1bf[:], in_=y1t, func=AF.Identity, bias=stat[:, 1:2], scale=stat[:, 0:1]).then_inc(b.sem("ln1"), 1)
        b.on("scalar", ln1n)

        st_t = b.bump("y1st", 16)
        tr_t = b.bump("y1tr", 16 * 16)

        def y1st(eng, tt=tt, t=ln_t):
            eng.wait_ge(b.sem("ln1"), t)
            eng.dma_start(out=y1tm_d[tt * 128 : (tt + 1) * 128, :], in_=y1bf[:]).then_inc(b.sem("y1st"), 16)
        b.on("sync", y1st)

        def y1tr(eng, tt=tt, t=ln_t, base=tr_t - 16 * 16):
            eng.wait_ge(b.sem("ln1"), t)
            for ec in range(8):
                if ec > 0:
                    eng.wait_ge(b.sem("y1tr"), base + 16 * (2 * ec - 1))
                eng.dma_start_transpose(out=trsp[:], in_=y1bf[:, ec * 128 : (ec + 1) * 128]).then_inc(b.sem("y1tr"), 16)
                eng.wait_ge(b.sem("y1tr"), base + 16 * (2 * ec + 1))
                eng.dma_start(out=y1fm_d[ec * 128 : (ec + 1) * 128, tt * 128 : (tt + 1) * 128], in_=trsp[:]).then_inc(b.sem("y1tr"), 16)
        b.on("scalar", y1tr)

    y1_all = b.semv["y1st"]
    y1tr_all = b.semv["y1tr"]

    def q_boot(eng, t=y1_all):
        eng.wait_ge(b.sem("y1st"), t)
        eng.dma_start(out=afc_sb[0:16, :], in_=Afc_d).then_inc(b.sem("qb"), 16)
        eng.dma_start(out=aproj_sb[0:16, :], in_=Aproj_d).then_inc(b.sem("qb"), 16)
        eng.dma_start(out=wchunk[0:16, :], in_=BfcT_d).then_inc(b.sem("qb"), 16)
        eng.dma_start(out=scrA[0:16, 0:1024], in_=BprojT_d).then_inc(b.sem("qb"), 16)
        eng.dma_start(out=wchunk[32:33, :], in_=bfc_d).then_inc(b.sem("qb"), 16)
        eng.dma_start(out=scrA[0:1, 1024:2048], in_=bproj_d).then_inc(b.sem("qb"), 16)
    b.bump("qb", 96)
    b.on("sync", q_boot)

    def q_boot2(eng):
        eng.wait_ge(b.sem("qb"), 96)
        eng.tensor_scalar_mul(b2fc[0:16, :], wchunk[0:16, :], 2.0)
        eng.tensor_scalar_mul(b2pr[0:16, :], scrA[0:16, 0:1024], 2.0)
        eng.tensor_copy(afc_bf[0:16, :], afc_sb[0:16, :])
        eng.tensor_copy(aproj_bf[0:16, :], aproj_sb[0:16, :])
        eng.tensor_copy(bfc_bf[:], wchunk[32:33, :])
        eng.tensor_copy(bpr_bf[:], scrA[0:1, 1024:2048]).then_inc(b.sem("qb2"), 1)
    b.bump("qb2", 1)
    b.on("vector", q_boot2)

    for (name, src_d, nch, width) in (("fc", WfcT_d, 8, 4096), ("pr", WprojT_d, 32, 1024)):
        for c in range(nch):
            ql_t = b.bump("ql", 16)

            def ql(eng, c=c, src_d=src_d, width=width, pv=b.semv.get("qm", 0)):
                eng.wait_ge(b.sem("qb2"), 1)
                if pv:
                    eng.wait_ge(b.sem("qm"), pv)
                eng.dma_start(out=wchunk[:, 0:width], in_=src_d[c * 128 : (c + 1) * 128, :]).then_inc(b.sem("ql"), 16)
            b.on("sync", ql)

            qm_t = b.bump("qm", 1)

            def qm(eng, name=name, c=c, width=width, t=ql_t):
                eng.wait_ge(b.sem("ql"), t)
                col = 0 if name == "fc" else 1
                eng.tensor_reduce(stat[:, 2:3], wchunk[:, 0:width], axis=AX.X, op=ALU.max, apply_absolute_value=True)
                if c == 0:
                    eng.tensor_copy(qstat[:, col : col + 1], stat[:, 2:3]).then_inc(b.sem("qm"), 1)
                else:
                    eng.tensor_max(qstat[:, col : col + 1], qstat[:, col : col + 1], stat[:, 2:3]).then_inc(b.sem("qm"), 1)
            b.on("vector", qm)

    qm_all = b.semv["qm"]

    def qredmm(eng, t=qm_all):
        eng.wait_ge(b.sem("qm"), t)
        eng.matmul(f2ps_f[0][0:2, 0:128], qstat[:, 0:2], ident_sb[:], start=True, stop=True).then_inc(b.sem("qred"), 1)
    b.bump("qred", 1)
    b.on("tensor", qredmm)

    def qredv(eng):
        eng.wait_ge(b.sem("qred"), 1)
        eng.tensor_copy(sqt[0:2, 0:128], f2ps_f[0][0:2, 0:128])
        eng.reduce_max(stat[0:2, 0:1], sqt[0:2, 0:128], axis=AX.X).then_inc(b.sem("qred"), 1)
    b.bump("qred", 1)
    b.on("vector", qredv)

    def qredmm2(eng):
        eng.wait_ge(b.sem("qred"), 2)
        eng.matmul(f2ps_f[0][0:1, 0:2], stat[0:2, 0:1], ident_sb[0:2, 0:2], start=True, stop=True).then_inc(b.sem("qred"), 1)
    b.bump("qred", 1)
    b.on("tensor", qredmm2)

    def qredv2(eng):
        eng.wait_ge(b.sem("qred"), 3)
        eng.tensor_copy(stat[0:1, 2:4], f2ps_f[0][0:1, 0:2]).then_inc(b.sem("qred"), 1)
    b.bump("qred", 1)
    b.on("vector", qredv2)

    def qredmm3(eng):
        eng.wait_ge(b.sem("qred"), 4)
        eng.matmul(f2ps_f[0][:, 0:2], ones1[:], stat[0:1, 2:4], start=True, stop=True).then_inc(b.sem("qred"), 1)
    b.bump("qred", 1)
    b.on("tensor", qredmm3)

    def qredv3(eng):
        eng.wait_ge(b.sem("qred"), 5)
        eng.tensor_copy(qstat[:, 2:4], f2ps_f[0][:, 0:2]).then_inc(b.sem("qred"), 1)
    b.bump("qred", 1)
    b.on("vector", qredv3)

    def qrs(eng):
        eng.wait_ge(b.sem("qred"), 6)
        eng.reciprocal(qstat[:, 4:5], qstat[:, 2:3])
        eng.tensor_scalar_mul(qstat[:, 4:5], qstat[:, 4:5], 127.0)
        eng.reciprocal(qstat[:, 5:6], qstat[:, 3:4])
        eng.tensor_scalar_mul(qstat[:, 5:6], qstat[:, 5:6], 127.0)
        eng.tensor_scalar_mul(qstat[:, 6:7], qstat[:, 2:3], 1.0 / 127.0)
        eng.tensor_scalar_mul(qstat[:, 7:8], qstat[:, 3:4], 1.0 / 127.0)
        eng.tensor_scalar_mul(qstat[:, 0:1], qstat[:, 6:7], -1.0)
        eng.tensor_scalar_mul(qstat[:, 1:2], qstat[:, 7:8], -1.0).then_inc(b.sem("qrs"), 1)
    b.bump("qrs", 1)
    b.on("vector", qrs)

    for (name, src_d, dst_d, nch, width, A_sb, B_sb, rcol, ncol, amax_col) in (
        ("fc", WfcT_d, wefffc_d, 8, 4096, afc_bf, b2fc, 4, 0, 2),
        ("pr", WprojT_d, weffpr_d, 32, 1024, aproj_bf, b2pr, 5, 1, 3),
    ):
        for c in range(nch):
            ql2_t = b.bump("ql2", 16)

            def ql2(eng, c=c, src_d=src_d, width=width, pv=b.semv.get("qst", 0)):
                eng.wait_ge(b.sem("qrs"), 1)
                if pv:
                    eng.wait_ge(b.sem("qst"), pv)
                eng.dma_start(out=wchunk[:, 0:width], in_=src_d[c * 128 : (c + 1) * 128, :]).then_inc(b.sem("ql2"), 16)
            b.on("sync", ql2)

            nos = width // 1024
            for osi in range(nos):
                lo_t = b.bump("qlo", 1)

                def qlo(eng, name=name, c=c, osi=osi, A_sb=A_sb, B_sb=B_sb, t=ql2_t, pv=b.semv.get("qev", 0)):
                    eng.wait_ge(b.sem("ql2"), t)
                    if pv:
                        eng.wait_ge(b.sem("qev"), pv)
                    eng.matmul(qps_f[:], A_sb[0:16, c * 128 : (c + 1) * 128], B_sb[0:16, osi * 1024 : osi * 1024 + 512], start=True, stop=True)
                    eng.matmul(kps_f[:], A_sb[0:16, c * 128 : (c + 1) * 128], B_sb[0:16, osi * 1024 + 512 : (osi + 1) * 1024], start=True, stop=True).then_inc(b.sem("qlo"), 1)
                b.on("tensor", qlo)

                qa_t = b.bump("qact", 1)

                def qact(eng, name=name, osi=osi, rcol=rcol, amax_col=amax_col, ncol=ncol, t=lo_t, pq=b.semv.get("qint", 0)):
                    sl = slice(osi * 1024, (osi + 1) * 1024)
                    eng.wait_ge(b.sem("qlo"), t)
                    if pq:
                        eng.wait_ge(b.sem("qint"), pq)
                    eng.activation(out=scrA[:, sl], in_=wchunk[:, sl], func=AF.Identity, bias=16384.0, scale=qstat[:, rcol : rcol + 1]).then_inc(b.sem("qact"), 1)
                b.on("scalar", qact)

                qi_t = b.bump("qint", 1)

                def qint(eng, osi=osi, t=qa_t):
                    sl = slice(osi * 1024, (osi + 1) * 1024)
                    eng.wait_ge(b.sem("qact"), t)
                    eng.tensor_copy(wchunkI[:, sl], scrA[:, sl])
                    eng.tensor_copy(wchunk[:, sl], wchunkI[:, sl])
                    eng.tensor_sub(scrA[:, sl], scrA[:, sl], wchunk[:, sl])
                    eng.tensor_single_scalar(scrA[:, sl], scrA[:, sl], 0.5, ALU.is_ge)
                    eng.tensor_add(scrA[:, sl], scrA[:, sl], wchunk[:, sl]).then_inc(b.sem("qint"), 1)
                b.on("vector", qint)

                qc_t = b.bump("qchain", 1)

                def qchain(eng, osi=osi, ncol=ncol, amax_col=amax_col, t=qi_t):
                    sl = slice(osi * 1024, (osi + 1) * 1024)
                    eng.wait_ge(b.sem("qint"), t)
                    eng.activation(out=wchunk[:, sl], in_=scrA[:, sl], func=AF.Relu, bias=-16256.0, scale=1.0)
                    eng.activation(out=scrA[:, sl], in_=wchunk[:, sl], func=AF.Relu, bias=255.0, scale=-1.0)
                    eng.activation(out=wchunk[:, sl], in_=scrA[:, sl], func=AF.Identity, bias=qstat[:, amax_col : amax_col + 1], scale=qstat[:, ncol : ncol + 1]).then_inc(b.sem("qchain"), 1)
                b.on("scalar", qchain)

                qe_t = b.bump("qev", 1)

                def qev(eng, osi=osi, t=qc_t):
                    eng.wait_ge(b.sem("qchain"), t)
                    eng.tensor_add(weffo[:, osi * 1024 : osi * 1024 + 512], wchunk[:, osi * 1024 : osi * 1024 + 512], qps_f[:])
                    eng.tensor_add(weffo[:, osi * 1024 + 512 : (osi + 1) * 1024], wchunk[:, osi * 1024 + 512 : (osi + 1) * 1024], kps_f[:]).then_inc(b.sem("qev"), 1)
                b.on("vector", qev)

            qs_t = b.bump("qst", 16)
            qe_all = b.semv["qev"]

            def qst(eng, c=c, dst_d=dst_d, width=width, t=qe_all):
                eng.wait_ge(b.sem("qev"), t)
                eng.dma_start(out=dst_d[c * 128 : (c + 1) * 128, :], in_=weffo[:, 0:width]).then_inc(b.sem("qst"), 16)
            b.on("gpsimd", qst)

    y2ps = [(s1ps_f[0], s1ps_f[1]), (s1ps_f[2], s1ps_f[3]), (f2ps_f[0], f2ps_f[1]), (qps_f, kps_f)]
    for t5 in range(8):
        yl_t = b.bump("fy1", 16 * 8)

        def fy1(eng, t5=t5, pv=b.semv.get("fgel", 0), qt=b.semv.get("qst", 0), ytr=y1tr_all):
            eng.wait_ge(b.sem("qst"), qt)
            eng.wait_ge(b.sem("y1tr"), ytr)
            if t5 > 0:
                eng.wait_ge(b.sem("fgel"), pv)
            for ic in range(8):
                eng.dma_start(out=y1f_all[:, ic * 512 : (ic + 1) * 512], in_=y1fm_d[ic * 128 : (ic + 1) * 128, t5 * 512 : (t5 + 1) * 512]).then_inc(b.sem("fy1"), 16)
        b.on("sync", fy1)

        for oc in range(32):
            j = oc % 2
            wb_t = b.bump(f"fwb{j}", 16)

            def fwb(eng, t5=t5, oc=oc, j=j, pv=b.semv.get("fmm", 0), qt=b.semv.get("qst", 0)):
                if t5 * 32 + oc < 2:
                    eng.wait_ge(b.sem("qst"), qt)
                else:
                    eng.wait_ge(b.sem("fmm"), pv - 8)
                src = wefffc_d.rearrange("(c p) o -> p c o", p=128)[:, :, oc * 128 : (oc + 1) * 128]
                eng.dma_start(out=wfc_col[j][:], in_=src).then_inc(b.sem(f"fwb{j}"), 16)
            b.on("sync", fwb)

            for ic in range(8):
                fm_t = b.bump("fmm", 1)

                def fmm(eng, t5=t5, oc=oc, ic=ic, j=j, wt=wb_t, yt=yl_t, gl=b.semv.get("fgel", 0), la=b.semv.get("ln2add", 0)):
                    if ic == 0:
                        eng.wait_ge(b.sem(f"fwb{j}"), wt)
                        eng.wait_ge(b.sem("fy1"), yt)
                        if oc == 0 and t5 > 0 and la:
                            eng.wait_ge(b.sem("ln2add"), la)
                        if t5 + oc > 0:
                            eng.wait_ge(b.sem("fgel"), gl)
                        eng.matmul(qps_f[:], bfc_bf[0:1, oc * 128 : (oc + 1) * 128], ones_bf[0:1, 0:512], start=True, stop=False)
                    eng.matmul(qps_f[:], wfc_col[j][:, ic * 128 : (ic + 1) * 128], y1f_all[:, ic * 512 : (ic + 1) * 512], start=False, stop=(ic == 7)).then_inc(b.sem("fmm"), 1)
                b.on("tensor", fmm)

            fg_t = b.bump("fgel", 1)

            def fgel(eng, t5=t5, oc=oc, t=b.semv["fmm"]):
                eng.wait_ge(b.sem("fmm"), t)
                eng.activation(out=gtile[:, oc * 512 : (oc + 1) * 512], in_=qps_f[:], func=AF.Gelu, bias=0.0, scale=1.0).then_inc(b.sem("fgel"), 1)
            b.on("scalar", fgel)

        fg_all = b.semv["fgel"]
        for oc in range(32):
            j = oc % 2
            wp_t = b.bump(f"fwp{j}", 16)

            def fwp(eng, t5=t5, oc=oc, j=j, pv=b.semv.get("f2mm", 0)):
                if t5 + oc > 0:
                    eng.wait_ge(b.sem("f2mm"), max(0, pv - 1))
                eng.dma_start(out=wpr_buf[j][:], in_=weffpr_d[oc * 128 : (oc + 1) * 128, :]).then_inc(b.sem(f"fwp{j}"), 16)
            b.on("scalar", fwp)

            f2_t = b.bump("f2mm", 1)

            def f2mm(eng, t5=t5, oc=oc, j=j, wt=wp_t, ga=fg_all, ln=b.semv.get("ln2add", 0)):
                eng.wait_ge(b.sem(f"fwp{j}"), wt)
                eng.wait_ge(b.sem("fgel"), ga)
                if oc == 0 and t5 > 0:
                    eng.wait_ge(b.sem("ln2add"), ln)
                for ts4 in range(4):
                    pa, pb = y2ps[ts4]
                    if oc == 0:
                        eng.matmul(pa[:], ones_bf[0:1, 0:128], bpr_bf[0:1, 0:512], start=True, stop=False)
                        eng.matmul(pb[:], ones_bf[0:1, 0:128], bpr_bf[0:1, 512:1024], start=True, stop=False)
                    g_blk = gtile[:, oc * 512 + ts4 * 128 : oc * 512 + (ts4 + 1) * 128]
                    eng.matmul(pa[:], g_blk, wpr_buf[j][:, 0:512], start=False, stop=(oc == 31))
                    i = eng.matmul(pb[:], g_blk, wpr_buf[j][:, 512:1024], start=False, stop=(oc == 31))
                i.then_inc(b.sem("f2mm"), 1)
            b.on("tensor", f2mm)

        f2_all = b.semv["f2mm"]
        for ts4 in range(4):
            rl_t = b.bump("fyr", 16)

            def fyr(eng, t5=t5, ts4=ts4, pv=b.semv.get("ln2add", 0)):
                if t5 + ts4 > 0:
                    eng.wait_ge(b.sem("ln2add"), pv)
                r0 = t5 * 512 + ts4 * 128
                eng.dma_start(out=y1res[:], in_=y1tm_d[r0 : r0 + 128, :]).then_inc(b.sem("fyr"), 16)
            b.on("sync", fyr)

            l2a_t = b.bump("ln2add", 1)

            def ln2a(eng, t5=t5, ts4=ts4, t=f2_all, rt=rl_t):
                eng.wait_ge(b.sem("f2mm"), t)
                eng.wait_ge(b.sem("fyr"), rt)
                pa, pb = y2ps[ts4]
                eng.tensor_add(y2t[:, 0:512], y1res[:, 0:512], pa[:])
                eng.tensor_add(y2t[:, 512:1024], y1res[:, 512:1024], pb[:]).then_inc(b.sem("ln2add"), 1)
            b.on("vector", ln2a)

            l2s_t = b.bump("ln2s", 1)

            def ln2s(eng, t=l2a_t):
                eng.wait_ge(b.sem("ln2add"), t)
                eng.tensor_tensor(sq2, y2t, y2t, ALU.mult)
                eng.reduce_sum(stat[:, 0:1], y2t, axis=AX.X)
                eng.reduce_sum(stat[:, 1:2], sq2, axis=AX.X)
                eng.tensor_scalar_mul(stat[:, 2:3], stat[:, 0:1], 1.0 / 1024.0)
                eng.tensor_scalar_mul(stat[:, 3:4], stat[:, 1:2], 1.0 / 1024.0)
                eng.tensor_tensor(stat[:, 4:5], stat[:, 2:3], stat[:, 2:3], ALU.mult)
                eng.tensor_sub(stat[:, 5:6], stat[:, 3:4], stat[:, 4:5])
                eng.tensor_scalar_add(stat[:, 6:7], stat[:, 5:6], 1e-5).then_inc(b.sem("ln2s"), 1)
            b.on("vector", ln2s)

            l2q_t = b.bump("ln2q", 1)

            def ln2q(eng, t=l2s_t):
                eng.wait_ge(b.sem("ln2s"), t)
                eng.activation(out=stat[:, 7:8], in_=stat[:, 6:7], func=AF.Sqrt, bias=0.0, scale=1.0).then_inc(b.sem("ln2q"), 1)
            b.on("scalar", ln2q)

            l2r_t = b.bump("ln2r", 1)

            def ln2r(eng, t=l2q_t):
                eng.wait_ge(b.sem("ln2q"), t)
                eng.reciprocal(stat[:, 0:1], stat[:, 7:8])
                eng.tensor_tensor(stat[:, 1:2], stat[:, 2:3], stat[:, 0:1], ALU.mult)
                eng.tensor_scalar_mul(stat[:, 1:2], stat[:, 1:2], -1.0).then_inc(b.sem("ln2r"), 1)
            b.on("vector", ln2r)

            l2n_t = b.bump("ln2", 1)

            def ln2n(eng, t=l2r_t, ot=b.semv.get("outst", 0)):
                eng.wait_ge(b.sem("ln2r"), t)
                if ot:
                    eng.wait_ge(b.sem("outst"), ot)
                eng.activation(out=out_tile, in_=y2t, func=AF.Identity, bias=stat[:, 1:2], scale=stat[:, 0:1]).then_inc(b.sem("ln2"), 1)
            b.on("scalar", ln2n)

            os_t = b.bump("outst", 16)

            def outst(eng, t5=t5, ts4=ts4, t=l2n_t):
                eng.wait_ge(b.sem("ln2"), t)
                r0 = t5 * 512 + ts4 * 128
                eng.dma_start(out=out_d[r0 : r0 + 128, :], in_=out_tile).then_inc(b.sem("outst"), 16)
            b.on("sync", outst)

    out_all = b.semv["outst"]

    def fin(eng, t=out_all):
        eng.wait_ge(b.sem("outst"), t)
    b.on("gpsimd", fin)


    if upto == "agg" and p5_stop is not None:
        return b
    if upto == "agg":
        agg_out = b.dram_out("agg_dbg", [1024, 4096], BF16)
        bounce = pool[:, 0:4096].bitcast(BF16)[:, 0:4096]

        def aggout(eng, t=b.semv["aggc"]):
            eng.wait_ge(b.sem("aggc"), t)
            n = 0
            for c in range(8):
                eng.dma_start(out=bounce, in_=aggfm_d[c * 128 : (c + 1) * 128, :]).then_inc(b.sem("outd3"), 16)
                n += 16
                eng.wait_ge(b.sem("outd3"), n)
                eng.dma_start(out=agg_out[c * 128 : (c + 1) * 128, :], in_=bounce).then_inc(b.sem("outd3"), 16)
                n += 16
                eng.wait_ge(b.sem("outd3"), n)
        b.bump("outd3", 16 * 16)
        b.on("gpsimd", aggout)
        return b

    return b




_BUILD_CACHE = {}


def _get_build():
    if "b" not in _BUILD_CACHE:
        b = build_p123(upto="full", use_cc=True)
        nc = b.finalize()
        _BUILD_CACHE["b"] = b
        _BUILD_CACHE["nc"] = nc
    return _BUILD_CACHE["b"], _BUILD_CACHE["nc"]


def kernel(**inputs):
    from concourse.bass_utils import run_bass_kernel_spmd

    b, nc = _get_build()
    in_maps = []
    for core in range(8):
        m = host_prep_attn(inputs, core)
        m.update(b.const_arrays)
        in_maps.append(m)
    res = run_bass_kernel_spmd(nc, in_maps, core_ids=list(range(8)))
    out = np.stack([np.asarray(res.results[c]["out"]) for c in range(8)], axis=0)
    return out.astype(np.float32)


# version 10
# speedup vs baseline: 1.2499x; 1.0039x over previous
import numpy as np
import concourse.bass as bass
import concourse.mybir as mybir

F32 = mybir.dt.float32
BF16 = mybir.dt.bfloat16
AF = mybir.ActivationFunctionType
ALU = mybir.AluOpType
AX = mybir.AxisListType
U32 = mybir.dt.uint32
from concourse.bass_isa_rust import ReduceOp as _ReduceOp
from concourse.library_config import attn as _LIB_ATTN
_RED_MAX = _ReduceOp.max if hasattr(_ReduceOp, 'max') else None
I32 = mybir.dt.int32

T = 4096
E = 1024
NF2 = 33
EW = 256
NH = E // EW
PERM = np.array([2 * p + h for h in (0, 1) for p in range(32)])
DBG_NOACC = False


def make_consts():
    c = {}
    s2 = np.arange(64)[:, None]
    f2 = np.arange(64)[None, :]
    M1 = np.exp(-2j * np.pi * s2 * f2 / 64.0)
    z = np.zeros((64, 64))
    c["c1re_bd"] = np.block([[M1.real, z], [z, M1.real]]).astype(np.float32)
    c["c1im_bd"] = np.block([[M1.imag, z], [z, M1.imag]]).astype(np.float32)

    s1 = np.arange(64)
    f1 = np.arange(64)
    L2 = np.zeros((NF2, 128, 128), np.float32)
    for ff2 in range(NF2):
        ang = -2.0 * np.pi * (s1[:, None] * (ff2 + 64.0 * f1[None, :])) / T
        cr, ci = np.cos(ang), np.sin(ang)
        L2[ff2, 0:64, 0:64] = cr[PERM, :]
        L2[ff2, 0:64, 64:128] = ci[PERM, :]
        L2[ff2, 64:128, 0:64] = -ci[PERM, :]
        L2[ff2, 64:128, 64:128] = cr[PERM, :]
    c["stage2_L"] = L2

    inner = np.exp(+2j * np.pi * np.outer(f1, s1) / 64.0)
    WA = np.zeros((128, 128), np.float32)
    WA[0:64, 0:64] = inner.real
    WA[0:64, 64:128] = inner.imag
    WA[64:128, 0:64] = -inner.imag
    WA[64:128, 64:128] = inner.real
    c["macA_rhs"] = WA

    d = np.where((np.arange(NF2) >= 1) & (np.arange(NF2) <= 31), 2.0, 1.0)
    tw = np.exp(+2j * np.pi * np.outer(s1, np.arange(NF2)) / T).T
    twf = d[:, None] * tw / (T * E)
    c["mac_tw_re"] = np.concatenate([twf.real, twf.real], axis=1).astype(np.float32)
    c["mac_tw_im"] = np.concatenate([twf.imag, twf.imag], axis=1).astype(np.float32)

    ang = +2.0 * np.pi * np.outer(np.arange(NF2), s1) / 64.0
    c["macB_cos"] = np.cos(ang).astype(np.float32)
    c["macB_msin"] = (-np.sin(ang)).astype(np.float32)

    import ml_dtypes
    bf = lambda a: np.asarray(a, dtype=ml_dtypes.bfloat16)
    c["c1re_bf"] = bf(c["c1re_bd"])
    c["c1im_bf"] = bf(c["c1im_bd"])
    c["st2v_re"] = bf(c["stage2_L"][:, :, 0:64].reshape(NF2 * 128, 64))
    c["st2v_im"] = bf(c["stage2_L"][:, :, 64:128].reshape(NF2 * 128, 64))
    LA = np.zeros((NF2, 128, 128), np.float32)
    f1 = np.arange(64)
    for ff2 in range(NF2):
        dd = 2.0 if 1 <= ff2 <= 31 else 1.0
        inner = np.exp(+2j * np.pi * np.outer(f1, s1.ravel() if s1.ndim > 1 else s1) / 64.0)
        tw = np.exp(+2j * np.pi * np.arange(64) * ff2 / T)[None, :]
        Mfull = dd * (1.0 / T) * inner * tw
        LA[ff2, 0:64, 0:64] = Mfull.real
        LA[ff2, 0:64, 64:128] = Mfull.imag
        LA[ff2, 64:128, 0:64] = -Mfull.imag
        LA[ff2, 64:128, 64:128] = Mfull.real
    c["invA_re"] = bf(LA[:, 0:64, :].reshape(NF2 * 64, 128))
    c["invA_im"] = bf(LA[:, 64:128, :].reshape(NF2 * 64, 128))
    angB = +2.0 * np.pi * np.outer(np.arange(NF2), np.arange(64)) / 64.0
    LB = np.zeros((66, 64), np.float32)
    LB[0:NF2, :] = np.cos(angB)
    LB[NF2 : 2 * NF2, :] = -np.sin(angB)
    c["invB"] = bf(LB)
    fvals = (np.arange(64)[:, None] * 64 + np.arange(NF2)[None, :]).reshape(-1)
    c["f_rep"] = np.broadcast_to(fvals.astype(np.float32), (128, 64 * NF2)).copy()
    return c


def host_prep_attn(inputs, core):
    x = np.asarray(inputs["L0_attn_hidden_states"])[core]
    d = {}
    xt = np.ascontiguousarray(x.T).reshape(E, 64, 64)
    xp = np.empty((32, E, 128), np.float32)
    for p in range(32):
        xp[p, :, 0:64] = xt[:, :, 2 * p]
        xp[p, :, 64:128] = xt[:, :, 2 * p + 1]
    d["xpair"] = np.ascontiguousarray(
        xp.reshape(32, 8, 128, 128).transpose(0, 2, 1, 3).reshape(32, 128, E))
    d["Wq"] = np.ascontiguousarray(np.asarray(inputs["L0_attn_Wq"], np.float32))
    d["Wk"] = np.ascontiguousarray(np.asarray(inputs["L0_attn_Wk"], np.float32))
    d["bq"] = np.ascontiguousarray(np.asarray(inputs["L0_attn_bq"], np.float32))[None, :]
    d["bk"] = np.ascontiguousarray(np.asarray(inputs["L0_attn_bk"], np.float32))[None, :]
    d["Wv"] = np.ascontiguousarray(np.asarray(inputs["L0_attn_Wv"], np.float32))
    d["bv"] = np.ascontiguousarray(np.asarray(inputs["L0_attn_bv"], np.float32))[None, :]
    d["Wo"] = np.ascontiguousarray(np.asarray(inputs["L0_attn_Wo"], np.float32))
    d["bo"] = np.ascontiguousarray(np.asarray(inputs["L0_attn_bo"], np.float32))[None, :]
    d["x_tm"] = np.ascontiguousarray(x.astype(np.float32))
    d["WfcT"] = np.ascontiguousarray(np.asarray(inputs["L0_ffn_W_fc"], np.float32).T)
    d["WprojT"] = np.ascontiguousarray(np.asarray(inputs["L0_ffn_W_proj"], np.float32).T)
    d["Afc"] = np.ascontiguousarray(np.asarray(inputs["L0_ffn_A_fc"], np.float32))
    d["BfcT"] = np.ascontiguousarray(np.asarray(inputs["L0_ffn_B_fc"], np.float32).T)
    d["Aproj"] = np.ascontiguousarray(np.asarray(inputs["L0_ffn_A_proj"], np.float32))
    d["BprojT"] = np.ascontiguousarray(np.asarray(inputs["L0_ffn_B_proj"], np.float32).T)
    d["bfc"] = np.ascontiguousarray(np.asarray(inputs["L0_ffn_b_fc"], np.float32))[None, :]
    d["bproj"] = np.ascontiguousarray(np.asarray(inputs["L0_ffn_b_proj"], np.float32))[None, :]
    return d


class B:

    def __init__(self):
        self.nc = bass.Bass("TRN2", target_bir_lowering=False)
        self.cmds = {k: [] for k in ["sync", "tensor", "vector", "scalar", "gpsimd"]}
        self.semv = {}
        self.sems = {}
        self.const_arrays = {}

    def sem(self, name):
        if name not in self.sems:
            self.sems[name] = self.nc.semaphore(name).__enter__()
            self.semv[name] = 0
        return self.sems[name]

    def bump(self, name, n):
        self.sem(name)
        self.semv[name] += n
        return self.semv[name]

    def on(self, eng, fn):
        self.cmds[eng].append(fn)

    def dram_in(self, name, shape, dtype=F32):
        return self.nc.dram_tensor(name, list(shape), dtype, kind="ExternalInput").ap()

    def dram_out(self, name, shape, dtype=F32):
        return self.nc.dram_tensor(name, list(shape), dtype, kind="ExternalOutput").ap()

    def sb(self, name, shape, dtype=F32):
        return self.nc.alloc_sbuf_tensor(name, list(shape), dtype)

    def ps(self, name, shape, dtype=F32):
        return self.nc.alloc_psum_tensor(name, list(shape), dtype)

    def const_in(self, name, arr):
        arr = np.ascontiguousarray(arr)
        import ml_dtypes
        dt_ = F32 if arr.dtype == np.float32 else (BF16 if arr.dtype == ml_dtypes.bfloat16 else None)
        assert dt_ is not None, arr.dtype
        ap = self.dram_in(name, arr.shape, dt_)
        self.const_arrays[name] = arr
        return ap

    class _AutoSer:
        _SKIP = {"wait_ge", "dma_start", "dma_start_transpose", "then_inc",
                 "sem_inc", "nop", "drain"}

        class _Proxy:
            def __init__(self, ser, ins):
                self._ser = ser
                self._ins = ins

            def then_inc(self, sem, n):
                self._ins.then_inc(sem, n)
                c = self._ser._ucount.get(id(sem), 0) + n
                self._ser._ucount[id(sem)] = c
                self._ser._token = (sem, c)
                self._ser._last = None
                return self._ins

            def __getattr__(self, name):
                return getattr(self._ins, name)

        def __init__(self, eng, sem):
            self._eng = eng
            self._sem = sem
            self._count = 0
            self._token = None
            self._last = None
            self._ucount = {}

        def _flush(self):
            if self._last is not None:
                self._last.then_inc(self._sem, 1)
                self._count += 1
                self._token = (self._sem, self._count)
                self._last = None

        def __getattr__(self, name):
            attr = getattr(self._eng, name)
            if name.startswith("_") or name in self._SKIP or not callable(attr):
                return attr

            def call(*a, **k):
                self._flush()
                if self._token is not None:
                    self._eng.wait_ge(self._token[0], self._token[1])
                    self._token = None
                ins = attr(*a, **k)
                if hasattr(ins, "then_inc"):
                    self._last = ins
                    return B._AutoSer._Proxy(self, ins)
                return ins

            return call

    def finalize(self):
        nc = self.nc
        with nc.Block() as block:
            @block.sync
            def _(eng):
                for fn in self.cmds["sync"]:
                    fn(eng)

            @block.tensor
            def _(eng):
                for fn in self.cmds["tensor"]:
                    fn(eng)

            @block.vector
            def _(eng):
                w = B._AutoSer(eng, self.sem("autoV"))
                for fn in self.cmds["vector"]:
                    fn(w)

            @block.scalar
            def _(eng):
                w = B._AutoSer(eng, self.sem("autoS"))
                for fn in self.cmds["scalar"]:
                    fn(w)

            @block.gpsimd
            def _(eng):
                w = B._AutoSer(eng, self.sem("autoG"))
                for fn in self.cmds["gpsimd"]:
                    fn(w)
        return nc


def build_p123(dump=(), dump_slice=0, upto="mac", p5_stop=None, use_cc=False):
    b = B()
    C = make_consts()
    dump = set(dump)

    xpair = b.dram_in("xpair", [32, 128, E])
    Wq_d = b.dram_in("Wq", [E, E])
    Wk_d = b.dram_in("Wk", [E, E])
    bq_d = b.dram_in("bq", [1, E])
    bk_d = b.dram_in("bk", [1, E])
    Wv_d = b.dram_in("Wv", [E, E])
    bv_d = b.dram_in("bv", [1, E])
    c1re_d = b.const_in("c1re_bd", C["c1re_bd"])
    c1im_d = b.const_in("c1im_bd", C["c1im_bd"])
    st2_d = b.const_in("stage2_L", C["stage2_L"].reshape(NF2 * 128, 128))
    macA_d = b.const_in("macA_rhs", C["macA_rhs"])
    mtwre_d = b.const_in("mac_tw_re", C["mac_tw_re"])
    mtwim_d = b.const_in("mac_tw_im", C["mac_tw_im"])
    macBc_d = b.const_in("macB_cos", C["macB_cos"])
    macBs_d = b.const_in("macB_msin", C["macB_msin"])
    c1rebf_d = b.const_in("c1re_bf", C["c1re_bf"])
    c1imbf_d = b.const_in("c1im_bf", C["c1im_bf"])
    st2vre_d = b.const_in("st2v_re", C["st2v_re"])
    st2vim_d = b.const_in("st2v_im", C["st2v_im"])
    invAre_d = b.const_in("invA_re", C["invA_re"])
    invAim_d = b.const_in("invA_im", C["invA_im"])
    invB_d = b.const_in("invB", C["invB"])
    mac_out = b.dram_out("mac", [64, 64])
    cvals = np.broadcast_to(np.array([-np.pi, 2.0 * np.pi / 4096.0, 0.0, 1.0,
                                      -1.0, 16512.5, -16384.0, 255.0,
                                      16384.0, -16256.0], np.float32), (128, 10)).copy()
    ident_d = b.const_in("ident128", np.eye(128, dtype=np.float32))
    ident_sb = b.sb("ident_sb", [128, 128])
    import ml_dtypes as _mld
    identbf_d = b.const_in("ident64bf", np.asarray(np.tile(np.eye(64), (2, 1)), dtype=_mld.bfloat16))
    identbf_sb = b.sb("identbf_sb", [128, 64], BF16)
    cvals_d = b.const_in("cvals", cvals)
    csb = b.sb("csb", [128, 10])
    if upto in ("topk",):
        g_out = b.dram_out("g", [2, 64, 8 * NF2])

    dbg = {}
    if "rbq" in dump:
        dbg["rbq"] = b.dram_out("d_rbq", [128, NF2 * EW])
    if "qf" in dump:
        dbg["qf"] = b.dram_out("d_qf", [2, 128, EW])
    if "ppack" in dump:
        dbg["ppack"] = b.dram_out("d_ppack", [128, NF2])
    if "qbuf" in dump:
        dbg["qbuf"] = b.dram_out("d_qbuf", [128, EW])
    if "s1buf" in dump:
        dbg["s1buf"] = b.dram_out("d_s1buf", [2, 128, EW])
    if "pparts" in dump:
        dbg["pparts"] = b.dram_out("d_pparts", [64, NF2 * 4])

    xp_sb = [b.sb(f"xp{j}", [128, E]) for j in range(2)]
    wq_sb = b.sb("wq_sb", [128, 8 * EW])
    wk_sb = b.sb("wk_sb", [128, 8 * EW])
    bias_sb = b.sb("bias_sb", [1, 2 * E])
    ones1 = b.sb("ones1", [1, 128])
    qbuf = [b.sb(f"qbuf{j}", [128, EW]) for j in range(2)]
    kbuf = [b.sb(f"kbuf{j}", [128, EW]) for j in range(2)]
    s1buf = [[b.sb(f"s1buf{j}_{c}", [128, EW]) for c in range(4)] for j in range(2)]
    c1re_sb = b.sb("c1re_sb", [128, 128])
    c1im_sb = b.sb("c1im_sb", [128, 128])
    b._pool_base = 229344 - b.nc.sbuf_bytes_remaining
    pool = b.sb("pool", [128, 28800])

    def bfat(name, shape, off_f32):
        return b.nc.alloc_sbuf_tensor_at(name, list(shape), BF16, offset=b._pool_base + off_f32 * 4)

    rbq = pool[:, 0:8448]
    rbk = pool[:, 8448:16896]
    st2_sb = pool[:, 16896:21120]
    pparts = b.sb("pparts", [64, NF2 * 4])
    ppack = b.sb("ppack", [128, NF2])
    macA_sb = b.sb("macA_sb", [128, 128])
    mtw_sb = b.sb("mtw_sb", [NF2, 256])
    u_sb = b.sb("u_sb", [NF2, 128])
    up_sb = b.sb("up_sb", [NF2, 128])
    macB_sb = b.sb("macB_sb", [NF2, 128])
    mac_sb = b.sb("mac_sb", [64, 64])

    qps_f = b.ps("qps", [128, 512])
    kps_f = b.ps("kps", [128, 512])
    s1ps_f = [b.ps(f"s1ps{c}", [128, 512]) for c in range(4)]
    f2ps_f = [b.ps(f"f2ps{c}", [128, 512]) for c in range(2)]
    qps = qps_f[:, 0:EW]
    kps = kps_f[:, 0:EW]
    s1ps = [t[:, 0:EW] for t in s1ps_f]
    f2ps = [t[:, 0:EW] for t in f2ps_f]

    def p0(eng):
        for dst, src in [
            (c1re_sb[:], c1re_d), (c1im_sb[:], c1im_d),
            (st2_sb[:], st2_d.rearrange("(f p) m -> p f m", p=128)),
            (bias_sb[0:1, 0:E], bq_d), (bias_sb[0:1, E : 2 * E], bk_d),
            (macA_sb[:], macA_d), (mtw_sb[:, 0:128], mtwre_d),
            (mtw_sb[:, 128:256], mtwim_d), (macB_sb[:, 0:64], macBc_d),
            (macB_sb[:, 64:128], macBs_d),
            (csb[:], cvals_d),
            (ident_sb[:], ident_d),
            (identbf_sb[:], identbf_d),
        ]:
            eng.dma_start(out=dst, in_=src).then_inc(b.sem("boot"), 16)
    b.bump("boot", 16 * 13)
    b.on("sync", p0)
    xpair_bf_d = b.nc.dram_tensor("xpair_bf", [32, 128, E], BF16, kind="Internal").ap()
    for _i, _v in enumerate([-np.pi, 2.0 * np.pi / 4096.0, 0.0, 1.0, -1.0, 16512.5, -16384.0, 255.0, 16384.0, -16256.0]):
        b.nc.const_aps.register(F32, _v, csb[:, _i : _i + 1])
    boot_t = b.semv["boot"]

    def p0v(eng):
        eng.memset(ones1[:], 1.0).then_inc(b.sem("ones"), 1)
    b.bump("ones", 1)
    b.on("vector", p0v)

    for H in range(NH):
        wload_t = b.bump("wload", 32)

        def p1w(eng, H=H, wt=b.semv["proj"] if "proj" in b.semv else 0):
            if H >= 1:
                eng.wait_ge(b.sem("proj"), wt)
            src_q = Wq_d.rearrange("(c p) o -> p c o", p=128)[:, :, H * EW : (H + 1) * EW]
            src_k = Wk_d.rearrange("(c p) o -> p c o", p=128)[:, :, H * EW : (H + 1) * EW]
            eng.dma_start(out=wq_sb[:], in_=src_q).then_inc(b.sem("wload"), 16)
            eng.dma_start(out=wk_sb[:], in_=src_k).then_inc(b.sem("wload"), 16)
        b.on("sync", p1w)

        for p in range(32):
            j = p % 2
            need_proj = b.semv.get("proj", 0)
            lsem = f"load{j}"
            load_t = b.bump(lsem, 16)

            def l(eng, p=p, j=j, H=H, need=need_proj, lsem=lsem):
                if p >= 2 or H >= 1:
                    eng.wait_ge(b.sem("proj"), need)
                eng.dma_start(out=xp_sb[j][:], in_=xpair[p]).then_inc(b.sem(lsem), 16)
            b.on("sync", l)

            qevac_need = b.semv.get("qevac", 0)
            proj_t = b.bump("proj", 1)

            def mm(eng, p=p, j=j, H=H, lt=load_t, wt=wload_t, qn=qevac_need, lsem=lsem):
                eng.wait_ge(b.sem(lsem), lt)
                if p == 0:
                    eng.wait_ge(b.sem("boot"), boot_t)
                    eng.wait_ge(b.sem("ones"), 1)
                eng.wait_ge(b.sem("wload"), wt)
                if p >= 1 or H >= 1:
                    eng.wait_ge(b.sem("qevac"), qn)
                eng.matmul(qps[:], ones1[:], bias_sb[0:1, H * EW : (H + 1) * EW], start=True, stop=False)
                eng.matmul(kps[:], ones1[:], bias_sb[0:1, E + H * EW : E + (H + 1) * EW], start=True, stop=False)
                i = None
                for c in range(8):
                    lhsT = xp_sb[j][:, c * 128 : (c + 1) * 128]
                    eng.matmul(qps[:], lhsT, wq_sb[:, c * EW : (c + 1) * EW], start=False, stop=(c == 7))
                    i = eng.matmul(kps[:], lhsT, wk_sb[:, c * EW : (c + 1) * EW], start=False, stop=(c == 7))
                i.then_inc(b.sem("proj"), 1)
            b.on("tensor", mm)

            s1_need = b.semv.get("s1mm", 0)
            qevac_t = b.bump("qevac", 1)

            def ev(eng, p=p, j=j, H=H, pt=proj_t, sn=s1_need):
                eng.wait_ge(b.sem("proj"), pt)
                if p >= 2 or H >= 1:
                    eng.wait_ge(b.sem("s1mm"), sn)
                eng.tensor_copy(qbuf[j][:], qps[:])
                eng.tensor_copy(kbuf[j][:], kps[:]).then_inc(b.sem("qevac"), 1)
            b.on("vector", ev)

            s1evac_need = b.semv.get("s1evac", 0)
            s1_t = b.bump("s1mm", 1)

            def s1(eng, p=p, j=j, H=H, qt=qevac_t, en=s1evac_need):
                eng.wait_ge(b.sem("qevac"), qt)
                if p >= 1 or H >= 1:
                    eng.wait_ge(b.sem("s1evac"), en)
                eng.matmul(s1ps[0][:], c1re_sb[:], qbuf[j][:], start=True, stop=True)
                eng.matmul(s1ps[1][:], c1im_sb[:], qbuf[j][:], start=True, stop=True)
                eng.matmul(s1ps[2][:], c1re_sb[:], kbuf[j][:], start=True, stop=True)
                eng.matmul(s1ps[3][:], c1im_sb[:], kbuf[j][:], start=True, stop=True).then_inc(b.sem("s1mm"), 1)
            b.on("tensor", s1)

            corner_need = b.semv.get("corner", 0)
            s1evac_t = b.bump("s1evac", 1)

            def ev1(eng, p=p, j=j, H=H, st=s1_t, cn=corner_need):
                eng.wait_ge(b.sem("s1mm"), st)
                if p >= 2 or H >= 1:
                    eng.wait_ge(b.sem("corner"), cn)
                for c in range(4):
                    i = eng.tensor_copy(s1buf[j][c][:], s1ps[c][:])
                i.then_inc(b.sem("s1evac"), 1)
            b.on("vector", ev1)

            if "qbuf" in dump and H == dump_slice and p == 5:
                def dqb(eng, j=j, t=b.semv["qevac"]):
                    eng.wait_ge(b.sem("qevac"), t)
                    eng.dma_start(out=dbg["qbuf"], in_=qbuf[j][:]).then_inc(b.sem("dbg"), 16)
                b.bump("dbg", 16)
                b.on("gpsimd", dqb)

            if "s1buf" in dump and H == dump_slice and p == 5:
                def dsb(eng, j=j, t=b.semv["s1evac"]):
                    eng.wait_ge(b.sem("s1evac"), t)
                    eng.dma_start(out=dbg["s1buf"][0], in_=s1buf[j][0][:]).then_inc(b.sem("dbg"), 16)
                    eng.dma_start(out=dbg["s1buf"][1], in_=s1buf[j][1][:]).then_inc(b.sem("dbg"), 16)
                b.bump("dbg", 32)
                b.on("gpsimd", dsb)

            corner_t = b.bump("corner", 16 * 8)

            def corner(eng, p=p, j=j, H=H, et=s1evac_t):
                eng.wait_ge(b.sem("s1evac"), et)
                for (c, rb, base) in ((0, rbq, 0), (1, rbq, 64), (2, rbk, 0), (3, rbk, 64)):
                    for h in (0, 1):
                        src = s1buf[j][c][h * 64 : h * 64 + NF2, :]
                        r = base + h * 32 + p
                        eng.dma_start(out=rb[r : r + 1, :], in_=src).then_inc(b.sem("corner"), 16)
            b.on("gpsimd", corner)

            if H < 2 and p % 2 == 1:
                q = H * 16 + p // 2
                b.bump("xcast", 16)

                def xck(eng, q=q):
                    eng.dma_start(out=xpair_bf_d[q], in_=xpair[q]).then_inc(b.sem("xcast"), 16)
                b.on("gpsimd", xck)

        corner_all = b.semv["corner"]
        if "rbq" in dump and H == dump_slice:
            def drb(eng, ca=corner_all):
                eng.wait_ge(b.sem("corner"), ca)
                eng.dma_start(out=dbg["rbq"], in_=rbq[:]).then_inc(b.sem("dbg"), 16)
            b.bump("dbg", 16)
            b.on("gpsimd", drb)

        for f2 in range(NF2):
            pprod_need = b.semv.get("pprod", 0)
            st2_t = b.bump("st2mm", 1)

            def st2(eng, f2=f2, H=H, ca=corner_all, pn=pprod_need):
                eng.wait_ge(b.sem("corner"), ca)
                if not (H == 0 and f2 == 0):
                    eng.wait_ge(b.sem("pprod"), pn)
                L = st2_sb[:, f2 * 128 : (f2 + 1) * 128]
                eng.matmul(f2ps[0][:], L, rbq[:, f2 * EW : (f2 + 1) * EW], start=True, stop=True)
                eng.matmul(f2ps[1][:], L, rbk[:, f2 * EW : (f2 + 1) * EW], start=True, stop=True).then_inc(b.sem("st2mm"), 1)
            b.on("tensor", st2)

            if "qf" in dump and H == 0 and f2 == 0:
                def dqf(eng, st=st2_t):
                    eng.wait_ge(b.sem("st2mm"), st)
                    eng.tensor_copy(u_sb[0:NF2, :] if False else qbuf[0][:], f2ps[0][:])
                    eng.tensor_copy(kbuf[0][:], f2ps[1][:]).then_inc(b.sem("dbgv"), 1)
                b.bump("dbgv", 1)
                b.on("vector", dqf)

                def dqf2(eng, t=b.semv["dbgv"]):
                    eng.wait_ge(b.sem("dbgv"), t)
                    eng.dma_start(out=dbg["qf"][0], in_=qbuf[0][:]).then_inc(b.sem("dbg"), 16)
                    eng.dma_start(out=dbg["qf"][1], in_=kbuf[0][:]).then_inc(b.sem("dbg"), 16)
                b.bump("dbg", 32)
                b.on("gpsimd", dqf2)

            pprod_t = b.bump("pprod", 1)

            def pp(eng, f2=f2, H=H, st=st2_t):
                eng.wait_ge(b.sem("st2mm"), st)
                eng.tensor_copy(qf_sb[:], f2ps[0][:])
                qre = qf_sb[0:64, :]
                qim = qf_sb[64:128, :]
                kre = f2ps[1][0:64, :]
                kim = f2ps[1][64:128, :]
                col = lambda k: pparts[0:64, f2 * 4 + k : f2 * 4 + k + 1]
                pairs = ((qre, kre), (qim, kim), (qim, kre), (qre, kim))
                if H == 0 or DBG_NOACC:
                    for k, (a, bb) in enumerate(pairs):
                        eng.tensor_tensor(ttr_scr[:], a, bb, ALU.mult)
                        i = eng.reduce_sum(col(k), ttr_scr[:], axis=AX.X)
                    i.then_inc(b.sem("pprod"), 1)
                else:
                    for k, (a, bb) in enumerate(pairs):
                        eng.tensor_tensor(ttr_scr[:], a, bb, ALU.mult)
                        eng.reduce_sum(pscr[0:64, k : k + 1], ttr_scr[:], axis=AX.X)
                    for k in range(4):
                        i = eng.tensor_add(col(k), col(k), pscr[0:64, k : k + 1])
                    i.then_inc(b.sem("pprod"), 1)
            b.on("vector", pp)

        if "pparts" in dump and H == dump_slice:
            def dppp(eng, t=b.semv["pprod"]):
                eng.wait_ge(b.sem("pprod"), t)
                eng.dma_start(out=dbg["pparts"], in_=pparts[:]).then_inc(b.sem("dbg"), 16)
            b.bump("dbg", 16)
            b.on("gpsimd", dppp)

    pscr = b.sb("pscr", [64, 4])
    ttr_scr = b.sb("ttr_scr", [64, EW])
    qf_sb = b.sb("qf_sb", [128, EW])

    pprod_all = b.semv["pprod"]

    def pcomb(eng, pa=pprod_all):
        eng.wait_ge(b.sem("pprod"), pa)
        v = pparts[0:64, :].rearrange("p (f k) -> p f k", k=4)
        eng.tensor_add(ppack[0:64, :], v[:, :, 0], v[:, :, 1])
        eng.tensor_sub(ppack[64:128, :], v[:, :, 2], v[:, :, 3]).then_inc(b.sem("pcomb"), 1)
    b.bump("pcomb", 1)
    b.on("vector", pcomb)
    if "ppack" in dump:
        def dpp(eng, t=b.semv["pcomb"]):
            eng.wait_ge(b.sem("pcomb"), t)
            eng.dma_start(out=dbg["ppack"], in_=ppack[:]).then_inc(b.sem("dbg"), 16)
        b.bump("dbg", 16)
        b.on("gpsimd", dpp)

    uaps = f2ps[0][0:NF2, 0:128]
    macps = f2ps[1][0:64, 0:64]

    def macA(eng, t=b.semv["pcomb"]):
        eng.wait_ge(b.sem("pcomb"), t)
        eng.matmul(uaps, ppack[:], macA_sb[:], start=True, stop=True).then_inc(b.sem("macA"), 1)
    b.bump("macA", 1)
    b.on("tensor", macA)

    def mactw(eng, t=b.semv["macA"]):
        eng.wait_ge(b.sem("macA"), t)
        eng.tensor_copy(u_sb[:], uaps)
        ure, uim = u_sb[:, 0:64], u_sb[:, 64:128]
        twre_r, twre_i = mtw_sb[:, 0:64], mtw_sb[:, 128:192]
        eng.tensor_tensor(up_sb[:, 0:64], ure, twre_r, ALU.mult)
        eng.tensor_tensor(pscr2[:, 0:64], uim, twre_i, ALU.mult)
        eng.tensor_sub(up_sb[:, 0:64], up_sb[:, 0:64], pscr2[:, 0:64])
        eng.tensor_tensor(up_sb[:, 64:128], ure, twre_i, ALU.mult)
        eng.tensor_tensor(pscr2[:, 64:128], uim, twre_r, ALU.mult)
        eng.tensor_add(up_sb[:, 64:128], up_sb[:, 64:128], pscr2[:, 64:128]).then_inc(b.sem("mactw"), 1)
    pscr2 = b.sb("pscr2", [NF2, 128])
    b.bump("mactw", 1)
    b.on("vector", mactw)

    def macB(eng, t=b.semv["mactw"]):
        eng.wait_ge(b.sem("mactw"), t)
        eng.matmul(macps, macB_sb[:, 0:64], up_sb[:, 0:64], start=True, stop=False)
        eng.matmul(macps, macB_sb[:, 64:128], up_sb[:, 64:128], start=False, stop=True).then_inc(b.sem("macB"), 1)
    b.bump("macB", 1)
    b.on("tensor", macB)

    def macev(eng, t=b.semv["macB"]):
        eng.wait_ge(b.sem("macB"), t)
        eng.tensor_copy(mac_sb[:], macps).then_inc(b.sem("macev"), 1)
    b.bump("macev", 1)
    b.on("vector", macev)

    b.bump("outd", 16)
    dbg_total = b.semv.get("dbg", 0)

    def macout2(eng, t=b.semv["macev"], dt=dbg_total):
        eng.wait_ge(b.sem("macev"), t)
        eng.dma_start(out=mac_out, in_=mac_sb[:]).then_inc(b.sem("outd"), 16)
        eng.wait_ge(b.sem("outd"), 16)
        if dt:
            eng.wait_ge(b.sem("dbg"), dt)
    b.on("gpsimd", macout2)

    if upto == "mac":
        return b

    mac_all = pool[0:8, 0:4096]
    workA = pool[0:8, 4096:8192]
    workB = pool[0:8, 8192:12288]
    wm = pool[0:8, 12288:16384]
    wmp = pool[:, 25344:25600]
    rbw = pool[:, 25600:25864]
    m8 = b.sb("m8", [8, 8])
    mscr = pool[0:8, 21120:25216]
    s1w_ev = b.sb("s1w_ev", [128, 16])
    gt_re = b.sb("gt_re", [64, 8 * NF2])
    gt_im = b.sb("gt_im", [64, 8 * NF2])

    if use_cc:
        cc_in = b.nc.dram_tensor("cc_in", [1, 4096], F32)
        cc_out = b.nc.dram_tensor("cc_out", [8, 4096], F32)
        nrep_t = b.bump("macrep", 16 * 2)
        b.bump("ccd", 1)

        def macrep(eng, t=b.semv["macev"]):
            eng.wait_ge(b.sem("macev"), t)
            eng.dma_start(out=cc_in.ap().rearrange("q (a c) -> (q a) c", a=64), in_=mac_sb[:]).then_inc(b.sem("macrep"), 16)
            eng.wait_ge(b.sem("macrep"), 16)
            eng.collective_compute(
                "AllGather", ALU.bypass,
                replica_groups=[list(range(8))],
                ins=[cc_in.ap().opt()], outs=[cc_out.ap().opt()],
            ).then_inc(b.sem("ccd"), 1)
            eng.wait_ge(b.sem("ccd"), 1)
            eng.dma_start(out=mac_all, in_=cc_out.ap()).then_inc(b.sem("macrep"), 16)
        b.on("gpsimd", macrep)
    else:
        nrep_t = b.bump("macrep", 16 * 8)

        def macrep(eng, t=b.semv["macev"]):
            eng.wait_ge(b.sem("macev"), t)
            for bb in range(8):
                eng.dma_start(out=mac_all[bb : bb + 1, :], in_=mac_sb[:]).then_inc(b.sem("macrep"), 16)
        b.on("gpsimd", macrep)

    _tks0 = [b.semv.get("tks", 0)]
    b.bump("tks", 9)
    tk_t = b.bump("topk", 1)

    def topk(eng, t=nrep_t):
        eng.wait_ge(b.sem("macrep"), t)
        wtmp = wm
        bufs = [mac_all, workA, workB, wtmp]
        for r in range(3):
            src, dst = bufs[r], bufs[r + 1]
            eng.max(out=m8[:], in_=src).then_inc(b.sem("tks"), 1)
            eng.wait_ge(b.sem("tks"), b.bump("tks", 0) if False else 3 * r + 1 + _tks0[0])
            eng.tensor_single_scalar(mscr, src, m8[:, 7:8], ALU.is_ge)
            eng.scalar_tensor_tensor(dst, mscr, -1e30, src, ALU.mult, ALU.add).then_inc(b.sem("tks"), 2)
            eng.wait_ge(b.sem("tks"), 3 * r + 3 + _tks0[0])
        eng.tensor_single_scalar(mscr, bufs[3], -1e29, ALU.is_le)
        eng.tensor_mul(wm, mscr, mac_all).then_inc(b.sem("topk"), 1)
    b.on("vector", topk)

    for p in range(32):
        wtr_t = b.bump("wtr", 1)

        def wtr(eng, p=p, t=tk_t, ev=b.semv.get("wtrev", 0)):
            eng.wait_ge(b.sem("topk"), t)
            if p > 0:
                eng.wait_ge(b.sem("wtrev"), ev)
            wmv = wm.rearrange("q (s2 s1) -> q s1 s2", s1=64)
            eng.matmul(s1ps[3][0:64, 0:8], wmv[:, 2 * p, :], ident_sb[0:8, 0:8], start=True, stop=True)
            eng.matmul(s1ps[3][64:128, 0:8], wmv[:, 2 * p + 1, :], ident_sb[0:8, 0:8], start=True, stop=True).then_inc(b.sem("wtr"), 1)
        b.on("tensor", wtr)

        wtrev_t = b.bump("wtrev", 1)

        def wtrev(eng, p=p, t=wtr_t):
            eng.wait_ge(b.sem("wtr"), t)
            eng.tensor_copy(wmp[:, p * 8 : (p + 1) * 8], s1ps[3][:, 0:8]).then_inc(b.sem("wtrev"), 1)
        b.on("vector", wtrev)
    wml_t = b.semv["wtrev"]

    for p in range(32):
        ws1_t = b.bump("ws1", 1)

        def ws1(eng, p=p, t=wml_t, ev=b.semv.get("ws1ev", 0)):
            eng.wait_ge(b.sem("wtrev"), t)
            if p > 0:
                eng.wait_ge(b.sem("ws1ev"), ev)
            eng.matmul(s1ps[0][:, 0:8], c1re_sb[:], wmp[:, p * 8 : (p + 1) * 8], start=True, stop=True)
            eng.matmul(s1ps[1][:, 0:8], c1im_sb[:], wmp[:, p * 8 : (p + 1) * 8], start=True, stop=True).then_inc(b.sem("ws1"), 1)
        b.on("tensor", ws1)

        ws1ev_t = b.bump("ws1ev", 1)

        def ws1ev(eng, p=p, t=ws1_t, cn=b.semv.get("wcorner", 0)):
            eng.wait_ge(b.sem("ws1"), t)
            if p > 0:
                eng.wait_ge(b.sem("wcorner"), cn)
            eng.tensor_copy(s1w_ev[:, 0:8], s1ps[0][:, 0:8])
            eng.tensor_copy(s1w_ev[:, 8:16], s1ps[1][:, 0:8]).then_inc(b.sem("ws1ev"), 1)
        b.on("vector", ws1ev)

        wc_t = b.bump("wcorner", 16 * 4)

        def wcorner(eng, p=p, t=ws1ev_t):
            eng.wait_ge(b.sem("ws1ev"), t)
            for (c0, base) in ((0, 0), (8, 64)):
                for h in (0, 1):
                    src = s1w_ev[h * 64 : h * 64 + NF2, c0 : c0 + 8]
                    r = base + h * 32 + p
                    eng.dma_start(out=rbw[r : r + 1, :].rearrange("q (f m) -> q f m", f=NF2), in_=src).then_inc(b.sem("wcorner"), 16)
        b.on("gpsimd", wcorner)

    wall = b.semv["wcorner"]
    for f2 in range(NF2):
        wg_t = b.bump("wg", 1)

        def wg(eng, f2=f2, t=wall, ev=b.semv.get("wgev", 0)):
            eng.wait_ge(b.sem("wcorner"), t)
            if f2 > 0:
                eng.wait_ge(b.sem("wgev"), ev)
            eng.matmul(s1ps[2][:, 0:8], st2_sb[:, f2 * 128 : (f2 + 1) * 128], rbw[:, f2 * 8 : (f2 + 1) * 8], start=True, stop=True).then_inc(b.sem("wg"), 1)
        b.on("tensor", wg)

        wgev_t = b.bump("wgev", 1)

        def wgev(eng, f2=f2, t=wg_t):
            eng.wait_ge(b.sem("wg"), t)
            dst_re = gt_re[:, f2::NF2]
            dst_im = gt_im[:, f2::NF2]
            eng.tensor_copy(dst_re, s1ps[2][0:64, 0:8])
            eng.tensor_scalar_mul(dst_im, s1ps[2][64:128, 0:8], -1.0).then_inc(b.sem("wgev"), 1)
        b.on("vector", wgev)

    if upto == "topk":
        dbg_total2 = b.semv.get("dbg", 0)
        wm_out = b.dram_out("wm", [8, 4096])
        wk_out = b.dram_out("wk_dbg", [8, 4096])

        def tkout(eng, t=b.semv["wgev"], dt=dbg_total2):
            eng.wait_ge(b.sem("wgev"), t)
            eng.dma_start(out=wk_out, in_=workA).then_inc(b.sem("outd2"), 16)
            eng.dma_start(out=wm_out, in_=wm).then_inc(b.sem("outd2"), 16)
            eng.dma_start(out=g_out[0], in_=gt_re[:]).then_inc(b.sem("outd2"), 16)
            eng.dma_start(out=g_out[1], in_=gt_im[:]).then_inc(b.sem("outd2"), 16)
            eng.wait_ge(b.sem("outd2"), b.semv["outd2"])
            if dt:
                eng.wait_ge(b.sem("dbg"), dt)
        b.bump("outd2", 64)
        b.on("gpsimd", tkout)
        return b

    aggfm_d = b.nc.dram_tensor("aggfm", [1024, 4096], BF16, kind="Internal").ap()
    rbv = bfat("rbv", [128, 8448], 0)
    vfre = bfat("vfre", [64, 8448], 4224)
    vfim = bfat("vfim", [64, 8448], 8448)
    agre = bfat("agre", [64, 8448], 12672)
    agim = bfat("agim", [64, 8448], 16896)
    gsc1 = bfat("gsc1", [64, 2112], 21120)
    gsc2 = bfat("gsc2", [64, 2112], 22176)
    rbB = bfat("rbB", [66, 16384], 4224)
    aggt = bfat("aggt", [64, 16384], 12672)
    invAre_sb = b.sb("invAre_sb", [64, 4224], BF16)
    invAim_sb = b.sb("invAim_sb", [64, 4224], BF16)
    st2vre_sb = bfat("st2vre_sb", [128, 2112], 23232)
    st2vim_sb = bfat("st2vim_sb", [128, 2112], 24288)
    gt_re_bf = bfat("gt_re_bf", [64, 264], 25864)
    gt_im_bf = bfat("gt_im_bf", [64, 264], 26000)

    xpv = [b.sb(f"xpv{j}", [128, E], BF16) for j in range(2)]
    vbuf = [b.sb(f"vbuf{j}", [128, EW], BF16) for j in range(2)]
    vs1ev = [b.sb(f"vs1ev{j}", [128, 2 * EW], BF16) for j in range(2)]
    wv_sb = b.sb("wv_sb", [128, 8 * EW], BF16)
    ones_bf = b.sb("ones_bf", [1, 512], BF16)
    bv_sb = b.sb("bv_sb", [1, E])
    bv_bf = b.sb("bv_bf", [1, E], BF16)
    c1rebf_sb = b.sb("c1rebf_sb", [128, 128], BF16)
    c1imbf_sb = b.sb("c1imbf_sb", [128, 128], BF16)
    invB_sb = b.sb("invB_sb", [66, 64], BF16)
    uaev = b.sb("uaev", [128, EW], BF16)
    uaT = b.sb("uaT", [128, EW], BF16)

    vb_t = b.bump("vboot", 16 * 5)

    def vboot(eng, t=b.semv["topk"]):
        eng.wait_ge(b.sem("topk"), t)
        eng.dma_start(out=st2vre_sb[:], in_=st2vre_d.rearrange("(f p) m -> p f m", p=128)).then_inc(b.sem("vboot"), 16)
        eng.dma_start(out=st2vim_sb[:], in_=st2vim_d.rearrange("(f p) m -> p f m", p=128)).then_inc(b.sem("vboot"), 16)
        eng.dma_start(out=c1rebf_sb[:], in_=c1rebf_d).then_inc(b.sem("vboot"), 16)
        eng.dma_start(out=c1imbf_sb[:], in_=c1imbf_d).then_inc(b.sem("vboot"), 16)
        eng.dma_start(out=bv_sb[:], in_=bv_d).then_inc(b.sem("vboot"), 16)
        eng.dma_start(out=invAre_sb[:], in_=invAre_d.rearrange("(f p) m -> p f m", p=64)).then_inc(b.sem("iaload"), 16)
        eng.dma_start(out=invAim_sb[:], in_=invAim_d.rearrange("(f p) m -> p f m", p=64)).then_inc(b.sem("iaload"), 16)
    b.bump("iaload", 32)
    b.on("sync", vboot)

    def vboot2(eng, t=vb_t):
        eng.wait_ge(b.sem("vboot"), t)
        eng.memset(ones_bf[:], 1.0)
        eng.tensor_copy(bv_bf[:], bv_sb[:]).then_inc(b.sem("vboot2"), 1)
    vb2_t = b.bump("vboot2", 1)
    b.on("vector", vboot2)

    def gtbf(eng, t=b.semv["wgev"], t2=vb2_t):
        eng.wait_ge(b.sem("wgev"), t)
        eng.wait_ge(b.sem("vboot2"), t2 - 1 if False else t2)
        eng.tensor_copy(gt_re_bf[:], gt_re[:])
        eng.tensor_copy(gt_im_bf[:], gt_im[:]).then_inc(b.sem("gtbf"), 1)
    gtbf_t = b.bump("gtbf", 1)
    b.on("vector", gtbf)

    for H in range(NH):
        wvl_t = b.bump("wvl", 16)

        def wvl(eng, H=H, t=vb_t, pv=b.semv.get("vproj", 0)):
            eng.wait_ge(b.sem("vboot"), t)
            if H > 0:
                eng.wait_ge(b.sem("vproj"), pv)
            src = Wv_d.rearrange("(c p) o -> p c o", p=128)[:, :, H * EW : (H + 1) * EW]
            eng.dma_start(out=wv_sb[:], in_=src).then_inc(b.sem("wvl"), 16)
        b.on("gpsimd", wvl)

        for p in range(32):
            j = p % 2
            vlsem = f"vload{j}"
            vload_t = b.bump(vlsem, 16)

            def vload(eng, p=p, j=j, H=H, pv=b.semv.get("vproj", 0), vlsem=vlsem):
                if H == 0:
                    eng.wait_ge(b.sem("xcast"), 16 * (p + 1))
                if p >= 2 or H >= 1:
                    eng.wait_ge(b.sem("vproj"), pv)
                eng.dma_start(out=xpv[j][:], in_=xpair_bf_d[p]).then_inc(b.sem(vlsem), 16)
            b.on("sync" if p % 2 == 0 else "scalar", vload)

            vproj_t = b.bump("vproj", 1)

            def vproj(eng, p=p, j=j, H=H, lt=vload_t, wt=wvl_t, ev=b.semv.get("vevac", 0), vlsem=vlsem):
                eng.wait_ge(b.sem(vlsem), lt)
                eng.wait_ge(b.sem("wvl"), wt)
                if p == 0 and H == 0:
                    eng.wait_ge(b.sem("vboot2"), 1)
                if (p >= 2 or H >= 1) and ev >= 1:
                    eng.wait_ge(b.sem("vevac"), ev - 1)
                vp = [qps, kps][p % 2]
                eng.matmul(vp[:], ones_bf[0:1, 0:128], bv_bf[0:1, H * EW : (H + 1) * EW], start=True, stop=False)
                i = None
                for c in range(8):
                    i = eng.matmul(vp[:], xpv[j][:, c * 128 : (c + 1) * 128], wv_sb[:, c * EW : (c + 1) * EW], start=False, stop=(c == 7))
                i.then_inc(b.sem("vproj"), 1)
            b.on("tensor", vproj)

            vevac_t = b.bump("vevac", 1)

            def vevac(eng, p=p, j=j, t=vproj_t, sn=b.semv.get("vs1", 0)):
                eng.wait_ge(b.sem("vproj"), t)
                if p >= 2 or H >= 1:
                    eng.wait_ge(b.sem("vs1"), sn)
                eng.tensor_copy(vbuf[j][:], [qps, kps][p % 2][:]).then_inc(b.sem("vevac"), 1)
            b.on("vector", vevac)

            vs1_t = b.bump("vs1", 1)

            def vs1(eng, p=p, j=j, t=vevac_t, en=b.semv.get("vs1ev", 0)):
                eng.wait_ge(b.sem("vevac"), t)
                if (p >= 2 or H >= 1) and en >= 1:
                    eng.wait_ge(b.sem("vs1ev"), en - 1)
                pr = s1ps[0 + 2 * (p % 2)]
                pi = s1ps[1 + 2 * (p % 2)]
                eng.matmul(pr[:], c1rebf_sb[:], vbuf[j][:], start=True, stop=True)
                eng.matmul(pi[:], c1imbf_sb[:], vbuf[j][:], start=True, stop=True).then_inc(b.sem("vs1"), 1)
            b.on("tensor", vs1)

            vs1ev_t = b.bump("vs1ev", 1)

            def vs1e(eng, p=p, j=j, t=vs1_t, cn=b.semv.get("vcorner", 0)):
                eng.wait_ge(b.sem("vs1"), t)
                if p >= 2 or H >= 1:
                    eng.wait_ge(b.sem("vcorner"), cn)
                pr = s1ps[0 + 2 * (p % 2)]
                pi = s1ps[1 + 2 * (p % 2)]
                eng.tensor_copy(vs1ev[j][:, 0:EW], pr[:])
                eng.tensor_copy(vs1ev[j][:, EW : 2 * EW], pi[:]).then_inc(b.sem("vs1ev"), 1)
            b.on("vector", vs1e)

            vc_t = b.bump("vcorner", 16 * 4)

            def vcorner(eng, p=p, j=j, t=vs1ev_t):
                eng.wait_ge(b.sem("vs1ev"), t)
                for (c0, base) in ((0, 0), (EW, 64)):
                    for h in (0, 1):
                        src = vs1ev[j][h * 64 : h * 64 + NF2, c0 : c0 + EW]
                        r = base + h * 32 + p
                        eng.dma_start(out=rbv[r : r + 1, :], in_=src).then_inc(b.sem("vcorner"), 16)
            b.on("sync" if p % 2 == 0 else "scalar", vcorner)

        if p5_stop == "pairs":
            continue
        vc_all = b.semv["vcorner"]
        for f2 in range(NF2):
            vst2_t = b.bump("vst2", 1)

            def vst2(eng, f2=f2, t=vc_all, ev=b.semv.get("vfev", 0)):
                eng.wait_ge(b.sem("vcorner"), t)
                if not (f2 == 0):
                    eng.wait_ge(b.sem("vfev"), ev)
                eng.matmul(f2ps[0][0:64, :], st2vre_sb[:, f2 * 64 : (f2 + 1) * 64], rbv[:, f2 * EW : (f2 + 1) * EW], start=True, stop=True)
                eng.matmul(f2ps[1][0:64, :], st2vim_sb[:, f2 * 64 : (f2 + 1) * 64], rbv[:, f2 * EW : (f2 + 1) * EW], start=True, stop=True).then_inc(b.sem("vst2"), 1)
            b.on("tensor", vst2)

            vfev_t = b.bump("vfev", 1)

            def vfev(eng, f2=f2, t=vst2_t):
                eng.wait_ge(b.sem("vst2"), t)
                eng.tensor_copy(vfre[:, f2 * EW : (f2 + 1) * EW], f2ps[0][0:64, :])
                eng.tensor_copy(vfim[:, f2 * EW : (f2 + 1) * EW], f2ps[1][0:64, :]).then_inc(b.sem("vfev"), 1)
            b.on("vector", vfev)

        if p5_stop == "vst2":
            continue
        vf_all = b.semv["vfev"]
        gm_t = b.bump("gmul", 1)

        def gmul(eng, H=H, t=vf_all, gt=gtbf_t, ac=b.semv.get("aggc", 0)):
            eng.wait_ge(b.sem("vfev"), t)
            eng.wait_ge(b.sem("gtbf"), gt)
            if ac:
                eng.wait_ge(b.sem("aggc"), ac)
            for jj in range(4):
                m = (H * 4 + jj) % 8
                gr = gt_re_bf[:, m * NF2 : (m + 1) * NF2].unsqueeze(2).broadcast_to([64, NF2, 64])
                gi = gt_im_bf[:, m * NF2 : (m + 1) * NF2].unsqueeze(2).broadcast_to([64, NF2, 64])
                sl = lambda tns: tns[:, jj * 64 : 8448 : EW].rearrange("p (f e) -> p f e", f=NF2) if False else tns.rearrange("p (f e) -> p f e", e=EW)[:, :, jj * 64 : (jj + 1) * 64]
                vre = sl(vfre)
                vim = sl(vfim)
                are = sl(agre)
                aim = sl(agim)
                g1 = gsc1[:].rearrange("p (f e) -> p f e", e=64)
                g2 = gsc2[:].rearrange("p (f e) -> p f e", e=64)
                eng.tensor_tensor(g1, gr, vre, ALU.mult)
                eng.tensor_tensor(g2, gi, vim, ALU.mult)
                eng.tensor_sub(are, g1, g2)
                eng.tensor_tensor(g1, gr, vim, ALU.mult)
                eng.tensor_tensor(g2, gi, vre, ALU.mult)
                eng.tensor_add(aim, g1, g2)
            eng.engine_nop().then_inc(b.sem("gmul"), 1)
        b.on("vector", gmul)

        if p5_stop == "gmul":
            continue

        for f2 in range(NF2):
            ia_mm_t = b.bump("iamm", 1)

            def iamm(eng, f2=f2, H=H, t=gm_t, ev=b.semv.get("iaev", 0), tv=b.semv.get("iatrev", 0)):
                eng.wait_ge(b.sem("gmul"), t)
                eng.wait_ge(b.sem("iaload"), 32)
                if not (H == 0 and f2 == 0):
                    eng.wait_ge(b.sem("iaev"), ev)
                eng.matmul(kps[:], invAre_sb[:, f2 * 128 : (f2 + 1) * 128], agre[:, f2 * EW : (f2 + 1) * EW], start=True, stop=False)
                eng.matmul(kps[:], invAim_sb[:, f2 * 128 : (f2 + 1) * 128], agim[:, f2 * EW : (f2 + 1) * EW], start=False, stop=True).then_inc(b.sem("iamm"), 1)
            b.on("tensor", iamm)

            iaev_t = b.bump("iaev", 1)

            def iaev(eng, f2=f2, t=ia_mm_t, tn=b.semv.get("iatr", 0)):
                eng.wait_ge(b.sem("iamm"), t)
                if tn:
                    eng.wait_ge(b.sem("iatr"), tn)
                eng.tensor_copy(uaev[:], kps[:]).then_inc(b.sem("iaev"), 1)
            b.on("vector", iaev)

            if p5_stop == "iamm1":
                continue
            iatr_t = b.bump("iatr", 1)

            def iatr(eng, f2=f2, t=iaev_t, en=b.semv.get("iatrev", 0)):
                eng.wait_ge(b.sem("iaev"), t)
                if en:
                    eng.wait_ge(b.sem("iatrev"), en)
                tb = [s1ps[0], s1ps[1], f2ps[0], f2ps[1]]
                i = None
                for c in range(2):
                    for h in range(2):
                        i = eng.matmul(tb[c * 2 + h][:, 0:64],
                                       uaev[c * 64 : (c + 1) * 64, h * 128 : (h + 1) * 128],
                                       identbf_sb[c * 64 : (c + 1) * 64, :], start=True, stop=True)
                i.then_inc(b.sem("iatr"), 1)
            b.on("tensor", iatr)

            iatrev_t = b.bump("iatrev", 1)

            def iatrev(eng, f2=f2, t=iatr_t, cn=b.semv.get("iac", 0)):
                eng.wait_ge(b.sem("iatr"), t)
                if cn:
                    eng.wait_ge(b.sem("iac"), cn)
                tb = [s1ps[0], s1ps[1], f2ps[0], f2ps[1]]
                for q in range(4):
                    i = eng.tensor_copy(uaT[:, q * 64 : (q + 1) * 64], tb[q][:, 0:64])
                i.then_inc(b.sem("iatrev"), 1)
            b.on("vector", iatrev)

            if p5_stop == "iamm2":
                continue
            iac_t = b.bump("iac", 16 * 4)

            def iac(eng, f2=f2, t=iatrev_t):
                eng.wait_ge(b.sem("iatrev"), t)
                for c in range(2):
                    for h in range(2):
                        src = uaT[:, (c * 2 + h) * 64 : (c * 2 + h) * 64 + 64]
                        dst = rbB[c * NF2 + f2 : c * NF2 + f2 + 1, h * 8192 : (h + 1) * 8192]
                        eng.dma_start(out=dst, in_=src).then_inc(b.sem("iac"), 16)
            b.on("sync" if f2 % 2 == 0 else "scalar", iac)

        if p5_stop in ("iamm", "iamm1", "iamm2"):
            continue
        iac_all = b.semv["iac"]
        if H == 0:
            def ibload(eng, t=vb_t):
                eng.wait_ge(b.sem("vboot"), t)
                eng.dma_start(out=invB_sb[:], in_=invB_d).then_inc(b.sem("ibload"), 16)
            b.bump("ibload", 16)
            b.on("sync", ibload)

        for ch in range(64):
            sb_t = b.bump("stB", 1)

            def stB(eng, ch=ch, t=iac_all, ev=b.semv.get("stBev", 0)):
                eng.wait_ge(b.sem("iac"), t)
                eng.wait_ge(b.sem("ibload"), 16)
                if not (ch == 0):
                    eng.wait_ge(b.sem("stBev"), ev)
                eng.matmul(s1ps[2][0:64, :], invB_sb[:], rbB[:, ch * EW : (ch + 1) * EW], start=True, stop=True).then_inc(b.sem("stB"), 1)
            b.on("tensor", stB)

            sbe_t = b.bump("stBev", 1)

            def stBev(eng, ch=ch, t=sb_t, cn=b.semv.get("aggc", 0)):
                eng.wait_ge(b.sem("stB"), t)
                if ch == 0 and cn:
                    eng.wait_ge(b.sem("aggc"), cn)
                eng.tensor_copy(aggt[:, ch * EW : (ch + 1) * EW], s1ps[2][0:64, :]).then_inc(b.sem("stBev"), 1)
            b.on("vector", stBev)

        agc_t = b.bump("aggc", 16 * 64)
        sbev_all = b.semv["stBev"]

        def aggc(eng, H=H, t=sbev_all):
            eng.wait_ge(b.sem("stBev"), t)
            for s2 in range(64):
                src = aggt[s2 : s2 + 1, :].rearrange("q (e s1) -> q e s1", s1=64)
                dst = aggfm_d[H * EW : (H + 1) * EW, 64 * s2 : 64 * s2 + 64]
                eng.dma_start(out=dst, in_=src).then_inc(b.sem("aggc"), 16)
        b.on("scalar", aggc)

    Wo_d = b.dram_in("Wo", [E, E])
    bo_d = b.dram_in("bo", [1, E])
    xtm_d = b.dram_in("x_tm", [T, E])
    WfcT_d = b.dram_in("WfcT", [E, 4096])
    WprojT_d = b.dram_in("WprojT", [4096, E])
    Afc_d = b.dram_in("Afc", [16, E])
    BfcT_d = b.dram_in("BfcT", [16, 4096])
    Aproj_d = b.dram_in("Aproj", [16, 4096])
    BprojT_d = b.dram_in("BprojT", [16, E])
    bfc_d = b.dram_in("bfc", [1, 4096])
    bproj_d = b.dram_in("bproj", [1, E])
    out_d = b.dram_out("out", [T, E])
    y1tm_d = b.nc.dram_tensor("y1tm", [T, E], BF16, kind="Internal").ap()
    y1fm_d = b.nc.dram_tensor("y1fm", [E, T], BF16, kind="Internal").ap()
    wefffc_d = b.nc.dram_tensor("wefffc", [E, 4096], BF16, kind="Internal").ap()
    weffpr_d = b.nc.dram_tensor("weffpr", [4096, E], BF16, kind="Internal").ap()

    wo_all = bfat("wo_all", [128, 8192], 5760)
    agg_all = [bfat(f"agg_all{j}", [128, 1024], 9856 + j * 512) for j in range(2)]
    x_tile = pool[:, 1536:2560]
    y1t = pool[:, 2560:3584]
    sqt = pool[:, 3584:4608]
    stat = pool[:, 4608:4640]
    y1bf = bfat("y1bf", [128, 1024], 4640)
    trsp = bfat("trsp", [128, 128], 5152)
    bo_bf = bfat("bo_bf", [1, 1024], 5224)
    afc_sb = pool[:, 5760:6784]
    aproj_sb = pool[:, 6784:10880]
    afc_bf = bfat("afc_bf", [16, 1024], 5248)
    aproj_bf = bfat("aproj_bf", [16, 4096], 26400)
    b2fc = bfat("b2fc", [16, 4096], 10880)
    b2pr = bfat("b2pr", [16, 1024], 12928)
    bfc_bf = bfat("bfc_bf", [1, 4096], 13440)
    bpr_bf = bfat("bpr_bf", [1, 1024], 15488)
    wchunk = pool[:, 16000:20096]
    wchunkI = pool[:, 16000:20096].bitcast(I32)[:, 0:4096]
    scrA = pool[:, 20096:24192]
    scrAI = pool[:, 20096:24192].bitcast(I32)[:, 0:4096]
    weffo = bfat("weffo", [128, 4096], 24192)
    qstat = b.sb("qstat", [128, 8])
    y1f_all = bfat("y1f_all", [128, 4096], 10880)
    gtile = bfat("gtile", [128, 16384], 16000)
    wfc_col = [bfat(f"wfc_col{j}", [128, 1024], 24192 + j * 512) for j in range(2)]
    wpr_buf = [bfat(f"wpr_buf{j}", [128, 1024], j * 512) for j in range(2)]
    y1res = bfat("y1res", [128, 1024], 4640)
    out_tile = pool[:, 1536:2560]
    y2t = pool[:, 2560:3584]
    sq2 = pool[:, 3584:4608]
    bo_sb_f = b.sb("bo_sb_f", [1, E])

    aggc_all = b.semv["aggc"]

    def p6boot(eng, t=aggc_all):
        eng.wait_ge(b.sem("aggc"), t)
        eng.dma_start(out=bo_sb_f[:], in_=bo_d).then_inc(b.sem("p6b"), 16)
    b.bump("p6b", 16)
    b.on("sync", p6boot)

    def p6wo(eng, t=aggc_all):
        eng.wait_ge(b.sem("aggc"), t)
        eng.dma_start(out=wo_all[:], in_=Wo_d.rearrange("(c p) o -> p c o", p=128)).then_inc(b.sem("p6w"), 16)
    b.bump("p6w", 16)
    b.on("gpsimd", p6wo)

    def p6boot2(eng):
        eng.wait_ge(b.sem("p6b"), 16)
        eng.tensor_copy(bo_bf[:], bo_sb_f[:]).then_inc(b.sem("p6b2"), 1)
    b.bump("p6b2", 1)
    b.on("vector", p6boot2)

    for tt in range(32):
        xl_t = b.bump("p6x", 16)

        def p6x(eng, tt=tt, pv=b.semv.get("ln1", 0)):
            if tt == 0:
                eng.wait_ge(b.sem("p6b"), 16)
            else:
                eng.wait_ge(b.sem("ln1"), pv)
            eng.dma_start(out=x_tile, in_=xtm_d[tt * 128 : (tt + 1) * 128, :]).then_inc(b.sem("p6x"), 16)
        b.on("sync", p6x)

        j = tt % 2
        al_t = b.bump(f"p6a{j}", 16)

        def p6a(eng, tt=tt, j=j, pv=b.semv.get("p6mm", 0)):
            if tt >= 2:
                eng.wait_ge(b.sem("p6mm"), pv - 8)
            src = aggfm_d.rearrange("(c p) t -> p c t", p=128)[:, :, tt * 128 : (tt + 1) * 128]
            eng.dma_start(out=agg_all[j][:], in_=src).then_inc(b.sem(f"p6a{j}"), 16)
        b.on("sync", p6a)

        for ec in range(8):
            mm_t = b.bump("p6mm", 1)

            def p6mm(eng, tt=tt, ec=ec, j=j, at=al_t, wt=16, ln=b.semv.get("ln1add", 0)):
                if ec == 0:
                    eng.wait_ge(b.sem(f"p6a{j}"), at)
                    eng.wait_ge(b.sem("p6w"), wt)
                    if tt > 0:
                        eng.wait_ge(b.sem("ln1add"), ln)
                    eng.wait_ge(b.sem("p6b2"), 1)
                    eng.matmul(qps_f[:], ones_bf[0:1, 0:128], bo_bf[0:1, 0:512], start=True, stop=False)
                    eng.matmul(kps_f[:], ones_bf[0:1, 0:128], bo_bf[0:1, 512:1024], start=True, stop=False)
                eng.matmul(qps_f[:], agg_all[j][:, ec * 128 : (ec + 1) * 128], wo_all[:, ec * 1024 : ec * 1024 + 512], start=False, stop=(ec == 7))
                eng.matmul(kps_f[:], agg_all[j][:, ec * 128 : (ec + 1) * 128], wo_all[:, ec * 1024 + 512 : (ec + 1) * 1024], start=False, stop=(ec == 7)).then_inc(b.sem("p6mm"), 1)
            b.on("tensor", p6mm)

        mma = b.semv["p6mm"]
        la_t = b.bump("ln1add", 1)

        def ln1a(eng, tt=tt, t=mma, xt=xl_t):
            eng.wait_ge(b.sem("p6mm"), t)
            eng.wait_ge(b.sem("p6x"), xt)
            eng.tensor_add(y1t[:, 0:512], x_tile[:, 0:512], qps_f[:])
            eng.tensor_add(y1t[:, 512:1024], x_tile[:, 512:1024], kps_f[:]).then_inc(b.sem("ln1add"), 1)
        b.on("vector", ln1a)

        ls_t = b.bump("ln1s", 1)

        def ln1s(eng, tt=tt, t=la_t):
            eng.wait_ge(b.sem("ln1add"), t)
            eng.tensor_tensor(sqt, y1t, y1t, ALU.mult)
            eng.reduce_sum(stat[:, 0:1], y1t, axis=AX.X)
            eng.reduce_sum(stat[:, 1:2], sqt, axis=AX.X)
            eng.tensor_scalar_mul(stat[:, 2:3], stat[:, 0:1], 1.0 / 1024.0)
            eng.tensor_scalar_mul(stat[:, 3:4], stat[:, 1:2], 1.0 / 1024.0)
            eng.tensor_tensor(stat[:, 4:5], stat[:, 2:3], stat[:, 2:3], ALU.mult)
            eng.tensor_sub(stat[:, 5:6], stat[:, 3:4], stat[:, 4:5])
            eng.tensor_scalar_add(stat[:, 6:7], stat[:, 5:6], 1e-5).then_inc(b.sem("ln1s"), 1)
        b.on("vector", ln1s)

        lq_t = b.bump("ln1q", 1)

        def ln1q(eng, tt=tt, t=ls_t):
            eng.wait_ge(b.sem("ln1s"), t)
            eng.activation(out=stat[:, 7:8], in_=stat[:, 6:7], func=AF.Sqrt, bias=0.0, scale=1.0).then_inc(b.sem("ln1q"), 1)
        b.on("scalar", ln1q)

        lr_t = b.bump("ln1r", 1)

        def ln1r(eng, tt=tt, t=lq_t):
            eng.wait_ge(b.sem("ln1q"), t)
            eng.reciprocal(stat[:, 0:1], stat[:, 7:8])
            eng.tensor_tensor(stat[:, 1:2], stat[:, 2:3], stat[:, 0:1], ALU.mult)
            eng.tensor_scalar_mul(stat[:, 1:2], stat[:, 1:2], -1.0).then_inc(b.sem("ln1r"), 1)
        b.on("vector", ln1r)

        ln_t = b.bump("ln1", 1)

        def ln1n(eng, tt=tt, t=lr_t, pv=b.semv.get("y1st", 0), pt=b.semv.get("y1tr", 0)):
            eng.wait_ge(b.sem("ln1r"), t)
            if pv:
                eng.wait_ge(b.sem("y1st"), pv)
            if pt:
                eng.wait_ge(b.sem("y1tr"), pt)
            eng.activation(out=y1bf[:], in_=y1t, func=AF.Identity, bias=stat[:, 1:2], scale=stat[:, 0:1]).then_inc(b.sem("ln1"), 1)
        b.on("scalar", ln1n)

        st_t = b.bump("y1st", 16)
        tr_t = b.bump("y1tr", 16 * 16)

        def y1st(eng, tt=tt, t=ln_t):
            eng.wait_ge(b.sem("ln1"), t)
            eng.dma_start(out=y1tm_d[tt * 128 : (tt + 1) * 128, :], in_=y1bf[:]).then_inc(b.sem("y1st"), 16)
        b.on("sync", y1st)

        def y1tr(eng, tt=tt, t=ln_t, base=tr_t - 16 * 16):
            eng.wait_ge(b.sem("ln1"), t)
            for ec in range(8):
                if ec > 0:
                    eng.wait_ge(b.sem("y1tr"), base + 16 * (2 * ec - 1))
                eng.dma_start_transpose(out=trsp[:], in_=y1bf[:, ec * 128 : (ec + 1) * 128]).then_inc(b.sem("y1tr"), 16)
                eng.wait_ge(b.sem("y1tr"), base + 16 * (2 * ec + 1))
                eng.dma_start(out=y1fm_d[ec * 128 : (ec + 1) * 128, tt * 128 : (tt + 1) * 128], in_=trsp[:]).then_inc(b.sem("y1tr"), 16)
        b.on("scalar", y1tr)

    y1_all = b.semv["y1st"]
    y1tr_all = b.semv["y1tr"]

    def q_boot(eng, t=y1_all):
        eng.wait_ge(b.sem("y1st"), t)
        eng.dma_start(out=afc_sb[0:16, :], in_=Afc_d).then_inc(b.sem("qb"), 16)
        eng.dma_start(out=aproj_sb[0:16, :], in_=Aproj_d).then_inc(b.sem("qb"), 16)
        eng.dma_start(out=wchunk[0:16, :], in_=BfcT_d).then_inc(b.sem("qb"), 16)
        eng.dma_start(out=scrA[0:16, 0:1024], in_=BprojT_d).then_inc(b.sem("qb"), 16)
        eng.dma_start(out=wchunk[32:33, :], in_=bfc_d).then_inc(b.sem("qb"), 16)
        eng.dma_start(out=scrA[0:1, 1024:2048], in_=bproj_d).then_inc(b.sem("qb"), 16)
    b.bump("qb", 96)
    b.on("sync", q_boot)

    def q_boot2(eng):
        eng.wait_ge(b.sem("qb"), 96)
        eng.tensor_scalar_mul(b2fc[0:16, :], wchunk[0:16, :], 2.0)
        eng.tensor_scalar_mul(b2pr[0:16, :], scrA[0:16, 0:1024], 2.0)
        eng.tensor_copy(afc_bf[0:16, :], afc_sb[0:16, :])
        eng.tensor_copy(aproj_bf[0:16, :], aproj_sb[0:16, :])
        eng.tensor_copy(bfc_bf[:], wchunk[32:33, :])
        eng.tensor_copy(bpr_bf[:], scrA[0:1, 1024:2048]).then_inc(b.sem("qb2"), 1)
    b.bump("qb2", 1)
    b.on("vector", q_boot2)

    for (name, src_d, nch, width) in (("fc", WfcT_d, 8, 4096), ("pr", WprojT_d, 32, 1024)):
        for c in range(nch):
            ql_t = b.bump("ql", 16)

            def ql(eng, c=c, src_d=src_d, width=width, pv=b.semv.get("qm", 0)):
                eng.wait_ge(b.sem("qb2"), 1)
                if pv:
                    eng.wait_ge(b.sem("qm"), pv)
                eng.dma_start(out=wchunk[:, 0:width], in_=src_d[c * 128 : (c + 1) * 128, :]).then_inc(b.sem("ql"), 16)
            b.on("sync", ql)

            qm_t = b.bump("qm", 1)

            def qm(eng, name=name, c=c, width=width, t=ql_t):
                eng.wait_ge(b.sem("ql"), t)
                col = 0 if name == "fc" else 1
                eng.tensor_reduce(stat[:, 2:3], wchunk[:, 0:width], axis=AX.X, op=ALU.max, apply_absolute_value=True)
                if c == 0:
                    eng.tensor_copy(qstat[:, col : col + 1], stat[:, 2:3]).then_inc(b.sem("qm"), 1)
                else:
                    eng.tensor_max(qstat[:, col : col + 1], qstat[:, col : col + 1], stat[:, 2:3]).then_inc(b.sem("qm"), 1)
            b.on("vector", qm)

    qm_all = b.semv["qm"]

    def qredmm(eng, t=qm_all):
        eng.wait_ge(b.sem("qm"), t)
        eng.matmul(f2ps_f[0][0:2, 0:128], qstat[:, 0:2], ident_sb[:], start=True, stop=True).then_inc(b.sem("qred"), 1)
    b.bump("qred", 1)
    b.on("tensor", qredmm)

    def qredv(eng):
        eng.wait_ge(b.sem("qred"), 1)
        eng.tensor_copy(sqt[0:2, 0:128], f2ps_f[0][0:2, 0:128])
        eng.reduce_max(stat[0:2, 0:1], sqt[0:2, 0:128], axis=AX.X).then_inc(b.sem("qred"), 1)
    b.bump("qred", 1)
    b.on("vector", qredv)

    def qredmm2(eng):
        eng.wait_ge(b.sem("qred"), 2)
        eng.matmul(f2ps_f[0][0:1, 0:2], stat[0:2, 0:1], ident_sb[0:2, 0:2], start=True, stop=True).then_inc(b.sem("qred"), 1)
    b.bump("qred", 1)
    b.on("tensor", qredmm2)

    def qredv2(eng):
        eng.wait_ge(b.sem("qred"), 3)
        eng.tensor_copy(stat[0:1, 2:4], f2ps_f[0][0:1, 0:2]).then_inc(b.sem("qred"), 1)
    b.bump("qred", 1)
    b.on("vector", qredv2)

    def qredmm3(eng):
        eng.wait_ge(b.sem("qred"), 4)
        eng.matmul(f2ps_f[0][:, 0:2], ones1[:], stat[0:1, 2:4], start=True, stop=True).then_inc(b.sem("qred"), 1)
    b.bump("qred", 1)
    b.on("tensor", qredmm3)

    def qredv3(eng):
        eng.wait_ge(b.sem("qred"), 5)
        eng.tensor_copy(qstat[:, 2:4], f2ps_f[0][:, 0:2]).then_inc(b.sem("qred"), 1)
    b.bump("qred", 1)
    b.on("vector", qredv3)

    def qrs(eng):
        eng.wait_ge(b.sem("qred"), 6)
        eng.reciprocal(qstat[:, 4:5], qstat[:, 2:3])
        eng.tensor_scalar_mul(qstat[:, 4:5], qstat[:, 4:5], 127.0)
        eng.reciprocal(qstat[:, 5:6], qstat[:, 3:4])
        eng.tensor_scalar_mul(qstat[:, 5:6], qstat[:, 5:6], 127.0)
        eng.tensor_scalar_mul(qstat[:, 6:7], qstat[:, 2:3], 1.0 / 127.0)
        eng.tensor_scalar_mul(qstat[:, 7:8], qstat[:, 3:4], 1.0 / 127.0)
        eng.tensor_scalar_mul(qstat[:, 0:1], qstat[:, 6:7], -1.0)
        eng.tensor_scalar_mul(qstat[:, 1:2], qstat[:, 7:8], -1.0).then_inc(b.sem("qrs"), 1)
    b.bump("qrs", 1)
    b.on("vector", qrs)

    for (name, src_d, dst_d, nch, width, A_sb, B_sb, rcol, ncol, amax_col) in (
        ("fc", WfcT_d, wefffc_d, 8, 4096, afc_bf, b2fc, 4, 0, 2),
        ("pr", WprojT_d, weffpr_d, 32, 1024, aproj_bf, b2pr, 5, 1, 3),
    ):
        for c in range(nch):
            ql2_t = b.bump("ql2", 16)

            def ql2(eng, c=c, src_d=src_d, width=width, pv=b.semv.get("qst", 0)):
                eng.wait_ge(b.sem("qrs"), 1)
                if pv:
                    eng.wait_ge(b.sem("qst"), pv)
                eng.dma_start(out=wchunk[:, 0:width], in_=src_d[c * 128 : (c + 1) * 128, :]).then_inc(b.sem("ql2"), 16)
            b.on("sync", ql2)

            nos = width // 1024
            for osi in range(nos):
                lo_t = b.bump("qlo", 1)

                def qlo(eng, name=name, c=c, osi=osi, A_sb=A_sb, B_sb=B_sb, t=ql2_t, pv=b.semv.get("qev", 0)):
                    eng.wait_ge(b.sem("ql2"), t)
                    if pv:
                        eng.wait_ge(b.sem("qev"), pv)
                    eng.matmul(qps_f[:], A_sb[0:16, c * 128 : (c + 1) * 128], B_sb[0:16, osi * 1024 : osi * 1024 + 512], start=True, stop=True)
                    eng.matmul(kps_f[:], A_sb[0:16, c * 128 : (c + 1) * 128], B_sb[0:16, osi * 1024 + 512 : (osi + 1) * 1024], start=True, stop=True).then_inc(b.sem("qlo"), 1)
                b.on("tensor", qlo)

                qa_t = b.bump("qact", 1)

                def qact(eng, name=name, osi=osi, rcol=rcol, amax_col=amax_col, ncol=ncol, t=lo_t, pq=b.semv.get("qint", 0)):
                    sl = slice(osi * 1024, (osi + 1) * 1024)
                    eng.wait_ge(b.sem("qlo"), t)
                    if pq:
                        eng.wait_ge(b.sem("qint"), pq)
                    eng.activation(out=scrA[:, sl], in_=wchunk[:, sl], func=AF.Identity, bias=16384.0, scale=qstat[:, rcol : rcol + 1]).then_inc(b.sem("qact"), 1)
                b.on("scalar", qact)

                qi_t = b.bump("qint", 1)

                def qint(eng, osi=osi, t=qa_t):
                    sl = slice(osi * 1024, (osi + 1) * 1024)
                    eng.wait_ge(b.sem("qact"), t)
                    eng.tensor_copy(wchunkI[:, sl], scrA[:, sl])
                    eng.tensor_copy(wchunk[:, sl], wchunkI[:, sl])
                    eng.tensor_sub(scrA[:, sl], scrA[:, sl], wchunk[:, sl])
                    eng.tensor_single_scalar(scrA[:, sl], scrA[:, sl], 0.5, ALU.is_ge)
                    eng.tensor_add(scrA[:, sl], scrA[:, sl], wchunk[:, sl]).then_inc(b.sem("qint"), 1)
                b.on("vector", qint)

                qc_t = b.bump("qchain", 1)

                def qchain(eng, osi=osi, ncol=ncol, amax_col=amax_col, t=qi_t):
                    sl = slice(osi * 1024, (osi + 1) * 1024)
                    eng.wait_ge(b.sem("qint"), t)
                    eng.activation(out=wchunk[:, sl], in_=scrA[:, sl], func=AF.Relu, bias=-16256.0, scale=1.0)
                    eng.activation(out=scrA[:, sl], in_=wchunk[:, sl], func=AF.Relu, bias=255.0, scale=-1.0)
                    eng.activation(out=wchunk[:, sl], in_=scrA[:, sl], func=AF.Identity, bias=qstat[:, amax_col : amax_col + 1], scale=qstat[:, ncol : ncol + 1]).then_inc(b.sem("qchain"), 1)
                b.on("scalar", qchain)

                qe_t = b.bump("qev", 1)

                def qev(eng, osi=osi, t=qc_t):
                    eng.wait_ge(b.sem("qchain"), t)
                    eng.tensor_add(weffo[:, osi * 1024 : osi * 1024 + 512], wchunk[:, osi * 1024 : osi * 1024 + 512], qps_f[:])
                    eng.tensor_add(weffo[:, osi * 1024 + 512 : (osi + 1) * 1024], wchunk[:, osi * 1024 + 512 : (osi + 1) * 1024], kps_f[:]).then_inc(b.sem("qev"), 1)
                b.on("vector", qev)

            qs_t = b.bump("qst", 16)
            qe_all = b.semv["qev"]

            def qst(eng, c=c, dst_d=dst_d, width=width, t=qe_all):
                eng.wait_ge(b.sem("qev"), t)
                eng.dma_start(out=dst_d[c * 128 : (c + 1) * 128, :], in_=weffo[:, 0:width]).then_inc(b.sem("qst"), 16)
            b.on("gpsimd", qst)

    y2ps = [(s1ps_f[0], s1ps_f[1]), (s1ps_f[2], s1ps_f[3]), (f2ps_f[0], f2ps_f[1]), (qps_f, kps_f)]
    for t5 in range(8):
        yl_t = b.bump("fy1", 16 * 8)

        def fy1(eng, t5=t5, pv=b.semv.get("fgel", 0), qt=b.semv.get("qst", 0), ytr=y1tr_all):
            eng.wait_ge(b.sem("qst"), qt)
            eng.wait_ge(b.sem("y1tr"), ytr)
            if t5 > 0:
                eng.wait_ge(b.sem("fgel"), pv)
            for ic in range(8):
                eng.dma_start(out=y1f_all[:, ic * 512 : (ic + 1) * 512], in_=y1fm_d[ic * 128 : (ic + 1) * 128, t5 * 512 : (t5 + 1) * 512]).then_inc(b.sem("fy1"), 16)
        b.on("sync", fy1)

        for oc in range(32):
            j = oc % 2
            wb_t = b.bump(f"fwb{j}", 16)

            def fwb(eng, t5=t5, oc=oc, j=j, pv=b.semv.get("fmm", 0), qt=b.semv.get("qst", 0)):
                if t5 * 32 + oc < 2:
                    eng.wait_ge(b.sem("qst"), qt)
                else:
                    eng.wait_ge(b.sem("fmm"), pv - 8)
                src = wefffc_d.rearrange("(c p) o -> p c o", p=128)[:, :, oc * 128 : (oc + 1) * 128]
                eng.dma_start(out=wfc_col[j][:], in_=src).then_inc(b.sem(f"fwb{j}"), 16)
            b.on("sync", fwb)

            for ic in range(8):
                fm_t = b.bump("fmm", 1)

                def fmm(eng, t5=t5, oc=oc, ic=ic, j=j, wt=wb_t, yt=yl_t, gl=b.semv.get("fgel", 0), la=b.semv.get("ln2add", 0)):
                    if ic == 0:
                        eng.wait_ge(b.sem(f"fwb{j}"), wt)
                        eng.wait_ge(b.sem("fy1"), yt)
                        if oc == 0 and t5 > 0 and la:
                            eng.wait_ge(b.sem("ln2add"), la)
                        if t5 + oc > 0:
                            eng.wait_ge(b.sem("fgel"), gl)
                        eng.matmul(qps_f[:], bfc_bf[0:1, oc * 128 : (oc + 1) * 128], ones_bf[0:1, 0:512], start=True, stop=False)
                    eng.matmul(qps_f[:], wfc_col[j][:, ic * 128 : (ic + 1) * 128], y1f_all[:, ic * 512 : (ic + 1) * 512], start=False, stop=(ic == 7)).then_inc(b.sem("fmm"), 1)
                b.on("tensor", fmm)

            fg_t = b.bump("fgel", 1)

            def fgel(eng, t5=t5, oc=oc, t=b.semv["fmm"]):
                eng.wait_ge(b.sem("fmm"), t)
                eng.activation(out=gtile[:, oc * 512 : (oc + 1) * 512], in_=qps_f[:], func=AF.Gelu, bias=0.0, scale=1.0).then_inc(b.sem("fgel"), 1)
            b.on("scalar", fgel)

        fg_all = b.semv["fgel"]
        for oc in range(32):
            j = oc % 2
            wp_t = b.bump(f"fwp{j}", 16)

            def fwp(eng, t5=t5, oc=oc, j=j, pv=b.semv.get("f2mm", 0)):
                if t5 + oc > 0:
                    eng.wait_ge(b.sem("f2mm"), max(0, pv - 1))
                eng.dma_start(out=wpr_buf[j][:], in_=weffpr_d[oc * 128 : (oc + 1) * 128, :]).then_inc(b.sem(f"fwp{j}"), 16)
            b.on("scalar", fwp)

            f2_t = b.bump("f2mm", 1)

            def f2mm(eng, t5=t5, oc=oc, j=j, wt=wp_t, ga=fg_all, ln=b.semv.get("ln2add", 0)):
                eng.wait_ge(b.sem(f"fwp{j}"), wt)
                eng.wait_ge(b.sem("fgel"), ga)
                if oc == 0 and t5 > 0:
                    eng.wait_ge(b.sem("ln2add"), ln)
                for ts4 in range(4):
                    pa, pb = y2ps[ts4]
                    if oc == 0:
                        eng.matmul(pa[:], ones_bf[0:1, 0:128], bpr_bf[0:1, 0:512], start=True, stop=False)
                        eng.matmul(pb[:], ones_bf[0:1, 0:128], bpr_bf[0:1, 512:1024], start=True, stop=False)
                    g_blk = gtile[:, oc * 512 + ts4 * 128 : oc * 512 + (ts4 + 1) * 128]
                    eng.matmul(pa[:], g_blk, wpr_buf[j][:, 0:512], start=False, stop=(oc == 31))
                    i = eng.matmul(pb[:], g_blk, wpr_buf[j][:, 512:1024], start=False, stop=(oc == 31))
                i.then_inc(b.sem("f2mm"), 1)
            b.on("tensor", f2mm)

        f2_all = b.semv["f2mm"]
        for ts4 in range(4):
            rl_t = b.bump("fyr", 16)

            def fyr(eng, t5=t5, ts4=ts4, pv=b.semv.get("ln2add", 0)):
                if t5 + ts4 > 0:
                    eng.wait_ge(b.sem("ln2add"), pv)
                r0 = t5 * 512 + ts4 * 128
                eng.dma_start(out=y1res[:], in_=y1tm_d[r0 : r0 + 128, :]).then_inc(b.sem("fyr"), 16)
            b.on("sync", fyr)

            l2a_t = b.bump("ln2add", 1)

            def ln2a(eng, t5=t5, ts4=ts4, t=f2_all, rt=rl_t):
                eng.wait_ge(b.sem("f2mm"), t)
                eng.wait_ge(b.sem("fyr"), rt)
                pa, pb = y2ps[ts4]
                eng.tensor_add(y2t[:, 0:512], y1res[:, 0:512], pa[:])
                eng.tensor_add(y2t[:, 512:1024], y1res[:, 512:1024], pb[:]).then_inc(b.sem("ln2add"), 1)
            b.on("vector", ln2a)

            l2s_t = b.bump("ln2s", 1)

            def ln2s(eng, t=l2a_t):
                eng.wait_ge(b.sem("ln2add"), t)
                eng.tensor_tensor(sq2, y2t, y2t, ALU.mult)
                eng.reduce_sum(stat[:, 0:1], y2t, axis=AX.X)
                eng.reduce_sum(stat[:, 1:2], sq2, axis=AX.X)
                eng.tensor_scalar_mul(stat[:, 2:3], stat[:, 0:1], 1.0 / 1024.0)
                eng.tensor_scalar_mul(stat[:, 3:4], stat[:, 1:2], 1.0 / 1024.0)
                eng.tensor_tensor(stat[:, 4:5], stat[:, 2:3], stat[:, 2:3], ALU.mult)
                eng.tensor_sub(stat[:, 5:6], stat[:, 3:4], stat[:, 4:5])
                eng.tensor_scalar_add(stat[:, 6:7], stat[:, 5:6], 1e-5).then_inc(b.sem("ln2s"), 1)
            b.on("vector", ln2s)

            l2q_t = b.bump("ln2q", 1)

            def ln2q(eng, t=l2s_t):
                eng.wait_ge(b.sem("ln2s"), t)
                eng.activation(out=stat[:, 7:8], in_=stat[:, 6:7], func=AF.Sqrt, bias=0.0, scale=1.0).then_inc(b.sem("ln2q"), 1)
            b.on("scalar", ln2q)

            l2r_t = b.bump("ln2r", 1)

            def ln2r(eng, t=l2q_t):
                eng.wait_ge(b.sem("ln2q"), t)
                eng.reciprocal(stat[:, 0:1], stat[:, 7:8])
                eng.tensor_tensor(stat[:, 1:2], stat[:, 2:3], stat[:, 0:1], ALU.mult)
                eng.tensor_scalar_mul(stat[:, 1:2], stat[:, 1:2], -1.0).then_inc(b.sem("ln2r"), 1)
            b.on("vector", ln2r)

            l2n_t = b.bump("ln2", 1)

            def ln2n(eng, t=l2r_t, ot=b.semv.get("outst", 0)):
                eng.wait_ge(b.sem("ln2r"), t)
                if ot:
                    eng.wait_ge(b.sem("outst"), ot)
                eng.activation(out=out_tile, in_=y2t, func=AF.Identity, bias=stat[:, 1:2], scale=stat[:, 0:1]).then_inc(b.sem("ln2"), 1)
            b.on("scalar", ln2n)

            os_t = b.bump("outst", 16)

            def outst(eng, t5=t5, ts4=ts4, t=l2n_t):
                eng.wait_ge(b.sem("ln2"), t)
                r0 = t5 * 512 + ts4 * 128
                eng.dma_start(out=out_d[r0 : r0 + 128, :], in_=out_tile).then_inc(b.sem("outst"), 16)
            b.on("sync", outst)

    out_all = b.semv["outst"]

    def fin(eng, t=out_all):
        eng.wait_ge(b.sem("outst"), t)
    b.on("gpsimd", fin)


    if upto == "agg" and p5_stop is not None:
        return b
    if upto == "agg":
        agg_out = b.dram_out("agg_dbg", [1024, 4096], BF16)
        bounce = pool[:, 0:4096].bitcast(BF16)[:, 0:4096]

        def aggout(eng, t=b.semv["aggc"]):
            eng.wait_ge(b.sem("aggc"), t)
            n = 0
            for c in range(8):
                eng.dma_start(out=bounce, in_=aggfm_d[c * 128 : (c + 1) * 128, :]).then_inc(b.sem("outd3"), 16)
                n += 16
                eng.wait_ge(b.sem("outd3"), n)
                eng.dma_start(out=agg_out[c * 128 : (c + 1) * 128, :], in_=bounce).then_inc(b.sem("outd3"), 16)
                n += 16
                eng.wait_ge(b.sem("outd3"), n)
        b.bump("outd3", 16 * 16)
        b.on("gpsimd", aggout)
        return b

    return b




_BUILD_CACHE = {}


def _get_build():
    if "b" not in _BUILD_CACHE:
        b = build_p123(upto="full", use_cc=True)
        nc = b.finalize()
        _BUILD_CACHE["b"] = b
        _BUILD_CACHE["nc"] = nc
    return _BUILD_CACHE["b"], _BUILD_CACHE["nc"]


def kernel(**inputs):
    from concourse.bass_utils import run_bass_kernel_spmd

    b, nc = _get_build()
    in_maps = []
    for core in range(8):
        m = host_prep_attn(inputs, core)
        m.update(b.const_arrays)
        in_maps.append(m)
    res = run_bass_kernel_spmd(nc, in_maps, core_ids=list(range(8)))
    out = np.stack([np.asarray(res.results[c]["out"]) for c in range(8)], axis=0)
    return out.astype(np.float32)


# version 11 (final)
# speedup vs baseline: 1.3093x; 1.0475x over previous
import numpy as np
import concourse.bass as bass
import concourse.mybir as mybir

F32 = mybir.dt.float32
BF16 = mybir.dt.bfloat16
AF = mybir.ActivationFunctionType
ALU = mybir.AluOpType
AX = mybir.AxisListType
U32 = mybir.dt.uint32
from concourse.bass_isa_rust import ReduceOp as _ReduceOp
from concourse.library_config import attn as _LIB_ATTN
_RED_MAX = _ReduceOp.max if hasattr(_ReduceOp, 'max') else None
I32 = mybir.dt.int32

T = 4096
E = 1024
NF2 = 33
EW = 256
NH = E // EW
PERM = np.array([2 * p + h for h in (0, 1) for p in range(32)])
DBG_NOACC = False


def make_consts():
    c = {}
    s2 = np.arange(64)[:, None]
    f2 = np.arange(64)[None, :]
    M1 = np.exp(-2j * np.pi * s2 * f2 / 64.0)
    z = np.zeros((64, 64))
    c["c1re_bd"] = np.block([[M1.real, z], [z, M1.real]]).astype(np.float32)
    c["c1im_bd"] = np.block([[M1.imag, z], [z, M1.imag]]).astype(np.float32)

    s1 = np.arange(64)
    f1 = np.arange(64)
    L2 = np.zeros((NF2, 128, 128), np.float32)
    for ff2 in range(NF2):
        ang = -2.0 * np.pi * (s1[:, None] * (ff2 + 64.0 * f1[None, :])) / T
        cr, ci = np.cos(ang), np.sin(ang)
        L2[ff2, 0:64, 0:64] = cr[PERM, :]
        L2[ff2, 0:64, 64:128] = ci[PERM, :]
        L2[ff2, 64:128, 0:64] = -ci[PERM, :]
        L2[ff2, 64:128, 64:128] = cr[PERM, :]
    c["stage2_L"] = L2

    inner = np.exp(+2j * np.pi * np.outer(f1, s1) / 64.0)
    WA = np.zeros((128, 128), np.float32)
    WA[0:64, 0:64] = inner.real
    WA[0:64, 64:128] = inner.imag
    WA[64:128, 0:64] = -inner.imag
    WA[64:128, 64:128] = inner.real
    c["macA_rhs"] = WA

    d = np.where((np.arange(NF2) >= 1) & (np.arange(NF2) <= 31), 2.0, 1.0)
    tw = np.exp(+2j * np.pi * np.outer(s1, np.arange(NF2)) / T).T
    twf = d[:, None] * tw / (T * E)
    c["mac_tw_re"] = np.concatenate([twf.real, twf.real], axis=1).astype(np.float32)
    c["mac_tw_im"] = np.concatenate([twf.imag, twf.imag], axis=1).astype(np.float32)

    ang = +2.0 * np.pi * np.outer(np.arange(NF2), s1) / 64.0
    c["macB_cos"] = np.cos(ang).astype(np.float32)
    c["macB_msin"] = (-np.sin(ang)).astype(np.float32)

    import ml_dtypes
    bf = lambda a: np.asarray(a, dtype=ml_dtypes.bfloat16)
    c["c1re_bf"] = bf(c["c1re_bd"])
    c["c1im_bf"] = bf(c["c1im_bd"])
    c["st2v_re"] = bf(c["stage2_L"][:, :, 0:64].reshape(NF2 * 128, 64))
    c["st2v_im"] = bf(c["stage2_L"][:, :, 64:128].reshape(NF2 * 128, 64))
    LA = np.zeros((NF2, 128, 128), np.float32)
    f1 = np.arange(64)
    for ff2 in range(NF2):
        dd = 2.0 if 1 <= ff2 <= 31 else 1.0
        inner = np.exp(+2j * np.pi * np.outer(f1, s1.ravel() if s1.ndim > 1 else s1) / 64.0)
        tw = np.exp(+2j * np.pi * np.arange(64) * ff2 / T)[None, :]
        Mfull = dd * (1.0 / T) * inner * tw
        LA[ff2, 0:64, 0:64] = Mfull.real
        LA[ff2, 0:64, 64:128] = Mfull.imag
        LA[ff2, 64:128, 0:64] = -Mfull.imag
        LA[ff2, 64:128, 64:128] = Mfull.real
    c["invA_re"] = bf(LA[:, 0:64, :].reshape(NF2 * 64, 128))
    c["invA_im"] = bf(LA[:, 64:128, :].reshape(NF2 * 64, 128))
    angB = +2.0 * np.pi * np.outer(np.arange(NF2), np.arange(64)) / 64.0
    LB = np.zeros((66, 64), np.float32)
    LB[0:NF2, :] = np.cos(angB)
    LB[NF2 : 2 * NF2, :] = -np.sin(angB)
    c["invB"] = bf(LB)
    fvals = (np.arange(64)[:, None] * 64 + np.arange(NF2)[None, :]).reshape(-1)
    c["f_rep"] = np.broadcast_to(fvals.astype(np.float32), (128, 64 * NF2)).copy()
    return c


def host_prep_attn(inputs, core):
    x = np.asarray(inputs["L0_attn_hidden_states"])[core]
    d = {}
    xt = np.ascontiguousarray(x.T).reshape(E, 64, 64)
    xp = np.empty((32, E, 128), np.float32)
    for p in range(32):
        xp[p, :, 0:64] = xt[:, :, 2 * p]
        xp[p, :, 64:128] = xt[:, :, 2 * p + 1]
    d["xpair"] = np.ascontiguousarray(
        xp.reshape(32, 8, 128, 128).transpose(0, 2, 1, 3).reshape(32, 128, E))
    d["Wq"] = np.ascontiguousarray(np.asarray(inputs["L0_attn_Wq"], np.float32))
    d["Wk"] = np.ascontiguousarray(np.asarray(inputs["L0_attn_Wk"], np.float32))
    d["bq"] = np.ascontiguousarray(np.asarray(inputs["L0_attn_bq"], np.float32))[None, :]
    d["bk"] = np.ascontiguousarray(np.asarray(inputs["L0_attn_bk"], np.float32))[None, :]
    d["Wv"] = np.ascontiguousarray(np.asarray(inputs["L0_attn_Wv"], np.float32))
    d["bv"] = np.ascontiguousarray(np.asarray(inputs["L0_attn_bv"], np.float32))[None, :]
    d["Wo"] = np.ascontiguousarray(np.asarray(inputs["L0_attn_Wo"], np.float32))
    d["bo"] = np.ascontiguousarray(np.asarray(inputs["L0_attn_bo"], np.float32))[None, :]
    d["x_tm"] = np.ascontiguousarray(x.astype(np.float32))
    d["WfcT"] = np.ascontiguousarray(np.asarray(inputs["L0_ffn_W_fc"], np.float32).T)
    d["WprojT"] = np.ascontiguousarray(np.asarray(inputs["L0_ffn_W_proj"], np.float32).T)
    d["Afc"] = np.ascontiguousarray(np.asarray(inputs["L0_ffn_A_fc"], np.float32))
    d["BfcT"] = np.ascontiguousarray(np.asarray(inputs["L0_ffn_B_fc"], np.float32).T)
    d["Aproj"] = np.ascontiguousarray(np.asarray(inputs["L0_ffn_A_proj"], np.float32))
    d["BprojT"] = np.ascontiguousarray(np.asarray(inputs["L0_ffn_B_proj"], np.float32).T)
    d["bfc"] = np.ascontiguousarray(np.asarray(inputs["L0_ffn_b_fc"], np.float32))[None, :]
    d["bproj"] = np.ascontiguousarray(np.asarray(inputs["L0_ffn_b_proj"], np.float32))[None, :]
    return d


class B:

    def __init__(self):
        self.nc = bass.Bass("TRN2", target_bir_lowering=False)
        self.cmds = {k: [] for k in ["sync", "tensor", "vector", "scalar", "gpsimd"]}
        self.semv = {}
        self.sems = {}
        self.const_arrays = {}

    def sem(self, name):
        if name not in self.sems:
            self.sems[name] = self.nc.semaphore(name).__enter__()
            self.semv[name] = 0
        return self.sems[name]

    def bump(self, name, n):
        self.sem(name)
        self.semv[name] += n
        return self.semv[name]

    def on(self, eng, fn):
        self.cmds[eng].append(fn)

    def dram_in(self, name, shape, dtype=F32):
        return self.nc.dram_tensor(name, list(shape), dtype, kind="ExternalInput").ap()

    def dram_out(self, name, shape, dtype=F32):
        return self.nc.dram_tensor(name, list(shape), dtype, kind="ExternalOutput").ap()

    def sb(self, name, shape, dtype=F32):
        return self.nc.alloc_sbuf_tensor(name, list(shape), dtype)

    def ps(self, name, shape, dtype=F32):
        return self.nc.alloc_psum_tensor(name, list(shape), dtype)

    def const_in(self, name, arr):
        arr = np.ascontiguousarray(arr)
        import ml_dtypes
        dt_ = F32 if arr.dtype == np.float32 else (BF16 if arr.dtype == ml_dtypes.bfloat16 else None)
        assert dt_ is not None, arr.dtype
        ap = self.dram_in(name, arr.shape, dt_)
        self.const_arrays[name] = arr
        return ap

    class _AutoSer:
        _SKIP = {"wait_ge", "dma_start", "dma_start_transpose", "then_inc",
                 "sem_inc", "nop", "drain"}

        class _Proxy:
            def __init__(self, ser, ins):
                self._ser = ser
                self._ins = ins

            def then_inc(self, sem, n):
                self._ins.then_inc(sem, n)
                c = self._ser._ucount.get(id(sem), 0) + n
                self._ser._ucount[id(sem)] = c
                self._ser._token = (sem, c)
                self._ser._last = None
                return self._ins

            def __getattr__(self, name):
                return getattr(self._ins, name)

        def __init__(self, eng, sem):
            self._eng = eng
            self._sem = sem
            self._count = 0
            self._token = None
            self._last = None
            self._ucount = {}

        def _flush(self):
            if self._last is not None:
                self._last.then_inc(self._sem, 1)
                self._count += 1
                self._token = (self._sem, self._count)
                self._last = None

        def __getattr__(self, name):
            attr = getattr(self._eng, name)
            if name.startswith("_") or name in self._SKIP or not callable(attr):
                return attr

            def call(*a, **k):
                self._flush()
                if self._token is not None:
                    self._eng.wait_ge(self._token[0], self._token[1])
                    self._token = None
                ins = attr(*a, **k)
                if hasattr(ins, "then_inc"):
                    self._last = ins
                    return B._AutoSer._Proxy(self, ins)
                return ins

            return call

    def finalize(self):
        nc = self.nc
        with nc.Block() as block:
            @block.sync
            def _(eng):
                for fn in self.cmds["sync"]:
                    fn(eng)

            @block.tensor
            def _(eng):
                for fn in self.cmds["tensor"]:
                    fn(eng)

            @block.vector
            def _(eng):
                w = B._AutoSer(eng, self.sem("autoV"))
                for fn in self.cmds["vector"]:
                    fn(w)

            @block.scalar
            def _(eng):
                w = B._AutoSer(eng, self.sem("autoS"))
                for fn in self.cmds["scalar"]:
                    fn(w)

            @block.gpsimd
            def _(eng):
                w = B._AutoSer(eng, self.sem("autoG"))
                for fn in self.cmds["gpsimd"]:
                    fn(w)
        return nc


def build_p123(dump=(), dump_slice=0, upto="mac", p5_stop=None, use_cc=False):
    b = B()
    C = make_consts()
    dump = set(dump)

    xpair = b.dram_in("xpair", [32, 128, E])
    Wq_d = b.dram_in("Wq", [E, E])
    Wk_d = b.dram_in("Wk", [E, E])
    bq_d = b.dram_in("bq", [1, E])
    bk_d = b.dram_in("bk", [1, E])
    Wv_d = b.dram_in("Wv", [E, E])
    bv_d = b.dram_in("bv", [1, E])
    c1re_d = b.const_in("c1re_bd", C["c1re_bd"])
    c1im_d = b.const_in("c1im_bd", C["c1im_bd"])
    st2_d = b.const_in("stage2_L", C["stage2_L"].reshape(NF2 * 128, 128))
    macA_d = b.const_in("macA_rhs", C["macA_rhs"])
    mtwre_d = b.const_in("mac_tw_re", C["mac_tw_re"])
    mtwim_d = b.const_in("mac_tw_im", C["mac_tw_im"])
    macBc_d = b.const_in("macB_cos", C["macB_cos"])
    macBs_d = b.const_in("macB_msin", C["macB_msin"])
    c1rebf_d = b.const_in("c1re_bf", C["c1re_bf"])
    c1imbf_d = b.const_in("c1im_bf", C["c1im_bf"])
    st2vre_d = b.const_in("st2v_re", C["st2v_re"])
    st2vim_d = b.const_in("st2v_im", C["st2v_im"])
    invAre_d = b.const_in("invA_re", C["invA_re"])
    invAim_d = b.const_in("invA_im", C["invA_im"])
    invB_d = b.const_in("invB", C["invB"])
    mac_out = b.dram_out("mac", [64, 64])
    cvals = np.broadcast_to(np.array([-np.pi, 2.0 * np.pi / 4096.0, 0.0, 1.0,
                                      -1.0, 16512.5, -16384.0, 255.0,
                                      16384.0, -16256.0], np.float32), (128, 10)).copy()
    ident_d = b.const_in("ident128", np.eye(128, dtype=np.float32))
    ident_sb = b.sb("ident_sb", [128, 128])
    import ml_dtypes as _mld
    identbf_d = b.const_in("ident64bf", np.asarray(np.tile(np.eye(64), (2, 1)), dtype=_mld.bfloat16))
    identbf_sb = b.sb("identbf_sb", [128, 64], BF16)
    cvals_d = b.const_in("cvals", cvals)
    csb = b.sb("csb", [128, 10])
    if upto in ("topk",):
        g_out = b.dram_out("g", [2, 64, 8 * NF2])

    dbg = {}
    if "rbq" in dump:
        dbg["rbq"] = b.dram_out("d_rbq", [128, NF2 * EW])
    if "qf" in dump:
        dbg["qf"] = b.dram_out("d_qf", [2, 128, EW])
    if "ppack" in dump:
        dbg["ppack"] = b.dram_out("d_ppack", [128, NF2])
    if "qbuf" in dump:
        dbg["qbuf"] = b.dram_out("d_qbuf", [128, EW])
    if "s1buf" in dump:
        dbg["s1buf"] = b.dram_out("d_s1buf", [2, 128, EW])
    if "pparts" in dump:
        dbg["pparts"] = b.dram_out("d_pparts", [64, NF2 * 4])

    xp_sb = [b.sb(f"xp{j}", [128, E]) for j in range(2)]
    wq_sb = b.sb("wq_sb", [128, 8 * EW])
    wk_sb = b.sb("wk_sb", [128, 8 * EW])
    bias_sb = b.sb("bias_sb", [1, 2 * E])
    ones1 = b.sb("ones1", [1, 128])
    qbuf = [b.sb(f"qbuf{j}", [128, EW]) for j in range(2)]
    kbuf = [b.sb(f"kbuf{j}", [128, EW]) for j in range(2)]
    s1buf = [[b.sb(f"s1buf{j}_{c}", [128, EW]) for c in range(4)] for j in range(2)]
    c1re_sb = b.sb("c1re_sb", [128, 128])
    c1im_sb = b.sb("c1im_sb", [128, 128])
    b._pool_base = 229344 - b.nc.sbuf_bytes_remaining
    pool = b.sb("pool", [128, 28800])

    def bfat(name, shape, off_f32):
        return b.nc.alloc_sbuf_tensor_at(name, list(shape), BF16, offset=b._pool_base + off_f32 * 4)

    rbq = pool[:, 0:8448]
    rbk = pool[:, 8448:16896]
    st2_sb = pool[:, 16896:21120]
    pparts = b.sb("pparts", [64, NF2 * 4])
    ppack = b.sb("ppack", [128, NF2])
    macA_sb = b.sb("macA_sb", [128, 128])
    mtw_sb = b.sb("mtw_sb", [NF2, 256])
    u_sb = b.sb("u_sb", [NF2, 128])
    up_sb = b.sb("up_sb", [NF2, 128])
    macB_sb = b.sb("macB_sb", [NF2, 128])
    mac_sb = b.sb("mac_sb", [64, 64])

    qps_f = b.ps("qps", [128, 512])
    kps_f = b.ps("kps", [128, 512])
    s1ps_f = [b.ps(f"s1ps{c}", [128, 512]) for c in range(4)]
    f2ps_f = [b.ps(f"f2ps{c}", [128, 512]) for c in range(2)]
    qps = qps_f[:, 0:EW]
    kps = kps_f[:, 0:EW]
    s1ps = [t[:, 0:EW] for t in s1ps_f]
    f2ps = [t[:, 0:EW] for t in f2ps_f]

    def p0(eng):
        for dst, src in [
            (c1re_sb[:], c1re_d), (c1im_sb[:], c1im_d),
            (st2_sb[:], st2_d.rearrange("(f p) m -> p f m", p=128)),
            (bias_sb[0:1, 0:E], bq_d), (bias_sb[0:1, E : 2 * E], bk_d),
            (macA_sb[:], macA_d), (mtw_sb[:, 0:128], mtwre_d),
            (mtw_sb[:, 128:256], mtwim_d), (macB_sb[:, 0:64], macBc_d),
            (macB_sb[:, 64:128], macBs_d),
            (csb[:], cvals_d),
            (ident_sb[:], ident_d),
            (identbf_sb[:], identbf_d),
        ]:
            eng.dma_start(out=dst, in_=src).then_inc(b.sem("boot"), 16)
    b.bump("boot", 16 * 13)
    b.on("sync", p0)
    xpair_bf_d = b.nc.dram_tensor("xpair_bf", [32, 128, E], BF16, kind="Internal").ap()
    for _i, _v in enumerate([-np.pi, 2.0 * np.pi / 4096.0, 0.0, 1.0, -1.0, 16512.5, -16384.0, 255.0, 16384.0, -16256.0]):
        b.nc.const_aps.register(F32, _v, csb[:, _i : _i + 1])
    boot_t = b.semv["boot"]

    def p0v(eng):
        eng.memset(ones1[:], 1.0).then_inc(b.sem("ones"), 1)
    b.bump("ones", 1)
    b.on("vector", p0v)

    for H in range(NH):
        wload_t = b.bump("wload", 32)

        def p1w(eng, H=H, wt=b.semv["proj"] if "proj" in b.semv else 0):
            if H >= 1:
                eng.wait_ge(b.sem("proj"), wt)
            src_q = Wq_d.rearrange("(c p) o -> p c o", p=128)[:, :, H * EW : (H + 1) * EW]
            src_k = Wk_d.rearrange("(c p) o -> p c o", p=128)[:, :, H * EW : (H + 1) * EW]
            eng.dma_start(out=wq_sb[:], in_=src_q).then_inc(b.sem("wload"), 16)
            eng.dma_start(out=wk_sb[:], in_=src_k).then_inc(b.sem("wload"), 16)
        b.on("sync", p1w)

        for p in range(32):
            j = p % 2
            need_proj = b.semv.get("proj", 0)
            lsem = f"load{j}"
            load_t = b.bump(lsem, 16)

            def l(eng, p=p, j=j, H=H, need=need_proj, lsem=lsem):
                if p >= 2 or H >= 1:
                    eng.wait_ge(b.sem("proj"), need)
                eng.dma_start(out=xp_sb[j][:], in_=xpair[p]).then_inc(b.sem(lsem), 16)
            b.on("sync", l)

            qevac_need = b.semv.get("qevac", 0)
            proj_t = b.bump("proj", 1)

            def mm(eng, p=p, j=j, H=H, lt=load_t, wt=wload_t, qn=qevac_need, lsem=lsem):
                eng.wait_ge(b.sem(lsem), lt)
                if p == 0:
                    eng.wait_ge(b.sem("boot"), boot_t)
                    eng.wait_ge(b.sem("ones"), 1)
                eng.wait_ge(b.sem("wload"), wt)
                if p >= 1 or H >= 1:
                    eng.wait_ge(b.sem("qevac"), qn)
                eng.matmul(qps[:], ones1[:], bias_sb[0:1, H * EW : (H + 1) * EW], start=True, stop=False)
                eng.matmul(kps[:], ones1[:], bias_sb[0:1, E + H * EW : E + (H + 1) * EW], start=True, stop=False)
                i = None
                for c in range(8):
                    lhsT = xp_sb[j][:, c * 128 : (c + 1) * 128]
                    eng.matmul(qps[:], lhsT, wq_sb[:, c * EW : (c + 1) * EW], start=False, stop=(c == 7))
                    i = eng.matmul(kps[:], lhsT, wk_sb[:, c * EW : (c + 1) * EW], start=False, stop=(c == 7))
                i.then_inc(b.sem("proj"), 1)
            b.on("tensor", mm)

            s1_need = b.semv.get("s1mm", 0)
            qevac_t = b.bump("qevac", 1)

            def ev(eng, p=p, j=j, H=H, pt=proj_t, sn=s1_need):
                eng.wait_ge(b.sem("proj"), pt)
                if p >= 2 or H >= 1:
                    eng.wait_ge(b.sem("s1mm"), sn)
                eng.tensor_copy(qbuf[j][:], qps[:])
                eng.tensor_copy(kbuf[j][:], kps[:]).then_inc(b.sem("qevac"), 1)
            b.on("vector", ev)

            s1evac_need = b.semv.get("s1evac", 0)
            s1_t = b.bump("s1mm", 1)

            def s1(eng, p=p, j=j, H=H, qt=qevac_t, en=s1evac_need):
                eng.wait_ge(b.sem("qevac"), qt)
                if p >= 1 or H >= 1:
                    eng.wait_ge(b.sem("s1evac"), en)
                eng.matmul(s1ps[0][:], c1re_sb[:], qbuf[j][:], start=True, stop=True)
                eng.matmul(s1ps[1][:], c1im_sb[:], qbuf[j][:], start=True, stop=True)
                eng.matmul(s1ps[2][:], c1re_sb[:], kbuf[j][:], start=True, stop=True)
                eng.matmul(s1ps[3][:], c1im_sb[:], kbuf[j][:], start=True, stop=True).then_inc(b.sem("s1mm"), 1)
            b.on("tensor", s1)

            corner_need = b.semv.get("corner", 0)
            s1evac_t = b.bump("s1evac", 1)

            def ev1(eng, p=p, j=j, H=H, st=s1_t, cn=corner_need):
                eng.wait_ge(b.sem("s1mm"), st)
                if p >= 2 or H >= 1:
                    eng.wait_ge(b.sem("corner"), cn)
                for c in range(4):
                    i = eng.tensor_copy(s1buf[j][c][:], s1ps[c][:])
                i.then_inc(b.sem("s1evac"), 1)
            b.on("vector", ev1)

            if "qbuf" in dump and H == dump_slice and p == 5:
                def dqb(eng, j=j, t=b.semv["qevac"]):
                    eng.wait_ge(b.sem("qevac"), t)
                    eng.dma_start(out=dbg["qbuf"], in_=qbuf[j][:]).then_inc(b.sem("dbg"), 16)
                b.bump("dbg", 16)
                b.on("gpsimd", dqb)

            if "s1buf" in dump and H == dump_slice and p == 5:
                def dsb(eng, j=j, t=b.semv["s1evac"]):
                    eng.wait_ge(b.sem("s1evac"), t)
                    eng.dma_start(out=dbg["s1buf"][0], in_=s1buf[j][0][:]).then_inc(b.sem("dbg"), 16)
                    eng.dma_start(out=dbg["s1buf"][1], in_=s1buf[j][1][:]).then_inc(b.sem("dbg"), 16)
                b.bump("dbg", 32)
                b.on("gpsimd", dsb)

            corner_t = b.bump("corner", 16 * 8)

            def corner(eng, p=p, j=j, H=H, et=s1evac_t):
                eng.wait_ge(b.sem("s1evac"), et)
                for (c, rb, base) in ((0, rbq, 0), (1, rbq, 64), (2, rbk, 0), (3, rbk, 64)):
                    for h in (0, 1):
                        src = s1buf[j][c][h * 64 : h * 64 + NF2, :]
                        r = base + h * 32 + p
                        eng.dma_start(out=rb[r : r + 1, :], in_=src).then_inc(b.sem("corner"), 16)
            b.on("gpsimd", corner)

            if H < 2 and p % 2 == 1:
                q = H * 16 + p // 2
                b.bump("xcast", 16)

                def xck(eng, q=q):
                    eng.dma_start(out=xpair_bf_d[q], in_=xpair[q]).then_inc(b.sem("xcast"), 16)
                b.on("gpsimd", xck)

        corner_all = b.semv["corner"]
        if "rbq" in dump and H == dump_slice:
            def drb(eng, ca=corner_all):
                eng.wait_ge(b.sem("corner"), ca)
                eng.dma_start(out=dbg["rbq"], in_=rbq[:]).then_inc(b.sem("dbg"), 16)
            b.bump("dbg", 16)
            b.on("gpsimd", drb)

        for f2 in range(NF2):
            pprod_need = b.semv.get("pprod", 0)
            st2_t = b.bump("st2mm", 1)

            def st2(eng, f2=f2, H=H, ca=corner_all, pn=pprod_need):
                eng.wait_ge(b.sem("corner"), ca)
                if not (H == 0 and f2 == 0):
                    eng.wait_ge(b.sem("pprod"), pn)
                L = st2_sb[:, f2 * 128 : (f2 + 1) * 128]
                eng.matmul(f2ps[0][:], L, rbq[:, f2 * EW : (f2 + 1) * EW], start=True, stop=True)
                eng.matmul(f2ps[1][:], L, rbk[:, f2 * EW : (f2 + 1) * EW], start=True, stop=True).then_inc(b.sem("st2mm"), 1)
            b.on("tensor", st2)

            if "qf" in dump and H == 0 and f2 == 0:
                def dqf(eng, st=st2_t):
                    eng.wait_ge(b.sem("st2mm"), st)
                    eng.tensor_copy(u_sb[0:NF2, :] if False else qbuf[0][:], f2ps[0][:])
                    eng.tensor_copy(kbuf[0][:], f2ps[1][:]).then_inc(b.sem("dbgv"), 1)
                b.bump("dbgv", 1)
                b.on("vector", dqf)

                def dqf2(eng, t=b.semv["dbgv"]):
                    eng.wait_ge(b.sem("dbgv"), t)
                    eng.dma_start(out=dbg["qf"][0], in_=qbuf[0][:]).then_inc(b.sem("dbg"), 16)
                    eng.dma_start(out=dbg["qf"][1], in_=kbuf[0][:]).then_inc(b.sem("dbg"), 16)
                b.bump("dbg", 32)
                b.on("gpsimd", dqf2)

            pprod_t = b.bump("pprod", 1)

            def pp(eng, f2=f2, H=H, st=st2_t):
                eng.wait_ge(b.sem("st2mm"), st)
                eng.tensor_copy(qf_sb[:], f2ps[0][:])
                qre = qf_sb[0:64, :]
                qim = qf_sb[64:128, :]
                kre = f2ps[1][0:64, :]
                kim = f2ps[1][64:128, :]
                col = lambda k: pparts[0:64, f2 * 4 + k : f2 * 4 + k + 1]
                pairs = ((qre, kre), (qim, kim), (qim, kre), (qre, kim))
                if H == 0 or DBG_NOACC:
                    for k, (a, bb) in enumerate(pairs):
                        eng.tensor_tensor(ttr_scr[:], a, bb, ALU.mult)
                        i = eng.reduce_sum(col(k), ttr_scr[:], axis=AX.X)
                    i.then_inc(b.sem("pprod"), 1)
                else:
                    for k, (a, bb) in enumerate(pairs):
                        eng.tensor_tensor(ttr_scr[:], a, bb, ALU.mult)
                        eng.reduce_sum(pscr[0:64, k : k + 1], ttr_scr[:], axis=AX.X)
                    for k in range(4):
                        i = eng.tensor_add(col(k), col(k), pscr[0:64, k : k + 1])
                    i.then_inc(b.sem("pprod"), 1)
            b.on("vector", pp)

        if "pparts" in dump and H == dump_slice:
            def dppp(eng, t=b.semv["pprod"]):
                eng.wait_ge(b.sem("pprod"), t)
                eng.dma_start(out=dbg["pparts"], in_=pparts[:]).then_inc(b.sem("dbg"), 16)
            b.bump("dbg", 16)
            b.on("gpsimd", dppp)

    pscr = b.sb("pscr", [64, 4])
    ttr_scr = b.sb("ttr_scr", [64, EW])
    qf_sb = b.sb("qf_sb", [128, EW])

    pprod_all = b.semv["pprod"]

    def pcomb(eng, pa=pprod_all):
        eng.wait_ge(b.sem("pprod"), pa)
        v = pparts[0:64, :].rearrange("p (f k) -> p f k", k=4)
        eng.tensor_add(ppack[0:64, :], v[:, :, 0], v[:, :, 1])
        eng.tensor_sub(ppack[64:128, :], v[:, :, 2], v[:, :, 3]).then_inc(b.sem("pcomb"), 1)
    b.bump("pcomb", 1)
    b.on("vector", pcomb)
    if "ppack" in dump:
        def dpp(eng, t=b.semv["pcomb"]):
            eng.wait_ge(b.sem("pcomb"), t)
            eng.dma_start(out=dbg["ppack"], in_=ppack[:]).then_inc(b.sem("dbg"), 16)
        b.bump("dbg", 16)
        b.on("gpsimd", dpp)

    uaps = f2ps[0][0:NF2, 0:128]
    macps = f2ps[1][0:64, 0:64]

    def macA(eng, t=b.semv["pcomb"]):
        eng.wait_ge(b.sem("pcomb"), t)
        eng.matmul(uaps, ppack[:], macA_sb[:], start=True, stop=True).then_inc(b.sem("macA"), 1)
    b.bump("macA", 1)
    b.on("tensor", macA)

    def mactw(eng, t=b.semv["macA"]):
        eng.wait_ge(b.sem("macA"), t)
        eng.tensor_copy(u_sb[:], uaps)
        ure, uim = u_sb[:, 0:64], u_sb[:, 64:128]
        twre_r, twre_i = mtw_sb[:, 0:64], mtw_sb[:, 128:192]
        eng.tensor_tensor(up_sb[:, 0:64], ure, twre_r, ALU.mult)
        eng.tensor_tensor(pscr2[:, 0:64], uim, twre_i, ALU.mult)
        eng.tensor_sub(up_sb[:, 0:64], up_sb[:, 0:64], pscr2[:, 0:64])
        eng.tensor_tensor(up_sb[:, 64:128], ure, twre_i, ALU.mult)
        eng.tensor_tensor(pscr2[:, 64:128], uim, twre_r, ALU.mult)
        eng.tensor_add(up_sb[:, 64:128], up_sb[:, 64:128], pscr2[:, 64:128]).then_inc(b.sem("mactw"), 1)
    pscr2 = b.sb("pscr2", [NF2, 128])
    b.bump("mactw", 1)
    b.on("vector", mactw)

    def macB(eng, t=b.semv["mactw"]):
        eng.wait_ge(b.sem("mactw"), t)
        eng.matmul(macps, macB_sb[:, 0:64], up_sb[:, 0:64], start=True, stop=False)
        eng.matmul(macps, macB_sb[:, 64:128], up_sb[:, 64:128], start=False, stop=True).then_inc(b.sem("macB"), 1)
    b.bump("macB", 1)
    b.on("tensor", macB)

    def macev(eng, t=b.semv["macB"]):
        eng.wait_ge(b.sem("macB"), t)
        eng.tensor_copy(mac_sb[:], macps).then_inc(b.sem("macev"), 1)
    b.bump("macev", 1)
    b.on("vector", macev)

    b.bump("outd", 16)
    dbg_total = b.semv.get("dbg", 0)

    def macout2(eng, t=b.semv["macev"], dt=dbg_total):
        eng.wait_ge(b.sem("macev"), t)
        eng.dma_start(out=mac_out, in_=mac_sb[:]).then_inc(b.sem("outd"), 16)
        eng.wait_ge(b.sem("outd"), 16)
        if dt:
            eng.wait_ge(b.sem("dbg"), dt)
    b.on("gpsimd", macout2)

    if upto == "mac":
        return b

    mac_all = pool[0:8, 0:4096]
    workA = pool[0:8, 4096:8192]
    workB = pool[0:8, 8192:12288]
    wm = pool[0:8, 12288:16384]
    wmp = pool[:, 25344:25600]
    rbw = pool[:, 25600:25864]
    m8 = b.sb("m8", [8, 8])
    mscr = pool[0:8, 21120:25216]
    s1w_ev = b.sb("s1w_ev", [128, 16])
    gt_re = b.sb("gt_re", [64, 8 * NF2])
    gt_im = b.sb("gt_im", [64, 8 * NF2])

    if use_cc:
        cc_in = b.nc.dram_tensor("cc_in", [1, 4096], F32)
        cc_out = b.nc.dram_tensor("cc_out", [8, 4096], F32)
        nrep_t = b.bump("macrep", 16 * 2)
        b.bump("ccd", 1)

        def macrep(eng, t=b.semv["macev"]):
            eng.wait_ge(b.sem("macev"), t)
            eng.dma_start(out=cc_in.ap().rearrange("q (a c) -> (q a) c", a=64), in_=mac_sb[:]).then_inc(b.sem("macrep"), 16)
            eng.wait_ge(b.sem("macrep"), 16)
            eng.collective_compute(
                "AllGather", ALU.bypass,
                replica_groups=[list(range(8))],
                ins=[cc_in.ap().opt()], outs=[cc_out.ap().opt()],
            ).then_inc(b.sem("ccd"), 1)
            eng.wait_ge(b.sem("ccd"), 1)
            eng.dma_start(out=mac_all, in_=cc_out.ap()).then_inc(b.sem("macrep"), 16)
        b.on("gpsimd", macrep)
    else:
        nrep_t = b.bump("macrep", 16 * 8)

        def macrep(eng, t=b.semv["macev"]):
            eng.wait_ge(b.sem("macev"), t)
            for bb in range(8):
                eng.dma_start(out=mac_all[bb : bb + 1, :], in_=mac_sb[:]).then_inc(b.sem("macrep"), 16)
        b.on("gpsimd", macrep)

    _tks0 = [b.semv.get("tks", 0)]
    b.bump("tks", 9)
    tk_t = b.bump("topk", 1)

    def topk(eng, t=nrep_t):
        eng.wait_ge(b.sem("macrep"), t)
        wtmp = wm
        bufs = [mac_all, workA, workB, wtmp]
        for r in range(3):
            src, dst = bufs[r], bufs[r + 1]
            eng.max(out=m8[:], in_=src).then_inc(b.sem("tks"), 1)
            eng.wait_ge(b.sem("tks"), b.bump("tks", 0) if False else 3 * r + 1 + _tks0[0])
            eng.tensor_single_scalar(mscr, src, m8[:, 7:8], ALU.is_ge)
            eng.scalar_tensor_tensor(dst, mscr, -1e30, src, ALU.mult, ALU.add).then_inc(b.sem("tks"), 2)
            eng.wait_ge(b.sem("tks"), 3 * r + 3 + _tks0[0])
        eng.tensor_single_scalar(mscr, bufs[3], -1e29, ALU.is_le)
        eng.tensor_mul(wm, mscr, mac_all).then_inc(b.sem("topk"), 1)
    b.on("vector", topk)

    for p in range(32):
        wtr_t = b.bump("wtr", 1)

        def wtr(eng, p=p, t=tk_t, ev=b.semv.get("wtrev", 0)):
            eng.wait_ge(b.sem("topk"), t)
            if p > 0:
                eng.wait_ge(b.sem("wtrev"), ev)
            wmv = wm.rearrange("q (s2 s1) -> q s1 s2", s1=64)
            eng.matmul(s1ps[3][0:64, 0:8], wmv[:, 2 * p, :], ident_sb[0:8, 0:8], start=True, stop=True)
            eng.matmul(s1ps[3][64:128, 0:8], wmv[:, 2 * p + 1, :], ident_sb[0:8, 0:8], start=True, stop=True).then_inc(b.sem("wtr"), 1)
        b.on("tensor", wtr)

        wtrev_t = b.bump("wtrev", 1)

        def wtrev(eng, p=p, t=wtr_t):
            eng.wait_ge(b.sem("wtr"), t)
            eng.tensor_copy(wmp[:, p * 8 : (p + 1) * 8], s1ps[3][:, 0:8]).then_inc(b.sem("wtrev"), 1)
        b.on("vector", wtrev)
    wml_t = b.semv["wtrev"]

    for p in range(32):
        ws1_t = b.bump("ws1", 1)

        def ws1(eng, p=p, t=wml_t, ev=b.semv.get("ws1ev", 0)):
            eng.wait_ge(b.sem("wtrev"), t)
            if p > 0:
                eng.wait_ge(b.sem("ws1ev"), ev)
            eng.matmul(s1ps[0][:, 0:8], c1re_sb[:], wmp[:, p * 8 : (p + 1) * 8], start=True, stop=True)
            eng.matmul(s1ps[1][:, 0:8], c1im_sb[:], wmp[:, p * 8 : (p + 1) * 8], start=True, stop=True).then_inc(b.sem("ws1"), 1)
        b.on("tensor", ws1)

        ws1ev_t = b.bump("ws1ev", 1)

        def ws1ev(eng, p=p, t=ws1_t, cn=b.semv.get("wcorner", 0)):
            eng.wait_ge(b.sem("ws1"), t)
            if p > 0:
                eng.wait_ge(b.sem("wcorner"), cn)
            eng.tensor_copy(s1w_ev[:, 0:8], s1ps[0][:, 0:8])
            eng.tensor_copy(s1w_ev[:, 8:16], s1ps[1][:, 0:8]).then_inc(b.sem("ws1ev"), 1)
        b.on("vector", ws1ev)

        wc_t = b.bump("wcorner", 16 * 4)

        def wcorner(eng, p=p, t=ws1ev_t):
            eng.wait_ge(b.sem("ws1ev"), t)
            for (c0, base) in ((0, 0), (8, 64)):
                for h in (0, 1):
                    src = s1w_ev[h * 64 : h * 64 + NF2, c0 : c0 + 8]
                    r = base + h * 32 + p
                    eng.dma_start(out=rbw[r : r + 1, :].rearrange("q (f m) -> q f m", f=NF2), in_=src).then_inc(b.sem("wcorner"), 16)
        b.on("gpsimd", wcorner)

    wall = b.semv["wcorner"]
    for f2 in range(NF2):
        wg_t = b.bump("wg", 1)

        def wg(eng, f2=f2, t=wall, ev=b.semv.get("wgev", 0)):
            eng.wait_ge(b.sem("wcorner"), t)
            if f2 > 0:
                eng.wait_ge(b.sem("wgev"), ev)
            eng.matmul(s1ps[2][:, 0:8], st2_sb[:, f2 * 128 : (f2 + 1) * 128], rbw[:, f2 * 8 : (f2 + 1) * 8], start=True, stop=True).then_inc(b.sem("wg"), 1)
        b.on("tensor", wg)

        wgev_t = b.bump("wgev", 1)

        def wgev(eng, f2=f2, t=wg_t):
            eng.wait_ge(b.sem("wg"), t)
            dst_re = gt_re[:, f2::NF2]
            dst_im = gt_im[:, f2::NF2]
            eng.tensor_copy(dst_re, s1ps[2][0:64, 0:8])
            eng.tensor_scalar_mul(dst_im, s1ps[2][64:128, 0:8], -1.0).then_inc(b.sem("wgev"), 1)
        b.on("vector", wgev)

    if upto == "topk":
        dbg_total2 = b.semv.get("dbg", 0)
        wm_out = b.dram_out("wm", [8, 4096])
        wk_out = b.dram_out("wk_dbg", [8, 4096])

        def tkout(eng, t=b.semv["wgev"], dt=dbg_total2):
            eng.wait_ge(b.sem("wgev"), t)
            eng.dma_start(out=wk_out, in_=workA).then_inc(b.sem("outd2"), 16)
            eng.dma_start(out=wm_out, in_=wm).then_inc(b.sem("outd2"), 16)
            eng.dma_start(out=g_out[0], in_=gt_re[:]).then_inc(b.sem("outd2"), 16)
            eng.dma_start(out=g_out[1], in_=gt_im[:]).then_inc(b.sem("outd2"), 16)
            eng.wait_ge(b.sem("outd2"), b.semv["outd2"])
            if dt:
                eng.wait_ge(b.sem("dbg"), dt)
        b.bump("outd2", 64)
        b.on("gpsimd", tkout)
        return b

    aggfm_d = b.nc.dram_tensor("aggfm", [1024, 4096], BF16, kind="Internal").ap()
    rbv = bfat("rbv", [128, 8448], 0)
    vfre = bfat("vfre", [64, 8448], 4224)
    vfim = bfat("vfim", [64, 8448], 8448)
    agre = bfat("agre", [64, 8448], 12672)
    agim = bfat("agim", [64, 8448], 16896)
    gsc1 = bfat("gsc1", [64, 2112], 21120)
    gsc2 = bfat("gsc2", [64, 2112], 22176)
    rbB = bfat("rbB", [66, 16384], 4224)
    aggt = bfat("aggt", [64, 16384], 12672)
    invAre_sb = b.sb("invAre_sb", [64, 4224], BF16)
    invAim_sb = b.sb("invAim_sb", [64, 4224], BF16)
    st2vre_sb = bfat("st2vre_sb", [128, 2112], 23232)
    st2vim_sb = bfat("st2vim_sb", [128, 2112], 24288)
    gt_re_bf = bfat("gt_re_bf", [64, 264], 25864)
    gt_im_bf = bfat("gt_im_bf", [64, 264], 26000)

    xpv = [b.sb(f"xpv{j}", [128, E], BF16) for j in range(2)]
    vbuf = [b.sb(f"vbuf{j}", [128, EW], BF16) for j in range(2)]
    vs1ev = [b.sb(f"vs1ev{j}", [128, 2 * EW], BF16) for j in range(2)]
    wv_sb = b.sb("wv_sb", [128, 8 * EW], BF16)
    ones_bf = b.sb("ones_bf", [1, 512], BF16)
    bv_sb = b.sb("bv_sb", [1, E])
    bv_bf = b.sb("bv_bf", [1, E], BF16)
    c1rebf_sb = b.sb("c1rebf_sb", [128, 128], BF16)
    c1imbf_sb = b.sb("c1imbf_sb", [128, 128], BF16)
    invB_sb = b.sb("invB_sb", [66, 64], BF16)
    uaev = b.sb("uaev", [128, EW], BF16)
    uaT = b.sb("uaT", [128, EW], BF16)

    vb_t = b.bump("vboot", 16 * 5)

    def vboot(eng, t=b.semv["topk"]):
        eng.wait_ge(b.sem("topk"), t)
        eng.dma_start(out=st2vre_sb[:], in_=st2vre_d.rearrange("(f p) m -> p f m", p=128)).then_inc(b.sem("vboot"), 16)
        eng.dma_start(out=st2vim_sb[:], in_=st2vim_d.rearrange("(f p) m -> p f m", p=128)).then_inc(b.sem("vboot"), 16)
        eng.dma_start(out=c1rebf_sb[:], in_=c1rebf_d).then_inc(b.sem("vboot"), 16)
        eng.dma_start(out=c1imbf_sb[:], in_=c1imbf_d).then_inc(b.sem("vboot"), 16)
        eng.dma_start(out=bv_sb[:], in_=bv_d).then_inc(b.sem("vboot"), 16)
        eng.dma_start(out=invAre_sb[:], in_=invAre_d.rearrange("(f p) m -> p f m", p=64)).then_inc(b.sem("iaload"), 16)
        eng.dma_start(out=invAim_sb[:], in_=invAim_d.rearrange("(f p) m -> p f m", p=64)).then_inc(b.sem("iaload"), 16)
    b.bump("iaload", 32)
    b.on("sync", vboot)

    def vboot2(eng, t=vb_t):
        eng.wait_ge(b.sem("vboot"), t)
        eng.memset(ones_bf[:], 1.0)
        eng.tensor_copy(bv_bf[:], bv_sb[:]).then_inc(b.sem("vboot2"), 1)
    vb2_t = b.bump("vboot2", 1)
    b.on("vector", vboot2)

    def gtbf(eng, t=b.semv["wgev"], t2=vb2_t):
        eng.wait_ge(b.sem("wgev"), t)
        eng.wait_ge(b.sem("vboot2"), t2 - 1 if False else t2)
        eng.tensor_copy(gt_re_bf[:], gt_re[:])
        eng.tensor_copy(gt_im_bf[:], gt_im[:]).then_inc(b.sem("gtbf"), 1)
    gtbf_t = b.bump("gtbf", 1)
    b.on("vector", gtbf)

    for H in range(NH):
        wvl_t = b.bump("wvl", 16)

        def wvl(eng, H=H, t=vb_t, pv=b.semv.get("vproj", 0)):
            eng.wait_ge(b.sem("vboot"), t)
            if H > 0:
                eng.wait_ge(b.sem("vproj"), pv)
            src = Wv_d.rearrange("(c p) o -> p c o", p=128)[:, :, H * EW : (H + 1) * EW]
            eng.dma_start(out=wv_sb[:], in_=src).then_inc(b.sem("wvl"), 16)
        b.on("gpsimd", wvl)

        for p in range(32):
            j = p % 2
            vlsem = f"vload{j}"
            vload_t = b.bump(vlsem, 16)

            def vload(eng, p=p, j=j, H=H, pv=b.semv.get("vproj", 0), vlsem=vlsem):
                if H == 0:
                    eng.wait_ge(b.sem("xcast"), 16 * (p + 1))
                if p >= 2 or H >= 1:
                    eng.wait_ge(b.sem("vproj"), pv)
                eng.dma_start(out=xpv[j][:], in_=xpair_bf_d[p]).then_inc(b.sem(vlsem), 16)
            b.on("sync" if p % 2 == 0 else "scalar", vload)

            vproj_t = b.bump("vproj", 1)

            def vproj(eng, p=p, j=j, H=H, lt=vload_t, wt=wvl_t, ev=b.semv.get("vevac", 0), vlsem=vlsem):
                eng.wait_ge(b.sem(vlsem), lt)
                eng.wait_ge(b.sem("wvl"), wt)
                if p == 0 and H == 0:
                    eng.wait_ge(b.sem("vboot2"), 1)
                if (p >= 2 or H >= 1) and ev >= 1:
                    eng.wait_ge(b.sem("vevac"), ev - 1)
                vp = [qps, kps][p % 2]
                eng.matmul(vp[:], ones_bf[0:1, 0:128], bv_bf[0:1, H * EW : (H + 1) * EW], start=True, stop=False)
                i = None
                for c in range(8):
                    i = eng.matmul(vp[:], xpv[j][:, c * 128 : (c + 1) * 128], wv_sb[:, c * EW : (c + 1) * EW], start=False, stop=(c == 7))
                i.then_inc(b.sem("vproj"), 1)
            b.on("tensor", vproj)

            vevac_t = b.bump("vevac", 1)

            def vevac(eng, p=p, j=j, t=vproj_t, sn=b.semv.get("vs1", 0)):
                eng.wait_ge(b.sem("vproj"), t)
                if p >= 2 or H >= 1:
                    eng.wait_ge(b.sem("vs1"), sn)
                eng.tensor_copy(vbuf[j][:], [qps, kps][p % 2][:]).then_inc(b.sem("vevac"), 1)
            b.on("vector", vevac)

            vs1_t = b.bump("vs1", 1)

            def vs1(eng, p=p, j=j, t=vevac_t, en=b.semv.get("vs1ev", 0)):
                eng.wait_ge(b.sem("vevac"), t)
                if (p >= 2 or H >= 1) and en >= 1:
                    eng.wait_ge(b.sem("vs1ev"), en - 1)
                pr = s1ps[0 + 2 * (p % 2)]
                pi = s1ps[1 + 2 * (p % 2)]
                eng.matmul(pr[:], c1rebf_sb[:], vbuf[j][:], start=True, stop=True)
                eng.matmul(pi[:], c1imbf_sb[:], vbuf[j][:], start=True, stop=True).then_inc(b.sem("vs1"), 1)
            b.on("tensor", vs1)

            vs1ev_t = b.bump("vs1ev", 1)

            def vs1e(eng, p=p, j=j, t=vs1_t, cn=b.semv.get("vcorner", 0)):
                eng.wait_ge(b.sem("vs1"), t)
                if p >= 2 or H >= 1:
                    eng.wait_ge(b.sem("vcorner"), cn)
                pr = s1ps[0 + 2 * (p % 2)]
                pi = s1ps[1 + 2 * (p % 2)]
                eng.tensor_copy(vs1ev[j][:, 0:EW], pr[:])
                eng.tensor_copy(vs1ev[j][:, EW : 2 * EW], pi[:]).then_inc(b.sem("vs1ev"), 1)
            b.on("vector", vs1e)

            vc_t = b.bump("vcorner", 16 * 4)

            def vcorner(eng, p=p, j=j, t=vs1ev_t):
                eng.wait_ge(b.sem("vs1ev"), t)
                for (c0, base) in ((0, 0), (EW, 64)):
                    for h in (0, 1):
                        src = vs1ev[j][h * 64 : h * 64 + NF2, c0 : c0 + EW]
                        r = base + h * 32 + p
                        eng.dma_start(out=rbv[r : r + 1, :], in_=src).then_inc(b.sem("vcorner"), 16)
            b.on("sync" if p % 2 == 0 else "scalar", vcorner)

        if p5_stop == "pairs":
            continue
        vc_all = b.semv["vcorner"]
        for f2 in range(NF2):
            vst2_t = b.bump("vst2", 1)

            def vst2(eng, f2=f2, t=vc_all, ev=b.semv.get("vfev", 0)):
                eng.wait_ge(b.sem("vcorner"), t)
                if not (f2 == 0):
                    eng.wait_ge(b.sem("vfev"), ev)
                eng.matmul(f2ps[0][0:64, :], st2vre_sb[:, f2 * 64 : (f2 + 1) * 64], rbv[:, f2 * EW : (f2 + 1) * EW], start=True, stop=True)
                eng.matmul(f2ps[1][0:64, :], st2vim_sb[:, f2 * 64 : (f2 + 1) * 64], rbv[:, f2 * EW : (f2 + 1) * EW], start=True, stop=True).then_inc(b.sem("vst2"), 1)
            b.on("tensor", vst2)

            vfev_t = b.bump("vfev", 1)

            def vfev(eng, f2=f2, t=vst2_t):
                eng.wait_ge(b.sem("vst2"), t)
                eng.tensor_copy(vfre[:, f2 * EW : (f2 + 1) * EW], f2ps[0][0:64, :])
                eng.tensor_copy(vfim[:, f2 * EW : (f2 + 1) * EW], f2ps[1][0:64, :]).then_inc(b.sem("vfev"), 1)
            b.on("vector", vfev)

        if p5_stop == "vst2":
            continue
        vf_all = b.semv["vfev"]
        gm_t = b.bump("gmul", 1)

        def gmul(eng, H=H, t=vf_all, gt=gtbf_t, ac=b.semv.get("aggc", 0)):
            eng.wait_ge(b.sem("vfev"), t)
            eng.wait_ge(b.sem("gtbf"), gt)
            if ac:
                eng.wait_ge(b.sem("aggc"), ac)
            for jj in range(4):
                m = (H * 4 + jj) % 8
                gr = gt_re_bf[:, m * NF2 : (m + 1) * NF2].unsqueeze(2).broadcast_to([64, NF2, 64])
                gi = gt_im_bf[:, m * NF2 : (m + 1) * NF2].unsqueeze(2).broadcast_to([64, NF2, 64])
                sl = lambda tns: tns[:, jj * 64 : 8448 : EW].rearrange("p (f e) -> p f e", f=NF2) if False else tns.rearrange("p (f e) -> p f e", e=EW)[:, :, jj * 64 : (jj + 1) * 64]
                vre = sl(vfre)
                vim = sl(vfim)
                are = sl(agre)
                aim = sl(agim)
                g1 = gsc1[:].rearrange("p (f e) -> p f e", e=64)
                g2 = gsc2[:].rearrange("p (f e) -> p f e", e=64)
                eng.tensor_tensor(g1, gr, vre, ALU.mult)
                eng.tensor_tensor(g2, gi, vim, ALU.mult)
                eng.tensor_sub(are, g1, g2)
                eng.tensor_tensor(g1, gr, vim, ALU.mult)
                eng.tensor_tensor(g2, gi, vre, ALU.mult)
                eng.tensor_add(aim, g1, g2)
            eng.engine_nop().then_inc(b.sem("gmul"), 1)
        b.on("vector", gmul)

        if p5_stop == "gmul":
            continue

        for f2 in range(NF2):
            ia_mm_t = b.bump("iamm", 1)

            def iamm(eng, f2=f2, H=H, t=gm_t, ev=b.semv.get("iaev", 0), tv=b.semv.get("iatrev", 0)):
                eng.wait_ge(b.sem("gmul"), t)
                eng.wait_ge(b.sem("iaload"), 32)
                if not (H == 0 and f2 == 0):
                    eng.wait_ge(b.sem("iaev"), ev)
                eng.matmul(kps[:], invAre_sb[:, f2 * 128 : (f2 + 1) * 128], agre[:, f2 * EW : (f2 + 1) * EW], start=True, stop=False)
                eng.matmul(kps[:], invAim_sb[:, f2 * 128 : (f2 + 1) * 128], agim[:, f2 * EW : (f2 + 1) * EW], start=False, stop=True).then_inc(b.sem("iamm"), 1)
            b.on("tensor", iamm)

            iaev_t = b.bump("iaev", 1)

            def iaev(eng, f2=f2, t=ia_mm_t, tn=b.semv.get("iatr", 0)):
                eng.wait_ge(b.sem("iamm"), t)
                if tn:
                    eng.wait_ge(b.sem("iatr"), tn)
                eng.tensor_copy(uaev[:], kps[:]).then_inc(b.sem("iaev"), 1)
            b.on("vector", iaev)

            if p5_stop == "iamm1":
                continue
            iatr_t = b.bump("iatr", 1)

            def iatr(eng, f2=f2, t=iaev_t, en=b.semv.get("iatrev", 0)):
                eng.wait_ge(b.sem("iaev"), t)
                if en:
                    eng.wait_ge(b.sem("iatrev"), en)
                tb = [s1ps[0], s1ps[1], f2ps[0], f2ps[1]]
                i = None
                for c in range(2):
                    for h in range(2):
                        i = eng.matmul(tb[c * 2 + h][:, 0:64],
                                       uaev[c * 64 : (c + 1) * 64, h * 128 : (h + 1) * 128],
                                       identbf_sb[c * 64 : (c + 1) * 64, :], start=True, stop=True)
                i.then_inc(b.sem("iatr"), 1)
            b.on("tensor", iatr)

            iatrev_t = b.bump("iatrev", 1)

            def iatrev(eng, f2=f2, t=iatr_t, cn=b.semv.get("iac", 0)):
                eng.wait_ge(b.sem("iatr"), t)
                if cn:
                    eng.wait_ge(b.sem("iac"), cn)
                tb = [s1ps[0], s1ps[1], f2ps[0], f2ps[1]]
                for q in range(4):
                    i = eng.tensor_copy(uaT[:, q * 64 : (q + 1) * 64], tb[q][:, 0:64])
                i.then_inc(b.sem("iatrev"), 1)
            b.on("vector", iatrev)

            if p5_stop == "iamm2":
                continue
            iac_t = b.bump("iac", 16 * 4)

            def iac(eng, f2=f2, t=iatrev_t):
                eng.wait_ge(b.sem("iatrev"), t)
                for c in range(2):
                    for h in range(2):
                        src = uaT[:, (c * 2 + h) * 64 : (c * 2 + h) * 64 + 64]
                        dst = rbB[c * NF2 + f2 : c * NF2 + f2 + 1, h * 8192 : (h + 1) * 8192]
                        eng.dma_start(out=dst, in_=src).then_inc(b.sem("iac"), 16)
            b.on("sync" if f2 % 2 == 0 else "scalar", iac)

        if p5_stop in ("iamm", "iamm1", "iamm2"):
            continue
        iac_all = b.semv["iac"]
        if H == 0:
            def ibload(eng, t=vb_t):
                eng.wait_ge(b.sem("vboot"), t)
                eng.dma_start(out=invB_sb[:], in_=invB_d).then_inc(b.sem("ibload"), 16)
            b.bump("ibload", 16)
            b.on("sync", ibload)

        for ch in range(64):
            sb_t = b.bump("stB", 1)

            def stB(eng, ch=ch, t=iac_all, ev=b.semv.get("stBev", 0)):
                eng.wait_ge(b.sem("iac"), t)
                eng.wait_ge(b.sem("ibload"), 16)
                if not (ch == 0):
                    eng.wait_ge(b.sem("stBev"), ev)
                eng.matmul(s1ps[2][0:64, :], invB_sb[:], rbB[:, ch * EW : (ch + 1) * EW], start=True, stop=True).then_inc(b.sem("stB"), 1)
            b.on("tensor", stB)

            sbe_t = b.bump("stBev", 1)

            def stBev(eng, ch=ch, t=sb_t, cn=b.semv.get("aggc", 0)):
                eng.wait_ge(b.sem("stB"), t)
                if ch == 0 and cn:
                    eng.wait_ge(b.sem("aggc"), cn)
                eng.tensor_copy(aggt[:, ch * EW : (ch + 1) * EW], s1ps[2][0:64, :]).then_inc(b.sem("stBev"), 1)
            b.on("vector", stBev)

        agc_t = b.bump("aggc", 16 * 64)
        sbev_all = b.semv["stBev"]

        def aggc(eng, H=H, t=sbev_all):
            eng.wait_ge(b.sem("stBev"), t)
            for s2 in range(64):
                src = aggt[s2 : s2 + 1, :].rearrange("q (e s1) -> q e s1", s1=64)
                dst = aggfm_d[H * EW : (H + 1) * EW, 64 * s2 : 64 * s2 + 64]
                eng.dma_start(out=dst, in_=src).then_inc(b.sem("aggc"), 16)
        b.on("scalar", aggc)

    Wo_d = b.dram_in("Wo", [E, E])
    bo_d = b.dram_in("bo", [1, E])
    xtm_d = b.dram_in("x_tm", [T, E])
    WfcT_d = b.dram_in("WfcT", [E, 4096])
    WprojT_d = b.dram_in("WprojT", [4096, E])
    Afc_d = b.dram_in("Afc", [16, E])
    BfcT_d = b.dram_in("BfcT", [16, 4096])
    Aproj_d = b.dram_in("Aproj", [16, 4096])
    BprojT_d = b.dram_in("BprojT", [16, E])
    bfc_d = b.dram_in("bfc", [1, 4096])
    bproj_d = b.dram_in("bproj", [1, E])
    out_d = b.dram_out("out", [T, E])
    y1tm_d = b.nc.dram_tensor("y1tm", [T, E], BF16, kind="Internal").ap()
    y1fm_d = b.nc.dram_tensor("y1fm", [E, T], BF16, kind="Internal").ap()
    wefffc_d = b.nc.dram_tensor("wefffc", [E, 4096], BF16, kind="Internal").ap()
    weffpr_d = b.nc.dram_tensor("weffpr", [4096, E], BF16, kind="Internal").ap()

    wo_all = bfat("wo_all", [128, 8192], 5760)
    agg_all = [bfat(f"agg_all{j}", [128, 1024], 9856 + j * 512) for j in range(2)]
    x_tile = pool[:, 1536:2560]
    y1t = pool[:, 2560:3584]
    sqt = pool[:, 3584:4608]
    stat = pool[:, 4608:4640]
    y1bf = bfat("y1bf", [128, 1024], 4640)
    trsp = bfat("trsp", [128, 128], 5152)
    bo_bf = bfat("bo_bf", [1, 1024], 5224)
    afc_sb = pool[:, 5760:6784]
    aproj_sb = pool[:, 6784:10880]
    afc_bf = bfat("afc_bf", [16, 1024], 5248)
    aproj_bf = bfat("aproj_bf", [16, 4096], 26400)
    b2fc = bfat("b2fc", [16, 4096], 10880)
    b2pr = bfat("b2pr", [16, 1024], 12928)
    bfc_bf = bfat("bfc_bf", [1, 4096], 13440)
    bpr_bf = bfat("bpr_bf", [1, 1024], 15488)
    wchunk = pool[:, 16000:20096]
    wchunkI = pool[:, 16000:20096].bitcast(I32)[:, 0:4096]
    scrA = pool[:, 20096:24192]
    scrAI = pool[:, 20096:24192].bitcast(I32)[:, 0:4096]
    weffo = bfat("weffo", [128, 4096], 24192)
    qstat = b.sb("qstat", [128, 8])
    y1f_all = bfat("y1f_all", [128, 4096], 10880)
    gtile = bfat("gtile", [128, 16384], 16000)
    wfc_col = [bfat(f"wfc_col{j}", [128, 1024], 24192 + j * 512) for j in range(2)]
    wpr_buf = [bfat(f"wpr_buf{j}", [128, 1024], j * 512) for j in range(2)]
    y1res = bfat("y1res", [128, 1024], 4640)
    out_tile = pool[:, 1536:2560]
    y2t = pool[:, 2560:3584]
    sq2 = pool[:, 3584:4608]
    bo_sb_f = b.sb("bo_sb_f", [1, E])

    if upto == "agg" and p5_stop is not None:
        return b
    aggc_all = b.semv["aggc"]

    def p6boot(eng, t=aggc_all):
        eng.wait_ge(b.sem("aggc"), t)
        eng.dma_start(out=bo_sb_f[:], in_=bo_d).then_inc(b.sem("p6b"), 16)
    b.bump("p6b", 16)
    b.on("sync", p6boot)

    def p6wo(eng, t=aggc_all):
        eng.wait_ge(b.sem("aggc"), t)
        eng.dma_start(out=wo_all[:], in_=Wo_d.rearrange("(c p) o -> p c o", p=128)).then_inc(b.sem("p6w"), 16)
    b.bump("p6w", 16)
    b.on("gpsimd", p6wo)

    def p6boot2(eng):
        eng.wait_ge(b.sem("p6b"), 16)
        eng.tensor_copy(bo_bf[:], bo_sb_f[:]).then_inc(b.sem("p6b2"), 1)
    b.bump("p6b2", 1)
    b.on("vector", p6boot2)

    for tt in range(32):
        xl_t = b.bump("p6x", 16)

        def p6x(eng, tt=tt, pv=b.semv.get("ln1", 0)):
            if tt == 0:
                eng.wait_ge(b.sem("p6b"), 16)
            else:
                eng.wait_ge(b.sem("ln1"), pv)
            eng.dma_start(out=x_tile, in_=xtm_d[tt * 128 : (tt + 1) * 128, :]).then_inc(b.sem("p6x"), 16)
        b.on("sync", p6x)

        j = tt % 2
        al_t = b.bump(f"p6a{j}", 16)

        def p6a(eng, tt=tt, j=j, pv=b.semv.get("p6mm", 0)):
            if tt >= 2:
                eng.wait_ge(b.sem("p6mm"), pv - 8)
            src = aggfm_d.rearrange("(c p) t -> p c t", p=128)[:, :, tt * 128 : (tt + 1) * 128]
            eng.dma_start(out=agg_all[j][:], in_=src).then_inc(b.sem(f"p6a{j}"), 16)
        b.on("sync", p6a)

        for ec in range(8):
            mm_t = b.bump("p6mm", 1)

            def p6mm(eng, tt=tt, ec=ec, j=j, at=al_t, wt=16, ln=b.semv.get("ln1add", 0)):
                if ec == 0:
                    eng.wait_ge(b.sem(f"p6a{j}"), at)
                    eng.wait_ge(b.sem("p6w"), wt)
                    if tt > 0:
                        eng.wait_ge(b.sem("ln1add"), ln)
                    eng.wait_ge(b.sem("p6b2"), 1)
                    eng.matmul(qps_f[:], ones_bf[0:1, 0:128], bo_bf[0:1, 0:512], start=True, stop=False)
                    eng.matmul(kps_f[:], ones_bf[0:1, 0:128], bo_bf[0:1, 512:1024], start=True, stop=False)
                eng.matmul(qps_f[:], agg_all[j][:, ec * 128 : (ec + 1) * 128], wo_all[:, ec * 1024 : ec * 1024 + 512], start=False, stop=(ec == 7))
                eng.matmul(kps_f[:], agg_all[j][:, ec * 128 : (ec + 1) * 128], wo_all[:, ec * 1024 + 512 : (ec + 1) * 1024], start=False, stop=(ec == 7)).then_inc(b.sem("p6mm"), 1)
            b.on("tensor", p6mm)

        mma = b.semv["p6mm"]
        la_t = b.bump("ln1add", 1)

        def ln1a(eng, tt=tt, t=mma, xt=xl_t):
            eng.wait_ge(b.sem("p6mm"), t)
            eng.wait_ge(b.sem("p6x"), xt)
            eng.tensor_add(y1t[:, 0:512], x_tile[:, 0:512], qps_f[:])
            eng.tensor_add(y1t[:, 512:1024], x_tile[:, 512:1024], kps_f[:]).then_inc(b.sem("ln1add"), 1)
        b.on("vector", ln1a)

        ls_t = b.bump("ln1s", 1)

        def ln1s(eng, tt=tt, t=la_t):
            eng.wait_ge(b.sem("ln1add"), t)
            eng.tensor_tensor(sqt, y1t, y1t, ALU.mult)
            eng.reduce_sum(stat[:, 0:1], y1t, axis=AX.X)
            eng.reduce_sum(stat[:, 1:2], sqt, axis=AX.X)
            eng.tensor_scalar_mul(stat[:, 2:3], stat[:, 0:1], 1.0 / 1024.0)
            eng.tensor_scalar_mul(stat[:, 3:4], stat[:, 1:2], 1.0 / 1024.0)
            eng.tensor_tensor(stat[:, 4:5], stat[:, 2:3], stat[:, 2:3], ALU.mult)
            eng.tensor_sub(stat[:, 5:6], stat[:, 3:4], stat[:, 4:5])
            eng.tensor_scalar_add(stat[:, 6:7], stat[:, 5:6], 1e-5).then_inc(b.sem("ln1s"), 1)
        b.on("vector", ln1s)

        lq_t = b.bump("ln1q", 1)

        def ln1q(eng, tt=tt, t=ls_t):
            eng.wait_ge(b.sem("ln1s"), t)
            eng.activation(out=stat[:, 7:8], in_=stat[:, 6:7], func=AF.Sqrt, bias=0.0, scale=1.0).then_inc(b.sem("ln1q"), 1)
        b.on("scalar", ln1q)

        lr_t = b.bump("ln1r", 1)

        def ln1r(eng, tt=tt, t=lq_t):
            eng.wait_ge(b.sem("ln1q"), t)
            eng.reciprocal(stat[:, 0:1], stat[:, 7:8])
            eng.tensor_tensor(stat[:, 1:2], stat[:, 2:3], stat[:, 0:1], ALU.mult)
            eng.tensor_scalar_mul(stat[:, 1:2], stat[:, 1:2], -1.0).then_inc(b.sem("ln1r"), 1)
        b.on("vector", ln1r)

        ln_t = b.bump("ln1", 1)

        def ln1n(eng, tt=tt, t=lr_t, pv=b.semv.get("y1st", 0), pt=b.semv.get("y1tr", 0)):
            eng.wait_ge(b.sem("ln1r"), t)
            if pv:
                eng.wait_ge(b.sem("y1st"), pv)
            if pt:
                eng.wait_ge(b.sem("y1tr"), pt)
            eng.activation(out=y1bf[:], in_=y1t, func=AF.Identity, bias=stat[:, 1:2], scale=stat[:, 0:1]).then_inc(b.sem("ln1"), 1)
        b.on("scalar", ln1n)

        st_t = b.bump("y1st", 16)
        tr_t = b.bump("y1tr", 16 * 16)

        def y1st(eng, tt=tt, t=ln_t):
            eng.wait_ge(b.sem("ln1"), t)
            eng.dma_start(out=y1tm_d[tt * 128 : (tt + 1) * 128, :], in_=y1bf[:]).then_inc(b.sem("y1st"), 16)
        b.on("sync", y1st)

        def y1tr(eng, tt=tt, t=ln_t, base=tr_t - 16 * 16):
            eng.wait_ge(b.sem("ln1"), t)
            for ec in range(8):
                if ec > 0:
                    eng.wait_ge(b.sem("y1tr"), base + 16 * (2 * ec - 1))
                eng.dma_start_transpose(out=trsp[:], in_=y1bf[:, ec * 128 : (ec + 1) * 128]).then_inc(b.sem("y1tr"), 16)
                eng.wait_ge(b.sem("y1tr"), base + 16 * (2 * ec + 1))
                eng.dma_start(out=y1fm_d[ec * 128 : (ec + 1) * 128, tt * 128 : (tt + 1) * 128], in_=trsp[:]).then_inc(b.sem("y1tr"), 16)
        b.on("scalar", y1tr)

    if upto == "p6":
        return b
    y1_all = b.semv["y1st"]
    y1tr_all = b.semv["y1tr"]

    def q_boot(eng, t=y1_all):
        eng.wait_ge(b.sem("y1st"), t)
        eng.dma_start(out=afc_sb[0:16, :], in_=Afc_d).then_inc(b.sem("qb"), 16)
        eng.dma_start(out=aproj_sb[0:16, :], in_=Aproj_d).then_inc(b.sem("qb"), 16)
        eng.dma_start(out=wchunk[0:16, :], in_=BfcT_d).then_inc(b.sem("qb"), 16)
        eng.dma_start(out=scrA[0:16, 0:1024], in_=BprojT_d).then_inc(b.sem("qb"), 16)
        eng.dma_start(out=wchunk[32:33, :], in_=bfc_d).then_inc(b.sem("qb"), 16)
        eng.dma_start(out=scrA[0:1, 1024:2048], in_=bproj_d).then_inc(b.sem("qb"), 16)
    b.bump("qb", 96)
    b.on("sync", q_boot)

    def q_boot2(eng):
        eng.wait_ge(b.sem("qb"), 96)
        eng.tensor_scalar_mul(b2fc[0:16, :], wchunk[0:16, :], 2.0)
        eng.tensor_scalar_mul(b2pr[0:16, :], scrA[0:16, 0:1024], 2.0)
        eng.tensor_copy(afc_bf[0:16, :], afc_sb[0:16, :])
        eng.tensor_copy(aproj_bf[0:16, :], aproj_sb[0:16, :])
        eng.tensor_copy(bfc_bf[:], wchunk[32:33, :])
        eng.tensor_copy(bpr_bf[:], scrA[0:1, 1024:2048]).then_inc(b.sem("qb2"), 1)
    b.bump("qb2", 1)
    b.on("vector", q_boot2)

    gidx = 0
    for (name, src_d, nch, width) in (("fc", WfcT_d, 8, 4096), ("pr", WprojT_d, 32, 1024)):
        for c in range(nch):
            buf = wchunk if gidx % 2 == 0 else scrA
            ql_t = b.bump("ql", 16)

            def ql(eng, c=c, src_d=src_d, width=width, buf=buf, pv=b.semv.get("qm", 0)):
                eng.wait_ge(b.sem("qb2"), 1)
                if pv >= 2:
                    eng.wait_ge(b.sem("qm"), pv - 1)
                eng.dma_start(out=buf[:, 0:width], in_=src_d[c * 128 : (c + 1) * 128, :]).then_inc(b.sem("ql"), 16)
            b.on("sync" if gidx % 2 == 0 else "gpsimd", ql)

            qm_t = b.bump("qm", 1)

            def qm(eng, name=name, c=c, width=width, buf=buf, t=ql_t):
                eng.wait_ge(b.sem("ql"), t)
                col = 0 if name == "fc" else 1
                eng.tensor_reduce(stat[:, 2:3], buf[:, 0:width], axis=AX.X, op=ALU.max, apply_absolute_value=True)
                if c == 0:
                    eng.tensor_copy(qstat[:, col : col + 1], stat[:, 2:3]).then_inc(b.sem("qm"), 1)
                else:
                    eng.tensor_max(qstat[:, col : col + 1], qstat[:, col : col + 1], stat[:, 2:3]).then_inc(b.sem("qm"), 1)
            b.on("vector", qm)
            gidx += 1

    qm_all = b.semv["qm"]

    def qredmm(eng, t=qm_all):
        eng.wait_ge(b.sem("qm"), t)
        eng.matmul(f2ps_f[0][0:2, 0:128], qstat[:, 0:2], ident_sb[:], start=True, stop=True).then_inc(b.sem("qred"), 1)
    b.bump("qred", 1)
    b.on("tensor", qredmm)

    def qredv(eng):
        eng.wait_ge(b.sem("qred"), 1)
        eng.tensor_copy(sqt[0:2, 0:128], f2ps_f[0][0:2, 0:128])
        eng.reduce_max(stat[0:2, 0:1], sqt[0:2, 0:128], axis=AX.X).then_inc(b.sem("qred"), 1)
    b.bump("qred", 1)
    b.on("vector", qredv)

    def qredmm2(eng):
        eng.wait_ge(b.sem("qred"), 2)
        eng.matmul(f2ps_f[0][0:1, 0:2], stat[0:2, 0:1], ident_sb[0:2, 0:2], start=True, stop=True).then_inc(b.sem("qred"), 1)
    b.bump("qred", 1)
    b.on("tensor", qredmm2)

    def qredv2(eng):
        eng.wait_ge(b.sem("qred"), 3)
        eng.tensor_copy(stat[0:1, 2:4], f2ps_f[0][0:1, 0:2]).then_inc(b.sem("qred"), 1)
    b.bump("qred", 1)
    b.on("vector", qredv2)

    def qredmm3(eng):
        eng.wait_ge(b.sem("qred"), 4)
        eng.matmul(f2ps_f[0][:, 0:2], ones1[:], stat[0:1, 2:4], start=True, stop=True).then_inc(b.sem("qred"), 1)
    b.bump("qred", 1)
    b.on("tensor", qredmm3)

    def qredv3(eng):
        eng.wait_ge(b.sem("qred"), 5)
        eng.tensor_copy(qstat[:, 2:4], f2ps_f[0][:, 0:2]).then_inc(b.sem("qred"), 1)
    b.bump("qred", 1)
    b.on("vector", qredv3)

    def qrs(eng):
        eng.wait_ge(b.sem("qred"), 6)
        eng.reciprocal(qstat[:, 4:5], qstat[:, 2:3])
        eng.tensor_scalar_mul(qstat[:, 4:5], qstat[:, 4:5], 127.0)
        eng.reciprocal(qstat[:, 5:6], qstat[:, 3:4])
        eng.tensor_scalar_mul(qstat[:, 5:6], qstat[:, 5:6], 127.0)
        eng.tensor_scalar_mul(qstat[:, 6:7], qstat[:, 2:3], 1.0 / 127.0)
        eng.tensor_scalar_mul(qstat[:, 7:8], qstat[:, 3:4], 1.0 / 127.0)
        eng.tensor_scalar_mul(qstat[:, 0:1], qstat[:, 6:7], -1.0)
        eng.tensor_scalar_mul(qstat[:, 1:2], qstat[:, 7:8], -1.0).then_inc(b.sem("qrs"), 1)
    b.bump("qrs", 1)
    b.on("vector", qrs)

    for (name, src_d, dst_d, nch, width, A_sb, B_sb, rcol, ncol, amax_col) in (
        ("fc", WfcT_d, wefffc_d, 8, 4096, afc_bf, b2fc, 4, 0, 2),
        ("pr", WprojT_d, weffpr_d, 32, 1024, aproj_bf, b2pr, 5, 1, 3),
    ):
        nos = width // 1024
        for c in range(nch):
            cb = 0 if name == "fc" else (c % 2) * 1024
            qev_pre = b.semv.get("qev", 0)
            qst_pre = b.semv.get("qst", 0)
            ql2_t0 = b.semv.get("ql2", 0)
            for q in range(nos):
                b.bump("ql2", 16)

            def ql2(eng, name=name, c=c, src_d=src_d, width=width, nos=nos, cb=cb, qev_pre=qev_pre):
                eng.wait_ge(b.sem("qrs"), 1)
                for q in range(nos):
                    if name == "fc":
                        if qev_pre >= nos:
                            eng.wait_ge(b.sem("qev"), qev_pre - nos + q + 1)
                    else:
                        eng.wait_ge(b.sem("qev"), 32 if c < 2 else qev_pre - 1)
                    eng.dma_start(out=wchunk[:, cb + q * 1024 : cb + (q + 1) * 1024],
                                  in_=src_d[c * 128 : (c + 1) * 128, q * 1024 : (q + 1) * 1024]).then_inc(b.sem("ql2"), 16)
            b.on("sync" if c % 2 == 0 else "gpsimd", ql2)

            for osi in range(nos):
                qpar = b.semv.get("qlo", 0) % 2
                qpab = (qps_f, kps_f) if qpar == 0 else (s1ps_f[0], s1ps_f[1])
                lo_t = b.bump("qlo", 1)

                def qlo(eng, name=name, c=c, osi=osi, A_sb=A_sb, B_sb=B_sb, qpab=qpab, lt=ql2_t0 + 16 * (osi + 1), pv=b.semv.get("qev", 0)):
                    eng.wait_ge(b.sem("ql2"), lt)
                    if pv >= 2:
                        eng.wait_ge(b.sem("qev"), pv - 1)
                    eng.matmul(qpab[0][:], A_sb[0:16, c * 128 : (c + 1) * 128], B_sb[0:16, osi * 1024 : osi * 1024 + 512], start=True, stop=True)
                    eng.matmul(qpab[1][:], A_sb[0:16, c * 128 : (c + 1) * 128], B_sb[0:16, osi * 1024 + 512 : (osi + 1) * 1024], start=True, stop=True).then_inc(b.sem("qlo"), 1)
                b.on("tensor", qlo)

                qa_t = b.bump("qact", 1)

                def qact(eng, osi=osi, cb=cb, rcol=rcol, t=lo_t, pq=b.semv.get("qint", 0)):
                    sl = slice(cb + osi * 1024, cb + (osi + 1) * 1024)
                    eng.wait_ge(b.sem("qlo"), t)
                    if pq:
                        eng.wait_ge(b.sem("qint"), pq)
                    eng.activation(out=scrA[:, sl], in_=wchunk[:, sl], func=AF.Identity, bias=16384.0, scale=qstat[:, rcol : rcol + 1]).then_inc(b.sem("qact"), 1)
                b.on("scalar", qact)

                qi_t = b.bump("qint", 1)

                def qint(eng, osi=osi, cb=cb, t=qa_t):
                    sl = slice(cb + osi * 1024, cb + (osi + 1) * 1024)
                    eng.wait_ge(b.sem("qact"), t)
                    eng.tensor_copy(wchunkI[:, sl], scrA[:, sl])
                    eng.tensor_copy(wchunk[:, sl], wchunkI[:, sl])
                    eng.tensor_sub(scrA[:, sl], scrA[:, sl], wchunk[:, sl])
                    eng.tensor_single_scalar(scrA[:, sl], scrA[:, sl], 0.5, ALU.is_ge)
                    eng.tensor_add(scrA[:, sl], scrA[:, sl], wchunk[:, sl]).then_inc(b.sem("qint"), 1)
                b.on("vector", qint)

                qc_t = b.bump("qchain", 1)

                def qchain(eng, osi=osi, cb=cb, ncol=ncol, amax_col=amax_col, t=qi_t):
                    sl = slice(cb + osi * 1024, cb + (osi + 1) * 1024)
                    eng.wait_ge(b.sem("qint"), t)
                    eng.activation(out=wchunk[:, sl], in_=scrA[:, sl], func=AF.Relu, bias=-16256.0, scale=1.0)
                    eng.activation(out=scrA[:, sl], in_=wchunk[:, sl], func=AF.Relu, bias=255.0, scale=-1.0)
                    eng.activation(out=wchunk[:, sl], in_=scrA[:, sl], func=AF.Identity, bias=qstat[:, amax_col : amax_col + 1], scale=qstat[:, ncol : ncol + 1]).then_inc(b.sem("qchain"), 1)
                b.on("scalar", qchain)

                qe_t = b.bump("qev", 1)

                def qev(eng, name=name, c=c, osi=osi, cb=cb, qpab=qpab, t=qc_t, stp=qst_pre):
                    eng.wait_ge(b.sem("qchain"), t)
                    if osi == 0:
                        need = stp if (name == "fc" or c < 2) else stp - 16
                        if need > 0:
                            eng.wait_ge(b.sem("qst"), need)
                    eng.tensor_add(weffo[:, cb + osi * 1024 : cb + osi * 1024 + 512], wchunk[:, cb + osi * 1024 : cb + osi * 1024 + 512], qpab[0][:])
                    eng.tensor_add(weffo[:, cb + osi * 1024 + 512 : cb + (osi + 1) * 1024], wchunk[:, cb + osi * 1024 + 512 : cb + (osi + 1) * 1024], qpab[1][:]).then_inc(b.sem("qev"), 1)
                b.on("vector", qev)

            qs_t = b.bump("qst", 16)
            qe_all = b.semv["qev"]

            def qst(eng, c=c, cb=cb, dst_d=dst_d, width=width, t=qe_all):
                eng.wait_ge(b.sem("qev"), t)
                eng.dma_start(out=dst_d[c * 128 : (c + 1) * 128, :], in_=weffo[:, cb : cb + width]).then_inc(b.sem("qst"), 16)
            b.on("sync" if c % 2 == 0 else "gpsimd", qst)

    if upto == "quant":
        return b
    y2ps = [(s1ps_f[0], s1ps_f[1]), (s1ps_f[2], s1ps_f[3]), (f2ps_f[0], f2ps_f[1]), (qps_f, kps_f)]
    for t5 in range(8):
        yl_t = b.bump("fy1", 16 * 8)

        def fy1(eng, t5=t5, pv=b.semv.get("fgel", 0), qt=b.semv.get("qst", 0), ytr=y1tr_all):
            eng.wait_ge(b.sem("qst"), qt)
            eng.wait_ge(b.sem("y1tr"), ytr)
            if t5 > 0:
                eng.wait_ge(b.sem("fgel"), pv)
            for ic in range(8):
                eng.dma_start(out=y1f_all[:, ic * 512 : (ic + 1) * 512], in_=y1fm_d[ic * 128 : (ic + 1) * 128, t5 * 512 : (t5 + 1) * 512]).then_inc(b.sem("fy1"), 16)
        b.on("sync", fy1)

        for oc in range(32):
            j = oc % 2
            wb_t = b.bump(f"fwb{j}", 16)

            def fwb(eng, t5=t5, oc=oc, j=j, pv=b.semv.get("fmm", 0), qt=b.semv.get("qst", 0)):
                if t5 * 32 + oc < 2:
                    eng.wait_ge(b.sem("qst"), qt)
                else:
                    eng.wait_ge(b.sem("fmm"), pv - 8)
                src = wefffc_d.rearrange("(c p) o -> p c o", p=128)[:, :, oc * 128 : (oc + 1) * 128]
                eng.dma_start(out=wfc_col[j][:], in_=src).then_inc(b.sem(f"fwb{j}"), 16)
            b.on("sync" if j == 0 else "gpsimd", fwb)

            for ic in range(8):
                fm_t = b.bump("fmm", 1)

                def fmm(eng, t5=t5, oc=oc, ic=ic, j=j, wt=wb_t, yt=yl_t, gl=b.semv.get("fgel", 0), la=b.semv.get("ln2add", 0)):
                    if ic == 0:
                        eng.wait_ge(b.sem(f"fwb{j}"), wt)
                        eng.wait_ge(b.sem("fy1"), yt)
                        if oc == 0 and t5 > 0 and la:
                            eng.wait_ge(b.sem("ln2add"), la)
                        if t5 + oc > 0:
                            eng.wait_ge(b.sem("fgel"), gl)
                        eng.matmul(qps_f[:], bfc_bf[0:1, oc * 128 : (oc + 1) * 128], ones_bf[0:1, 0:512], start=True, stop=False)
                    eng.matmul(qps_f[:], wfc_col[j][:, ic * 128 : (ic + 1) * 128], y1f_all[:, ic * 512 : (ic + 1) * 512], start=False, stop=(ic == 7)).then_inc(b.sem("fmm"), 1)
                b.on("tensor", fmm)

            fg_t = b.bump("fgel", 1)

            def fgel(eng, t5=t5, oc=oc, t=b.semv["fmm"]):
                eng.wait_ge(b.sem("fmm"), t)
                eng.activation(out=gtile[:, oc * 512 : (oc + 1) * 512], in_=qps_f[:], func=AF.Gelu, bias=0.0, scale=1.0).then_inc(b.sem("fgel"), 1)
            b.on("scalar", fgel)

        fg_all = b.semv["fgel"]
        for oc in range(32):
            j = oc % 2
            wp_t = b.bump(f"fwp{j}", 16)

            def fwp(eng, t5=t5, oc=oc, j=j, pv=b.semv.get("f2mm", 0)):
                if t5 + oc > 0:
                    eng.wait_ge(b.sem("f2mm"), max(0, pv - 1))
                eng.dma_start(out=wpr_buf[j][:], in_=weffpr_d[oc * 128 : (oc + 1) * 128, :]).then_inc(b.sem(f"fwp{j}"), 16)
            b.on("scalar", fwp)

            f2_t = b.bump("f2mm", 1)

            def f2mm(eng, t5=t5, oc=oc, j=j, wt=wp_t, ga=fg_all, ln=b.semv.get("ln2add", 0)):
                eng.wait_ge(b.sem(f"fwp{j}"), wt)
                eng.wait_ge(b.sem("fgel"), ga)
                if oc == 0 and t5 > 0:
                    eng.wait_ge(b.sem("ln2add"), ln)
                for ts4 in range(4):
                    pa, pb = y2ps[ts4]
                    if oc == 0:
                        eng.matmul(pa[:], ones_bf[0:1, 0:128], bpr_bf[0:1, 0:512], start=True, stop=False)
                        eng.matmul(pb[:], ones_bf[0:1, 0:128], bpr_bf[0:1, 512:1024], start=True, stop=False)
                    g_blk = gtile[:, oc * 512 + ts4 * 128 : oc * 512 + (ts4 + 1) * 128]
                    eng.matmul(pa[:], g_blk, wpr_buf[j][:, 0:512], start=False, stop=(oc == 31))
                    i = eng.matmul(pb[:], g_blk, wpr_buf[j][:, 512:1024], start=False, stop=(oc == 31))
                i.then_inc(b.sem("f2mm"), 1)
            b.on("tensor", f2mm)

        f2_all = b.semv["f2mm"]
        for ts4 in range(4):
            rl_t = b.bump("fyr", 16)

            def fyr(eng, t5=t5, ts4=ts4, pv=b.semv.get("ln2add", 0)):
                if t5 + ts4 > 0:
                    eng.wait_ge(b.sem("ln2add"), pv)
                r0 = t5 * 512 + ts4 * 128
                eng.dma_start(out=y1res[:], in_=y1tm_d[r0 : r0 + 128, :]).then_inc(b.sem("fyr"), 16)
            b.on("sync", fyr)

            l2a_t = b.bump("ln2add", 1)

            def ln2a(eng, t5=t5, ts4=ts4, t=f2_all, rt=rl_t):
                eng.wait_ge(b.sem("f2mm"), t)
                eng.wait_ge(b.sem("fyr"), rt)
                pa, pb = y2ps[ts4]
                eng.tensor_add(y2t[:, 0:512], y1res[:, 0:512], pa[:])
                eng.tensor_add(y2t[:, 512:1024], y1res[:, 512:1024], pb[:]).then_inc(b.sem("ln2add"), 1)
            b.on("vector", ln2a)

            l2s_t = b.bump("ln2s", 1)

            def ln2s(eng, t=l2a_t):
                eng.wait_ge(b.sem("ln2add"), t)
                eng.tensor_tensor(sq2, y2t, y2t, ALU.mult)
                eng.reduce_sum(stat[:, 0:1], y2t, axis=AX.X)
                eng.reduce_sum(stat[:, 1:2], sq2, axis=AX.X)
                eng.tensor_scalar_mul(stat[:, 2:3], stat[:, 0:1], 1.0 / 1024.0)
                eng.tensor_scalar_mul(stat[:, 3:4], stat[:, 1:2], 1.0 / 1024.0)
                eng.tensor_tensor(stat[:, 4:5], stat[:, 2:3], stat[:, 2:3], ALU.mult)
                eng.tensor_sub(stat[:, 5:6], stat[:, 3:4], stat[:, 4:5])
                eng.tensor_scalar_add(stat[:, 6:7], stat[:, 5:6], 1e-5).then_inc(b.sem("ln2s"), 1)
            b.on("vector", ln2s)

            l2q_t = b.bump("ln2q", 1)

            def ln2q(eng, t=l2s_t):
                eng.wait_ge(b.sem("ln2s"), t)
                eng.activation(out=stat[:, 7:8], in_=stat[:, 6:7], func=AF.Sqrt, bias=0.0, scale=1.0).then_inc(b.sem("ln2q"), 1)
            b.on("scalar", ln2q)

            l2r_t = b.bump("ln2r", 1)

            def ln2r(eng, t=l2q_t):
                eng.wait_ge(b.sem("ln2q"), t)
                eng.reciprocal(stat[:, 0:1], stat[:, 7:8])
                eng.tensor_tensor(stat[:, 1:2], stat[:, 2:3], stat[:, 0:1], ALU.mult)
                eng.tensor_scalar_mul(stat[:, 1:2], stat[:, 1:2], -1.0).then_inc(b.sem("ln2r"), 1)
            b.on("vector", ln2r)

            l2n_t = b.bump("ln2", 1)

            def ln2n(eng, t=l2r_t, ot=b.semv.get("outst", 0)):
                eng.wait_ge(b.sem("ln2r"), t)
                if ot:
                    eng.wait_ge(b.sem("outst"), ot)
                eng.activation(out=out_tile, in_=y2t, func=AF.Identity, bias=stat[:, 1:2], scale=stat[:, 0:1]).then_inc(b.sem("ln2"), 1)
            b.on("scalar", ln2n)

            os_t = b.bump("outst", 16)

            def outst(eng, t5=t5, ts4=ts4, t=l2n_t):
                eng.wait_ge(b.sem("ln2"), t)
                r0 = t5 * 512 + ts4 * 128
                eng.dma_start(out=out_d[r0 : r0 + 128, :], in_=out_tile).then_inc(b.sem("outst"), 16)
            b.on("sync", outst)

    out_all = b.semv["outst"]

    def fin(eng, t=out_all):
        eng.wait_ge(b.sem("outst"), t)
    b.on("gpsimd", fin)


    if upto == "agg" and p5_stop is not None:
        return b
    if upto == "agg":
        agg_out = b.dram_out("agg_dbg", [1024, 4096], BF16)
        bounce = pool[:, 0:4096].bitcast(BF16)[:, 0:4096]

        def aggout(eng, t=b.semv["aggc"]):
            eng.wait_ge(b.sem("aggc"), t)
            n = 0
            for c in range(8):
                eng.dma_start(out=bounce, in_=aggfm_d[c * 128 : (c + 1) * 128, :]).then_inc(b.sem("outd3"), 16)
                n += 16
                eng.wait_ge(b.sem("outd3"), n)
                eng.dma_start(out=agg_out[c * 128 : (c + 1) * 128, :], in_=bounce).then_inc(b.sem("outd3"), 16)
                n += 16
                eng.wait_ge(b.sem("outd3"), n)
        b.bump("outd3", 16 * 16)
        b.on("gpsimd", aggout)
        return b

    return b




_BUILD_CACHE = {}


def _get_build():
    if "b" not in _BUILD_CACHE:
        b = build_p123(upto="full", use_cc=True)
        nc = b.finalize()
        _BUILD_CACHE["b"] = b
        _BUILD_CACHE["nc"] = nc
    return _BUILD_CACHE["b"], _BUILD_CACHE["nc"]


def kernel(**inputs):
    from concourse.bass_utils import run_bass_kernel_spmd

    b, nc = _get_build()
    in_maps = []
    for core in range(8):
        m = host_prep_attn(inputs, core)
        m.update(b.const_arrays)
        in_maps.append(m)
    res = run_bass_kernel_spmd(nc, in_maps, core_ids=list(range(8)))
    out = np.stack([np.asarray(res.results[c]["out"]) for c in range(8)], axis=0)
    return out.astype(np.float32)
